# version 07
# speedup vs baseline: 1.1310x; 1.0258x over previous
import contextlib
from collections import defaultdict

import numpy as np
import concourse.bass as bass
import concourse.mybir as mybir

F32 = mybir.dt.float32
BF16 = mybir.dt.bfloat16
I32 = mybir.dt.int32
AF = mybir.ActivationFunctionType
ALU = mybir.AluOpType
AX = mybir.AxisListType

ALL = "__all__"

ENG_SECTION = {"pe": "tensor", "dve": "vector", "act": "scalar",
               "pool": "gpsimd", "sp": "sync"}


def rk(name, sub=ALL):
    return (name, sub)


class Prog:

    def __init__(self, nc, n_sp_dma_sems=6, n_pool_dma_sems=4):
        self.nc = nc
        self.stack = contextlib.ExitStack()
        self.ops = []
        self.sems = {}
        self._sem_order = []
        for e in ("pe", "dve", "act", "pool"):
            self._mk_sem(e)
        self.dma_sems = {"sp": [], "pool": []}
        for i in range(n_sp_dma_sems):
            self.dma_sems["sp"].append(self._mk_sem(f"dsp{i}"))
        for i in range(n_pool_dma_sems):
            self.dma_sems["pool"].append(self._mk_sem(f"dpl{i}"))
        self._dma_rr = {"sp": 0, "pool": 0}
        self.cum = defaultdict(int)
        self.last_writer = {}
        self.readers = defaultdict(list)
        self.name_keys = defaultdict(set)
        self.clock = defaultdict(dict)
        self.snap = {}
        self.eng_inc_ops = defaultdict(list)
        self.buffers = {}
        self.scopes = []
        self.pending_fence = {}
        nums = [self.sems[n].num for n in self._sem_order]
        assert nums == list(range(nums[0], nums[0] + len(nums))), nums
        self._sem_lo, self._sem_hi = nums[0], nums[-1]

    def _mk_sem(self, name):
        h = self.stack.enter_context(self.nc.semaphore(name))
        self.sems[name] = h
        self._sem_order.append(name)
        return name

    def sbuf(self, name, shape, dtype):
        st = self.scopes[-1] if self.scopes else self.stack
        h = st.enter_context(self.nc.sbuf_tensor(name, list(shape), dtype))
        self.buffers[name] = h
        return h

    def push_scope(self):
        self.scopes.append(contextlib.ExitStack())

    def pop_scope(self):
        self.scopes.pop().close()

    def fence(self):
        for s in self._sem_order:
            if self.cum[s] > 0:
                self.pending_fence[s] = self.cum[s]

    def psum(self, name, shape, dtype=F32):
        h = self.stack.enter_context(self.nc.psum_tensor(name, list(shape), dtype))
        self.buffers[name] = h
        return h

    def _conflict_keys(self, key):
        name, sub = key
        subs = self.name_keys.get(name)
        if not subs:
            return ()
        if sub == ALL:
            return [(name, s) for s in subs]
        out = []
        if sub in subs:
            out.append((name, sub))
        if ALL in subs:
            out.append((name, ALL))
        return out

    def op(self, engine, emit, reads=(), writes=(), sem=None, inc=1, tag=""):
        idx = len(self.ops)
        sem = sem or engine
        deps = set()
        for r in reads:
            for ck in self._conflict_keys(r):
                if ck in self.last_writer:
                    deps.add(self.last_writer[ck])
        for w in writes:
            for ck in self._conflict_keys(w):
                if ck in self.last_writer:
                    deps.add(self.last_writer[ck])
                for rd in self.readers.get(ck, ()):
                    deps.add(rd)
        deps.discard(idx)

        clk = self.clock[engine]
        need = {}
        for d in sorted(deps):
            dop = self.ops[d]
            if dop["engine"] == "pe" and engine == "pe":
                continue
            if dop["inc"] > 0:
                s, v = dop["sem"], dop["cum"]
            else:
                cover = None
                for (ci, cs, cv) in self.eng_inc_ops[dop["engine"]]:
                    if ci > d:
                        cover = (ci, cs, cv)
                        break
                assert cover is not None and cover[0] < idx, (
                    f"op {d} ({dop['tag']}) has inc=0 and no covering inc-op "
                    f"before consumer {idx} ({tag})")
                s, v = cover[1], cover[2]
            if clk.get(s, 0) >= v:
                continue
            if need.get(s, 0) < v:
                need[s] = v

        for s, v in self.pending_fence.items():
            if clk.get(s, 0) < v and need.get(s, 0) < v:
                need[s] = v

        if inc == 16:
            prev = self.cum[sem]
            if prev > 0 and clk.get(sem, 0) < prev and need.get(sem, 0) < prev:
                need[sem] = prev

        waits = sorted(need.items())
        for s, v in waits:
            if clk.get(s, 0) < v:
                clk[s] = v
        for d in deps:
            snap = self.snap.get(d)
            if snap:
                for s, v in snap.items():
                    if clk.get(s, 0) < v:
                        clk[s] = v

        self.cum[sem] += inc
        o = dict(idx=idx, engine=engine, emit=emit, sem=sem, inc=inc,
                 cum=self.cum[sem], waits=waits, tag=tag)
        self.ops.append(o)
        if inc > 0:
            self.eng_inc_ops[engine].append((idx, sem, self.cum[sem]))
        sn = dict(clk)
        sn[sem] = max(sn.get(sem, 0), self.cum[sem])
        self.snap[idx] = sn
        for r in reads:
            self.name_keys[r[0]].add(r[1])
            self.readers[r].append(idx)
        for w in writes:
            self.name_keys[w[0]].add(w[1])
            self.last_writer[w] = idx
            self.readers[w] = []
        return idx

    def dma(self, queue, out, in_, reads=(), writes=(), tag="", **kw):
        sems = self.dma_sems[queue]
        sem = sems[self._dma_rr[queue] % len(sems)]
        self._dma_rr[queue] += 1

        def emit(eng, out=out, in_=in_, kw=kw):
            return eng.dma_start(out=out, in_=in_, **kw)
        return self.op(queue, emit, reads=reads, writes=writes, sem=sem,
                       inc=16, tag=tag)

    def build(self):
        per_engine = defaultdict(list)
        for o in self.ops:
            per_engine[o["engine"]].append(o)
        sems = self.sems
        blk = self.nc.Block()
        block = blk.__enter__()
        for ename in ("sp", "pool", "pe", "dve", "act"):
            ops = per_engine.get(ename, [])
            section = getattr(block, ENG_SECTION[ename])

            def body(eng, _ops=ops):
                for o in _ops:
                    for s, v in o["waits"]:
                        eng.wait_ge(sems[s], v)
                    inst = o["emit"](eng)
                    if o["inc"] and inst is not None:
                        inst.then_inc(sems[o["sem"]], o["inc"])
            section(body)
        finals = [(s, self.cum[s]) for s in self._sem_order if self.cum[s] > 0]

        def tail(eng, finals=finals):
            for s, v in finals:
                if s != "pool":
                    eng.wait_ge(sems[s], v)
        block.gpsimd(tail)
        blk.__exit__(None, None, None)
        self.nc.gpsimd.sem_clear(range(self._sem_lo, self._sem_hi + 1))

    def close(self):
        self.stack.close()



class Common:

    def __init__(self, p: Prog):
        nc = p.nc
        self.p = p
        self.ident_bf = p.sbuf("c_identbf", [128, 128], BF16)
        self.mask_keepge = p.sbuf("c_maskge", [128, 128], BF16)
        self.maskadd = p.sbuf("c_maskadd", [128, 128], BF16)
        self.iota = p.sbuf("c_iota", [128, 128], I32)
        self.ones_col_bf = p.sbuf("c_ones_col", [128, 1], BF16)
        self.ones_row_f = p.sbuf("c_ones_row", [1, 128], F32)
        self.ones_row_bf = p.sbuf("c_ones_row_bf", [1, 128], BF16)
        self.eps_col = p.sbuf("c_eps", [128, 1], F32)
        self.zero_col = p.sbuf("c_zero", [128, 1], F32)
        self.tiny = p.sbuf("c_tiny", [128, 1], F32)

        p.op("pool", lambda e: e.iota(self.iota[:], pattern=[[-1, 128]], base=0,
                                      channel_multiplier=1),
             writes=[rk("c_iota")], tag="iota")
        p.op("dve", lambda e: e.tensor_scalar(self.ident_bf[:], self.iota[:],
                                              0.0, None, ALU.is_equal),
             reads=[rk("c_iota")], writes=[rk("c_identbf")], tag="ident")
        p.op("dve", lambda e: e.tensor_scalar(self.mask_keepge[:], self.iota[:],
                                              0.0, None, ALU.is_le),
             reads=[rk("c_iota")], writes=[rk("c_maskge")], tag="maskge")
        p.op("dve", lambda e: e.tensor_scalar(self.maskadd[:], self.iota[:],
                                              0.0, -1e30, ALU.is_gt, ALU.mult),
             reads=[rk("c_iota")], writes=[rk("c_maskadd")], tag="maskadd")
        p.op("dve", lambda e: e.memset(self.ones_col_bf[:], 1.0),
             writes=[rk("c_ones_col")])
        p.op("dve", lambda e: e.memset(self.ones_row_f[:], 1.0),
             writes=[rk("c_ones_row")])
        p.op("dve", lambda e: e.memset(self.ones_row_bf[:], 1.0),
             writes=[rk("c_ones_row_bf")])
        p.op("dve", lambda e: e.memset(self.eps_col[:], 1e-5),
             writes=[rk("c_eps")])
        p.op("dve", lambda e: e.memset(self.zero_col[:], 0.0),
             writes=[rk("c_zero")])
        p.op("dve", lambda e: e.memset(self.tiny[:], 1e-30),
             writes=[rk("c_tiny")])



def emit_ln(p: Prog, com: Common, geo, src_buf, src_name, dst_buf, dst_name,
            *, affine=None, fuse_double=False, out_tm=None, out_tm_name=None,
            scratch=None, tilesel=None, keep_tile0=None, out_stats=None):
    NT, T = geo["NT"], geo["T"]
    sc = scratch
    tiles = list(range(NT) if tilesel is None else tilesel)
    mv = sc["mv_all"]

    for ti in tiles:
        par = ti % 2
        bn = sc[par]["bn"]
        P_ = str(par)
        xsl = src_buf[:, ti * 1024: ti * 1024 + 1024]
        rx = rk(src_name, ti)
        p.op("dve", lambda e, a=xsl, bn=bn: e.bn_stats(bn[:, 0:6], a[:, 0:512]),
             reads=[rx], writes=[rk("s_bn" + P_)], tag=f"bns{ti}")
        p.op("dve", lambda e, a=xsl, bn=bn: e.bn_stats(bn[:, 6:12], a[:, 512:1024]),
             reads=[rx], writes=[rk("s_bn2" + P_)], tag=f"bns2{ti}")
        p.op("dve", lambda e, ti=ti, bn=bn: e.bn_aggr(
                mv[:, 2 * ti:2 * ti + 2], bn[:]),
             reads=[rk("s_bn" + P_), rk("s_bn2" + P_)], writes=[rk("s_mv", ti)],
             tag=f"bna{ti}")

    mvk = [rk("s_mv", ti) for ti in tiles]
    mvv = mv[:, 0:2 * NT].rearrange("p (t two) -> p two t", two=2)
    var_cols = mvv[:, 1:2, :].squeeze(1)
    m_cols = mvv[:, 0:1, :].squeeze(1)
    r0 = sc["r0_all"]
    rho = sc["rho_all"]
    nm = sc["nm_all"]

    p.op("act", lambda e: e.activation(r0[:, 0:NT], var_cols, AF.Ln,
                                       bias=com.eps_col[:, 0:1], scale=1.0),
         reads=mvk + [rk("c_eps")], writes=[rk("s_r0a")], tag="lnv")
    p.op("act", lambda e: e.activation(r0[:, 0:NT], r0[:, 0:NT], AF.Exp,
                                       bias=com.zero_col[:, 0:1], scale=-0.5),
         reads=[rk("s_r0a"), rk("c_zero")], writes=[rk("s_r0a")], tag="exv")
    if fuse_double:
        p.op("dve", lambda e: e.tensor_tensor(rho[:, 0:NT], r0[:, 0:NT],
                                              r0[:, 0:NT], ALU.mult),
             reads=[rk("s_r0a")], writes=[rk("s_rhoa")], tag="fd1")
        p.op("dve", lambda e: e.tensor_tensor(rho[:, 0:NT], rho[:, 0:NT],
                                              var_cols, ALU.mult),
             reads=[rk("s_rhoa")] + mvk, writes=[rk("s_rhoa")], tag="fd2")
        p.op("act", lambda e: e.activation(rho[:, 0:NT], rho[:, 0:NT], AF.Ln,
                                           bias=com.eps_col[:, 0:1], scale=1.0),
             reads=[rk("s_rhoa"), rk("c_eps")], writes=[rk("s_rhoa")], tag="fd3")
        p.op("act", lambda e: e.activation(rho[:, 0:NT], rho[:, 0:NT], AF.Exp,
                                           bias=com.zero_col[:, 0:1], scale=-0.5),
             reads=[rk("s_rhoa"), rk("c_zero")], writes=[rk("s_rhoa")], tag="fd4")
        p.op("dve", lambda e: e.tensor_tensor(rho[:, 0:NT], rho[:, 0:NT],
                                              r0[:, 0:NT], ALU.mult),
             reads=[rk("s_rhoa"), rk("s_r0a")], writes=[rk("s_rhoa")], tag="fd5")
        rhok = "s_rhoa"
    else:
        rho = r0
        rhok = "s_r0a"
    p.op("dve", lambda e, rho=rho: e.tensor_tensor(nm[:, 0:NT], m_cols,
                                                   rho[:, 0:NT], ALU.mult),
         reads=mvk + [rk(rhok)], writes=[rk("s_nma")], tag="nm1")
    p.op("dve", lambda e: e.tensor_scalar(nm[:, 0:NT], nm[:, 0:NT], -1.0, None,
                                          ALU.mult),
         reads=[rk("s_nma")], writes=[rk("s_nma")], tag="nm2")
    if out_stats is not None:
        stb, stk = out_stats
        st2 = stb[:].rearrange("p (t two) -> p two t", two=2)
        p.op("dve", lambda e, st2=st2: e.tensor_copy(
                st2[:, 0:1, 0:NT].squeeze(1), r0[:, 0:NT]),
             reads=[rk("s_r0a")], writes=[rk(stk, "r")], tag="str")
        p.op("dve", lambda e, st2=st2: e.tensor_tensor(
                st2[:, 1:2, 0:NT].squeeze(1), m_cols, r0[:, 0:NT], ALU.mult),
             reads=mvk + [rk("s_r0a")], writes=[rk(stk, "n")], tag="stn1")
        p.op("dve", lambda e, st2=st2: e.tensor_scalar(
                st2[:, 1:2, 0:NT].squeeze(1), st2[:, 1:2, 0:NT].squeeze(1),
                -1.0, None, ALU.mult),
             reads=[rk(stk, "n")], writes=[rk(stk, "n")], tag="stn2")

    if out_tm is not None:
        nm0 = sc["nm0_all"]
        p.op("dve", lambda e: e.tensor_tensor(nm0[:, 0:NT], m_cols,
                                              r0[:, 0:NT], ALU.mult),
             reads=mvk + [rk("s_r0a")], writes=[rk("s_nm0a")], tag="ynm1")
        p.op("dve", lambda e: e.tensor_scalar(nm0[:, 0:NT], nm0[:, 0:NT],
                                              -1.0, None, ALU.mult),
             reads=[rk("s_nm0a")], writes=[rk("s_nm0a")], tag="ynm2")

    ps6h = sc["ps6"]
    pst6 = (lambda a, b: ps6h[:, a // 2: b // 2].bitcast(BF16)) \
        if ps6h is not None else None
    for ti in tiles:
        par = ti % 2
        P_ = str(par)
        xsl = src_buf[:, ti * 1024: ti * 1024 + 1024]
        rx = rk(src_name, ti)
        if out_tm is not None:
            ysl = out_tm[:, ti * 1024: ti * 1024 + 1024]
            p.op("dve", lambda e, a=xsl, y=ysl, ti=ti, sc=sc: e.tensor_scalar(
                    y, a, r0[:, ti:ti + 1], sc["nm0_all"][:, ti:ti + 1],
                    ALU.mult, ALU.add),
                 reads=[rx, rk("s_r0a"), rk("s_nm0a")],
                 writes=[rk(out_tm_name, ti)], tag=f"y{ti}")
        if keep_tile0 is not None and ti == 0:
            nbf_buf, nbf_key = keep_tile0
        else:
            nbf_buf, nbf_key = sc[par]["nbf"], "s_nbf" + P_
        tb = sc["pst"] if (par == 1 or pst6 is None) else pst6
        tbk = "ps7" if (par == 1 or pst6 is None) else "ps6"
        for hh in range(2):
            p.op("dve", lambda e, a=xsl, ti=ti, nb=nbf_buf, rho=rho, hh=hh:
                 e.tensor_scalar(
                    nb[:, hh * 512:(hh + 1) * 512],
                    a[:, hh * 512:(hh + 1) * 512],
                    rho[:, ti:ti + 1], nm[:, ti:ti + 1], ALU.mult, ALU.add),
                 reads=[rx, rk(rhok), rk("s_nma")],
                 writes=[rk(nbf_key, hh)], tag=f"nbf{ti}.{hh}")
            for j in range(4 * hh, 4 * hh + 4):
                p.op("pe", lambda e, j=j, nb=nbf_buf, tb=tb: e.transpose(
                        tb(j * 128, (j + 1) * 128),
                        nb[:, j * 128:(j + 1) * 128], com.ident_bf[:]),
                     reads=[rk(nbf_key, hh), rk("c_identbf")],
                     writes=[rk(tbk)], inc=(1 if j == 4 * hh + 3 else 0),
                     tag=f"tr{ti}.{j}")
        if affine is not None:
            wc, bc, wname, bname = affine
            for j in range(8):
                if j % 2 == 0:
                    p.op("dve", lambda e, j=j, ti=ti, tb=tb: e.tensor_scalar(
                            dst_buf[:, j * T + ti * 128: j * T + ti * 128 + 128],
                            tb(j * 128, (j + 1) * 128),
                            wc[:, j:j + 1], bc[:, j:j + 1], ALU.mult, ALU.add),
                         reads=[rk(tbk), rk(wname), rk(bname)],
                         writes=[rk(dst_name, ti)], tag=f"aff{ti}.{j}")
                else:
                    p.op("act", lambda e, j=j, ti=ti, tb=tb: e.activation(
                            dst_buf[:, j * T + ti * 128: j * T + ti * 128 + 128],
                            tb(j * 128, (j + 1) * 128), AF.Identity,
                            bias=bc[:, j:j + 1], scale=wc[:, j:j + 1]),
                         reads=[rk(tbk), rk(wname), rk(bname)],
                         writes=[rk(dst_name, ti)], tag=f"affa{ti}.{j}")
        else:
            dst3 = dst_buf[:].rearrange("p (c t) -> p c t", c=8)
            p.op("act", lambda e, ti=ti, d3=dst3, tb=tb: e.activation(
                    d3[:, :, ti * 128: ti * 128 + 128],
                    tb(0, 1024).rearrange("p (c t) -> p c t", c=8),
                    AF.Copy, bias=0.0, scale=1.0),
                 reads=[rk(tbk)],
                 writes=[rk(dst_name, ti)], tag=f"ev{ti}")


def ln_scratch(p: Prog, ps7=None, ps6=None):
    sc = {}
    for par in (0, 1):
        sc[par] = {
            "bn": p.sbuf(f"s_bn{par}", [128, 12], F32),
            "nbf": p.sbuf(f"s_nbf{par}", [128, 1024], BF16),
        }
    sc["mv_all"] = p.sbuf("s_mv_all", [128, 32], F32)
    sc["r0_all"] = p.sbuf("s_r0_all", [128, 16], F32)
    sc["rho_all"] = p.sbuf("s_rho_all", [128, 16], F32)
    sc["nm_all"] = p.sbuf("s_nm_all", [128, 16], F32)
    sc["nm0_all"] = p.sbuf("s_nm0_all", [128, 16], F32)
    assert ps7 is not None
    sc["ps7"] = ps7
    sc["ps6"] = ps6
    sc["pst"] = lambda a, b: ps7[:, a // 2: b // 2].bitcast(BF16)
    return sc



def emit_mha(p: Prog, com: Common, geo, B, wd, *, biased, name=""):
    NT, T = geo["NT"], geo["T"]
    NW = T // 512
    sc = B["lnsc"]
    n = name
    psb = B["psb"]
    psbn = B["psb_name"]
    psm, psmn = B["ps_misc"], B["ps_misc_name"]
    vx4 = B["v_ext"][:].rearrange("p (t h d) -> p t h d", h=2, d=65)

    emit_ln(p, com, geo, B["x"], "x_res", B["yT"], B["yT_name"], scratch=sc)

    if biased:
        p.dma("sp", B["bv_row"][:], wd["bv"], writes=[rk("bv_row")], tag=n + "bv")
        p.dma("sp", B["bqF"][:], wd["bq"], writes=[rk("bqF")])
        p.dma("sp", B["bkF"][:], wd["bk"], writes=[rk("bkF")])
        for h2 in range(2):
            p.op("pe", lambda e, h2=h2: e.matmul(
                    psm[:, 0:512], com.ones_row_f[0:1, :],
                    B["bv_row"][0:1, h2 * 512:(h2 + 1) * 512],
                    start=True, stop=True),
                 reads=[rk("bv_row"), rk("c_ones_row")], writes=[rk(psmn)],
                 tag=n + f"bvb{h2}")
            p.op("act", lambda e, h2=h2: e.activation(
                    B["bvb"][:, h2 * 512:(h2 + 1) * 512], psm[:, 0:512],
                    AF.Copy, bias=0.0, scale=1.0),
                 reads=[rk(psmn)], writes=[rk("bvb")], tag=n + f"bvbe{h2}")

    p.op("dve", lambda e: e.memset(vx4[:, :, :, 64:65], 1.0),
         writes=[rk("v_ext")], tag=n + "vones")

    for hp in range(8):
        for i, wn in enumerate(("wq", "wk", "wv")):
            p.dma("pool", B["wpair"][:, i * 1024:(i + 1) * 1024],
                  wd[wn][:, hp * 1024:(hp + 1) * 1024],
                  writes=[rk("wpair", wn)], tag=n + f"ld{wn}{hp}")

        for i, wn in enumerate(("wq", "wk", "wv")):
            for w in range(NW):
                bank, bname = psb[6], psbn[6]
                for ch in range(8):
                    p.op("pe", lambda e, i=i, w=w, ch=ch, bank=bank: e.matmul(
                            bank[:, 0:512],
                            B["wpair"][:, i * 1024 + ch * 128:
                                       i * 1024 + (ch + 1) * 128],
                            B["yT"][:, ch * T + w * 512: ch * T + w * 512 + 512],
                            start=(ch == 0), stop=(ch == 7)),
                         reads=[rk("wpair", wn), rk(B["yT_name"], ALL)],
                         writes=[rk(bname)], inc=(1 if ch == 7 else 0),
                         tag=n + f"prj{wn}{hp}.{w}.{ch}")
                if wn == "wv":
                    p.op("act", lambda e, w=w, bank=bank: e.activation(
                            B["vT"][:, w * 512:(w + 1) * 512], bank[:, 0:512],
                            AF.Copy, bias=0.0, scale=1.0),
                         reads=[rk(bname)], writes=[rk("vT", w)],
                         tag=n + f"vTev{hp}.{w}")
                else:
                    dstb = B["qT"] if wn == "wq" else B["kT"]
                    dname = "qT" if wn == "wq" else "kT"
                    if biased:
                        bcol = B["bqF"] if wn == "wq" else B["bkF"]
                        bnm = "bqF" if wn == "wq" else "bkF"
                        p.op("dve", lambda e, w=w, bank=bank, bcol=bcol,
                             dstb=dstb: e.tensor_scalar(
                                dstb[:, w * 512:(w + 1) * 512], bank[:, 0:512],
                                bcol[:, hp:hp + 1], None, ALU.add),
                             reads=[rk(bname), rk(bnm)],
                             writes=[rk(dname, w)], tag=n + f"{dname}ev{hp}.{w}")
                    else:
                        p.op("act", lambda e, w=w, bank=bank, dstb=dstb:
                             e.activation(dstb[:, w * 512:(w + 1) * 512],
                                          bank[:, 0:512], AF.Copy,
                                          bias=0.0, scale=1.0),
                             reads=[rk(bname)], writes=[rk(dname, w)],
                             tag=n + f"{dname}ev{hp}.{w}")

        for ti in range(NT):
            p.op("pe", lambda e, ti=ti: e.transpose(
                    sc["pst"](0, 128), B["vT"][:, ti * 128:(ti + 1) * 128],
                    com.ident_bf[:]),
                 reads=[rk("vT", ti // (NT // NW)), rk("c_identbf")],
                 writes=[rk("ps7")], tag=n + f"vtr{hp}.{ti}")
            dst = vx4[:, ti:ti + 1, :, 0:64]
            src = sc["pst"](0, 128).rearrange("p (h d) -> p h d", h=2
                                              ).unsqueeze(1)
            if biased:
                bsl = B["bvb"][:, hp * 128:(hp + 1) * 128].rearrange(
                    "p (h d) -> p h d", h=2).unsqueeze(1)
                p.op("dve", lambda e, dst=dst, src=src, bsl=bsl: e.tensor_tensor(
                        dst, src, bsl, ALU.add),
                     reads=[rk("ps7"), rk("bvb")], writes=[rk("v_ext")],
                     tag=n + f"vev{hp}.{ti}")
            else:
                p.op("dve", lambda e, dst=dst, src=src: e.tensor_copy(dst, src),
                     reads=[rk("ps7")], writes=[rk("v_ext")],
                     tag=n + f"vev{hp}.{ti}")

        p.op("dve", lambda e: e.memset(B["gmax"][:], 0.0), writes=[rk("gmax")],
             tag=n + f"gz{hp}")
        sqv = B["vT"][:, 0:T]
        for ci, (which, buf, wname) in enumerate(
                (("q", B["qT"], "qT"), ("k", B["kT"], "kT"))):
            p.op("dve", lambda e, buf=buf, sqv=sqv: e.tensor_tensor(
                    sqv, buf[:, 0:T], buf[:, 0:T], ALU.mult),
                 reads=[rk(wname, w2) for w2 in range(NW)],
                 writes=[rk("vT", ALL)], tag=n + f"sq{which}{hp}")
            for w in range(NW):
                p.op("pe", lambda e, w=w, sqv=sqv: e.matmul(
                        psm[0:2, 0:512], B["blk2"][:],
                        sqv[:, w * 512:(w + 1) * 512], start=True, stop=True),
                     reads=[rk("vT", ALL), rk("blk2")], writes=[rk(psmn)],
                     tag=n + f"nrm{which}{hp}.{w}")
                p.op("dve", lambda e: e.tensor_reduce(
                        B["n2"][:], psm[0:2, 0:512], AX.X, ALU.max),
                     reads=[rk(psmn)], writes=[rk("n2")],
                     tag=n + f"nrd{which}{hp}.{w}")
                p.op("dve", lambda e, ci=ci: e.tensor_tensor(
                        B["gmax"][:, ci:ci + 1], B["gmax"][:, ci:ci + 1],
                        B["n2"][:], ALU.max),
                     reads=[rk("n2"), rk("gmax")], writes=[rk("gmax")],
                     tag=n + f"nmx{which}{hp}.{w}")
        p.op("dve", lambda e: e.tensor_tensor(
                B["cb"][0:2, 0:1], B["gmax"][0:2, 0:1], B["gmax"][0:2, 1:2],
                ALU.mult),
             reads=[rk("gmax")], writes=[rk("cb")], tag=n + f"cbm{hp}")
        p.op("act", lambda e: e.activation(B["cb"][0:2, 0:1], B["cb"][0:2, 0:1],
                                           AF.Ln, bias=B["tiny"][0:2, 0:1],
                                           scale=1.0),
             reads=[rk("cb"), rk("tiny")], writes=[rk("cb")], tag=n + f"cbl{hp}")
        p.op("act", lambda e: e.activation(B["cb"][0:2, 0:1], B["cb"][0:2, 0:1],
                                           AF.Exp, bias=B["lnsm"][0:2, 0:1],
                                           scale=0.5),
             reads=[rk("cb"), rk("lnsm")], writes=[rk("cb")], tag=n + f"cbx{hp}")
        p.op("dve", lambda e: e.tensor_scalar(B["cb"][0:2, 0:1], B["cb"][0:2, 0:1],
                                              -1.0, None, ALU.mult),
             reads=[rk("cb")], writes=[rk("cb")], tag=n + f"cbn{hp}")
        p.dma("pool", B["cb2"][0:1, 0:1], B["cb"][1:2, 0:1], reads=[rk("cb")],
              writes=[rk("cb2")], tag=n + f"cbmv{hp}")
        p.op("dve", lambda e: e.tensor_tensor(
                B["cb"][0:1, 0:1], B["cb"][0:1, 0:1], B["cb2"][0:1, 0:1],
                ALU.min),
             reads=[rk("cb"), rk("cb2")], writes=[rk("cb")], tag=n + f"cbmn{hp}")
        p.op("pe", lambda e: e.matmul(
                psm[:, 0:1], com.ones_row_f[0:1, :], B["cb"][0:1, 0:1],
                start=True, stop=True),
             reads=[rk("cb"), rk("c_ones_row")], writes=[rk(psmn)],
             tag=n + f"cbb{hp}")
        p.op("dve", lambda e: e.tensor_copy(B["negc"][:, 0:1], psm[:, 0:1]),
             reads=[rk(psmn)], writes=[rk("negc")], tag=n + f"cbe{hp}")

        ps01, ps23 = B["ps01"], B["ps23"]

        def record_norm_lr(w):
            for sub in range(2):
                h = 2 * hp + sub
                pvs, pvsn = (psb[4], psbn[4]) if sub == 0 else (psb[5], psbn[5])
                p.op("act", lambda e, pvs=pvs, sub=sub: e.activation(
                        B["srow"][0:1, sub * 512:sub * 512 + 512],
                        pvs[64:65, 0:512], AF.Ln,
                        bias=B["tiny"][0:1, 0:1], scale=1.0),
                     reads=[rk(pvsn), rk("tiny")], writes=[rk("srow", sub)],
                     tag=n + f"L{h}.{w}")
                p.op("act", lambda e, sub=sub: e.activation(
                        B["rrow_bf"][0:1, sub * 512:sub * 512 + 512],
                        B["srow"][0:1, sub * 512:sub * 512 + 512], AF.Exp,
                        bias=B["lnpsc"][0:1, 0:1], scale=-1.0),
                     reads=[rk("srow", sub), rk("lnpsc")],
                     writes=[rk("rrow_bf", sub)], tag=n + f"R{h}.{w}")

        def record_norm_fin(w):
            for sub in range(2):
                h = 2 * hp + sub
                base = sub * 64
                pvs, pvsn = (psb[4], psbn[4]) if sub == 0 else (psb[5], psbn[5])
                p.op("pe", lambda e, sub=sub: e.matmul(
                        psm[0:64, 0:512], com.ones_row_bf[0:1, 0:64],
                        B["rrow_bf"][0:1, sub * 512:sub * 512 + 512],
                        start=True, stop=True),
                     reads=[rk("rrow_bf", sub), rk("c_ones_row_bf")],
                     writes=[rk(psmn)], tag=n + f"RB{h}.{w}")
                p.op("dve", lambda e, sub=sub: e.tensor_copy(
                        B["recb"][0:64, sub * 512:sub * 512 + 512],
                        psm[0:64, 0:512]),
                     reads=[rk(psmn)], writes=[rk("recb", sub)],
                     tag=n + f"RE{h}.{w}")
                p.op("dve", lambda e, pvs=pvs, base=base, hp=hp, w=w, sub=sub,
                     pvsn=pvsn: e.tensor_tensor(
                        B["zT"][base:base + 64,
                                hp * T + w * 512: hp * T + w * 512 + 512],
                        pvs[0:64, 0:512],
                        B["recb"][0:64, sub * 512:sub * 512 + 512], ALU.mult),
                     reads=[rk(pvsn), rk("recb", sub)],
                     writes=[rk(B["zT_name"], (hp, w))], tag=n + f"Z{h}.{w}")

        for w in range(NW):
            kmax = (w * 512 + 511) // 128
            for kc in range(kmax + 1):
                off = max(0, kc * 128 - w * 512)
                nq = 512 - off
                par = kc % 2
                spair = ps01 if par == 0 else ps23
                sk = (psbn[0], psbn[1]) if par == 0 else (psbn[2], psbn[3])
                diag = off > 0 or (kc + 1) * 128 > w * 512
                for sub in range(2):
                    p.op("pe", lambda e, w=w, kc=kc, off=off, nq=nq, sub=sub,
                         spair=spair, diag=diag: e.matmul(
                            spair[:, sub * 512: sub * 512 + nq],
                            B["kT"][sub * 64:sub * 64 + 64,
                                    kc * 128:(kc + 1) * 128],
                            B["qT"][sub * 64:sub * 64 + 64,
                                    w * 512 + off: w * 512 + 512],
                            start=True, stop=not diag),
                         reads=[rk("kT", ALL), rk("qT", ALL)],
                         writes=[rk(sk[sub])],
                         inc=(0 if diag else 1),
                         tag=n + f"S{2*hp+sub}.{w}.{kc}")
                    if diag:
                        p.op("pe", lambda e, sub=sub, spair=spair: e.matmul(
                                spair[:, sub * 512: sub * 512 + 128],
                                com.ident_bf[:], com.maskadd[:, 0:128],
                                start=False, stop=True),
                             reads=[rk("c_identbf"), rk("c_maskadd")],
                             writes=[rk(sk[sub])], tag=n + f"SM{2*hp+sub}.{w}.{kc}")
                if off == 0:
                    p.op("act", lambda e, kc=kc, spair=spair: e.activation(
                            B["PT"][:, kc * 1024: kc * 1024 + 1024],
                            spair[:, 0:1024],
                            AF.Exp, bias=B["negc"][:, 0:1], scale=0.125),
                         reads=[rk(sk[0]), rk(sk[1]), rk("negc")],
                         writes=[rk("PT", kc)],
                         tag=n + f"E{hp}.{w}.{kc}")
                else:
                    for sub in range(2):
                        p.op("act", lambda e, kc=kc, nq=nq, sub=sub,
                             spair=spair: e.activation(
                                B["PT"][:, kc * 1024 + sub * 512:
                                        kc * 1024 + sub * 512 + nq],
                                spair[:, sub * 512: sub * 512 + nq],
                                AF.Exp, bias=B["negc"][:, 0:1], scale=0.125),
                             reads=[rk(sk[sub]), rk("negc")],
                             writes=[rk("PT", kc)],
                             tag=n + f"E{2*hp+sub}.{w}.{kc}")
            if w > 0:
                record_norm_fin(w - 1)
            for kc in range(kmax + 1):
                off = max(0, kc * 128 - w * 512)
                nq = 512 - off
                for sub in range(2):
                    h = 2 * hp + sub
                    pvs, pvsn = psb[4], psbn[4]
                    if sub == 1:
                        pvs, pvsn = psb[5], psbn[5]
                    vsl = vx4[:, kc, sub, :]
                    p.op("pe", lambda e, kc=kc, off=off, nq=nq, vsl=vsl,
                         pvs=pvs, kmax=kmax, sub=sub: e.matmul(
                            pvs[0:65, off:off + nq], vsl,
                            B["PT"][:, kc * 1024 + sub * 512:
                                    kc * 1024 + sub * 512 + nq],
                            start=(kc == 0), stop=(kc == kmax)),
                         reads=[rk("v_ext"), rk("PT", kc)],
                         writes=[rk(pvsn)], inc=(1 if kc == kmax else 0),
                         tag=n + f"PV{h}.{w}.{kc}")
            record_norm_lr(w)
        record_norm_fin(NW - 1)

    wo_sb = B["PT"][:, 0:8192]
    p.dma("pool", wo_sb, wd["wo"], writes=[rk("PT", ALL)], tag=n + "ldwo")
    for ti in range(NT):
        for cf in range(2):
            bi = (2 * ti + cf) % 4
            bank, bname = psb[bi], psbn[bi]
            for ch in range(8):
                p.op("pe", lambda e, ti=ti, cf=cf, ch=ch, bank=bank: e.matmul(
                        bank[:, 0:512],
                        B["zT"][:, ch * T + ti * 128: ch * T + ti * 128 + 128],
                        wo_sb[:, ch * 1024 + cf * 512:
                              ch * 1024 + cf * 512 + 512],
                        start=(ch == 0), stop=(ch == 7)),
                     reads=[rk(B["zT_name"], ALL), rk("PT", ALL)],
                     writes=[rk(bname)], inc=(1 if ch == 7 else 0),
                     tag=n + f"O{ti}.{cf}")
            p.op("dve", lambda e, ti=ti, cf=cf, bank=bank: e.tensor_tensor(
                    B["x"][:, ti * 1024 + cf * 512: ti * 1024 + cf * 512 + 512],
                    B["x"][:, ti * 1024 + cf * 512: ti * 1024 + cf * 512 + 512],
                    bank[:, 0:512], ALU.add),
                 reads=[rk(bname), rk("x_res", ti)], writes=[rk("x_res", ti)],
                 tag=n + f"Oev{ti}.{cf}")


def mha_buffers(p: Prog, geo, x_buf, yT_buf, zT_buf, psb, psb_name,
                ps_misc, ps_misc_name, lnsc, biased, post_scale, pre="",
                ps01=None, ps23=None):
    import math
    _sbuf0 = p.sbuf
    p_sbuf = lambda nm, sh, dt: _sbuf0(pre + nm, sh, dt)
    NT, T = geo["NT"], geo["T"]
    B = {
        "x": x_buf, "yT": yT_buf, "yT_name": "bigbf_a",
        "zT": zT_buf, "zT_name": "bigbf_b",
        "psb": psb, "psb_name": psb_name,
        "ps_misc": ps_misc, "ps_misc_name": ps_misc_name,
        "lnsc": lnsc, "ps01": ps01, "ps23": ps23,
        "v_ext": p_sbuf("v_ext", [128, NT * 130], BF16),
        "qT": p_sbuf("qT", [128, T], BF16),
        "kT": p_sbuf("kT", [128, T], BF16),
        "vT": p_sbuf("vT", [128, T], BF16),
        "PT": p_sbuf("PT", [128, 16 * 1024], BF16),
        "wpair": p_sbuf("wpair", [128, 3 * 1024], BF16),
        "blk2": p_sbuf("blk2", [128, 2], BF16),
        "gmax": p_sbuf("gmax", [2, 2], F32),
        "n2": p_sbuf("n2", [2, 1], F32),
        "cb": p_sbuf("cb", [2, 1], F32),
        "cb2": p_sbuf("cb2", [1, 1], F32),
        "negc": p_sbuf("negc", [128, 2], F32),
        "bvb": p_sbuf("bvb", [128, 1024], BF16),
        "srow": p_sbuf("srow", [1, 1024], F32),
        "rrow_bf": p_sbuf("rrow_bf", [1, 1024], BF16),
        "recb": p_sbuf("recb", [64, 1024], BF16),
        "tiny": p_sbuf("tiny", [128, 1], F32),
        "lnsm": p_sbuf("lnsm", [128, 1], F32),
        "lnpsc": p_sbuf("lnpsc", [128, 1], F32),
        "bqF": p_sbuf("bqF", [128, 8], F32),
        "bkF": p_sbuf("bkF", [128, 8], F32),
        "bv_row": p_sbuf("bv_row", [1, 1024], F32),
    }
    p.op("dve", lambda e: e.memset(B["tiny"][:], 1e-30), writes=[rk("tiny")])
    p.op("dve", lambda e: e.memset(B["lnsm"][:], math.log(0.125)),
         writes=[rk("lnsm")])
    p.op("dve", lambda e: e.memset(B["lnpsc"][:], math.log(post_scale)),
         writes=[rk("lnpsc")])
    p.op("dve", lambda e: e.memset(B["blk2"][:], 0.0), writes=[rk("blk2")])
    p.op("dve", lambda e: e.memset(B["blk2"][0:64, 0:1], 1.0),
         reads=[rk("blk2")], writes=[rk("blk2")])
    p.op("dve", lambda e: e.memset(B["blk2"][64:128, 1:2], 1.0),
         reads=[rk("blk2")], writes=[rk("blk2")])
    return B



def emit_l1_ffn(p: Prog, com: Common, geo, B, wd, name="", wbf=None,
                do_prepass=True):
    NT, T = geo["NT"], geo["T"]
    NWIN = T // 512
    sc = B["lnsc"]
    n = name
    ps = B["ps8"]
    psn = B["ps8_name"]
    yT = B["yT"]
    if wbf is None:
        w1bf = p.nc.dram_tensor(n + "w1bf", [128, 32768], BF16,
                                kind="Internal").ap()
        w2bf = p.nc.dram_tensor(n + "w2bf", [128, 32768], BF16,
                                kind="Internal").ap()
    else:
        w1bf, w2bf = wbf
    if do_prepass:
        for qh in range(4):
            p.dma("pool", w1bf[:, qh * 8192:(qh + 1) * 8192],
                  wd["W1T"][:, qh * 8192:(qh + 1) * 8192],
                  writes=[rk("w1bf_d", qh)], tag=n + f"pc1.{qh}")
            p.dma("pool", w2bf[:, qh * 8192:(qh + 1) * 8192],
                  wd["W2T"][:, qh * 8192:(qh + 1) * 8192],
                  writes=[rk("w2bf_d", qh)], tag=n + f"pc2.{qh}")

    emit_ln(p, com, geo, B["x"], "x_res", yT, B["yT_name"], scratch=sc,
            keep_tile0=(B["lt_tm"], "lt_tm"))

    p.dma("sp", B["scale_sb"][:], wd["scale"], writes=[rk("scale_sb")],
          tag=n + "sc")
    for j in range(4):
        p.dma("pool", B["sc4"][j:j + 1, :], B["scale_sb"][0:1, :],
              reads=[rk("scale_sb")], writes=[rk("sc4")], tag=n + f"sc4.{j}")
    p.dma("sp", B["b1F"][:], wd["b1F"], writes=[rk("b1Fc")], tag=n + "b1")
    p.op("dve", lambda e: e.memset(B["b2pad"][:], 0.0), writes=[rk("b2pad")])
    p.dma("pool", B["b2pad"][0:1, :], wd["b2"], reads=[rk("b2pad")],
          writes=[rk("b2pad")], tag=n + "b2")
    p.op("dve", lambda e: e.memset(B["onespad"][:], 0.0), writes=[rk("onespad")])
    p.op("dve", lambda e: e.memset(B["onespad"][0:1, :], 1.0),
         reads=[rk("onespad")], writes=[rk("onespad")])
    for j in range(4):
        p.dma("pool", B["se_rows"][0:1, j * 1024:(j + 1) * 1024],
              B["lt_tm"][1 + 2 * j:2 + 2 * j, :], reads=[rk("lt_tm")],
              writes=[rk("se_rows")], tag=n + f"ser{j}")
        p.dma("pool", B["e4_rows"][0:1, j * 1024:(j + 1) * 1024],
              B["lt_tm"][9 + 2 * j:10 + 2 * j, :], reads=[rk("lt_tm")],
              writes=[rk("e4_rows")], tag=n + f"e4r{j}")
    p.op("dve", lambda e: e.memset(B["se_pad"][:], 0.0), writes=[rk("se_pad")])
    p.op("dve", lambda e: e.tensor_scalar(
            B["se_pad"][0:1, :], B["se_rows"][0:1, :],
            B["scale_sb"][0:1, 0:1], None, ALU.mult),
         reads=[rk("se_rows"), rk("scale_sb")], writes=[rk("se_pad")],
         tag=n + "sescl")
    p.op("dve", lambda e: e.memset(B["E_pad"][:], 0.0), writes=[rk("E_pad")])
    for j in range(4):
        for pm in range(4):
            src = B["e4_rows"][0:1, j * 1024 + pm: (j + 1) * 1024: 4]
            p.dma("pool", B["E_pad"][pm:pm + 1, j * 256:(j + 1) * 256], src,
                  reads=[rk("e4_rows")], writes=[rk("E_pad")],
                  tag=n + f"Ep{j}.{pm}", allow_slow_non_contiguous=True)

    p.op("dve", lambda e: e.memset(B["a1pad"][:], 0.0),
         writes=[rk("a1pad", "z")] + [rk("a1pad", j) for j in range(4)],
         tag=n + "a1pz")
    p.op("dve", lambda e: e.memset(B["g_pad"][:], 0.0),
         writes=[rk("g_pad", "z")] + [rk("g_pad", j) for j in range(4)],
         tag=n + "gpz")

    for w in range(NWIN):
        for ch in range(8):
            p.op("pe", lambda e, w=w, ch=ch: e.matmul(
                    ps[6][0:4, 0:512],
                    yT[:, ch * T + 2: ch * T + 10: 2],
                    yT[:, ch * T + w * 512: ch * T + (w + 1) * 512],
                    start=(ch == 0), stop=(ch == 7)),
                 reads=[rk(B["yT_name"], ALL)], writes=[rk(psn[6])],
                 inc=(1 if ch == 7 else 0), tag=n + f"a1.{w}.{ch}")
        p.op("dve", lambda e, w=w: e.tensor_copy(
                B["a1s"][0:4, w * 512:(w + 1) * 512], ps[6][0:4, 0:512]),
             reads=[rk(psn[6])], writes=[rk("a1s")], tag=n + f"a1e{w}")

    for w in range(NWIN):
        for j in range(4):
            p.dma("pool", B["a1pad"][0:1, j * 512:(j + 1) * 512],
                  B["a1s"][j:j + 1, w * 512:(w + 1) * 512],
                  reads=[rk("a1s")], writes=[rk("a1pad", j)],
                  tag=n + f"a1p{w}.{j}")
        def rec_gT(hc, w=w):
            j = hc // 8
            dch = hc % 8
            p.op("pe", lambda e, hc=hc, j=j, dch=dch: e.matmul(
                    ps[2 + j][0:4, 0:512],
                    yT[:, dch * T + 10: dch * T + 18: 2],
                    B["H"][:, hc * 512:(hc + 1) * 512],
                    start=(dch == 0), stop=(dch == 7)),
                 reads=[rk(B["yT_name"], ALL), rk("H", hc)],
                 writes=[rk(psn[2 + j])], inc=(1 if dch == 7 else 0),
                 tag=n + f"gT.{w}.{hc}")

        for hc in range(32):
            j = hc // 8
            slot = hc % 2
            if hc % 4 == 0:
                grp = (hc // 4) % 2
                p.dma("sp", B["w1buf"][:, grp * 4096:(grp + 1) * 4096],
                      w1bf[:, hc * 1024:(hc + 4) * 1024],
                      reads=[rk("w1bf_d", hc // 8)],
                      writes=[rk("wbuf", grp)], tag=n + f"w1.{w}.{hc}")
            grp = (hc // 4) % 2
            wof = grp * 4096 + (hc % 4) * 1024
            hps, hpsn = ps[slot], psn[slot]
            for ch in range(8):
                p.op("pe", lambda e, hc=hc, ch=ch, wof=wof, w=w, hps=hps:
                     e.matmul(
                        hps[:, 0:512],
                        B["w1buf"][:, wof + ch * 128: wof + (ch + 1) * 128],
                        yT[:, ch * T + w * 512: ch * T + (w + 1) * 512],
                        start=(ch == 0), stop=False),
                     reads=[rk("wbuf", grp), rk(B["yT_name"], ALL)],
                     writes=[rk(hpsn)], inc=0, tag=n + f"W1.{w}.{hc}.{ch}")
            p.op("pe", lambda e, hc=hc, j=j, hps=hps: e.matmul(
                    hps[:, 0:512],
                    B["se_pad"][:, hc * 128:(hc + 1) * 128],
                    B["a1pad"][:, j * 512:(j + 1) * 512],
                    start=False, stop=True),
                 reads=[rk("se_pad"), rk("a1pad", j), rk("a1pad", "z")],
                 writes=[rk(hpsn)], tag=n + f"lora1.{w}.{hc}")
            if hc > 0:
                rec_gT(hc - 1)
            p.op("act", lambda e, hc=hc, hps=hps: e.activation(
                    B["H"][:, hc * 512:(hc + 1) * 512], hps[:, 0:512],
                    AF.Gelu, bias=B["b1F"][:, hc:hc + 1], scale=1.0),
                 reads=[rk(hpsn), rk("b1Fc")], writes=[rk("H", hc)],
                 tag=n + f"gel.{w}.{hc}")
        rec_gT(31)
        for pm in range(4):
            p.op("dve", lambda e, pm=pm: e.tensor_scalar(
                    B["gT_sb"][0:4, pm * 512:(pm + 1) * 512],
                    ps[2 + pm][0:4, 0:512],
                    B["sc4"][0:4, 0:1], None, ALU.mult),
                 reads=[rk(psn[2 + pm]), rk("sc4")], writes=[rk("gT_sb", pm)],
                 tag=n + f"gTe.{w}.{pm}")
        for j in range(4):
            for pm in range(4):
                p.dma("pool", B["g_pad"][pm:pm + 1, j * 512:(j + 1) * 512],
                      B["gT_sb"][j:j + 1, pm * 512:(pm + 1) * 512],
                      reads=[rk("gT_sb", pm)], writes=[rk("g_pad", j)],
                      tag=n + f"gp.{w}.{j}.{pm}")

        for hc in range(32):
            if hc % 4 == 0:
                grp = (hc // 4) % 2
                p.dma("sp", B["w1buf"][:, grp * 4096:(grp + 1) * 4096],
                      w2bf[:, hc * 1024:(hc + 4) * 1024],
                      reads=[rk("w2bf_d", hc // 8)],
                      writes=[rk("wbuf", grp)], tag=n + f"w2.{w}.{hc}")
            grp = (hc // 4) % 2
            wof = grp * 4096 + (hc % 4) * 1024
            for tt in range(4):
                for cf in range(2):
                    bi = tt * 2 + cf
                    p.op("pe", lambda e, hc=hc, tt=tt, cf=cf, bi=bi, wof=wof,
                         w=w: e.matmul(
                            ps[bi][:, 0:512],
                            B["H"][:, hc * 512 + tt * 128:
                                   hc * 512 + (tt + 1) * 128],
                            B["w1buf"][:, wof + cf * 512:
                                       wof + (cf + 1) * 512],
                            start=(hc == 0), stop=False),
                         reads=[rk("wbuf", grp), rk("H", hc)],
                         writes=[rk(psn[bi])],
                         inc=(1 if (tt == 3 and cf == 1) else 0),
                         tag=n + f"W2.{w}.{hc}.{tt}.{cf}")
        for tt in range(4):
            for j in range(4):
                cf = j // 2
                bi = tt * 2 + cf
                p.op("pe", lambda e, tt=tt, j=j, bi=bi, cf=cf: e.matmul(
                        ps[bi][:, (j % 2) * 256: (j % 2) * 256 + 256],
                        B["g_pad"][:, j * 512 + tt * 128:
                                   j * 512 + (tt + 1) * 128],
                        B["E_pad"][:, j * 256:(j + 1) * 256],
                        start=False, stop=False),
                     reads=[rk("g_pad", j), rk("g_pad", "z"), rk("E_pad")],
                     writes=[rk(psn[bi])], inc=0, tag=n + f"lora2.{w}.{tt}.{j}")
            for cf in range(2):
                bi = tt * 2 + cf
                p.op("pe", lambda e, tt=tt, cf=cf, bi=bi: e.matmul(
                        ps[bi][:, 0:512], B["onespad"][:, 0:128],
                        B["b2pad"][:, cf * 512:(cf + 1) * 512],
                        start=False, stop=True),
                     reads=[rk("onespad"), rk("b2pad")],
                     writes=[rk(psn[bi])], tag=n + f"b2f.{w}.{tt}.{cf}")
                ti = w * 4 + tt
                p.op("dve", lambda e, ti=ti, cf=cf, bi=bi: e.scalar_tensor_tensor(
                        B["x"][:, ti * 1024 + cf * 512:
                               ti * 1024 + cf * 512 + 512],
                        ps[bi][:, 0:512], 0.5,
                        B["x"][:, ti * 1024 + cf * 512:
                               ti * 1024 + cf * 512 + 512],
                        ALU.mult, ALU.add),
                     reads=[rk(psn[bi]), rk("x_res", ti)],
                     writes=[rk("x_res", ti)], tag=n + f"xup.{w}.{tt}.{cf}")


def l1ffn_buffers(p: Prog, geo, x_buf, yT_buf, H_buf, ps8, ps8_name, lnsc,
                  pre=""):
    T = geo["T"]
    _sbuf0 = p.sbuf
    p_sbuf = lambda nm, sh, dt: _sbuf0(pre + nm, sh, dt)
    B = {
        "x": x_buf, "yT": yT_buf, "yT_name": "bigbf_a",
        "H": H_buf, "ps8": ps8, "ps8_name": ps8_name, "lnsc": lnsc,
        "lt_tm": p_sbuf("lt_tm", [128, 1024], BF16),
        "scale_sb": p_sbuf("scale_sb", [1, 1], F32),
        "sc4": p_sbuf("sc4", [4, 1], F32),
        "b1F": p_sbuf("b1Fc", [128, 32], F32),
        "b2pad": p_sbuf("b2pad", [128, 1024], BF16),
        "onespad": p_sbuf("onespad", [128, 128], BF16),
        "se_rows": p_sbuf("se_rows", [1, 4096], BF16),
        "e4_rows": p_sbuf("e4_rows", [1, 4096], BF16),
        "se_pad": p_sbuf("se_pad", [128, 4096], BF16),
        "E_pad": p_sbuf("E_pad", [128, 1024], BF16),
        "a1s": p_sbuf("a1s", [4, T], BF16),
        "a1pad": p_sbuf("a1pad", [128, 2048], BF16),
        "gT_sb": p_sbuf("gT_sb", [4, 2048], BF16),
        "g_pad": p_sbuf("g_pad", [128, 2048], BF16),
        "w1buf": p_sbuf("wbuf", [128, 8192], BF16),
    }
    return B


def alloc_psum_pool(p: Prog):
    ps01 = p.psum("psum01", [128, 1024], F32)
    ps23 = p.psum("psum23", [128, 1024], F32)
    ps = [ps01[:, 0:512], ps01[:, 512:1024],
          ps23[:, 0:512], ps23[:, 512:1024]]
    psn = ["ps0", "ps1", "ps2", "ps3"]
    hs = {}
    for i in range(4, 8):
        h = p.psum(f"ps{i}", [128, 512], F32)
        ps.append(h)
        psn.append(f"ps{i}")
        hs[i] = h
    return ps, psn, ps01, ps23, hs


def phase_barrier(p: Prog, names):
    if "barsc" not in p.buffers:
        p.sbuf("barsc", [1, 1], F32)
    bs = p.buffers["barsc"]
    for nm in names:
        p.op("dve", lambda e: e.memset(bs[:], 0.0),
             reads=[rk(nm)], writes=[rk(nm), rk("barsc")],
             tag=f"barrier-{nm}")



def emit_gating(p: Prog, com: Common, geo, B, nT, nT_name, simn, simn_key,
                sigb, sigb_key, rw_buf, rw_key, fold, name=""):
    NT, T = geo["NT"], geo["T"]
    NWIN = T // 512
    n = name
    ps = B["ps8"]
    psn = B["ps8_name"]
    g = B["gat"]

    for w in range(NWIN):
        for ch in range(8):
            p.op("dve", lambda e, ch=ch, w=w: e.tensor_tensor(
                    B["sq2w"][:], nT[:, ch * T + w * 512: ch * T + (w + 1) * 512],
                    nT[:, ch * T + w * 512: ch * T + (w + 1) * 512], ALU.mult),
                 reads=[rk(nT_name, ALL)], writes=[rk("sq2w")],
                 tag=n + f"gsq{w}.{ch}")
            p.op("pe", lambda e, ch=ch: e.matmul(
                    ps[6][0:1, 0:512], com.ones_col_bf[:], B["sq2w"][:],
                    start=(ch == 0), stop=(ch == 7)),
                 reads=[rk("sq2w"), rk("c_ones_col")], writes=[rk(psn[6])],
                 tag=n + f"gn{w}.{ch}")
        p.op("dve", lambda e, w=w: e.tensor_copy(
                g["n2row"][0:1, w * 512:(w + 1) * 512], ps[6][0:1, 0:512]),
             reads=[rk(psn[6])], writes=[rk("g_n2row")], tag=n + f"gnev{w}")
    for ti in range(NT):
        p.op("pe", lambda e, ti=ti: e.transpose(
                ps[6][:, 0:1], g["n2row"][0:1, ti * 128:(ti + 1) * 128],
                com.ones_row_f[0:1, 0:1]),
             reads=[rk("g_n2row"), rk("c_ones_row")], writes=[rk(psn[6])],
             tag=n + f"gtr{ti}")
        p.op("dve", lambda e, ti=ti: e.tensor_copy(
                g["n2c"][:, ti:ti + 1], ps[6][:, 0:1]),
             reads=[rk(psn[6])], writes=[rk("g_n2c")], tag=n + f"gtc{ti}")
    p.op("act", lambda e: e.activation(g["n2c"][:], g["n2c"][:], AF.Ln,
                                       bias=com.tiny[:, 0:1], scale=1.0),
         reads=[rk("g_n2c"), rk("c_tiny")], writes=[rk("g_n2c")], tag=n + "gln")
    p.op("act", lambda e: e.activation(g["rnc"][:], g["n2c"][:], AF.Exp,
                                       bias=com.zero_col[:, 0:1], scale=-0.5),
         reads=[rk("g_n2c"), rk("c_zero")], writes=[rk("g_rnc")], tag=n + "gex")

    for ti in range(NT):
        for ch in range(8):
            p.op("pe", lambda e, ti=ti, ch=ch: e.matmul(
                    ps[5][:, 0:8], nT[:, ch * T + ti * 128: ch * T + ti * 128 + 128],
                    simn[:, ch * 8:(ch + 1) * 8], start=(ch == 0), stop=(ch == 7)),
                 reads=[rk(nT_name, ALL), rk(simn_key)], writes=[rk(psn[5])],
                 inc=(1 if ch == 7 else 0), tag=n + f"glg{ti}.{ch}")
        p.op("dve", lambda e, ti=ti: e.tensor_scalar(
                g["lg"][:, ti * 8:(ti + 1) * 8], ps[5][:, 0:8],
                g["rnc"][:, ti:ti + 1], None, ALU.mult),
             reads=[rk(psn[5]), rk("g_rnc")], writes=[rk("g_lg", ti)],
             tag=n + f"glge{ti}")
        p.op("dve", lambda e, ti=ti: e.tensor_tensor(
                g["lg"][:, ti * 8:(ti + 1) * 8],
                g["lg"][:, ti * 8:(ti + 1) * 8], sigb[:], ALU.subtract),
             reads=[rk("g_lg", ti), rk(sigb_key)], writes=[rk("g_lg", ti)],
             tag=n + f"glgs{ti}")

    lgk = [rk("g_lg", ti) for ti in range(NT)]
    p.op("dve", lambda e: e.tensor_scalar(g["gated"][:, 0:NT * 8],
                                          g["lg"][:, 0:NT * 8], 0.0, None,
                                          ALU.max),
         reads=lgk, writes=[rk("g_gated")], tag=n + "ggt")
    p.op("dve", lambda e: e.tensor_scalar(g["amask"][:, 0:NT * 8],
                                          g["lg"][:, 0:NT * 8], 0.0, None,
                                          ALU.is_gt),
         reads=lgk, writes=[rk("g_amask")], tag=n + "gam")
    lg3 = g["lg"][:, 0:NT * 8].rearrange("p (t e) -> p t e", e=8)
    p.op("dve", lambda e: e.tensor_reduce(
            g["rowhas"][:, 0:NT], g["amask"][:, 0:NT * 8].rearrange(
                "p (t e) -> p t e", e=8), AX.X, ALU.add),
         reads=[rk("g_amask")], writes=[rk("g_rowhas")], tag=n + "grh")
    p.op("dve", lambda e: e.tensor_scalar(g["inact"][:, 0:NT],
                                          g["rowhas"][:, 0:NT], 0.0, None,
                                          ALU.is_le),
         reads=[rk("g_rowhas")], writes=[rk("g_inact")], tag=n + "gin")
    p.op("dve", lambda e, lg3=lg3: e.tensor_reduce(
            g["m1"][:, 0:NT], lg3, AX.X, ALU.max),
         reads=lgk, writes=[rk("g_m1")], tag=n + "gm1")
    for ti in range(NT):
        p.op("dve", lambda e, ti=ti: e.tensor_scalar(
                g["eq1"][:, 0:8], g["lg"][:, ti * 8:(ti + 1) * 8],
                g["m1"][:, ti:ti + 1], -1e30, ALU.is_equal, ALU.mult),
             reads=[rk("g_lg", ti), rk("g_m1")], writes=[rk("g_eq1")],
             tag=n + f"geq{ti}")
        p.op("dve", lambda e, ti=ti: e.tensor_tensor(
                g["lmask"][:, ti * 8:(ti + 1) * 8], g["eq1"][:, 0:8],
                g["lg"][:, ti * 8:(ti + 1) * 8], ALU.add),
             reads=[rk("g_eq1"), rk("g_lg", ti)], writes=[rk("g_lmask", ti)],
             tag=n + f"glm{ti}")
    p.op("dve", lambda e: e.tensor_reduce(
            g["m2"][:, 0:NT], g["lmask"][:, 0:NT * 8].rearrange(
                "p (t e) -> p t e", e=8), AX.X, ALU.max),
         reads=[rk("g_lmask", ti) for ti in range(NT)], writes=[rk("g_m2")],
         tag=n + "gm2")
    for ti in range(NT):
        p.op("dve", lambda e, ti=ti: e.tensor_scalar(
                g["eq1"][:, 0:8], g["lg"][:, ti * 8:(ti + 1) * 8],
                g["m2"][:, ti:ti + 1], None, ALU.is_ge),
             reads=[rk("g_lg", ti), rk("g_m2")], writes=[rk("g_eq1")],
             tag=n + f"gt2{ti}")
        p.op("dve", lambda e, ti=ti: e.tensor_tensor(
                g["eq1"][:, 0:8], g["eq1"][:, 0:8],
                g["amask"][:, ti * 8:(ti + 1) * 8], ALU.subtract),
             reads=[rk("g_eq1"), rk("g_amask")], writes=[rk("g_eq1")],
             tag=n + f"gd{ti}")
        p.op("dve", lambda e, ti=ti: e.scalar_tensor_tensor(
                g["mask"][:, ti * 8:(ti + 1) * 8], g["eq1"][:, 0:8],
                g["inact"][:, ti:ti + 1],
                g["amask"][:, ti * 8:(ti + 1) * 8], ALU.mult, ALU.add),
             reads=[rk("g_eq1"), rk("g_inact"), rk("g_amask")],
             writes=[rk("g_mask", ti)], tag=n + f"gmk{ti}")
    p.op("act", lambda e: e.activation(g["expg"][:, 0:NT * 8],
                                       g["gated"][:, 0:NT * 8], AF.Exp,
                                       bias=com.zero_col[:, 0:1], scale=1.0),
         reads=[rk("g_gated"), rk("c_zero")], writes=[rk("g_expg")],
         tag=n + "gxp")
    p.op("dve", lambda e: e.tensor_tensor(
            g["rwun"][:, 0:NT * 8], g["mask"][:, 0:NT * 8],
            g["expg"][:, 0:NT * 8], ALU.mult),
         reads=[rk("g_mask", ti) for ti in range(NT)] + [rk("g_expg")],
         writes=[rk("g_rwun")], tag=n + "gru")
    p.op("dve", lambda e: e.tensor_reduce(
            g["zsum"][:, 0:NT], g["rwun"][:, 0:NT * 8].rearrange(
                "p (t e) -> p t e", e=8), AX.X, ALU.add),
         reads=[rk("g_rwun")], writes=[rk("g_zsum")], tag=n + "gz")
    p.op("dve", lambda e: e.reciprocal(g["rz"][:, 0:NT], g["zsum"][:, 0:NT]),
         reads=[rk("g_zsum")], writes=[rk("g_rz")], tag=n + "grz")
    for ti in range(NT):
        p.op("dve", lambda e, ti=ti: e.tensor_scalar(
                rw_buf[:, ti * 8:(ti + 1) * 8], g["rwun"][:, ti * 8:(ti + 1) * 8],
                g["rz"][:, ti:ti + 1], fold, ALU.mult, ALU.mult),
             reads=[rk("g_rwun"), rk("g_rz")], writes=[rk(rw_key, ti)],
             tag=n + f"grw{ti}")


def gating_scratch(p: Prog, geo, pre=""):
    NT, T = geo["NT"], geo["T"]
    names = {"n2row": [1, T], "n2c": [128, NT], "rnc": [128, NT],
             "lg": [128, NT * 8], "gated": [128, NT * 8],
             "amask": [128, NT * 8], "rowhas": [128, NT], "inact": [128, NT],
             "m1": [128, NT], "m2": [128, NT], "eq1": [128, 8],
             "lmask": [128, NT * 8], "mask": [128, NT * 8],
             "expg": [128, NT * 8], "rwun": [128, NT * 8],
             "zsum": [128, NT], "rz": [128, NT]}
    d = {k: p.sbuf(pre + "g_" + k, sh, F32) for k, sh in names.items()}
    return d



def prep_sim(p: Prog, com: Common, B, sim_d, simn_buf, simn_key, name=""):
    n = name
    ps = B["ps8"]
    psn = B["ps8_name"]
    p.dma("sp", B["sim_f"][:], sim_d, writes=[rk("sim_f")], tag=n + "ld")
    p.op("dve", lambda e: e.tensor_tensor(B["sim_sq"][:], B["sim_f"][:],
                                          B["sim_f"][:], ALU.mult),
         reads=[rk("sim_f")], writes=[rk("sim_sq")], tag=n + "sq")
    for ch in range(8):
        p.op("pe", lambda e, ch=ch: e.matmul(
                ps[6][0:1, 0:8], B["ones_col_f"][:],
                B["sim_sq"][:, ch * 8:(ch + 1) * 8],
                start=(ch == 0), stop=(ch == 7)),
             reads=[rk("sim_sq"), rk("ones_col_f")], writes=[rk(psn[6])],
             inc=(1 if ch == 7 else 0), tag=n + f"n{ch}")
    p.op("act", lambda e: e.activation(B["srn"][0:1, 0:8], ps[6][0:1, 0:8],
                                       AF.Ln, bias=com.tiny[0:1, 0:1], scale=1.0),
         reads=[rk(psn[6]), rk("c_tiny")], writes=[rk("srn")], tag=n + "ln")
    p.op("act", lambda e: e.activation(B["srn"][0:1, 0:8], B["srn"][0:1, 0:8],
                                       AF.Exp, bias=com.zero_col[0:1, 0:1],
                                       scale=-0.5),
         reads=[rk("srn"), rk("c_zero")], writes=[rk("srn")], tag=n + "ex")
    p.op("pe", lambda e: e.matmul(ps[6][:, 0:8], com.ones_row_f[0:1, :],
                                  B["srn"][0:1, 0:8], start=True, stop=True),
         reads=[rk("srn"), rk("c_ones_row")], writes=[rk(psn[6])], tag=n + "bc")
    p.op("dve", lambda e: e.tensor_copy(B["srnb"][:], ps[6][:, 0:8]),
         reads=[rk(psn[6])], writes=[rk("srnb")], tag=n + "be")
    for ch in range(8):
        p.op("dve", lambda e, ch=ch: e.tensor_tensor(
                simn_buf[:, ch * 8:(ch + 1) * 8],
                B["sim_f"][:, ch * 8:(ch + 1) * 8], B["srnb"][:], ALU.mult),
             reads=[rk("sim_f"), rk("srnb")], writes=[rk(simn_key)],
             tag=n + f"m{ch}")


def prep_sig(p: Prog, com: Common, B, gates_d, sigb_buf, sigb_key, name=""):
    n = name
    ps = B["ps8"]
    psn = B["ps8_name"]
    p.dma("sp", B["grow"][:], gates_d, writes=[rk("grow")], tag=n + "ld")
    p.op("act", lambda e: e.activation(B["grow"][0:1, :], B["grow"][0:1, :],
                                       AF.Exp, bias=com.zero_col[0:1, 0:1],
                                       scale=-1.0),
         reads=[rk("grow"), rk("c_zero")], writes=[rk("grow")], tag=n + "ex")
    p.op("dve", lambda e: e.tensor_scalar(B["grow"][0:1, :], B["grow"][0:1, :],
                                          1.0, None, ALU.add),
         reads=[rk("grow")], writes=[rk("grow")], tag=n + "a1")
    p.op("dve", lambda e: e.reciprocal(B["grow"][0:1, :], B["grow"][0:1, :]),
         reads=[rk("grow")], writes=[rk("grow")], tag=n + "rc")
    p.op("pe", lambda e: e.matmul(ps[6][:, 0:8], com.ones_row_f[0:1, :],
                                  B["grow"][0:1, 0:8], start=True, stop=True),
         reads=[rk("grow"), rk("c_ones_row")], writes=[rk(psn[6])], tag=n + "bc")
    p.op("dve", lambda e: e.tensor_copy(sigb_buf[:], ps[6][:, 0:8]),
         reads=[rk(psn[6])], writes=[rk(sigb_key)], tag=n + "be")


def emit_l0_ffn(p: Prog, com: Common, geo, B, wd, name=""):
    NT, T = geo["NT"], geo["T"]
    n = name
    ps = B["ps8"]
    psn = B["ps8_name"]
    sc = B["lnsc"]
    n1T = B["bigbf_a"]
    n2T = B["bigbf_b"]
    vx = B["v_ext"][:, 0:NT * 129].rearrange("p (t d) -> p t d", d=129)

    emit_ln(p, com, geo, B["x"], "x_res", n1T, "bigbf_a",
            affine=(B["ln1w"], B["ln1b"], "ln1wF", "ln1bF"), fuse_double=True,
            out_stats=(B["ystats"], "ystats"), scratch=sc)

    emit_gating(p, com, geo, B, n1T, "bigbf_a", B["simn1"], "simn1",
                B["sigb1"], "sigb1", B["rw"], "rw", 1.0, name=n + "g1")

    pst6q = lambda a, b: ps[6][:, a // 2: b // 2].bitcast(BF16)
    p.op("dve", lambda e: e.memset(vx[:, :, 128:129], 1.0),
         writes=[rk("v_ext")], tag=n + "vone")
    for iw, wn in enumerate(("q", "k", "v")):
        p.dma("pool", B["wbig"][:], wd[("wq0", "wk0", "wv0")[iw]],
              writes=[rk("wbig")], tag=n + f"ldw{wn}")
        for ti in range(NT):
            for half in range(2):
                for ch in range(8):
                    p.op("pe", lambda e, ti=ti, half=half, ch=ch: e.matmul(
                            ps[half][:, 0:512],
                            n1T[:, ch * T + ti * 128: ch * T + ti * 128 + 128],
                            B["wbig"][:, ch * 1024 + half * 512:
                                      ch * 1024 + half * 512 + 512],
                            start=(ch == 0), stop=(ch == 7)),
                         reads=[rk("bigbf_a", ALL), rk("wbig")],
                         writes=[rk(psn[half])], inc=(1 if ch == 7 else 0),
                         tag=n + f"P{wn}{ti}.{half}.{ch}")
            tp = ti % 2
            qa = B["qacc"][:, tp * 128:(tp + 1) * 128]
            for e_ in range(8):
                bank = ps[e_ // 4]
                bnm = psn[e_ // 4]
                sl = (e_ % 4) * 128
                if e_ == 0:
                    p.op("dve", lambda e, bank=bank, sl=sl, ti=ti, qa=qa:
                         e.tensor_scalar(
                            qa, bank[:, sl:sl + 128],
                            B["rw"][:, ti * 8:ti * 8 + 1], None, ALU.mult),
                         reads=[rk(bnm), rk("rw", ti)], writes=[rk("qacc", tp)],
                         tag=n + f"W{wn}{ti}.0")
                else:
                    p.op("dve", lambda e, bank=bank, sl=sl, ti=ti, e_=e_, qa=qa:
                         e.scalar_tensor_tensor(
                            qa, bank[:, sl:sl + 128],
                            B["rw"][:, ti * 8 + e_:ti * 8 + e_ + 1],
                            qa, ALU.mult, ALU.add),
                         reads=[rk(bnm), rk("rw", ti), rk("qacc", tp)],
                         writes=[rk("qacc", tp)], tag=n + f"W{wn}{ti}.{e_}")
            if wn == "v":
                p.op("dve", lambda e, ti=ti, qa=qa: e.tensor_copy(
                        vx[:, ti, 0:128], qa),
                     reads=[rk("qacc", tp)], writes=[rk("v_ext")],
                     tag=n + f"vev{ti}")
            else:
                dst = B["qT"] if wn == "q" else B["kT"]
                dk = "qT" if wn == "q" else "kT"
                qb = B["qbf"][:, tp * 128:(tp + 1) * 128]
                tb = (pst6q if tp == 0 else sc["pst"])
                tbk = psn[6] if tp == 0 else "ps7"
                p.op("dve", lambda e, qa=qa, qb=qb: e.tensor_copy(qb, qa),
                     reads=[rk("qacc", tp)], writes=[rk("qbf", tp)],
                     tag=n + f"c{wn}{ti}")
                p.op("pe", lambda e, qb=qb, tb=tb: e.transpose(
                        tb(0, 128), qb, com.ident_bf[:]),
                     reads=[rk("qbf", tp), rk("c_identbf")], writes=[rk(tbk)],
                     tag=n + f"t{wn}{ti}")
                p.op("act", lambda e, ti=ti, dst=dst, tb=tb: e.activation(
                        dst[:, ti * 128:(ti + 1) * 128], tb(0, 128),
                        AF.Copy, bias=0.0, scale=1.0),
                     reads=[rk(tbk)], writes=[rk(dk, ti)],
                     tag=n + f"e{wn}{ti}")

    p.op("dve", lambda e: e.memset(B["gm1"][:], 0.0), writes=[rk("gm1")],
         tag=n + "gm0")
    sq2v = B["PT"][:, 0:T]
    for ci, (buf, bk) in enumerate(((B["qT"], "qT"), (B["kT"], "kT"))):
        p.op("dve", lambda e, buf=buf, sq2v=sq2v: e.tensor_tensor(
                sq2v, buf[:, 0:T], buf[:, 0:T], ALU.mult),
             reads=[rk(bk, ti) for ti in range(NT)], writes=[rk("PT")],
             tag=n + f"s2{ci}")
        for w in range(T // 512):
            p.op("pe", lambda e, w=w, sq2v=sq2v: e.matmul(
                    ps[6][0:1, 0:512], com.ones_col_bf[:],
                    sq2v[:, w * 512:(w + 1) * 512], start=True, stop=True),
                 reads=[rk("PT"), rk("c_ones_col")], writes=[rk(psn[6])],
                 tag=n + f"s2n{ci}.{w}")
            p.op("dve", lambda e: e.tensor_reduce(
                    B["n21"][0:1, 0:1], ps[6][0:1, 0:512], AX.X, ALU.max),
                 reads=[rk(psn[6])], writes=[rk("n21")], tag=n + f"s2r{ci}.{w}")
            p.op("dve", lambda e, ci=ci: e.tensor_tensor(
                    B["gm1"][0:1, ci:ci + 1], B["gm1"][0:1, ci:ci + 1],
                    B["n21"][0:1, 0:1], ALU.max),
                 reads=[rk("n21"), rk("gm1")], writes=[rk("gm1")],
                 tag=n + f"s2m{ci}.{w}")
    p.op("dve", lambda e: e.tensor_tensor(B["gm1"][0:1, 0:1], B["gm1"][0:1, 0:1],
                                          B["gm1"][0:1, 1:2], ALU.mult),
         reads=[rk("gm1")], writes=[rk("gm1")], tag=n + "cbp")
    p.op("act", lambda e: e.activation(B["gm1"][0:1, 0:1], B["gm1"][0:1, 0:1],
                                       AF.Ln, bias=com.tiny[0:1, 0:1], scale=1.0),
         reads=[rk("gm1"), rk("c_tiny")], writes=[rk("gm1")], tag=n + "cbl")
    p.op("act", lambda e: e.activation(B["gm1"][0:1, 0:1], B["gm1"][0:1, 0:1],
                                       AF.Exp, bias=B["lnsm128"][0:1, 0:1],
                                       scale=0.5),
         reads=[rk("gm1"), rk("lnsm128")], writes=[rk("gm1")], tag=n + "cbx")
    p.op("dve", lambda e: e.tensor_scalar(B["gm1"][0:1, 0:1], B["gm1"][0:1, 0:1],
                                          -1.0, None, ALU.mult),
         reads=[rk("gm1")], writes=[rk("gm1")], tag=n + "cbn")
    p.op("pe", lambda e: e.matmul(ps[6][:, 0:1], com.ones_row_f[0:1, :],
                                  B["gm1"][0:1, 0:1], start=True, stop=True),
         reads=[rk("gm1"), rk("c_ones_row")], writes=[rk(psn[6])], tag=n + "cbb")
    p.op("dve", lambda e: e.tensor_copy(B["negc1"][:], ps[6][:, 0:1]),
         reads=[rk(psn[6])], writes=[rk("negc1")], tag=n + "cbe")

    p.dma("pool", B["wbig2"][:], wd["o"], writes=[rk("wbig2")], tag=n + "ldo")
    sm128 = 1.0 / float(np.sqrt(128.0))
    pst6 = lambda a, b: ps[6][:, a // 2: b // 2].bitcast(BF16)
    for qt in range(NT):
        qp = qt % 2
        for kc in range(qt + 1):
            sps, spn = (ps[kc % 2], psn[kc % 2])
            diag = (kc == qt)
            p.op("pe", lambda e, qt=qt, kc=kc, sps=sps, diag=diag: e.matmul(
                    sps[:, 0:128], B["kT"][:, kc * 128:(kc + 1) * 128],
                    B["qT"][:, qt * 128:(qt + 1) * 128], start=True,
                    stop=not diag),
                 reads=[rk("kT", ALL), rk("qT", ALL)], writes=[rk(spn)],
                 inc=(0 if diag else 1), tag=n + f"S{qt}.{kc}")
            if diag:
                p.op("pe", lambda e, sps=sps: e.matmul(
                        sps[:, 0:128], com.ident_bf[:], com.maskadd[:],
                        start=False, stop=True),
                     reads=[rk("c_identbf"), rk("c_maskadd")], writes=[rk(spn)],
                     tag=n + f"SM{qt}.{kc}")
            p.op("act", lambda e, kc=kc, sps=sps, sm128=sm128, qp=qp:
                 e.activation(
                    B["PT"][:, qp * T + kc * 128: qp * T + (kc + 1) * 128],
                    sps[:, 0:128],
                    AF.Exp, bias=B["negc1"][:, 0:1], scale=sm128),
                 reads=[rk(spn), rk("negc1")], writes=[rk("PT", (qp, kc))],
                 tag=n + f"E{qt}.{kc}")
        pvb, pvbn = ps[2 + qp], psn[2 + qp]
        for kc in range(qt + 1):
            p.op("pe", lambda e, qt=qt, kc=kc, qp=qp, pvb=pvb: e.matmul(
                    pvb[:, 0:129],
                    B["PT"][:, qp * T + kc * 128: qp * T + (kc + 1) * 128],
                    B["v_ext"][:, kc * 129:(kc + 1) * 129],
                    start=(kc == 0), stop=(kc == qt)),
                 reads=[rk("PT", (qp, kc)), rk("v_ext")], writes=[rk(pvbn)],
                 inc=(1 if kc == qt else 0), tag=n + f"PV{qt}.{kc}")
        p.op("dve", lambda e, qp=qp, pvb=pvb: e.reciprocal(
                B["recipc"][:, qp:qp + 1], pvb[:, 128:129]),
             reads=[rk(pvbn)], writes=[rk("recipc", qp)], tag=n + f"rc{qt}")
        p.op("dve", lambda e, qt=qt, qp=qp: e.tensor_scalar(
                B["rwm"][:, qp * 8:(qp + 1) * 8], B["rw"][:, qt * 8:(qt + 1) * 8],
                B["recipc"][:, qp:qp + 1], None, ALU.mult),
             reads=[rk("rw", qt), rk("recipc", qp)], writes=[rk("rwm", qp)],
             tag=n + f"rwm{qt}")
        for e_ in range(8):
            ep = e_ % 2
            tb = (pst6 if ep == 0 else sc["pst"])
            tbk = psn[6] if ep == 0 else "ps7"
            p.op("dve", lambda e, e_=e_, qp=qp, ep=ep, pvb=pvb: e.tensor_scalar(
                    B["qbf"][:, ep * 128:(ep + 1) * 128], pvb[:, 0:128],
                    B["rwm"][:, qp * 8 + e_:qp * 8 + e_ + 1], None, ALU.mult),
                 reads=[rk(pvbn), rk("rwm", qp)], writes=[rk("qbf", ep)],
                 tag=n + f"aw{qt}.{e_}")
            p.op("pe", lambda e, ep=ep, tb=tb: e.transpose(
                    tb(0, 128), B["qbf"][:, ep * 128:(ep + 1) * 128],
                    com.ident_bf[:]),
                 reads=[rk("qbf", ep), rk("c_identbf")], writes=[rk(tbk)],
                 tag=n + f"awt{qt}.{e_}")
            p.op("act", lambda e, qt=qt, e_=e_, tb=tb: e.activation(
                    n2T[:, e_ * T + qt * 128: e_ * T + qt * 128 + 128],
                    tb(0, 128), AF.Copy, bias=0.0, scale=1.0),
                 reads=[rk(tbk)], writes=[rk("bigbf_b", (e_, qt))],
                 tag=n + f"awe{qt}.{e_}")

    phase_barrier(p, ["bigbf_a"])
    hsb = n1T
    for ti in range(NT):
        for half in range(2):
            bank, bnm = ps[3 + half], psn[3 + half]
            for ch in range(8):
                p.op("pe", lambda e, ti=ti, half=half, ch=ch, bank=bank: e.matmul(
                        bank[:, 0:512],
                        n2T[:, ch * T + ti * 128: ch * T + ti * 128 + 128],
                        B["wbig2"][:, ch * 1024 + half * 512:
                                   ch * 1024 + half * 512 + 512],
                        start=(ch == 0), stop=(ch == 7)),
                     reads=[rk("bigbf_b", ALL), rk("wbig2")],
                     writes=[rk(bnm)], inc=(1 if ch == 7 else 0),
                     tag=n + f"O{ti}.{half}")
            xsl = B["x"][:, ti * 1024 + half * 512: ti * 1024 + half * 512 + 512]
            p.op("dve", lambda e, ti=ti, xsl=xsl: e.tensor_scalar(
                    B["yscr"][:], xsl, B["ystats"][:, 2 * ti:2 * ti + 1],
                    B["ystats"][:, 2 * ti + 1:2 * ti + 2], ALU.mult, ALU.add),
                 reads=[rk("x_res", ti), rk("ystats", ti)], writes=[rk("yscr")],
                 tag=n + f"yr{ti}.{half}")
            p.op("dve", lambda e, ti=ti, half=half, bank=bank: e.tensor_tensor(
                    hsb[:, ti * 1024 + half * 512: ti * 1024 + half * 512 + 512],
                    B["yscr"][:], bank[:, 0:512], ALU.add),
                 reads=[rk(bnm), rk("yscr")], writes=[rk("bigbf_a", ti)],
                 tag=n + f"hs{ti}.{half}")
            p.op("dve", lambda e, bank=bank, xsl=xsl: e.scalar_tensor_tensor(
                    xsl, bank[:, 0:512], 0.5, xsl, ALU.mult, ALU.add),
                 reads=[rk(bnm), rk("x_res", ti), rk("yscr")],
                 writes=[rk("x_res", ti)], tag=n + f"xa{ti}.{half}")
            p.op("dve", lambda e, xsl=xsl: e.scalar_tensor_tensor(
                    xsl, B["yscr"][:], 0.5, xsl, ALU.mult, ALU.add),
                 reads=[rk("yscr"), rk("x_res", ti)], writes=[rk("x_res", ti)],
                 tag=n + f"xy{ti}.{half}")

    phase_barrier(p, ["bigbf_b"])
    emit_ln(p, com, geo, hsb, "bigbf_a", n2T, "bigbf_b",
            affine=(B["ln2w"], B["ln2b"], "ln2wF", "ln2bF"), scratch=sc)
    emit_gating(p, com, geo, B, n2T, "bigbf_b", B["simn2"], "simn2",
                B["sigb2"], "sigb2", B["rw2"], "rw2", 0.5, name=n + "g2")

    phase_barrier(p, ["bigbf_a"])
    He = n1T
    for ex in range(8):
        wslot = ex % 2
        wb1 = B["wbig"] if wslot == 0 else B["wbig2"]
        wk1 = "wbig" if wslot == 0 else "wbig2"
        p.dma("pool", wb1[:], wd["w1"][ex], writes=[rk(wk1)],
              tag=n + f"lw1.{ex}")
        wb2 = B["wbig2"] if wslot == 0 else B["wbig"]
        wk2 = "wbig2" if wslot == 0 else "wbig"
        p.dma("pool", wb2[:], wd["w2"][ex], writes=[rk(wk2)],
              tag=n + f"lw2.{ex}")
        w1banks = (0, 1, 6, 7)
        for ic in range(8):
            for w in range(T // 512):
                bi = w1banks[(ic * (T // 512) + w) % 4]
                bank, bnm = ps[bi], psn[bi]
                for ch in range(8):
                    p.op("pe", lambda e, ic=ic, w=w, ch=ch, bank=bank, wb1=wb1:
                         e.matmul(
                            bank[:, 0:512],
                            wb1[:, ch * 1024 + ic * 128: ch * 1024 + ic * 128 + 128],
                            n2T[:, ch * T + w * 512: ch * T + w * 512 + 512],
                            start=(ch == 0), stop=(ch == 7)),
                         reads=[rk(wk1), rk("bigbf_b", ALL)],
                         writes=[rk(bnm)], inc=(1 if ch == 7 else 0),
                         tag=n + f"H{ex}.{ic}.{w}")
                p.op("act", lambda e, ic=ic, w=w, bank=bank: e.activation(
                        He[:, ic * T + w * 512: ic * T + w * 512 + 512],
                        bank[:, 0:512], AF.Gelu, bias=com.zero_col[:, 0:1],
                        scale=1.0),
                     reads=[rk(bnm), rk("c_zero")],
                     writes=[rk("bigbf_a", (ic, w))], tag=n + f"G{ex}.{ic}.{w}")
        for ti in range(NT):
            for half in range(2):
                bi = 2 + (2 * ti + half) % 4
                bank, bnm = ps[bi], psn[bi]
                for ic in range(8):
                    p.op("pe", lambda e, ti=ti, half=half, ic=ic, bank=bank,
                         wb2=wb2: e.matmul(
                            bank[:, 0:512],
                            He[:, ic * T + ti * 128: ic * T + ti * 128 + 128],
                            wb2[:, ic * 1024 + half * 512:
                                ic * 1024 + half * 512 + 512],
                            start=(ic == 0), stop=(ic == 7)),
                         reads=[rk("bigbf_a", ALL), rk(wk2)],
                         writes=[rk(bnm)], inc=(1 if ic == 7 else 0),
                         tag=n + f"D{ex}.{ti}.{half}")
                xsl = B["x"][:, ti * 1024 + half * 512:
                             ti * 1024 + half * 512 + 512]
                p.op("dve", lambda e, ti=ti, ex=ex, bank=bank, xsl=xsl:
                     e.scalar_tensor_tensor(
                        xsl, bank[:, 0:512],
                        B["rw2"][:, ti * 8 + ex: ti * 8 + ex + 1], xsl,
                        ALU.mult, ALU.add),
                     reads=[rk(bnm), rk("rw2", ti), rk("x_res", ti)],
                     writes=[rk("x_res", ti)], tag=n + f"xm{ex}.{ti}.{half}")


def l0ffn_buffers(p: Prog, com: Common, geo, x_buf, bigbf_a, bigbf_b,
                  ps8, ps8_name, lnsc, gat, pre=""):
    import math
    _sbuf0 = p.sbuf
    p_sbuf = lambda nm, sh, dt: _sbuf0(pre + nm, sh, dt)
    NT, T = geo["NT"], geo["T"]
    B = {
        "x": x_buf, "bigbf_a": bigbf_a, "bigbf_b": bigbf_b,
        "ystats": p_sbuf("ystats", [128, 2 * NT], F32),
        "yscr": p_sbuf("yscr", [128, 512], F32),
        "ps8": ps8, "ps8_name": ps8_name, "lnsc": lnsc, "gat": gat,
        "v_ext": p_sbuf("v_ext", [128, NT * 130], BF16),
        "qT": p_sbuf("qT", [128, T], BF16),
        "kT": p_sbuf("kT", [128, T], BF16),
        "PT": p_sbuf("PT", [128, 2 * T], BF16),
        "sq2w": p_sbuf("sq2w", [128, 512], BF16),
        "qacc": p_sbuf("qacc", [128, 256], F32),
        "qbf": p_sbuf("qbf", [128, 256], BF16),
        "rw": p_sbuf("rw", [128, NT * 8], F32),
        "rw2": p_sbuf("rw2", [128, NT * 8], F32),
        "rwm": p_sbuf("rwm", [128, 16], F32),
        "recipc": p_sbuf("recipc", [128, 2], F32),
        "wbig": p_sbuf("wbig", [128, 8192], BF16),
        "wbig2": p_sbuf("wbig2", [128, 8192], BF16),
        "simn1": p_sbuf("simn1", [128, 64], BF16),
        "simn2": p_sbuf("simn2", [128, 64], BF16),
        "sigb1": p_sbuf("sigb1", [128, 8], F32),
        "sigb2": p_sbuf("sigb2", [128, 8], F32),
        "sim_f": p_sbuf("sim_f", [128, 64], F32),
        "sim_sq": p_sbuf("sim_sq", [128, 64], F32),
        "srn": p_sbuf("srn", [1, 8], F32),
        "srnb": p_sbuf("srnb", [128, 8], F32),
        "grow": p_sbuf("grow", [1, 8], F32),
        "ones_col_f": p_sbuf("ones_col_f", [128, 1], F32),
        "gm1": p_sbuf("gm1", [1, 2], F32),
        "n21": p_sbuf("n21", [1, 1], F32),
        "negc1": p_sbuf("negc1", [128, 1], F32),
        "lnsm128": p_sbuf("lnsm128", [128, 1], F32),
        "ln1w": p_sbuf("ln1wF", [128, 8], F32),
        "ln1b": p_sbuf("ln1bF", [128, 8], F32),
        "ln2w": p_sbuf("ln2wF", [128, 8], F32),
        "ln2b": p_sbuf("ln2bF", [128, 8], F32),
    }
    p.op("dve", lambda e: e.memset(B["ones_col_f"][:], 1.0),
         writes=[rk("ones_col_f")])
    p.op("dve", lambda e: e.memset(B["lnsm128"][:], math.log(1.0 / math.sqrt(128.0))),
         writes=[rk("lnsm128")])
    return B


def l0ffn_load_weights(p: Prog, B, wd):
    p.dma("sp", B["ln1w"][:], wd["ln1wF"], writes=[rk("ln1wF")])
    p.dma("sp", B["ln1b"][:], wd["ln1bF"], writes=[rk("ln1bF")])
    p.dma("sp", B["ln2w"][:], wd["ln2wF"], writes=[rk("ln2wF")])
    p.dma("sp", B["ln2b"][:], wd["ln2bF"], writes=[rk("ln2bF")])



from concourse.bass_utils import run_bass_kernel_spmd

GEO = {"NT": 16, "T": 2048}
NTK, TTOK = GEO["NT"], GEO["T"]


def _base_prog():
    nc = bass.Bass("TRN2", target_bir_lowering=False)
    x_d = nc.dram_tensor("x", [128, NTK * 1024], F32, kind="ExternalInput").ap()
    out_d = nc.dram_tensor("out", [128, NTK * 1024], F32,
                           kind="ExternalOutput").ap()
    p = Prog(nc)
    com = Common(p)
    ps, psn, ps01, ps23, hs = alloc_psum_pool(p)
    lnsc = ln_scratch(p, hs[7], hs[6])
    x_sb = p.sbuf("x_res", [128, NTK * 1024], F32)
    for ti in range(NTK):
        p.dma("sp", x_sb[:, ti * 1024:(ti + 1) * 1024],
              x_d[:, ti * 1024:(ti + 1) * 1024], writes=[rk("x_res", ti)])
    return nc, p, com, ps, psn, lnsc, x_sb, out_d, ps01, ps23


def _finish(nc, p, x_sb, out_d):
    for ti in range(NTK):
        p.dma("sp", out_d[:, ti * 1024:(ti + 1) * 1024],
              x_sb[:, ti * 1024:(ti + 1) * 1024], reads=[rk("x_res", ti)])
    p.build()
    return nc


def build_l0ffn():
    nc, p, com, ps, psn, lnsc, x_sb, out_d, ps01, ps23 = _base_prog()
    wd = {}
    for wn in ("wq0", "wk0", "wv0", "o"):
        wd[wn] = nc.dram_tensor(wn, [128, 8192], F32, kind="ExternalInput").ap()
    wd["w1"] = nc.dram_tensor("w1", [8, 128, 8192], F32, kind="ExternalInput").ap()
    wd["w2"] = nc.dram_tensor("w2", [8, 128, 8192], F32, kind="ExternalInput").ap()
    wd["sim1"] = nc.dram_tensor("sim1", [128, 64], F32, kind="ExternalInput").ap()
    wd["sim2"] = nc.dram_tensor("sim2", [128, 64], F32, kind="ExternalInput").ap()
    wd["g1"] = nc.dram_tensor("g1", [1, 8], F32, kind="ExternalInput").ap()
    wd["g2"] = nc.dram_tensor("g2", [1, 8], F32, kind="ExternalInput").ap()
    for wn in ("ln1wF", "ln1bF", "ln2wF", "ln2bF"):
        wd[wn] = nc.dram_tensor(wn + "_d", [128, 8], F32,
                                kind="ExternalInput").ap()
    gat = gating_scratch(p, GEO)
    bigbf_a = p.sbuf("bigbf_a", [128, 8 * TTOK], BF16)
    bigbf_b = p.sbuf("bigbf_b", [128, 8 * TTOK], BF16)
    B = l0ffn_buffers(p, com, GEO, x_sb, bigbf_a, bigbf_b, ps, psn,
                      lnsc, gat)
    l0ffn_load_weights(p, B, wd)
    prep_sim(p, com, B, wd["sim1"], B["simn1"], "simn1", name="s1")
    prep_sim(p, com, B, wd["sim2"], B["simn2"], "simn2", name="s2")
    prep_sig(p, com, B, wd["g1"], B["sigb1"], "sigb1", name="G1")
    prep_sig(p, com, B, wd["g2"], B["sigb2"], "sigb2", name="G2")
    emit_l0_ffn(p, com, GEO, B, wd, name="F")
    return _finish(nc, p, x_sb, out_d)


def build_mha(biased):
    nc, p, com, ps, psn, lnsc, x_sb, out_d, ps01, ps23 = _base_prog()
    wd = {}
    for wn in ("wq", "wk", "wv", "wo"):
        wd[wn] = nc.dram_tensor(wn, [128, 8192], F32, kind="ExternalInput").ap()
    if biased:
        wd["bq"] = nc.dram_tensor("bq", [128, 8], F32, kind="ExternalInput").ap()
        wd["bk"] = nc.dram_tensor("bk", [128, 8], F32, kind="ExternalInput").ap()
        wd["bv"] = nc.dram_tensor("bv", [1, 1024], F32, kind="ExternalInput").ap()
    yT = p.sbuf("bigbf_a", [128, 8 * TTOK], BF16)
    zT = p.sbuf("bigbf_b", [128, 8 * TTOK], BF16)
    B = mha_buffers(p, GEO, x_sb, yT, zT, ps, psn, ps[6], psn[6],
                    lnsc, biased, 0.125 if biased else 1.0,
                    ps01=ps01, ps23=ps23)
    emit_mha(p, com, GEO, B, wd, biased=biased, name="A")
    return _finish(nc, p, x_sb, out_d)


def build_l1ffn():
    nc, p, com, ps, psn, lnsc, x_sb, out_d, ps01, ps23 = _base_prog()
    wd = {
        "W1T": nc.dram_tensor("W1T", [128, 32 * 1024], F32,
                              kind="ExternalInput").ap(),
        "W2T": nc.dram_tensor("W2T", [128, 32 * 1024], F32,
                              kind="ExternalInput").ap(),
        "b1F": nc.dram_tensor("b1F_d", [128, 32], F32, kind="ExternalInput").ap(),
        "b2": nc.dram_tensor("b2_d", [1, 1024], F32, kind="ExternalInput").ap(),
        "scale": nc.dram_tensor("scale_d", [1, 1], F32, kind="ExternalInput").ap(),
    }
    yT = p.sbuf("bigbf_a", [128, 8 * TTOK], BF16)
    H = p.sbuf("bigbf_b", [128, 16384], BF16)
    B = l1ffn_buffers(p, GEO, x_sb, yT, H, ps, psn, lnsc)
    emit_l1_ffn(p, com, GEO, B, wd, name="F")
    return _finish(nc, p, x_sb, out_d)



def _hcm(w):
    X = w.shape[1]
    return np.ascontiguousarray(
        w.reshape(8, 128, X).transpose(1, 0, 2).reshape(128, 8 * X))


def _pair_major(w_cf):
    return np.ascontiguousarray(
        w_cf.reshape(8, 128, 8, 128).transpose(1, 2, 0, 3).reshape(128, 8192))


def _tok_major(x2d):
    return np.ascontiguousarray(
        x2d.reshape(NTK, 128, 1024).transpose(1, 0, 2).reshape(128, NTK * 1024))


def _prep_inputs(I):
    f32 = np.float32
    W = {}
    W["l0f"] = {
        "wq0": _hcm(np.ascontiguousarray(
            I["L0_ffn_q_proj"].transpose(1, 0, 2).reshape(1024, 1024))),
        "wk0": _hcm(np.ascontiguousarray(
            I["L0_ffn_k_proj"].transpose(1, 0, 2).reshape(1024, 1024))),
        "wv0": _hcm(np.ascontiguousarray(
            I["L0_ffn_v_proj"].transpose(1, 0, 2).reshape(1024, 1024))),
        "o": _hcm(np.ascontiguousarray(I["L0_ffn_o_proj"].reshape(1024, 1024))),
        "w1": np.stack([_hcm(I["L0_ffn_w1"][e]) for e in range(8)]),
        "w2": np.stack([_hcm(I["L0_ffn_w2"][e]) for e in range(8)]),
        "sim1": _hcm(np.ascontiguousarray(I["L0_ffn_smha_sim"])),
        "sim2": _hcm(np.ascontiguousarray(I["L0_ffn_moe_sim"])),
        "g1": I["L0_ffn_smha_gates"].reshape(1, 8).astype(f32),
        "g2": I["L0_ffn_moe_gates"].reshape(1, 8).astype(f32),
        "ln1wF_d": np.ascontiguousarray(I["L0_ffn_ln1_w"].reshape(8, 128).T),
        "ln1bF_d": np.ascontiguousarray(I["L0_ffn_ln1_b"].reshape(8, 128).T),
        "ln2wF_d": np.ascontiguousarray(I["L0_ffn_ln2_w"].reshape(8, 128).T),
        "ln2bF_d": np.ascontiguousarray(I["L0_ffn_ln2_b"].reshape(8, 128).T),
    }
    W["l0a"] = {
        "wq": _pair_major(np.ascontiguousarray(
            I["L0_attn_Wq"].transpose(2, 0, 1).reshape(1024, 1024))),
        "wk": _pair_major(np.ascontiguousarray(
            I["L0_attn_Wk"].transpose(2, 0, 1).reshape(1024, 1024))),
        "wv": _pair_major(np.ascontiguousarray(
            I["L0_attn_Wv"].transpose(2, 0, 1).reshape(1024, 1024))),
        "wo": _hcm(np.ascontiguousarray(I["L0_attn_proj"])),
        "bq": np.ascontiguousarray(I["L0_attn_bq"].reshape(-1).reshape(8, 128).T),
        "bk": np.ascontiguousarray(I["L0_attn_bk"].reshape(-1).reshape(8, 128).T),
        "bv": I["L0_attn_bv"].reshape(1, 1024).astype(f32),
    }
    W["l1a"] = {
        "wq": _pair_major(np.ascontiguousarray(I["L1_attn_wq"].T)),
        "wk": _pair_major(np.ascontiguousarray(I["L1_attn_wk"].T)),
        "wv": _pair_major(np.ascontiguousarray(I["L1_attn_wv"].T)),
        "wo": _hcm(np.ascontiguousarray(I["L1_attn_wo"].T)),
    }
    W["l1f"] = {
        "W1T": np.ascontiguousarray(
            I["L1_ffn_W1"].T.reshape(8, 128, 32, 128).transpose(1, 2, 0, 3)
            .reshape(128, 32768)),
        "W2T": np.ascontiguousarray(
            I["L1_ffn_W2"].T.reshape(32, 128, 1024).transpose(1, 0, 2)
            .reshape(128, 32768)),
        "b1F_d": np.ascontiguousarray(I["L1_ffn_b1"].reshape(32, 128).T),
        "b2_d": I["L1_ffn_b2"].reshape(1, 1024).astype(f32),
        "scale_d": np.asarray(I["L1_ffn_scale"], f32).reshape(1, 1),
    }
    for d in W.values():
        for k in d:
            d[k] = np.ascontiguousarray(np.asarray(d[k], f32))
    return W


_PROG_CACHE = {}


def _get_progs():
    if not _PROG_CACHE:
        _PROG_CACHE["l0f"] = build_l0ffn()
        _PROG_CACHE["l0a"] = build_mha(True)
        _PROG_CACHE["l1a"] = build_mha(False)
        _PROG_CACHE["l1f"] = build_l1ffn()
    return _PROG_CACHE



def build_fused():
    nc = bass.Bass("TRN2", target_bir_lowering=False)
    x_d = nc.dram_tensor("x", [128, NTK * 1024], F32, kind="ExternalInput").ap()
    out_d = nc.dram_tensor("out", [128, NTK * 1024], F32,
                           kind="ExternalOutput").ap()
    wd_l0f = {}
    for wn in ("wq0", "wk0", "wv0", "o"):
        wd_l0f[wn] = nc.dram_tensor("l0f_" + wn, [128, 8192], F32,
                                    kind="ExternalInput").ap()
    wd_l0f["w1"] = nc.dram_tensor("l0f_w1", [8, 128, 8192], F32,
                                  kind="ExternalInput").ap()
    wd_l0f["w2"] = nc.dram_tensor("l0f_w2", [8, 128, 8192], F32,
                                  kind="ExternalInput").ap()
    wd_l0f["sim1"] = nc.dram_tensor("l0f_sim1", [128, 64], F32,
                                    kind="ExternalInput").ap()
    wd_l0f["sim2"] = nc.dram_tensor("l0f_sim2", [128, 64], F32,
                                    kind="ExternalInput").ap()
    wd_l0f["g1"] = nc.dram_tensor("l0f_g1", [1, 8], F32,
                                  kind="ExternalInput").ap()
    wd_l0f["g2"] = nc.dram_tensor("l0f_g2", [1, 8], F32,
                                  kind="ExternalInput").ap()
    for wn in ("ln1wF", "ln1bF", "ln2wF", "ln2bF"):
        wd_l0f[wn] = nc.dram_tensor("l0f_" + wn, [128, 8], F32,
                                    kind="ExternalInput").ap()
    wd_l0a = {}
    for wn in ("wq", "wk", "wv", "wo"):
        wd_l0a[wn] = nc.dram_tensor("l0a_" + wn, [128, 8192], F32,
                                    kind="ExternalInput").ap()
    wd_l0a["bq"] = nc.dram_tensor("l0a_bq", [128, 8], F32,
                                  kind="ExternalInput").ap()
    wd_l0a["bk"] = nc.dram_tensor("l0a_bk", [128, 8], F32,
                                  kind="ExternalInput").ap()
    wd_l0a["bv"] = nc.dram_tensor("l0a_bv", [1, 1024], F32,
                                  kind="ExternalInput").ap()
    wd_l1a = {}
    for wn in ("wq", "wk", "wv", "wo"):
        wd_l1a[wn] = nc.dram_tensor("l1a_" + wn, [128, 8192], F32,
                                    kind="ExternalInput").ap()
    wd_l1f = {
        "W1T": nc.dram_tensor("l1f_W1T", [128, 32768], F32,
                              kind="ExternalInput").ap(),
        "W2T": nc.dram_tensor("l1f_W2T", [128, 32768], F32,
                              kind="ExternalInput").ap(),
        "b1F": nc.dram_tensor("l1f_b1F", [128, 32], F32,
                              kind="ExternalInput").ap(),
        "b2": nc.dram_tensor("l1f_b2", [1, 1024], F32,
                             kind="ExternalInput").ap(),
        "scale": nc.dram_tensor("l1f_scale", [1, 1], F32,
                                kind="ExternalInput").ap(),
    }

    p = Prog(nc)
    com = Common(p)
    ps, psn, ps01, ps23, hs = alloc_psum_pool(p)
    lnsc = ln_scratch(p, hs[7], hs[6])
    x_sb = p.sbuf("x_res", [128, NTK * 1024], F32)
    bigbf_a = p.sbuf("bigbf_a", [128, 8 * TTOK], BF16)
    bigbf_b = p.sbuf("bigbf_b", [128, max(8 * TTOK, 16384)], BF16)
    p.sbuf("barsc", [1, 1], F32)

    for ti in range(NTK):
        p.dma("sp", x_sb[:, ti * 1024:(ti + 1) * 1024],
              x_d[:, ti * 1024:(ti + 1) * 1024], writes=[rk("x_res", ti)])

    def phase_l0ffn(pre):
        p.push_scope()
        gat = gating_scratch(p, GEO, pre=pre)
        B = l0ffn_buffers(p, com, GEO, x_sb, bigbf_a, bigbf_b, ps, psn,
                          lnsc, gat, pre=pre)
        l0ffn_load_weights(p, B, wd_l0f)
        prep_sim(p, com, B, wd_l0f["sim1"], B["simn1"], "simn1", name=pre + "s1")
        prep_sim(p, com, B, wd_l0f["sim2"], B["simn2"], "simn2", name=pre + "s2")
        prep_sig(p, com, B, wd_l0f["g1"], B["sigb1"], "sigb1", name=pre + "G1")
        prep_sig(p, com, B, wd_l0f["g2"], B["sigb2"], "sigb2", name=pre + "G2")
        emit_l0_ffn(p, com, GEO, B, wd_l0f, name=pre)
        p.pop_scope()
        p.fence()

    def phase_mha(pre, wd, biased):
        p.push_scope()
        B = mha_buffers(p, GEO, x_sb, bigbf_a, bigbf_b, ps, psn,
                        ps[6], psn[6], lnsc, biased,
                        0.125 if biased else 1.0, pre=pre,
                        ps01=ps01, ps23=ps23)
        emit_mha(p, com, GEO, B, wd, biased=biased, name=pre)
        p.pop_scope()
        p.fence()

    l1f_wbf = [None]

    def phase_l1ffn(pre):
        p.push_scope()
        B = l1ffn_buffers(p, GEO, x_sb, bigbf_a, bigbf_b, ps, psn, lnsc,
                          pre=pre)
        if l1f_wbf[0] is None:
            w1bf = nc.dram_tensor("l1f_w1bf", [128, 32768], BF16,
                                  kind="Internal").ap()
            w2bf = nc.dram_tensor("l1f_w2bf", [128, 32768], BF16,
                                  kind="Internal").ap()
            l1f_wbf[0] = (w1bf, w2bf)
            emit_l1_ffn(p, com, GEO, B, wd_l1f, name=pre, wbf=l1f_wbf[0],
                        do_prepass=True)
        else:
            emit_l1_ffn(p, com, GEO, B, wd_l1f, name=pre, wbf=l1f_wbf[0],
                        do_prepass=False)
        p.pop_scope()
        p.fence()

    phase_l0ffn("A_")
    phase_mha("B_", wd_l0a, True)
    phase_l0ffn("C_")
    phase_l1ffn("D_")
    phase_mha("E_", wd_l1a, False)
    phase_l1ffn("F_")

    for ti in range(NTK):
        p.dma("sp", out_d[:, ti * 1024:(ti + 1) * 1024],
              x_sb[:, ti * 1024:(ti + 1) * 1024], reads=[rk("x_res", ti)])
    p.build()
    return nc


def _fused_inputs(W, x):
    m = {"x": x}
    for k, v in W["l0f"].items():
        key = {"ln1wF_d": "ln1wF", "ln1bF_d": "ln1bF",
               "ln2wF_d": "ln2wF", "ln2bF_d": "ln2bF"}.get(k, k)
        m["l0f_" + key] = v
    for k, v in W["l0a"].items():
        m["l0a_" + k] = v
    for k, v in W["l1a"].items():
        m["l1a_" + k] = v
    for k, v in W["l1f"].items():
        key = {"b1F_d": "b1F", "b2_d": "b2", "scale_d": "scale"}.get(k, k)
        m["l1f_" + key] = v
    return m


FUSED = True


def kernel(**inputs):
    I = {k: np.asarray(v) for k, v in inputs.items()}
    W = _prep_inputs(I)
    xs = [_tok_major(np.asarray(I["L0_attn_x"][b], np.float32))
          for b in range(8)]
    if FUSED:
        if "fused" not in _PROG_CACHE:
            _PROG_CACHE["fused"] = build_fused()
        nc = _PROG_CACHE["fused"]
        in_maps = [_fused_inputs(W, xs[b]) for b in range(8)]
        res = run_bass_kernel_spmd(nc, in_maps, core_ids=list(range(8)))
        xs = [np.asarray(res.results[b]["out"], np.float32) for b in range(8)]
    else:
        progs = _get_progs()
        phases = [("l0f", "l0f"), ("l0a", "l0a"), ("l0f", "l0f"),
                  ("l1f", "l1f"), ("l1a", "l1a"), ("l1f", "l1f")]
        for prog_key, wkey in phases:
            in_maps = [dict(W[wkey], x=xs[b]) for b in range(8)]
            res = run_bass_kernel_spmd(progs[prog_key], in_maps,
                                       core_ids=list(range(8)))
            xs = [np.asarray(res.results[b]["out"], np.float32)
                  for b in range(8)]
    out = np.stack([xs[b].reshape(128, NTK, 1024).transpose(1, 0, 2)
                    .reshape(TTOK, 1024) for b in range(8)])
    return out.astype(np.float32)


# version 08
# speedup vs baseline: 1.2436x; 1.0996x over previous
import contextlib
from collections import defaultdict

import numpy as np
import concourse.bass as bass
import concourse.mybir as mybir

F32 = mybir.dt.float32
BF16 = mybir.dt.bfloat16
I32 = mybir.dt.int32
AF = mybir.ActivationFunctionType
ALU = mybir.AluOpType
AX = mybir.AxisListType

ALL = "__all__"

ENG_SECTION = {"pe": "tensor", "dve": "vector", "act": "scalar",
               "pool": "gpsimd", "sp": "sync"}


def rk(name, sub=ALL):
    return (name, sub)


class Prog:

    def __init__(self, nc, n_sp_dma_sems=6, n_pool_dma_sems=4):
        self.nc = nc
        self.stack = contextlib.ExitStack()
        self.ops = []
        self.sems = {}
        self._sem_order = []
        for e in ("pe", "dve", "act", "pool"):
            self._mk_sem(e)
        self.dma_sems = {"sp": [], "pool": []}
        for i in range(n_sp_dma_sems):
            self.dma_sems["sp"].append(self._mk_sem(f"dsp{i}"))
        for i in range(n_pool_dma_sems):
            self.dma_sems["pool"].append(self._mk_sem(f"dpl{i}"))
        self._dma_rr = {"sp": 0, "pool": 0}
        self.cum = defaultdict(int)
        self.last_writer = {}
        self.readers = defaultdict(list)
        self.name_keys = defaultdict(set)
        self.clock = defaultdict(dict)
        self.snap = {}
        self.eng_inc_ops = defaultdict(list)
        self.buffers = {}
        self.scopes = []
        self.pending_fence = {}
        nums = [self.sems[n].num for n in self._sem_order]
        assert nums == list(range(nums[0], nums[0] + len(nums))), nums
        self._sem_lo, self._sem_hi = nums[0], nums[-1]

    def _mk_sem(self, name):
        h = self.stack.enter_context(self.nc.semaphore(name))
        self.sems[name] = h
        self._sem_order.append(name)
        return name

    def sbuf(self, name, shape, dtype):
        st = self.scopes[-1] if self.scopes else self.stack
        h = st.enter_context(self.nc.sbuf_tensor(name, list(shape), dtype))
        self.buffers[name] = h
        return h

    def push_scope(self):
        self.scopes.append(contextlib.ExitStack())

    def pop_scope(self):
        self.scopes.pop().close()

    def fence(self):
        for s in self._sem_order:
            if self.cum[s] > 0:
                self.pending_fence[s] = self.cum[s]

    def psum(self, name, shape, dtype=F32):
        h = self.stack.enter_context(self.nc.psum_tensor(name, list(shape), dtype))
        self.buffers[name] = h
        return h

    def _conflict_keys(self, key):
        name, sub = key
        subs = self.name_keys.get(name)
        if not subs:
            return ()
        if sub == ALL:
            return [(name, s) for s in subs]
        out = []
        if sub in subs:
            out.append((name, sub))
        if ALL in subs:
            out.append((name, ALL))
        return out

    def op(self, engine, emit, reads=(), writes=(), sem=None, inc=1, tag=""):
        idx = len(self.ops)
        sem = sem or engine
        deps = set()
        for r in reads:
            for ck in self._conflict_keys(r):
                if ck in self.last_writer:
                    deps.add(self.last_writer[ck])
        for w in writes:
            for ck in self._conflict_keys(w):
                if ck in self.last_writer:
                    deps.add(self.last_writer[ck])
                for rd in self.readers.get(ck, ()):
                    deps.add(rd)
        deps.discard(idx)

        clk = self.clock[engine]
        need = {}
        for d in sorted(deps):
            dop = self.ops[d]
            if dop["engine"] == "pe" and engine == "pe":
                continue
            if dop["inc"] > 0:
                s, v = dop["sem"], dop["cum"]
            else:
                cover = None
                for (ci, cs, cv) in self.eng_inc_ops[dop["engine"]]:
                    if ci > d:
                        cover = (ci, cs, cv)
                        break
                assert cover is not None and cover[0] < idx, (
                    f"op {d} ({dop['tag']}) has inc=0 and no covering inc-op "
                    f"before consumer {idx} ({tag})")
                s, v = cover[1], cover[2]
            if clk.get(s, 0) >= v:
                continue
            if need.get(s, 0) < v:
                need[s] = v

        for s, v in self.pending_fence.items():
            if clk.get(s, 0) < v and need.get(s, 0) < v:
                need[s] = v

        if inc == 16:
            prev = self.cum[sem]
            if prev > 0 and clk.get(sem, 0) < prev and need.get(sem, 0) < prev:
                need[sem] = prev

        waits = sorted(need.items())
        for s, v in waits:
            if clk.get(s, 0) < v:
                clk[s] = v
        for d in deps:
            snap = self.snap.get(d)
            if snap:
                for s, v in snap.items():
                    if clk.get(s, 0) < v:
                        clk[s] = v

        self.cum[sem] += inc
        o = dict(idx=idx, engine=engine, emit=emit, sem=sem, inc=inc,
                 cum=self.cum[sem], waits=waits, tag=tag)
        self.ops.append(o)
        if inc > 0:
            self.eng_inc_ops[engine].append((idx, sem, self.cum[sem]))
        sn = dict(clk)
        sn[sem] = max(sn.get(sem, 0), self.cum[sem])
        self.snap[idx] = sn
        for r in reads:
            self.name_keys[r[0]].add(r[1])
            self.readers[r].append(idx)
        for w in writes:
            self.name_keys[w[0]].add(w[1])
            self.last_writer[w] = idx
            self.readers[w] = []
        return idx

    def dma(self, queue, out, in_, reads=(), writes=(), tag="", **kw):
        sems = self.dma_sems[queue]
        sem = sems[self._dma_rr[queue] % len(sems)]
        self._dma_rr[queue] += 1

        def emit(eng, out=out, in_=in_, kw=kw):
            return eng.dma_start(out=out, in_=in_, **kw)
        return self.op(queue, emit, reads=reads, writes=writes, sem=sem,
                       inc=16, tag=tag)

    def build(self):
        per_engine = defaultdict(list)
        for o in self.ops:
            per_engine[o["engine"]].append(o)
        sems = self.sems
        blk = self.nc.Block()
        block = blk.__enter__()
        for ename in ("sp", "pool", "pe", "dve", "act"):
            ops = per_engine.get(ename, [])
            section = getattr(block, ENG_SECTION[ename])

            def body(eng, _ops=ops):
                for o in _ops:
                    for s, v in o["waits"]:
                        eng.wait_ge(sems[s], v)
                    inst = o["emit"](eng)
                    if o["inc"] and inst is not None:
                        inst.then_inc(sems[o["sem"]], o["inc"])
            section(body)
        finals = [(s, self.cum[s]) for s in self._sem_order if self.cum[s] > 0]

        def tail(eng, finals=finals):
            for s, v in finals:
                if s != "pool":
                    eng.wait_ge(sems[s], v)
        block.gpsimd(tail)
        blk.__exit__(None, None, None)
        self.nc.gpsimd.sem_clear(range(self._sem_lo, self._sem_hi + 1))

    def close(self):
        self.stack.close()



class Common:

    def __init__(self, p: Prog):
        nc = p.nc
        self.p = p
        self.ident_bf = p.sbuf("c_identbf", [128, 128], BF16)
        self.mask_keepge = p.sbuf("c_maskge", [128, 128], BF16)
        self.maskadd = p.sbuf("c_maskadd", [128, 128], BF16)
        self.iota = p.sbuf("c_iota", [128, 128], I32)
        self.ones_col_bf = p.sbuf("c_ones_col", [128, 1], BF16)
        self.ones_row_f = p.sbuf("c_ones_row", [1, 128], F32)
        self.ones_row_bf = p.sbuf("c_ones_row_bf", [1, 128], BF16)
        self.eps_col = p.sbuf("c_eps", [128, 1], F32)
        self.zero_col = p.sbuf("c_zero", [128, 1], F32)
        self.tiny = p.sbuf("c_tiny", [128, 1], F32)

        p.op("pool", lambda e: e.iota(self.iota[:], pattern=[[-1, 128]], base=0,
                                      channel_multiplier=1),
             writes=[rk("c_iota")], tag="iota")
        p.op("dve", lambda e: e.tensor_scalar(self.ident_bf[:], self.iota[:],
                                              0.0, None, ALU.is_equal),
             reads=[rk("c_iota")], writes=[rk("c_identbf")], tag="ident")
        p.op("dve", lambda e: e.tensor_scalar(self.mask_keepge[:], self.iota[:],
                                              0.0, None, ALU.is_le),
             reads=[rk("c_iota")], writes=[rk("c_maskge")], tag="maskge")
        p.op("dve", lambda e: e.tensor_scalar(self.maskadd[:], self.iota[:],
                                              0.0, -1e30, ALU.is_gt, ALU.mult),
             reads=[rk("c_iota")], writes=[rk("c_maskadd")], tag="maskadd")
        p.op("dve", lambda e: e.memset(self.ones_col_bf[:], 1.0),
             writes=[rk("c_ones_col")])
        p.op("dve", lambda e: e.memset(self.ones_row_f[:], 1.0),
             writes=[rk("c_ones_row")])
        p.op("dve", lambda e: e.memset(self.ones_row_bf[:], 1.0),
             writes=[rk("c_ones_row_bf")])
        p.op("dve", lambda e: e.memset(self.eps_col[:], 1e-5),
             writes=[rk("c_eps")])
        p.op("dve", lambda e: e.memset(self.zero_col[:], 0.0),
             writes=[rk("c_zero")])
        p.op("dve", lambda e: e.memset(self.tiny[:], 1e-30),
             writes=[rk("c_tiny")])



def emit_ln(p: Prog, com: Common, geo, src_buf, src_name, dst_buf, dst_name,
            *, affine=None, fuse_double=False, out_tm=None, out_tm_name=None,
            scratch=None, tilesel=None, keep_tile0=None, out_stats=None):
    NT, T = geo["NT"], geo["T"]
    sc = scratch
    tiles = list(range(NT) if tilesel is None else tilesel)
    mv = sc["mv_all"]

    for ti in tiles:
        par = ti % 2
        bn = sc[par]["bn"]
        P_ = str(par)
        xsl = src_buf[:, ti * 1024: ti * 1024 + 1024]
        rx = rk(src_name, ti)
        p.op("dve", lambda e, a=xsl, bn=bn: e.bn_stats(bn[:, 0:6], a[:, 0:512]),
             reads=[rx], writes=[rk("s_bn" + P_)], tag=f"bns{ti}")
        p.op("dve", lambda e, a=xsl, bn=bn: e.bn_stats(bn[:, 6:12], a[:, 512:1024]),
             reads=[rx], writes=[rk("s_bn2" + P_)], tag=f"bns2{ti}")
        p.op("dve", lambda e, ti=ti, bn=bn: e.bn_aggr(
                mv[:, 2 * ti:2 * ti + 2], bn[:]),
             reads=[rk("s_bn" + P_), rk("s_bn2" + P_)], writes=[rk("s_mv", ti)],
             tag=f"bna{ti}")

    mvk = [rk("s_mv", ti) for ti in tiles]
    mvv = mv[:, 0:2 * NT].rearrange("p (t two) -> p two t", two=2)
    var_cols = mvv[:, 1:2, :].squeeze(1)
    m_cols = mvv[:, 0:1, :].squeeze(1)
    r0 = sc["r0_all"]
    rho = sc["rho_all"]
    nm = sc["nm_all"]

    p.op("act", lambda e: e.activation(r0[:, 0:NT], var_cols, AF.Ln,
                                       bias=com.eps_col[:, 0:1], scale=1.0),
         reads=mvk + [rk("c_eps")], writes=[rk("s_r0a")], tag="lnv")
    p.op("act", lambda e: e.activation(r0[:, 0:NT], r0[:, 0:NT], AF.Exp,
                                       bias=com.zero_col[:, 0:1], scale=-0.5),
         reads=[rk("s_r0a"), rk("c_zero")], writes=[rk("s_r0a")], tag="exv")
    if fuse_double:
        p.op("dve", lambda e: e.tensor_tensor(rho[:, 0:NT], r0[:, 0:NT],
                                              r0[:, 0:NT], ALU.mult),
             reads=[rk("s_r0a")], writes=[rk("s_rhoa")], tag="fd1")
        p.op("dve", lambda e: e.tensor_tensor(rho[:, 0:NT], rho[:, 0:NT],
                                              var_cols, ALU.mult),
             reads=[rk("s_rhoa")] + mvk, writes=[rk("s_rhoa")], tag="fd2")
        p.op("act", lambda e: e.activation(rho[:, 0:NT], rho[:, 0:NT], AF.Ln,
                                           bias=com.eps_col[:, 0:1], scale=1.0),
             reads=[rk("s_rhoa"), rk("c_eps")], writes=[rk("s_rhoa")], tag="fd3")
        p.op("act", lambda e: e.activation(rho[:, 0:NT], rho[:, 0:NT], AF.Exp,
                                           bias=com.zero_col[:, 0:1], scale=-0.5),
             reads=[rk("s_rhoa"), rk("c_zero")], writes=[rk("s_rhoa")], tag="fd4")
        p.op("dve", lambda e: e.tensor_tensor(rho[:, 0:NT], rho[:, 0:NT],
                                              r0[:, 0:NT], ALU.mult),
             reads=[rk("s_rhoa"), rk("s_r0a")], writes=[rk("s_rhoa")], tag="fd5")
        rhok = "s_rhoa"
    else:
        rho = r0
        rhok = "s_r0a"
    p.op("dve", lambda e, rho=rho: e.tensor_tensor(nm[:, 0:NT], m_cols,
                                                   rho[:, 0:NT], ALU.mult),
         reads=mvk + [rk(rhok)], writes=[rk("s_nma")], tag="nm1")
    p.op("dve", lambda e: e.tensor_scalar(nm[:, 0:NT], nm[:, 0:NT], -1.0, None,
                                          ALU.mult),
         reads=[rk("s_nma")], writes=[rk("s_nma")], tag="nm2")
    if out_stats is not None:
        stb, stk = out_stats
        st2 = stb[:].rearrange("p (t two) -> p two t", two=2)
        p.op("dve", lambda e, st2=st2: e.tensor_copy(
                st2[:, 0:1, 0:NT].squeeze(1), r0[:, 0:NT]),
             reads=[rk("s_r0a")], writes=[rk(stk, "r")], tag="str")
        p.op("dve", lambda e, st2=st2: e.tensor_tensor(
                st2[:, 1:2, 0:NT].squeeze(1), m_cols, r0[:, 0:NT], ALU.mult),
             reads=mvk + [rk("s_r0a")], writes=[rk(stk, "n")], tag="stn1")
        p.op("dve", lambda e, st2=st2: e.tensor_scalar(
                st2[:, 1:2, 0:NT].squeeze(1), st2[:, 1:2, 0:NT].squeeze(1),
                -1.0, None, ALU.mult),
             reads=[rk(stk, "n")], writes=[rk(stk, "n")], tag="stn2")

    if out_tm is not None:
        nm0 = sc["nm0_all"]
        p.op("dve", lambda e: e.tensor_tensor(nm0[:, 0:NT], m_cols,
                                              r0[:, 0:NT], ALU.mult),
             reads=mvk + [rk("s_r0a")], writes=[rk("s_nm0a")], tag="ynm1")
        p.op("dve", lambda e: e.tensor_scalar(nm0[:, 0:NT], nm0[:, 0:NT],
                                              -1.0, None, ALU.mult),
             reads=[rk("s_nm0a")], writes=[rk("s_nm0a")], tag="ynm2")

    ps6h = sc["ps6"]
    pst6 = (lambda a, b: ps6h[:, a // 2: b // 2].bitcast(BF16)) \
        if ps6h is not None else None
    for ti in tiles:
        par = ti % 2
        P_ = str(par)
        xsl = src_buf[:, ti * 1024: ti * 1024 + 1024]
        rx = rk(src_name, ti)
        if out_tm is not None:
            ysl = out_tm[:, ti * 1024: ti * 1024 + 1024]
            p.op("dve", lambda e, a=xsl, y=ysl, ti=ti, sc=sc: e.tensor_scalar(
                    y, a, r0[:, ti:ti + 1], sc["nm0_all"][:, ti:ti + 1],
                    ALU.mult, ALU.add),
                 reads=[rx, rk("s_r0a"), rk("s_nm0a")],
                 writes=[rk(out_tm_name, ti)], tag=f"y{ti}")
        if keep_tile0 is not None and ti == 0:
            nbf_buf, nbf_key = keep_tile0
        else:
            nbf_buf, nbf_key = sc[par]["nbf"], "s_nbf" + P_
        tb = sc["pst"] if (par == 1 or pst6 is None) else pst6
        tbk = "ps7" if (par == 1 or pst6 is None) else "ps6"
        for hh in range(2):
            if hh == 0:
                p.op("dve", lambda e, a=xsl, ti=ti, nb=nbf_buf, rho=rho:
                     e.tensor_scalar(
                        nb[:, 0:512], a[:, 0:512],
                        rho[:, ti:ti + 1], nm[:, ti:ti + 1], ALU.mult, ALU.add),
                     reads=[rx, rk(rhok), rk("s_nma")],
                     writes=[rk(nbf_key, 0)], tag=f"nbf{ti}.0")
            else:
                p.op("act", lambda e, a=xsl, ti=ti, nb=nbf_buf, rho=rho:
                     e.activation(
                        nb[:, 512:1024], a[:, 512:1024], AF.Identity,
                        bias=nm[:, ti:ti + 1], scale=rho[:, ti:ti + 1]),
                     reads=[rx, rk(rhok), rk("s_nma")],
                     writes=[rk(nbf_key, 1)], tag=f"nbf{ti}.1")
            for j in range(4 * hh, 4 * hh + 4):
                p.op("pe", lambda e, j=j, nb=nbf_buf, tb=tb: e.transpose(
                        tb(j * 128, (j + 1) * 128),
                        nb[:, j * 128:(j + 1) * 128], com.ident_bf[:]),
                     reads=[rk(nbf_key, hh), rk("c_identbf")],
                     writes=[rk(tbk)], inc=(1 if j == 4 * hh + 3 else 0),
                     tag=f"tr{ti}.{j}")
        if affine is not None:
            wc, bc, wname, bname = affine
            for j in range(8):
                if j % 2 == 0:
                    p.op("dve", lambda e, j=j, ti=ti, tb=tb: e.tensor_scalar(
                            dst_buf[:, j * T + ti * 128: j * T + ti * 128 + 128],
                            tb(j * 128, (j + 1) * 128),
                            wc[:, j:j + 1], bc[:, j:j + 1], ALU.mult, ALU.add),
                         reads=[rk(tbk), rk(wname), rk(bname)],
                         writes=[rk(dst_name, ti)], tag=f"aff{ti}.{j}")
                else:
                    p.op("act", lambda e, j=j, ti=ti, tb=tb: e.activation(
                            dst_buf[:, j * T + ti * 128: j * T + ti * 128 + 128],
                            tb(j * 128, (j + 1) * 128), AF.Identity,
                            bias=bc[:, j:j + 1], scale=wc[:, j:j + 1]),
                         reads=[rk(tbk), rk(wname), rk(bname)],
                         writes=[rk(dst_name, ti)], tag=f"affa{ti}.{j}")
        else:
            dst3 = dst_buf[:].rearrange("p (c t) -> p c t", c=8)
            p.op("act", lambda e, ti=ti, d3=dst3, tb=tb: e.activation(
                    d3[:, :, ti * 128: ti * 128 + 128],
                    tb(0, 1024).rearrange("p (c t) -> p c t", c=8),
                    AF.Copy, bias=0.0, scale=1.0),
                 reads=[rk(tbk)],
                 writes=[rk(dst_name, ti)], tag=f"ev{ti}")


def ln_scratch(p: Prog, ps7=None, ps6=None):
    sc = {}
    for par in (0, 1):
        sc[par] = {
            "bn": p.sbuf(f"s_bn{par}", [128, 12], F32),
            "nbf": p.sbuf(f"s_nbf{par}", [128, 1024], BF16),
        }
    sc["mv_all"] = p.sbuf("s_mv_all", [128, 32], F32)
    sc["r0_all"] = p.sbuf("s_r0_all", [128, 16], F32)
    sc["rho_all"] = p.sbuf("s_rho_all", [128, 16], F32)
    sc["nm_all"] = p.sbuf("s_nm_all", [128, 16], F32)
    sc["nm0_all"] = p.sbuf("s_nm0_all", [128, 16], F32)
    assert ps7 is not None
    sc["ps7"] = ps7
    sc["ps6"] = ps6
    sc["pst"] = lambda a, b: ps7[:, a // 2: b // 2].bitcast(BF16)
    return sc



def emit_mha(p: Prog, com: Common, geo, B, wd, *, biased, name="",
             per_hp_hook=None):
    NT, T = geo["NT"], geo["T"]
    NW = T // 512
    sc = B["lnsc"]
    n = name
    psb = B["psb"]
    psbn = B["psb_name"]
    psm, psmn = B["ps_misc"], B["ps_misc_name"]
    vx4 = B["v_ext"][:].rearrange("p (t h d) -> p t h d", h=2, d=65)

    emit_ln(p, com, geo, B["x"], "x_res", B["yT"], B["yT_name"], scratch=sc)

    if biased:
        p.dma("sp", B["bv_row"][:], wd["bv"], writes=[rk("bv_row")], tag=n + "bv")
        p.dma("sp", B["bqF"][:], wd["bq"], writes=[rk("bqF")])
        p.dma("sp", B["bkF"][:], wd["bk"], writes=[rk("bkF")])
        for h2 in range(2):
            p.op("pe", lambda e, h2=h2: e.matmul(
                    psm[:, 0:512], com.ones_row_f[0:1, :],
                    B["bv_row"][0:1, h2 * 512:(h2 + 1) * 512],
                    start=True, stop=True),
                 reads=[rk("bv_row"), rk("c_ones_row")], writes=[rk(psmn)],
                 tag=n + f"bvb{h2}")
            p.op("act", lambda e, h2=h2: e.activation(
                    B["bvb"][:, h2 * 512:(h2 + 1) * 512], psm[:, 0:512],
                    AF.Copy, bias=0.0, scale=1.0),
                 reads=[rk(psmn)], writes=[rk("bvb")], tag=n + f"bvbe{h2}")

    p.op("dve", lambda e: e.memset(vx4[:, :, :, 64:65], 1.0),
         writes=[rk("v_ext")], tag=n + "vones")

    for hp in range(8):
        if per_hp_hook is not None:
            per_hp_hook(hp)
        for i, wn in enumerate(("wq", "wk", "wv")):
            p.dma("pool", B["wpair"][:, i * 1024:(i + 1) * 1024],
                  wd[wn][:, hp * 1024:(hp + 1) * 1024],
                  writes=[rk("wpair", wn)], tag=n + f"ld{wn}{hp}")

        for i, wn in enumerate(("wq", "wk", "wv")):
            for w in range(NW):
                bank, bname = ((psb[6], psbn[6]) if (i * NW + w) % 2 == 0
                               else (psb[4], psbn[4]))
                ytk = [rk(B["yT_name"], t) for t in range(4 * w, 4 * w + 4)]
                for ch in range(8):
                    p.op("pe", lambda e, i=i, w=w, ch=ch, bank=bank: e.matmul(
                            bank[:, 0:512],
                            B["wpair"][:, i * 1024 + ch * 128:
                                       i * 1024 + (ch + 1) * 128],
                            B["yT"][:, ch * T + w * 512: ch * T + w * 512 + 512],
                            start=(ch == 0), stop=(ch == 7)),
                         reads=[rk("wpair", wn)] + ytk,
                         writes=[rk(bname)], inc=(1 if ch == 7 else 0),
                         tag=n + f"prj{wn}{hp}.{w}.{ch}")
                if wn == "wv":
                    p.op("act", lambda e, w=w, bank=bank: e.activation(
                            B["vT"][:, w * 512:(w + 1) * 512], bank[:, 0:512],
                            AF.Copy, bias=0.0, scale=1.0),
                         reads=[rk(bname)], writes=[rk("vT", w)],
                         tag=n + f"vTev{hp}.{w}")
                else:
                    dstb = B["qT"] if wn == "wq" else B["kT"]
                    dname = "qT" if wn == "wq" else "kT"
                    if biased:
                        bcol = B["bqF"] if wn == "wq" else B["bkF"]
                        bnm = "bqF" if wn == "wq" else "bkF"
                        p.op("dve", lambda e, w=w, bank=bank, bcol=bcol,
                             dstb=dstb: e.tensor_scalar(
                                dstb[:, w * 512:(w + 1) * 512], bank[:, 0:512],
                                bcol[:, hp:hp + 1], None, ALU.add),
                             reads=[rk(bname), rk(bnm)],
                             writes=[rk(dname, w)], tag=n + f"{dname}ev{hp}.{w}")
                    else:
                        p.op("act", lambda e, w=w, bank=bank, dstb=dstb:
                             e.activation(dstb[:, w * 512:(w + 1) * 512],
                                          bank[:, 0:512], AF.Copy,
                                          bias=0.0, scale=1.0),
                             reads=[rk(bname)], writes=[rk(dname, w)],
                             tag=n + f"{dname}ev{hp}.{w}")

        pst6v = lambda a, b: sc["ps6"][:, a // 2: b // 2].bitcast(BF16)
        for ti in range(NT):
            tbv = sc["pst"] if ti % 2 else pst6v
            tbvk = "ps7" if ti % 2 else "ps6"
            p.op("pe", lambda e, ti=ti, tbv=tbv: e.transpose(
                    tbv(0, 128), B["vT"][:, ti * 128:(ti + 1) * 128],
                    com.ident_bf[:]),
                 reads=[rk("vT", ti // (NT // NW)), rk("c_identbf")],
                 writes=[rk(tbvk)], tag=n + f"vtr{hp}.{ti}")
            dst = vx4[:, ti:ti + 1, :, 0:64]
            src = tbv(0, 128).rearrange("p (h d) -> p h d", h=2
                                        ).unsqueeze(1)
            if biased:
                bsl = B["bvb"][:, hp * 128:(hp + 1) * 128].rearrange(
                    "p (h d) -> p h d", h=2).unsqueeze(1)
                p.op("dve", lambda e, dst=dst, src=src, bsl=bsl: e.tensor_tensor(
                        dst, src, bsl, ALU.add),
                     reads=[rk(tbvk), rk("bvb")], writes=[rk("v_ext")],
                     tag=n + f"vev{hp}.{ti}")
            else:
                p.op("dve", lambda e, dst=dst, src=src: e.tensor_copy(dst, src),
                     reads=[rk(tbvk)], writes=[rk("v_ext")],
                     tag=n + f"vev{hp}.{ti}")

        p.op("dve", lambda e: e.memset(B["gmax"][:], 0.0), writes=[rk("gmax")],
             tag=n + f"gz{hp}")
        sqv = B["vT"][:, 0:T]
        for ci, (which, buf, wname) in enumerate(
                (("q", B["qT"], "qT"), ("k", B["kT"], "kT"))):
            p.op("dve", lambda e, buf=buf, sqv=sqv: e.tensor_tensor(
                    sqv, buf[:, 0:T], buf[:, 0:T], ALU.mult),
                 reads=[rk(wname, w2) for w2 in range(NW)],
                 writes=[rk("vT", ALL)], tag=n + f"sq{which}{hp}")
            for w in range(NW):
                nb_, nbk_ = (psm, psmn) if w % 2 == 0 else (psb[5], psbn[5])
                p.op("pe", lambda e, w=w, sqv=sqv, nb_=nb_: e.matmul(
                        nb_[0:2, 0:512], B["blk2"][:],
                        sqv[:, w * 512:(w + 1) * 512], start=True, stop=True),
                     reads=[rk("vT", ALL), rk("blk2")], writes=[rk(nbk_)],
                     tag=n + f"nrm{which}{hp}.{w}")
                p.op("dve", lambda e, nb_=nb_: e.tensor_reduce(
                        B["n2"][:], nb_[0:2, 0:512], AX.X, ALU.max),
                     reads=[rk(nbk_)], writes=[rk("n2")],
                     tag=n + f"nrd{which}{hp}.{w}")
                p.op("dve", lambda e, ci=ci: e.tensor_tensor(
                        B["gmax"][:, ci:ci + 1], B["gmax"][:, ci:ci + 1],
                        B["n2"][:], ALU.max),
                     reads=[rk("n2"), rk("gmax")], writes=[rk("gmax")],
                     tag=n + f"nmx{which}{hp}.{w}")
        p.op("dve", lambda e: e.tensor_tensor(
                B["cb"][0:2, 0:1], B["gmax"][0:2, 0:1], B["gmax"][0:2, 1:2],
                ALU.mult),
             reads=[rk("gmax")], writes=[rk("cb")], tag=n + f"cbm{hp}")
        p.op("act", lambda e: e.activation(B["cb"][0:2, 0:1], B["cb"][0:2, 0:1],
                                           AF.Ln, bias=B["tiny"][0:2, 0:1],
                                           scale=1.0),
             reads=[rk("cb"), rk("tiny")], writes=[rk("cb")], tag=n + f"cbl{hp}")
        p.op("act", lambda e: e.activation(B["cb"][0:2, 0:1], B["cb"][0:2, 0:1],
                                           AF.Exp, bias=B["lnsm"][0:2, 0:1],
                                           scale=0.5),
             reads=[rk("cb"), rk("lnsm")], writes=[rk("cb")], tag=n + f"cbx{hp}")
        p.op("dve", lambda e: e.tensor_scalar(B["cb"][0:2, 0:1], B["cb"][0:2, 0:1],
                                              -1.0, None, ALU.mult),
             reads=[rk("cb")], writes=[rk("cb")], tag=n + f"cbn{hp}")
        p.dma("pool", B["cb2"][0:1, 0:1], B["cb"][1:2, 0:1], reads=[rk("cb")],
              writes=[rk("cb2")], tag=n + f"cbmv{hp}")
        p.op("dve", lambda e: e.tensor_tensor(
                B["cb"][0:1, 0:1], B["cb"][0:1, 0:1], B["cb2"][0:1, 0:1],
                ALU.min),
             reads=[rk("cb"), rk("cb2")], writes=[rk("cb")], tag=n + f"cbmn{hp}")
        p.op("pe", lambda e: e.matmul(
                psm[:, 0:1], com.ones_row_f[0:1, :], B["cb"][0:1, 0:1],
                start=True, stop=True),
             reads=[rk("cb"), rk("c_ones_row")], writes=[rk(psmn)],
             tag=n + f"cbb{hp}")
        p.op("dve", lambda e: e.tensor_copy(B["negc"][:, 0:1], psm[:, 0:1]),
             reads=[rk(psmn)], writes=[rk("negc")], tag=n + f"cbe{hp}")

        ps01, ps23 = B["ps01"], B["ps23"]

        def record_norm_lr(w):
            for sub in range(2):
                h = 2 * hp + sub
                pvs, pvsn = (psb[4], psbn[4]) if sub == 0 else (psb[5], psbn[5])
                p.op("act", lambda e, pvs=pvs, sub=sub: e.activation(
                        B["srow"][0:1, sub * 512:sub * 512 + 512],
                        pvs[64:65, 0:512], AF.Ln,
                        bias=B["tiny"][0:1, 0:1], scale=1.0),
                     reads=[rk(pvsn), rk("tiny")], writes=[rk("srow", sub)],
                     tag=n + f"L{h}.{w}")
                p.op("act", lambda e, sub=sub: e.activation(
                        B["rrow_bf"][0:1, sub * 512:sub * 512 + 512],
                        B["srow"][0:1, sub * 512:sub * 512 + 512], AF.Exp,
                        bias=B["lnpsc"][0:1, 0:1], scale=-1.0),
                     reads=[rk("srow", sub), rk("lnpsc")],
                     writes=[rk("rrow_bf", sub)], tag=n + f"R{h}.{w}")

        def record_norm_fin(w):
            for sub in range(2):
                h = 2 * hp + sub
                base = sub * 64
                pvs, pvsn = (psb[4], psbn[4]) if sub == 0 else (psb[5], psbn[5])
                p.op("pe", lambda e, sub=sub: e.matmul(
                        psm[0:64, 0:512], com.ones_row_bf[0:1, 0:64],
                        B["rrow_bf"][0:1, sub * 512:sub * 512 + 512],
                        start=True, stop=True),
                     reads=[rk("rrow_bf", sub), rk("c_ones_row_bf")],
                     writes=[rk(psmn)], tag=n + f"RB{h}.{w}")
                p.op("dve", lambda e, sub=sub: e.tensor_copy(
                        B["recb"][0:64, sub * 512:sub * 512 + 512],
                        psm[0:64, 0:512]),
                     reads=[rk(psmn)], writes=[rk("recb", sub)],
                     tag=n + f"RE{h}.{w}")
                p.op("dve", lambda e, pvs=pvs, base=base, hp=hp, w=w, sub=sub,
                     pvsn=pvsn: e.tensor_tensor(
                        B["zT"][base:base + 64,
                                hp * T + w * 512: hp * T + w * 512 + 512],
                        pvs[0:64, 0:512],
                        B["recb"][0:64, sub * 512:sub * 512 + 512], ALU.mult),
                     reads=[rk(pvsn), rk("recb", sub)],
                     writes=[rk(B["zT_name"], (hp, w))], tag=n + f"Z{h}.{w}")

        for w in range(NW):
            kmax = (w * 512 + 511) // 128
            for kc in range(kmax + 1):
                off = max(0, kc * 128 - w * 512)
                nq = 512 - off
                par = kc % 2
                spair = ps01 if par == 0 else ps23
                sk = (psbn[0], psbn[1]) if par == 0 else (psbn[2], psbn[3])
                diag = off > 0 or (kc + 1) * 128 > w * 512
                for sub in range(2):
                    p.op("pe", lambda e, w=w, kc=kc, off=off, nq=nq, sub=sub,
                         spair=spair, diag=diag: e.matmul(
                            spair[:, sub * 512: sub * 512 + nq],
                            B["kT"][sub * 64:sub * 64 + 64,
                                    kc * 128:(kc + 1) * 128],
                            B["qT"][sub * 64:sub * 64 + 64,
                                    w * 512 + off: w * 512 + 512],
                            start=True, stop=not diag),
                         reads=[rk("kT", ALL), rk("qT", ALL)],
                         writes=[rk(sk[sub])],
                         inc=(0 if diag else 1),
                         tag=n + f"S{2*hp+sub}.{w}.{kc}")
                    if diag:
                        p.op("pe", lambda e, sub=sub, spair=spair: e.matmul(
                                spair[:, sub * 512: sub * 512 + 128],
                                com.ident_bf[:], com.maskadd[:, 0:128],
                                start=False, stop=True),
                             reads=[rk("c_identbf"), rk("c_maskadd")],
                             writes=[rk(sk[sub])], tag=n + f"SM{2*hp+sub}.{w}.{kc}")
                if off == 0:
                    p.op("act", lambda e, kc=kc, spair=spair: e.activation(
                            B["PT"][:, kc * 1024: kc * 1024 + 1024],
                            spair[:, 0:1024],
                            AF.Exp, bias=B["negc"][:, 0:1], scale=0.125),
                         reads=[rk(sk[0]), rk(sk[1]), rk("negc")],
                         writes=[rk("PT", kc)],
                         tag=n + f"E{hp}.{w}.{kc}")
                else:
                    for sub in range(2):
                        p.op("act", lambda e, kc=kc, nq=nq, sub=sub,
                             spair=spair: e.activation(
                                B["PT"][:, kc * 1024 + sub * 512:
                                        kc * 1024 + sub * 512 + nq],
                                spair[:, sub * 512: sub * 512 + nq],
                                AF.Exp, bias=B["negc"][:, 0:1], scale=0.125),
                             reads=[rk(sk[sub]), rk("negc")],
                             writes=[rk("PT", kc)],
                             tag=n + f"E{2*hp+sub}.{w}.{kc}")
            if w > 0:
                record_norm_fin(w - 1)
            for kc in range(kmax + 1):
                off = max(0, kc * 128 - w * 512)
                nq = 512 - off
                for sub in range(2):
                    h = 2 * hp + sub
                    pvs, pvsn = psb[4], psbn[4]
                    if sub == 1:
                        pvs, pvsn = psb[5], psbn[5]
                    vsl = vx4[:, kc, sub, :]
                    p.op("pe", lambda e, kc=kc, off=off, nq=nq, vsl=vsl,
                         pvs=pvs, kmax=kmax, sub=sub: e.matmul(
                            pvs[0:65, off:off + nq], vsl,
                            B["PT"][:, kc * 1024 + sub * 512:
                                    kc * 1024 + sub * 512 + nq],
                            start=(kc == 0), stop=(kc == kmax)),
                         reads=[rk("v_ext"), rk("PT", kc)],
                         writes=[rk(pvsn)], inc=(1 if kc == kmax else 0),
                         tag=n + f"PV{h}.{w}.{kc}")
            record_norm_lr(w)
        record_norm_fin(NW - 1)

    wo_sb = B["PT"][:, 0:8192]
    p.dma("pool", wo_sb, wd["wo"], writes=[rk("PT", ALL)], tag=n + "ldwo")
    for ti in range(NT):
        for cf in range(2):
            bi = (2 * ti + cf) % 4
            bank, bname = psb[bi], psbn[bi]
            for ch in range(8):
                p.op("pe", lambda e, ti=ti, cf=cf, ch=ch, bank=bank: e.matmul(
                        bank[:, 0:512],
                        B["zT"][:, ch * T + ti * 128: ch * T + ti * 128 + 128],
                        wo_sb[:, ch * 1024 + cf * 512:
                              ch * 1024 + cf * 512 + 512],
                        start=(ch == 0), stop=(ch == 7)),
                     reads=[rk(B["zT_name"], ALL), rk("PT", ALL)],
                     writes=[rk(bname)], inc=(1 if ch == 7 else 0),
                     tag=n + f"O{ti}.{cf}")
            p.op("dve", lambda e, ti=ti, cf=cf, bank=bank: e.tensor_tensor(
                    B["x"][:, ti * 1024 + cf * 512: ti * 1024 + cf * 512 + 512],
                    B["x"][:, ti * 1024 + cf * 512: ti * 1024 + cf * 512 + 512],
                    bank[:, 0:512], ALU.add),
                 reads=[rk(bname), rk("x_res", ti)], writes=[rk("x_res", ti)],
                 tag=n + f"Oev{ti}.{cf}")


def mha_buffers(p: Prog, geo, x_buf, yT_buf, zT_buf, psb, psb_name,
                ps_misc, ps_misc_name, lnsc, biased, post_scale, pre="",
                ps01=None, ps23=None):
    import math
    _sbuf0 = p.sbuf
    p_sbuf = lambda nm, sh, dt: _sbuf0(pre + nm, sh, dt)
    NT, T = geo["NT"], geo["T"]
    B = {
        "x": x_buf, "yT": yT_buf, "yT_name": "bigbf_a",
        "zT": zT_buf, "zT_name": "bigbf_b",
        "psb": psb, "psb_name": psb_name,
        "ps_misc": ps_misc, "ps_misc_name": ps_misc_name,
        "lnsc": lnsc, "ps01": ps01, "ps23": ps23,
        "v_ext": p_sbuf("v_ext", [128, NT * 130], BF16),
        "qT": p_sbuf("qT", [128, T], BF16),
        "kT": p_sbuf("kT", [128, T], BF16),
        "vT": p_sbuf("vT", [128, T], BF16),
        "PT": p_sbuf("PT", [128, 16 * 1024], BF16),
        "wpair": p_sbuf("wpair", [128, 3 * 1024], BF16),
        "blk2": p_sbuf("blk2", [128, 2], BF16),
        "gmax": p_sbuf("gmax", [2, 2], F32),
        "n2": p_sbuf("n2", [2, 1], F32),
        "cb": p_sbuf("cb", [2, 1], F32),
        "cb2": p_sbuf("cb2", [1, 1], F32),
        "negc": p_sbuf("negc", [128, 2], F32),
        "bvb": p_sbuf("bvb", [128, 1024], BF16),
        "srow": p_sbuf("srow", [1, 1024], F32),
        "rrow_bf": p_sbuf("rrow_bf", [1, 1024], BF16),
        "recb": p_sbuf("recb", [64, 1024], BF16),
        "tiny": p_sbuf("tiny", [128, 1], F32),
        "lnsm": p_sbuf("lnsm", [128, 1], F32),
        "lnpsc": p_sbuf("lnpsc", [128, 1], F32),
        "bqF": p_sbuf("bqF", [128, 8], F32),
        "bkF": p_sbuf("bkF", [128, 8], F32),
        "bv_row": p_sbuf("bv_row", [1, 1024], F32),
    }
    p.op("dve", lambda e: e.memset(B["tiny"][:], 1e-30), writes=[rk("tiny")])
    p.op("dve", lambda e: e.memset(B["lnsm"][:], math.log(0.125)),
         writes=[rk("lnsm")])
    p.op("dve", lambda e: e.memset(B["lnpsc"][:], math.log(post_scale)),
         writes=[rk("lnpsc")])
    p.op("dve", lambda e: e.memset(B["blk2"][:], 0.0), writes=[rk("blk2")])
    p.op("dve", lambda e: e.memset(B["blk2"][0:64, 0:1], 1.0),
         reads=[rk("blk2")], writes=[rk("blk2")])
    p.op("dve", lambda e: e.memset(B["blk2"][64:128, 1:2], 1.0),
         reads=[rk("blk2")], writes=[rk("blk2")])
    return B



def emit_l1_ffn(p: Prog, com: Common, geo, B, wd, name="", wbf=None,
                do_prepass=True):
    NT, T = geo["NT"], geo["T"]
    NWIN = T // 512
    sc = B["lnsc"]
    n = name
    ps = B["ps8"]
    psn = B["ps8_name"]
    yT = B["yT"]
    if wbf is None:
        w1bf = p.nc.dram_tensor(n + "w1bf", [128, 32768], BF16,
                                kind="Internal").ap()
        w2bf = p.nc.dram_tensor(n + "w2bf", [128, 32768], BF16,
                                kind="Internal").ap()
    else:
        w1bf, w2bf = wbf
    if do_prepass:
        for qh in range(4):
            p.dma("pool", w1bf[:, qh * 8192:(qh + 1) * 8192],
                  wd["W1T"][:, qh * 8192:(qh + 1) * 8192],
                  writes=[rk("w1bf_d", qh)], tag=n + f"pc1.{qh}")
            p.dma("pool", w2bf[:, qh * 8192:(qh + 1) * 8192],
                  wd["W2T"][:, qh * 8192:(qh + 1) * 8192],
                  writes=[rk("w2bf_d", qh)], tag=n + f"pc2.{qh}")

    emit_ln(p, com, geo, B["x"], "x_res", yT, B["yT_name"], scratch=sc,
            keep_tile0=(B["lt_tm"], "lt_tm"))

    p.dma("sp", B["scale_sb"][:], wd["scale"], writes=[rk("scale_sb")],
          tag=n + "sc")
    for j in range(4):
        p.dma("pool", B["sc4"][j:j + 1, :], B["scale_sb"][0:1, :],
              reads=[rk("scale_sb")], writes=[rk("sc4")], tag=n + f"sc4.{j}")
    p.dma("sp", B["b1F"][:], wd["b1F"], writes=[rk("b1Fc")], tag=n + "b1")
    p.op("dve", lambda e: e.memset(B["b2pad"][:], 0.0), writes=[rk("b2pad")])
    p.dma("pool", B["b2pad"][0:1, :], wd["b2"], reads=[rk("b2pad")],
          writes=[rk("b2pad")], tag=n + "b2")
    p.op("dve", lambda e: e.memset(B["onespad"][:], 0.0), writes=[rk("onespad")])
    p.op("dve", lambda e: e.memset(B["onespad"][0:1, :], 1.0),
         reads=[rk("onespad")], writes=[rk("onespad")])
    for j in range(4):
        p.dma("pool", B["se_rows"][0:1, j * 1024:(j + 1) * 1024],
              B["lt_tm"][1 + 2 * j:2 + 2 * j, :], reads=[rk("lt_tm")],
              writes=[rk("se_rows")], tag=n + f"ser{j}")
        p.dma("pool", B["e4_rows"][0:1, j * 1024:(j + 1) * 1024],
              B["lt_tm"][9 + 2 * j:10 + 2 * j, :], reads=[rk("lt_tm")],
              writes=[rk("e4_rows")], tag=n + f"e4r{j}")
    p.op("dve", lambda e: e.memset(B["se_pad"][:], 0.0), writes=[rk("se_pad")])
    p.op("dve", lambda e: e.tensor_scalar(
            B["se_pad"][0:1, :], B["se_rows"][0:1, :],
            B["scale_sb"][0:1, 0:1], None, ALU.mult),
         reads=[rk("se_rows"), rk("scale_sb")], writes=[rk("se_pad")],
         tag=n + "sescl")
    p.op("dve", lambda e: e.memset(B["E_pad"][:], 0.0), writes=[rk("E_pad")])
    for j in range(4):
        for pm in range(4):
            src = B["e4_rows"][0:1, j * 1024 + pm: (j + 1) * 1024: 4]
            p.dma("pool", B["E_pad"][pm:pm + 1, j * 256:(j + 1) * 256], src,
                  reads=[rk("e4_rows")], writes=[rk("E_pad")],
                  tag=n + f"Ep{j}.{pm}", allow_slow_non_contiguous=True)

    p.op("dve", lambda e: e.memset(B["a1pad"][:], 0.0),
         writes=[rk("a1pad", "z")] + [rk("a1pad", j) for j in range(4)],
         tag=n + "a1pz")
    p.op("dve", lambda e: e.memset(B["g_pad"][:], 0.0),
         writes=[rk("g_pad", "z")] + [rk("g_pad", j) for j in range(4)],
         tag=n + "gpz")

    for w in range(NWIN):
        ytk = [rk(B["yT_name"], 0)] + [rk(B["yT_name"], t)
                                       for t in range(4 * w, 4 * w + 4)]
        for ch in range(8):
            p.op("pe", lambda e, w=w, ch=ch: e.matmul(
                    ps[6][0:4, 0:512],
                    yT[:, ch * T + 2: ch * T + 10: 2],
                    yT[:, ch * T + w * 512: ch * T + (w + 1) * 512],
                    start=(ch == 0), stop=(ch == 7)),
                 reads=ytk, writes=[rk(psn[6])],
                 inc=(1 if ch == 7 else 0), tag=n + f"a1.{w}.{ch}")
        p.op("dve", lambda e, w=w: e.tensor_copy(
                B["a1s"][0:4, w * 512:(w + 1) * 512], ps[6][0:4, 0:512]),
             reads=[rk(psn[6])], writes=[rk("a1s")], tag=n + f"a1e{w}")

    for w in range(NWIN):
        for j in range(4):
            p.dma("pool", B["a1pad"][0:1, j * 512:(j + 1) * 512],
                  B["a1s"][j:j + 1, w * 512:(w + 1) * 512],
                  reads=[rk("a1s")], writes=[rk("a1pad", j)],
                  tag=n + f"a1p{w}.{j}")
        def rec_gT(hc, w=w):
            j = hc // 8
            dch = hc % 8
            p.op("pe", lambda e, hc=hc, j=j, dch=dch: e.matmul(
                    ps[2 + j][0:4, 0:512],
                    yT[:, dch * T + 10: dch * T + 18: 2],
                    B["H"][:, hc * 512:(hc + 1) * 512],
                    start=(dch == 0), stop=(dch == 7)),
                 reads=[rk(B["yT_name"], 0), rk("H", hc)],
                 writes=[rk(psn[2 + j])], inc=(1 if dch == 7 else 0),
                 tag=n + f"gT.{w}.{hc}")

        for hc in range(32):
            j = hc // 8
            slot = hc % 2
            if hc % 4 == 0:
                grp = (hc // 4) % 2
                p.dma("sp", B["w1buf"][:, grp * 4096:(grp + 1) * 4096],
                      w1bf[:, hc * 1024:(hc + 4) * 1024],
                      reads=[rk("w1bf_d", hc // 8)],
                      writes=[rk("wbuf", grp)], tag=n + f"w1.{w}.{hc}")
            grp = (hc // 4) % 2
            wof = grp * 4096 + (hc % 4) * 1024
            hps, hpsn = ps[slot], psn[slot]
            for ch in range(8):
                p.op("pe", lambda e, hc=hc, ch=ch, wof=wof, w=w, hps=hps:
                     e.matmul(
                        hps[:, 0:512],
                        B["w1buf"][:, wof + ch * 128: wof + (ch + 1) * 128],
                        yT[:, ch * T + w * 512: ch * T + (w + 1) * 512],
                        start=(ch == 0), stop=False),
                     reads=[rk("wbuf", grp)] + [rk(B["yT_name"], t)
                                                for t in range(4 * w, 4 * w + 4)],
                     writes=[rk(hpsn)], inc=0, tag=n + f"W1.{w}.{hc}.{ch}")
            p.op("pe", lambda e, hc=hc, j=j, hps=hps: e.matmul(
                    hps[:, 0:512],
                    B["se_pad"][:, hc * 128:(hc + 1) * 128],
                    B["a1pad"][:, j * 512:(j + 1) * 512],
                    start=False, stop=True),
                 reads=[rk("se_pad"), rk("a1pad", j), rk("a1pad", "z")],
                 writes=[rk(hpsn)], tag=n + f"lora1.{w}.{hc}")
            if hc > 0:
                rec_gT(hc - 1)
            p.op("act", lambda e, hc=hc, hps=hps: e.activation(
                    B["H"][:, hc * 512:(hc + 1) * 512], hps[:, 0:512],
                    AF.Gelu, bias=B["b1F"][:, hc:hc + 1], scale=1.0),
                 reads=[rk(hpsn), rk("b1Fc")], writes=[rk("H", hc)],
                 tag=n + f"gel.{w}.{hc}")
        rec_gT(31)
        for pm in range(4):
            p.op("dve", lambda e, pm=pm: e.tensor_scalar(
                    B["gT_sb"][0:4, pm * 512:(pm + 1) * 512],
                    ps[2 + pm][0:4, 0:512],
                    B["sc4"][0:4, 0:1], None, ALU.mult),
                 reads=[rk(psn[2 + pm]), rk("sc4")], writes=[rk("gT_sb", pm)],
                 tag=n + f"gTe.{w}.{pm}")
        for j in range(4):
            for pm in range(4):
                p.dma("pool", B["g_pad"][pm:pm + 1, j * 512:(j + 1) * 512],
                      B["gT_sb"][j:j + 1, pm * 512:(pm + 1) * 512],
                      reads=[rk("gT_sb", pm)], writes=[rk("g_pad", j)],
                      tag=n + f"gp.{w}.{j}.{pm}")

        for hc in range(32):
            if hc % 4 == 0:
                grp = (hc // 4) % 2
                p.dma("sp", B["w1buf"][:, grp * 4096:(grp + 1) * 4096],
                      w2bf[:, hc * 1024:(hc + 4) * 1024],
                      reads=[rk("w2bf_d", hc // 8)],
                      writes=[rk("wbuf", grp)], tag=n + f"w2.{w}.{hc}")
            grp = (hc // 4) % 2
            wof = grp * 4096 + (hc % 4) * 1024
            for tt in range(4):
                for cf in range(2):
                    bi = tt * 2 + cf
                    p.op("pe", lambda e, hc=hc, tt=tt, cf=cf, bi=bi, wof=wof,
                         w=w: e.matmul(
                            ps[bi][:, 0:512],
                            B["H"][:, hc * 512 + tt * 128:
                                   hc * 512 + (tt + 1) * 128],
                            B["w1buf"][:, wof + cf * 512:
                                       wof + (cf + 1) * 512],
                            start=(hc == 0), stop=False),
                         reads=[rk("wbuf", grp), rk("H", hc)],
                         writes=[rk(psn[bi])],
                         inc=(1 if (tt == 3 and cf == 1) else 0),
                         tag=n + f"W2.{w}.{hc}.{tt}.{cf}")
        for tt in range(4):
            for j in range(4):
                cf = j // 2
                bi = tt * 2 + cf
                p.op("pe", lambda e, tt=tt, j=j, bi=bi, cf=cf: e.matmul(
                        ps[bi][:, (j % 2) * 256: (j % 2) * 256 + 256],
                        B["g_pad"][:, j * 512 + tt * 128:
                                   j * 512 + (tt + 1) * 128],
                        B["E_pad"][:, j * 256:(j + 1) * 256],
                        start=False, stop=False),
                     reads=[rk("g_pad", j), rk("g_pad", "z"), rk("E_pad")],
                     writes=[rk(psn[bi])], inc=0, tag=n + f"lora2.{w}.{tt}.{j}")
            for cf in range(2):
                bi = tt * 2 + cf
                p.op("pe", lambda e, tt=tt, cf=cf, bi=bi: e.matmul(
                        ps[bi][:, 0:512], B["onespad"][:, 0:128],
                        B["b2pad"][:, cf * 512:(cf + 1) * 512],
                        start=False, stop=True),
                     reads=[rk("onespad"), rk("b2pad")],
                     writes=[rk(psn[bi])], tag=n + f"b2f.{w}.{tt}.{cf}")
                ti = w * 4 + tt
                p.op("dve", lambda e, ti=ti, cf=cf, bi=bi: e.scalar_tensor_tensor(
                        B["x"][:, ti * 1024 + cf * 512:
                               ti * 1024 + cf * 512 + 512],
                        ps[bi][:, 0:512], 0.5,
                        B["x"][:, ti * 1024 + cf * 512:
                               ti * 1024 + cf * 512 + 512],
                        ALU.mult, ALU.add),
                     reads=[rk(psn[bi]), rk("x_res", ti)],
                     writes=[rk("x_res", ti)], tag=n + f"xup.{w}.{tt}.{cf}")


def l1ffn_buffers(p: Prog, geo, x_buf, yT_buf, H_buf, ps8, ps8_name, lnsc,
                  pre=""):
    T = geo["T"]
    _sbuf0 = p.sbuf
    p_sbuf = lambda nm, sh, dt: _sbuf0(pre + nm, sh, dt)
    B = {
        "x": x_buf, "yT": yT_buf, "yT_name": "bigbf_a",
        "H": H_buf, "ps8": ps8, "ps8_name": ps8_name, "lnsc": lnsc,
        "lt_tm": p_sbuf("lt_tm", [128, 1024], BF16),
        "scale_sb": p_sbuf("scale_sb", [1, 1], F32),
        "sc4": p_sbuf("sc4", [4, 1], F32),
        "b1F": p_sbuf("b1Fc", [128, 32], F32),
        "b2pad": p_sbuf("b2pad", [128, 1024], BF16),
        "onespad": p_sbuf("onespad", [128, 128], BF16),
        "se_rows": p_sbuf("se_rows", [1, 4096], BF16),
        "e4_rows": p_sbuf("e4_rows", [1, 4096], BF16),
        "se_pad": p_sbuf("se_pad", [128, 4096], BF16),
        "E_pad": p_sbuf("E_pad", [128, 1024], BF16),
        "a1s": p_sbuf("a1s", [4, T], BF16),
        "a1pad": p_sbuf("a1pad", [128, 2048], BF16),
        "gT_sb": p_sbuf("gT_sb", [4, 2048], BF16),
        "g_pad": p_sbuf("g_pad", [128, 2048], BF16),
        "w1buf": p_sbuf("wbuf", [128, 8192], BF16),
    }
    return B


def alloc_psum_pool(p: Prog):
    ps01 = p.psum("psum01", [128, 1024], F32)
    ps23 = p.psum("psum23", [128, 1024], F32)
    ps = [ps01[:, 0:512], ps01[:, 512:1024],
          ps23[:, 0:512], ps23[:, 512:1024]]
    psn = ["ps0", "ps1", "ps2", "ps3"]
    hs = {}
    for i in range(4, 8):
        h = p.psum(f"ps{i}", [128, 512], F32)
        ps.append(h)
        psn.append(f"ps{i}")
        hs[i] = h
    return ps, psn, ps01, ps23, hs


def phase_barrier(p: Prog, names):
    if "barsc" not in p.buffers:
        p.sbuf("barsc", [1, 1], F32)
    bs = p.buffers["barsc"]
    for nm in names:
        p.op("dve", lambda e: e.memset(bs[:], 0.0),
             reads=[rk(nm)], writes=[rk(nm), rk("barsc")],
             tag=f"barrier-{nm}")



def emit_gating(p: Prog, com: Common, geo, B, nT, nT_name, simn, simn_key,
                sigb, sigb_key, rw_buf, rw_key, fold, name=""):
    NT, T = geo["NT"], geo["T"]
    NWIN = T // 512
    n = name
    ps = B["ps8"]
    psn = B["ps8_name"]
    g = B["gat"]

    for w in range(NWIN):
        for ch in range(8):
            p.op("dve", lambda e, ch=ch, w=w: e.tensor_tensor(
                    B["sq2w"][:], nT[:, ch * T + w * 512: ch * T + (w + 1) * 512],
                    nT[:, ch * T + w * 512: ch * T + (w + 1) * 512], ALU.mult),
                 reads=[rk(nT_name, ALL)], writes=[rk("sq2w")],
                 tag=n + f"gsq{w}.{ch}")
            p.op("pe", lambda e, ch=ch: e.matmul(
                    ps[6][0:1, 0:512], com.ones_col_bf[:], B["sq2w"][:],
                    start=(ch == 0), stop=(ch == 7)),
                 reads=[rk("sq2w"), rk("c_ones_col")], writes=[rk(psn[6])],
                 tag=n + f"gn{w}.{ch}")
        p.op("dve", lambda e, w=w: e.tensor_copy(
                g["n2row"][0:1, w * 512:(w + 1) * 512], ps[6][0:1, 0:512]),
             reads=[rk(psn[6])], writes=[rk("g_n2row")], tag=n + f"gnev{w}")
    for ti in range(NT):
        p.op("pe", lambda e, ti=ti: e.transpose(
                ps[6][:, 0:1], g["n2row"][0:1, ti * 128:(ti + 1) * 128],
                com.ones_row_f[0:1, 0:1]),
             reads=[rk("g_n2row"), rk("c_ones_row")], writes=[rk(psn[6])],
             tag=n + f"gtr{ti}")
        p.op("dve", lambda e, ti=ti: e.tensor_copy(
                g["n2c"][:, ti:ti + 1], ps[6][:, 0:1]),
             reads=[rk(psn[6])], writes=[rk("g_n2c")], tag=n + f"gtc{ti}")
    p.op("act", lambda e: e.activation(g["n2c"][:], g["n2c"][:], AF.Ln,
                                       bias=com.tiny[:, 0:1], scale=1.0),
         reads=[rk("g_n2c"), rk("c_tiny")], writes=[rk("g_n2c")], tag=n + "gln")
    p.op("act", lambda e: e.activation(g["rnc"][:], g["n2c"][:], AF.Exp,
                                       bias=com.zero_col[:, 0:1], scale=-0.5),
         reads=[rk("g_n2c"), rk("c_zero")], writes=[rk("g_rnc")], tag=n + "gex")

    for ti in range(NT):
        for ch in range(8):
            p.op("pe", lambda e, ti=ti, ch=ch: e.matmul(
                    ps[5][:, 0:8], nT[:, ch * T + ti * 128: ch * T + ti * 128 + 128],
                    simn[:, ch * 8:(ch + 1) * 8], start=(ch == 0), stop=(ch == 7)),
                 reads=[rk(nT_name, ALL), rk(simn_key)], writes=[rk(psn[5])],
                 inc=(1 if ch == 7 else 0), tag=n + f"glg{ti}.{ch}")
        p.op("dve", lambda e, ti=ti: e.tensor_scalar(
                g["lg"][:, ti * 8:(ti + 1) * 8], ps[5][:, 0:8],
                g["rnc"][:, ti:ti + 1], None, ALU.mult),
             reads=[rk(psn[5]), rk("g_rnc")], writes=[rk("g_lg", ti)],
             tag=n + f"glge{ti}")
        p.op("dve", lambda e, ti=ti: e.tensor_tensor(
                g["lg"][:, ti * 8:(ti + 1) * 8],
                g["lg"][:, ti * 8:(ti + 1) * 8], sigb[:], ALU.subtract),
             reads=[rk("g_lg", ti), rk(sigb_key)], writes=[rk("g_lg", ti)],
             tag=n + f"glgs{ti}")

    lgk = [rk("g_lg", ti) for ti in range(NT)]
    p.op("dve", lambda e: e.tensor_scalar(g["gated"][:, 0:NT * 8],
                                          g["lg"][:, 0:NT * 8], 0.0, None,
                                          ALU.max),
         reads=lgk, writes=[rk("g_gated")], tag=n + "ggt")
    p.op("dve", lambda e: e.tensor_scalar(g["amask"][:, 0:NT * 8],
                                          g["lg"][:, 0:NT * 8], 0.0, None,
                                          ALU.is_gt),
         reads=lgk, writes=[rk("g_amask")], tag=n + "gam")
    lg3 = g["lg"][:, 0:NT * 8].rearrange("p (t e) -> p t e", e=8)
    p.op("dve", lambda e: e.tensor_reduce(
            g["rowhas"][:, 0:NT], g["amask"][:, 0:NT * 8].rearrange(
                "p (t e) -> p t e", e=8), AX.X, ALU.add),
         reads=[rk("g_amask")], writes=[rk("g_rowhas")], tag=n + "grh")
    p.op("dve", lambda e: e.tensor_scalar(g["inact"][:, 0:NT],
                                          g["rowhas"][:, 0:NT], 0.0, None,
                                          ALU.is_le),
         reads=[rk("g_rowhas")], writes=[rk("g_inact")], tag=n + "gin")
    p.op("dve", lambda e, lg3=lg3: e.tensor_reduce(
            g["m1"][:, 0:NT], lg3, AX.X, ALU.max),
         reads=lgk, writes=[rk("g_m1")], tag=n + "gm1")
    for ti in range(NT):
        p.op("dve", lambda e, ti=ti: e.tensor_scalar(
                g["eq1"][:, 0:8], g["lg"][:, ti * 8:(ti + 1) * 8],
                g["m1"][:, ti:ti + 1], -1e30, ALU.is_equal, ALU.mult),
             reads=[rk("g_lg", ti), rk("g_m1")], writes=[rk("g_eq1")],
             tag=n + f"geq{ti}")
        p.op("dve", lambda e, ti=ti: e.tensor_tensor(
                g["lmask"][:, ti * 8:(ti + 1) * 8], g["eq1"][:, 0:8],
                g["lg"][:, ti * 8:(ti + 1) * 8], ALU.add),
             reads=[rk("g_eq1"), rk("g_lg", ti)], writes=[rk("g_lmask", ti)],
             tag=n + f"glm{ti}")
    p.op("dve", lambda e: e.tensor_reduce(
            g["m2"][:, 0:NT], g["lmask"][:, 0:NT * 8].rearrange(
                "p (t e) -> p t e", e=8), AX.X, ALU.max),
         reads=[rk("g_lmask", ti) for ti in range(NT)], writes=[rk("g_m2")],
         tag=n + "gm2")
    for ti in range(NT):
        p.op("dve", lambda e, ti=ti: e.tensor_scalar(
                g["eq1"][:, 0:8], g["lg"][:, ti * 8:(ti + 1) * 8],
                g["m2"][:, ti:ti + 1], None, ALU.is_ge),
             reads=[rk("g_lg", ti), rk("g_m2")], writes=[rk("g_eq1")],
             tag=n + f"gt2{ti}")
        p.op("dve", lambda e, ti=ti: e.tensor_tensor(
                g["eq1"][:, 0:8], g["eq1"][:, 0:8],
                g["amask"][:, ti * 8:(ti + 1) * 8], ALU.subtract),
             reads=[rk("g_eq1"), rk("g_amask")], writes=[rk("g_eq1")],
             tag=n + f"gd{ti}")
        p.op("dve", lambda e, ti=ti: e.scalar_tensor_tensor(
                g["mask"][:, ti * 8:(ti + 1) * 8], g["eq1"][:, 0:8],
                g["inact"][:, ti:ti + 1],
                g["amask"][:, ti * 8:(ti + 1) * 8], ALU.mult, ALU.add),
             reads=[rk("g_eq1"), rk("g_inact"), rk("g_amask")],
             writes=[rk("g_mask", ti)], tag=n + f"gmk{ti}")
    p.op("act", lambda e: e.activation(g["expg"][:, 0:NT * 8],
                                       g["gated"][:, 0:NT * 8], AF.Exp,
                                       bias=com.zero_col[:, 0:1], scale=1.0),
         reads=[rk("g_gated"), rk("c_zero")], writes=[rk("g_expg")],
         tag=n + "gxp")
    p.op("dve", lambda e: e.tensor_tensor(
            g["rwun"][:, 0:NT * 8], g["mask"][:, 0:NT * 8],
            g["expg"][:, 0:NT * 8], ALU.mult),
         reads=[rk("g_mask", ti) for ti in range(NT)] + [rk("g_expg")],
         writes=[rk("g_rwun")], tag=n + "gru")
    p.op("dve", lambda e: e.tensor_reduce(
            g["zsum"][:, 0:NT], g["rwun"][:, 0:NT * 8].rearrange(
                "p (t e) -> p t e", e=8), AX.X, ALU.add),
         reads=[rk("g_rwun")], writes=[rk("g_zsum")], tag=n + "gz")
    p.op("dve", lambda e: e.reciprocal(g["rz"][:, 0:NT], g["zsum"][:, 0:NT]),
         reads=[rk("g_zsum")], writes=[rk("g_rz")], tag=n + "grz")
    for ti in range(NT):
        p.op("dve", lambda e, ti=ti: e.tensor_scalar(
                rw_buf[:, ti * 8:(ti + 1) * 8], g["rwun"][:, ti * 8:(ti + 1) * 8],
                g["rz"][:, ti:ti + 1], fold, ALU.mult, ALU.mult),
             reads=[rk("g_rwun"), rk("g_rz")], writes=[rk(rw_key, ti)],
             tag=n + f"grw{ti}")


def gating_scratch(p: Prog, geo, pre=""):
    NT, T = geo["NT"], geo["T"]
    names = {"n2row": [1, T], "n2c": [128, NT], "rnc": [128, NT],
             "lg": [128, NT * 8], "gated": [128, NT * 8],
             "amask": [128, NT * 8], "rowhas": [128, NT], "inact": [128, NT],
             "m1": [128, NT], "m2": [128, NT], "eq1": [128, 8],
             "lmask": [128, NT * 8], "mask": [128, NT * 8],
             "expg": [128, NT * 8], "rwun": [128, NT * 8],
             "zsum": [128, NT], "rz": [128, NT]}
    d = {k: p.sbuf(pre + "g_" + k, sh, F32) for k, sh in names.items()}
    return d



def prep_sim(p: Prog, com: Common, B, sim_d, simn_buf, simn_key, name=""):
    n = name
    ps = B["ps8"]
    psn = B["ps8_name"]
    p.dma("sp", B["sim_f"][:], sim_d, writes=[rk("sim_f")], tag=n + "ld")
    p.op("dve", lambda e: e.tensor_tensor(B["sim_sq"][:], B["sim_f"][:],
                                          B["sim_f"][:], ALU.mult),
         reads=[rk("sim_f")], writes=[rk("sim_sq")], tag=n + "sq")
    for ch in range(8):
        p.op("pe", lambda e, ch=ch: e.matmul(
                ps[6][0:1, 0:8], B["ones_col_f"][:],
                B["sim_sq"][:, ch * 8:(ch + 1) * 8],
                start=(ch == 0), stop=(ch == 7)),
             reads=[rk("sim_sq"), rk("ones_col_f")], writes=[rk(psn[6])],
             inc=(1 if ch == 7 else 0), tag=n + f"n{ch}")
    p.op("act", lambda e: e.activation(B["srn"][0:1, 0:8], ps[6][0:1, 0:8],
                                       AF.Ln, bias=com.tiny[0:1, 0:1], scale=1.0),
         reads=[rk(psn[6]), rk("c_tiny")], writes=[rk("srn")], tag=n + "ln")
    p.op("act", lambda e: e.activation(B["srn"][0:1, 0:8], B["srn"][0:1, 0:8],
                                       AF.Exp, bias=com.zero_col[0:1, 0:1],
                                       scale=-0.5),
         reads=[rk("srn"), rk("c_zero")], writes=[rk("srn")], tag=n + "ex")
    p.op("pe", lambda e: e.matmul(ps[6][:, 0:8], com.ones_row_f[0:1, :],
                                  B["srn"][0:1, 0:8], start=True, stop=True),
         reads=[rk("srn"), rk("c_ones_row")], writes=[rk(psn[6])], tag=n + "bc")
    p.op("dve", lambda e: e.tensor_copy(B["srnb"][:], ps[6][:, 0:8]),
         reads=[rk(psn[6])], writes=[rk("srnb")], tag=n + "be")
    for ch in range(8):
        p.op("dve", lambda e, ch=ch: e.tensor_tensor(
                simn_buf[:, ch * 8:(ch + 1) * 8],
                B["sim_f"][:, ch * 8:(ch + 1) * 8], B["srnb"][:], ALU.mult),
             reads=[rk("sim_f"), rk("srnb")], writes=[rk(simn_key)],
             tag=n + f"m{ch}")


def prep_sig(p: Prog, com: Common, B, gates_d, sigb_buf, sigb_key, name=""):
    n = name
    ps = B["ps8"]
    psn = B["ps8_name"]
    p.dma("sp", B["grow"][:], gates_d, writes=[rk("grow")], tag=n + "ld")
    p.op("act", lambda e: e.activation(B["grow"][0:1, :], B["grow"][0:1, :],
                                       AF.Exp, bias=com.zero_col[0:1, 0:1],
                                       scale=-1.0),
         reads=[rk("grow"), rk("c_zero")], writes=[rk("grow")], tag=n + "ex")
    p.op("dve", lambda e: e.tensor_scalar(B["grow"][0:1, :], B["grow"][0:1, :],
                                          1.0, None, ALU.add),
         reads=[rk("grow")], writes=[rk("grow")], tag=n + "a1")
    p.op("dve", lambda e: e.reciprocal(B["grow"][0:1, :], B["grow"][0:1, :]),
         reads=[rk("grow")], writes=[rk("grow")], tag=n + "rc")
    p.op("pe", lambda e: e.matmul(ps[6][:, 0:8], com.ones_row_f[0:1, :],
                                  B["grow"][0:1, 0:8], start=True, stop=True),
         reads=[rk("grow"), rk("c_ones_row")], writes=[rk(psn[6])], tag=n + "bc")
    p.op("dve", lambda e: e.tensor_copy(sigb_buf[:], ps[6][:, 0:8]),
         reads=[rk(psn[6])], writes=[rk(sigb_key)], tag=n + "be")


def emit_l0_ffn(p: Prog, com: Common, geo, B, wd, name=""):
    NT, T = geo["NT"], geo["T"]
    n = name
    ps = B["ps8"]
    psn = B["ps8_name"]
    sc = B["lnsc"]
    n1T = B["bigbf_a"]
    n2T = B["bigbf_b"]
    vx = B["v_ext"][:, 0:NT * 129].rearrange("p (t d) -> p t d", d=129)

    emit_ln(p, com, geo, B["x"], "x_res", n1T, "bigbf_a",
            affine=(B["ln1w"], B["ln1b"], "ln1wF", "ln1bF"), fuse_double=True,
            out_stats=(B["ystats"], "ystats"), scratch=sc)

    emit_gating(p, com, geo, B, n1T, "bigbf_a", B["simn1"], "simn1",
                B["sigb1"], "sigb1", B["rw"], "rw", 1.0, name=n + "g1")

    pst6q = lambda a, b: ps[6][:, a // 2: b // 2].bitcast(BF16)
    p.op("dve", lambda e: e.memset(vx[:, :, 128:129], 1.0),
         writes=[rk("v_ext")], tag=n + "vone")
    qkv_src = ("wq0", "wk0", "wv0")
    qkv_buf = ((B["wbig"], "wbig"), (B["wbig2"], "wbig2"), (B["wbig"], "wbig"))

    def ld_qkv(iw):
        wb_, wkk_ = qkv_buf[iw]
        for c4 in range(4):
            p.dma("pool", wb_[:, c4 * 2048:(c4 + 1) * 2048],
                  wd[qkv_src[iw]][:, c4 * 2048:(c4 + 1) * 2048],
                  writes=[rk(wkk_, c4)], tag=n + f"ld{qkv_src[iw]}.{c4}")

    ld_qkv(0)
    ld_qkv(1)
    for iw, wn in enumerate(("q", "k", "v")):
        wb, wkk = qkv_buf[iw]
        if iw == 1:
            ld_qkv(2)

        def rec_t(ti, wn=wn):
            tp = ti % 2
            dst = B["qT"] if wn == "q" else B["kT"]
            dk = "qT" if wn == "q" else "kT"
            qb = B["qbf"][:, tp * 128:(tp + 1) * 128]
            tb = (pst6q if tp == 0 else sc["pst"])
            tbk = psn[6] if tp == 0 else "ps7"
            p.op("pe", lambda e, qb=qb, tb=tb: e.transpose(
                    tb(0, 128), qb, com.ident_bf[:]),
                 reads=[rk("qbf", tp), rk("c_identbf")], writes=[rk(tbk)],
                 tag=n + f"t{wn}{ti}")
            p.op("act", lambda e, ti=ti, dst=dst, tb=tb: e.activation(
                    dst[:, ti * 128:(ti + 1) * 128], tb(0, 128),
                    AF.Copy, bias=0.0, scale=1.0),
                 reads=[rk(tbk)], writes=[rk(dk, ti)],
                 tag=n + f"e{wn}{ti}")

        for ti in range(NT):
            for half in range(2):
                for ch in range(8):
                    p.op("pe", lambda e, ti=ti, half=half, ch=ch, wb=wb:
                         e.matmul(
                            ps[half][:, 0:512],
                            n1T[:, ch * T + ti * 128: ch * T + ti * 128 + 128],
                            wb[:, ch * 1024 + half * 512:
                               ch * 1024 + half * 512 + 512],
                            start=(ch == 0), stop=(ch == 7)),
                         reads=[rk("bigbf_a", ti), rk(wkk, ch // 2)],
                         writes=[rk(psn[half])], inc=(1 if ch == 7 else 0),
                         tag=n + f"P{wn}{ti}.{half}.{ch}")
            if wn != "v" and ti > 0:
                rec_t(ti - 1)
            tp = ti % 2
            qa = B["qacc"][:, tp * 128:(tp + 1) * 128]
            for e_ in range(8):
                bank = ps[e_ // 4]
                bnm = psn[e_ // 4]
                sl = (e_ % 4) * 128
                if e_ == 0:
                    p.op("dve", lambda e, bank=bank, sl=sl, ti=ti, qa=qa:
                         e.tensor_scalar(
                            qa, bank[:, sl:sl + 128],
                            B["rw"][:, ti * 8:ti * 8 + 1], None, ALU.mult),
                         reads=[rk(bnm), rk("rw", ti)], writes=[rk("qacc", tp)],
                         tag=n + f"W{wn}{ti}.0")
                else:
                    p.op("dve", lambda e, bank=bank, sl=sl, ti=ti, e_=e_, qa=qa:
                         e.scalar_tensor_tensor(
                            qa, bank[:, sl:sl + 128],
                            B["rw"][:, ti * 8 + e_:ti * 8 + e_ + 1],
                            qa, ALU.mult, ALU.add),
                         reads=[rk(bnm), rk("rw", ti), rk("qacc", tp)],
                         writes=[rk("qacc", tp)], tag=n + f"W{wn}{ti}.{e_}")
            if wn == "v":
                p.op("dve", lambda e, ti=ti, qa=qa: e.tensor_copy(
                        vx[:, ti, 0:128], qa),
                     reads=[rk("qacc", tp)], writes=[rk("v_ext")],
                     tag=n + f"vev{ti}")
            else:
                qb = B["qbf"][:, tp * 128:(tp + 1) * 128]
                p.op("dve", lambda e, qa=qa, qb=qb: e.tensor_copy(qb, qa),
                     reads=[rk("qacc", tp)], writes=[rk("qbf", tp)],
                     tag=n + f"c{wn}{ti}")
        if wn != "v":
            rec_t(NT - 1)

    p.op("dve", lambda e: e.memset(B["gm1"][:], 0.0), writes=[rk("gm1")],
         tag=n + "gm0")
    sq2v = B["PT"][:, 0:T]
    for ci, (buf, bk) in enumerate(((B["qT"], "qT"), (B["kT"], "kT"))):
        p.op("dve", lambda e, buf=buf, sq2v=sq2v: e.tensor_tensor(
                sq2v, buf[:, 0:T], buf[:, 0:T], ALU.mult),
             reads=[rk(bk, ti) for ti in range(NT)], writes=[rk("PT")],
             tag=n + f"s2{ci}")
        for w in range(T // 512):
            sb_, sbk_ = (ps[6], psn[6]) if w % 2 == 0 else (ps[5], psn[5])
            p.op("pe", lambda e, w=w, sq2v=sq2v, sb_=sb_: e.matmul(
                    sb_[0:1, 0:512], com.ones_col_bf[:],
                    sq2v[:, w * 512:(w + 1) * 512], start=True, stop=True),
                 reads=[rk("PT"), rk("c_ones_col")], writes=[rk(sbk_)],
                 tag=n + f"s2n{ci}.{w}")
            p.op("dve", lambda e, sb_=sb_: e.tensor_reduce(
                    B["n21"][0:1, 0:1], sb_[0:1, 0:512], AX.X, ALU.max),
                 reads=[rk(sbk_)], writes=[rk("n21")], tag=n + f"s2r{ci}.{w}")
            p.op("dve", lambda e, ci=ci: e.tensor_tensor(
                    B["gm1"][0:1, ci:ci + 1], B["gm1"][0:1, ci:ci + 1],
                    B["n21"][0:1, 0:1], ALU.max),
                 reads=[rk("n21"), rk("gm1")], writes=[rk("gm1")],
                 tag=n + f"s2m{ci}.{w}")
    p.op("dve", lambda e: e.tensor_tensor(B["gm1"][0:1, 0:1], B["gm1"][0:1, 0:1],
                                          B["gm1"][0:1, 1:2], ALU.mult),
         reads=[rk("gm1")], writes=[rk("gm1")], tag=n + "cbp")
    p.op("act", lambda e: e.activation(B["gm1"][0:1, 0:1], B["gm1"][0:1, 0:1],
                                       AF.Ln, bias=com.tiny[0:1, 0:1], scale=1.0),
         reads=[rk("gm1"), rk("c_tiny")], writes=[rk("gm1")], tag=n + "cbl")
    p.op("act", lambda e: e.activation(B["gm1"][0:1, 0:1], B["gm1"][0:1, 0:1],
                                       AF.Exp, bias=B["lnsm128"][0:1, 0:1],
                                       scale=0.5),
         reads=[rk("gm1"), rk("lnsm128")], writes=[rk("gm1")], tag=n + "cbx")
    p.op("dve", lambda e: e.tensor_scalar(B["gm1"][0:1, 0:1], B["gm1"][0:1, 0:1],
                                          -1.0, None, ALU.mult),
         reads=[rk("gm1")], writes=[rk("gm1")], tag=n + "cbn")
    p.op("pe", lambda e: e.matmul(ps[6][:, 0:1], com.ones_row_f[0:1, :],
                                  B["gm1"][0:1, 0:1], start=True, stop=True),
         reads=[rk("gm1"), rk("c_ones_row")], writes=[rk(psn[6])], tag=n + "cbb")
    p.op("dve", lambda e: e.tensor_copy(B["negc1"][:], ps[6][:, 0:1]),
         reads=[rk(psn[6])], writes=[rk("negc1")], tag=n + "cbe")

    p.dma("pool", B["wbig2"][:], wd["o"], writes=[rk("wbig2")], tag=n + "ldo")
    sm128 = 1.0 / float(np.sqrt(128.0))
    pst6 = lambda a, b: ps[6][:, a // 2: b // 2].bitcast(BF16)
    for qt in range(NT):
        qp = qt % 2
        for kc in range(qt + 1):
            sps, spn = (ps[kc % 2], psn[kc % 2])
            diag = (kc == qt)
            p.op("pe", lambda e, qt=qt, kc=kc, sps=sps, diag=diag: e.matmul(
                    sps[:, 0:128], B["kT"][:, kc * 128:(kc + 1) * 128],
                    B["qT"][:, qt * 128:(qt + 1) * 128], start=True,
                    stop=not diag),
                 reads=[rk("kT", ALL), rk("qT", ALL)], writes=[rk(spn)],
                 inc=(0 if diag else 1), tag=n + f"S{qt}.{kc}")
            if diag:
                p.op("pe", lambda e, sps=sps: e.matmul(
                        sps[:, 0:128], com.ident_bf[:], com.maskadd[:],
                        start=False, stop=True),
                     reads=[rk("c_identbf"), rk("c_maskadd")], writes=[rk(spn)],
                     tag=n + f"SM{qt}.{kc}")
            p.op("act", lambda e, kc=kc, sps=sps, sm128=sm128, qp=qp:
                 e.activation(
                    B["PT"][:, qp * T + kc * 128: qp * T + (kc + 1) * 128],
                    sps[:, 0:128],
                    AF.Exp, bias=B["negc1"][:, 0:1], scale=sm128),
                 reads=[rk(spn), rk("negc1")], writes=[rk("PT", (qp, kc))],
                 tag=n + f"E{qt}.{kc}")
        pvb, pvbn = ps[2 + qp], psn[2 + qp]
        for kc in range(qt + 1):
            p.op("pe", lambda e, qt=qt, kc=kc, qp=qp, pvb=pvb: e.matmul(
                    pvb[:, 0:129],
                    B["PT"][:, qp * T + kc * 128: qp * T + (kc + 1) * 128],
                    B["v_ext"][:, kc * 129:(kc + 1) * 129],
                    start=(kc == 0), stop=(kc == qt)),
                 reads=[rk("PT", (qp, kc)), rk("v_ext")], writes=[rk(pvbn)],
                 inc=(1 if kc == qt else 0), tag=n + f"PV{qt}.{kc}")
        p.op("dve", lambda e, qp=qp, pvb=pvb: e.reciprocal(
                B["recipc"][:, qp:qp + 1], pvb[:, 128:129]),
             reads=[rk(pvbn)], writes=[rk("recipc", qp)], tag=n + f"rc{qt}")
        p.op("dve", lambda e, qt=qt, qp=qp: e.tensor_scalar(
                B["rwm"][:, qp * 8:(qp + 1) * 8], B["rw"][:, qt * 8:(qt + 1) * 8],
                B["recipc"][:, qp:qp + 1], None, ALU.mult),
             reads=[rk("rw", qt), rk("recipc", qp)], writes=[rk("rwm", qp)],
             tag=n + f"rwm{qt}")
        for e_ in range(8):
            ep = e_ % 2
            tb = (pst6 if ep == 0 else sc["pst"])
            tbk = psn[6] if ep == 0 else "ps7"
            p.op("dve", lambda e, e_=e_, qp=qp, ep=ep, pvb=pvb: e.tensor_scalar(
                    B["qbf"][:, ep * 128:(ep + 1) * 128], pvb[:, 0:128],
                    B["rwm"][:, qp * 8 + e_:qp * 8 + e_ + 1], None, ALU.mult),
                 reads=[rk(pvbn), rk("rwm", qp)], writes=[rk("qbf", ep)],
                 tag=n + f"aw{qt}.{e_}")
            p.op("pe", lambda e, ep=ep, tb=tb: e.transpose(
                    tb(0, 128), B["qbf"][:, ep * 128:(ep + 1) * 128],
                    com.ident_bf[:]),
                 reads=[rk("qbf", ep), rk("c_identbf")], writes=[rk(tbk)],
                 tag=n + f"awt{qt}.{e_}")
            p.op("act", lambda e, qt=qt, e_=e_, tb=tb: e.activation(
                    n2T[:, e_ * T + qt * 128: e_ * T + qt * 128 + 128],
                    tb(0, 128), AF.Copy, bias=0.0, scale=1.0),
                 reads=[rk(tbk)], writes=[rk("bigbf_b", (e_, qt))],
                 tag=n + f"awe{qt}.{e_}")

    phase_barrier(p, ["bigbf_a"])
    hsb = n1T
    for ti in range(NT):
        for half in range(2):
            bank, bnm = ps[3 + half], psn[3 + half]
            for ch in range(8):
                p.op("pe", lambda e, ti=ti, half=half, ch=ch, bank=bank: e.matmul(
                        bank[:, 0:512],
                        n2T[:, ch * T + ti * 128: ch * T + ti * 128 + 128],
                        B["wbig2"][:, ch * 1024 + half * 512:
                                   ch * 1024 + half * 512 + 512],
                        start=(ch == 0), stop=(ch == 7)),
                     reads=[rk("bigbf_b", ALL), rk("wbig2")],
                     writes=[rk(bnm)], inc=(1 if ch == 7 else 0),
                     tag=n + f"O{ti}.{half}")
            xsl = B["x"][:, ti * 1024 + half * 512: ti * 1024 + half * 512 + 512]
            p.op("dve", lambda e, ti=ti, xsl=xsl: e.tensor_scalar(
                    B["yscr"][:], xsl, B["ystats"][:, 2 * ti:2 * ti + 1],
                    B["ystats"][:, 2 * ti + 1:2 * ti + 2], ALU.mult, ALU.add),
                 reads=[rk("x_res", ti), rk("ystats", ti)], writes=[rk("yscr")],
                 tag=n + f"yr{ti}.{half}")
            p.op("dve", lambda e, ti=ti, half=half, bank=bank: e.tensor_tensor(
                    hsb[:, ti * 1024 + half * 512: ti * 1024 + half * 512 + 512],
                    B["yscr"][:], bank[:, 0:512], ALU.add),
                 reads=[rk(bnm), rk("yscr")], writes=[rk("bigbf_a", ti)],
                 tag=n + f"hs{ti}.{half}")
            p.op("dve", lambda e, bank=bank, xsl=xsl: e.scalar_tensor_tensor(
                    xsl, bank[:, 0:512], 0.5, xsl, ALU.mult, ALU.add),
                 reads=[rk(bnm), rk("x_res", ti), rk("yscr")],
                 writes=[rk("x_res", ti)], tag=n + f"xa{ti}.{half}")
            p.op("dve", lambda e, xsl=xsl: e.scalar_tensor_tensor(
                    xsl, B["yscr"][:], 0.5, xsl, ALU.mult, ALU.add),
                 reads=[rk("yscr"), rk("x_res", ti)], writes=[rk("x_res", ti)],
                 tag=n + f"xy{ti}.{half}")

    phase_barrier(p, ["bigbf_b"])
    emit_ln(p, com, geo, hsb, "bigbf_a", n2T, "bigbf_b",
            affine=(B["ln2w"], B["ln2b"], "ln2wF", "ln2bF"), scratch=sc)
    emit_gating(p, com, geo, B, n2T, "bigbf_b", B["simn2"], "simn2",
                B["sigb2"], "sigb2", B["rw2"], "rw2", 0.5, name=n + "g2")

    phase_barrier(p, ["bigbf_a"])
    He = n1T
    for ex in range(8):
        wslot = ex % 2
        wb1 = B["wbig"] if wslot == 0 else B["wbig2"]
        wk1 = "wbig" if wslot == 0 else "wbig2"
        p.dma("pool", wb1[:], wd["w1"][ex], writes=[rk(wk1)],
              tag=n + f"lw1.{ex}")
        wb2 = B["wbig2"] if wslot == 0 else B["wbig"]
        wk2 = "wbig2" if wslot == 0 else "wbig"
        p.dma("pool", wb2[:], wd["w2"][ex], writes=[rk(wk2)],
              tag=n + f"lw2.{ex}")
        w1banks = (0, 1, 6, 7)
        for ic in range(8):
            for w in range(T // 512):
                bi = w1banks[(ic * (T // 512) + w) % 4]
                bank, bnm = ps[bi], psn[bi]
                for ch in range(8):
                    p.op("pe", lambda e, ic=ic, w=w, ch=ch, bank=bank, wb1=wb1:
                         e.matmul(
                            bank[:, 0:512],
                            wb1[:, ch * 1024 + ic * 128: ch * 1024 + ic * 128 + 128],
                            n2T[:, ch * T + w * 512: ch * T + w * 512 + 512],
                            start=(ch == 0), stop=(ch == 7)),
                         reads=[rk(wk1), rk("bigbf_b", ALL)],
                         writes=[rk(bnm)], inc=(1 if ch == 7 else 0),
                         tag=n + f"H{ex}.{ic}.{w}")
                p.op("act", lambda e, ic=ic, w=w, bank=bank: e.activation(
                        He[:, ic * T + w * 512: ic * T + w * 512 + 512],
                        bank[:, 0:512], AF.Gelu, bias=com.zero_col[:, 0:1],
                        scale=1.0),
                     reads=[rk(bnm), rk("c_zero")],
                     writes=[rk("bigbf_a", (ic, w))], tag=n + f"G{ex}.{ic}.{w}")
        for ti in range(NT):
            for half in range(2):
                bi = 2 + (2 * ti + half) % 4
                bank, bnm = ps[bi], psn[bi]
                for ic in range(8):
                    p.op("pe", lambda e, ti=ti, half=half, ic=ic, bank=bank,
                         wb2=wb2: e.matmul(
                            bank[:, 0:512],
                            He[:, ic * T + ti * 128: ic * T + ti * 128 + 128],
                            wb2[:, ic * 1024 + half * 512:
                                ic * 1024 + half * 512 + 512],
                            start=(ic == 0), stop=(ic == 7)),
                         reads=[rk("bigbf_a", ALL), rk(wk2)],
                         writes=[rk(bnm)], inc=(1 if ic == 7 else 0),
                         tag=n + f"D{ex}.{ti}.{half}")
                xsl = B["x"][:, ti * 1024 + half * 512:
                             ti * 1024 + half * 512 + 512]
                p.op("dve", lambda e, ti=ti, ex=ex, bank=bank, xsl=xsl:
                     e.scalar_tensor_tensor(
                        xsl, bank[:, 0:512],
                        B["rw2"][:, ti * 8 + ex: ti * 8 + ex + 1], xsl,
                        ALU.mult, ALU.add),
                     reads=[rk(bnm), rk("rw2", ti), rk("x_res", ti)],
                     writes=[rk("x_res", ti)], tag=n + f"xm{ex}.{ti}.{half}")


def l0ffn_buffers(p: Prog, com: Common, geo, x_buf, bigbf_a, bigbf_b,
                  ps8, ps8_name, lnsc, gat, pre=""):
    import math
    _sbuf0 = p.sbuf
    p_sbuf = lambda nm, sh, dt: _sbuf0(pre + nm, sh, dt)
    NT, T = geo["NT"], geo["T"]
    B = {
        "x": x_buf, "bigbf_a": bigbf_a, "bigbf_b": bigbf_b,
        "ystats": p_sbuf("ystats", [128, 2 * NT], F32),
        "yscr": p_sbuf("yscr", [128, 512], F32),
        "ps8": ps8, "ps8_name": ps8_name, "lnsc": lnsc, "gat": gat,
        "v_ext": p_sbuf("v_ext", [128, NT * 130], BF16),
        "qT": p_sbuf("qT", [128, T], BF16),
        "kT": p_sbuf("kT", [128, T], BF16),
        "PT": p_sbuf("PT", [128, 2 * T], BF16),
        "sq2w": p_sbuf("sq2w", [128, 512], BF16),
        "qacc": p_sbuf("qacc", [128, 256], F32),
        "qbf": p_sbuf("qbf", [128, 256], BF16),
        "rw": p_sbuf("rw", [128, NT * 8], F32),
        "rw2": p_sbuf("rw2", [128, NT * 8], F32),
        "rwm": p_sbuf("rwm", [128, 16], F32),
        "recipc": p_sbuf("recipc", [128, 2], F32),
        "wbig": p_sbuf("wbig", [128, 8192], BF16),
        "wbig2": p_sbuf("wbig2", [128, 8192], BF16),
        "simn1": p_sbuf("simn1", [128, 64], BF16),
        "simn2": p_sbuf("simn2", [128, 64], BF16),
        "sigb1": p_sbuf("sigb1", [128, 8], F32),
        "sigb2": p_sbuf("sigb2", [128, 8], F32),
        "sim_f": p_sbuf("sim_f", [128, 64], F32),
        "sim_sq": p_sbuf("sim_sq", [128, 64], F32),
        "srn": p_sbuf("srn", [1, 8], F32),
        "srnb": p_sbuf("srnb", [128, 8], F32),
        "grow": p_sbuf("grow", [1, 8], F32),
        "ones_col_f": p_sbuf("ones_col_f", [128, 1], F32),
        "gm1": p_sbuf("gm1", [1, 2], F32),
        "n21": p_sbuf("n21", [1, 1], F32),
        "negc1": p_sbuf("negc1", [128, 1], F32),
        "lnsm128": p_sbuf("lnsm128", [128, 1], F32),
        "ln1w": p_sbuf("ln1wF", [128, 8], F32),
        "ln1b": p_sbuf("ln1bF", [128, 8], F32),
        "ln2w": p_sbuf("ln2wF", [128, 8], F32),
        "ln2b": p_sbuf("ln2bF", [128, 8], F32),
    }
    p.op("dve", lambda e: e.memset(B["ones_col_f"][:], 1.0),
         writes=[rk("ones_col_f")])
    p.op("dve", lambda e: e.memset(B["lnsm128"][:], math.log(1.0 / math.sqrt(128.0))),
         writes=[rk("lnsm128")])
    return B


def l0ffn_load_weights(p: Prog, B, wd):
    p.dma("sp", B["ln1w"][:], wd["ln1wF"], writes=[rk("ln1wF")])
    p.dma("sp", B["ln1b"][:], wd["ln1bF"], writes=[rk("ln1bF")])
    p.dma("sp", B["ln2w"][:], wd["ln2wF"], writes=[rk("ln2wF")])
    p.dma("sp", B["ln2b"][:], wd["ln2bF"], writes=[rk("ln2bF")])



from concourse.bass_utils import run_bass_kernel_spmd

GEO = {"NT": 16, "T": 2048}
NTK, TTOK = GEO["NT"], GEO["T"]


def _base_prog():
    nc = bass.Bass("TRN2", target_bir_lowering=False)
    x_d = nc.dram_tensor("x", [128, NTK * 1024], F32, kind="ExternalInput").ap()
    out_d = nc.dram_tensor("out", [128, NTK * 1024], F32,
                           kind="ExternalOutput").ap()
    p = Prog(nc)
    com = Common(p)
    ps, psn, ps01, ps23, hs = alloc_psum_pool(p)
    lnsc = ln_scratch(p, hs[7], hs[6])
    x_sb = p.sbuf("x_res", [128, NTK * 1024], F32)
    for ti in range(NTK):
        p.dma("sp", x_sb[:, ti * 1024:(ti + 1) * 1024],
              x_d[:, ti * 1024:(ti + 1) * 1024], writes=[rk("x_res", ti)])
    return nc, p, com, ps, psn, lnsc, x_sb, out_d, ps01, ps23


def _finish(nc, p, x_sb, out_d):
    for ti in range(NTK):
        p.dma("sp", out_d[:, ti * 1024:(ti + 1) * 1024],
              x_sb[:, ti * 1024:(ti + 1) * 1024], reads=[rk("x_res", ti)])
    p.build()
    return nc


def build_l0ffn():
    nc, p, com, ps, psn, lnsc, x_sb, out_d, ps01, ps23 = _base_prog()
    wd = {}
    for wn in ("wq0", "wk0", "wv0", "o"):
        wd[wn] = nc.dram_tensor(wn, [128, 8192], F32, kind="ExternalInput").ap()
    wd["w1"] = nc.dram_tensor("w1", [8, 128, 8192], F32, kind="ExternalInput").ap()
    wd["w2"] = nc.dram_tensor("w2", [8, 128, 8192], F32, kind="ExternalInput").ap()
    wd["sim1"] = nc.dram_tensor("sim1", [128, 64], F32, kind="ExternalInput").ap()
    wd["sim2"] = nc.dram_tensor("sim2", [128, 64], F32, kind="ExternalInput").ap()
    wd["g1"] = nc.dram_tensor("g1", [1, 8], F32, kind="ExternalInput").ap()
    wd["g2"] = nc.dram_tensor("g2", [1, 8], F32, kind="ExternalInput").ap()
    for wn in ("ln1wF", "ln1bF", "ln2wF", "ln2bF"):
        wd[wn] = nc.dram_tensor(wn + "_d", [128, 8], F32,
                                kind="ExternalInput").ap()
    gat = gating_scratch(p, GEO)
    bigbf_a = p.sbuf("bigbf_a", [128, 8 * TTOK], BF16)
    bigbf_b = p.sbuf("bigbf_b", [128, 8 * TTOK], BF16)
    B = l0ffn_buffers(p, com, GEO, x_sb, bigbf_a, bigbf_b, ps, psn,
                      lnsc, gat)
    l0ffn_load_weights(p, B, wd)
    prep_sim(p, com, B, wd["sim1"], B["simn1"], "simn1", name="s1")
    prep_sim(p, com, B, wd["sim2"], B["simn2"], "simn2", name="s2")
    prep_sig(p, com, B, wd["g1"], B["sigb1"], "sigb1", name="G1")
    prep_sig(p, com, B, wd["g2"], B["sigb2"], "sigb2", name="G2")
    emit_l0_ffn(p, com, GEO, B, wd, name="F")
    return _finish(nc, p, x_sb, out_d)


def build_mha(biased):
    nc, p, com, ps, psn, lnsc, x_sb, out_d, ps01, ps23 = _base_prog()
    wd = {}
    for wn in ("wq", "wk", "wv", "wo"):
        wd[wn] = nc.dram_tensor(wn, [128, 8192], F32, kind="ExternalInput").ap()
    if biased:
        wd["bq"] = nc.dram_tensor("bq", [128, 8], F32, kind="ExternalInput").ap()
        wd["bk"] = nc.dram_tensor("bk", [128, 8], F32, kind="ExternalInput").ap()
        wd["bv"] = nc.dram_tensor("bv", [1, 1024], F32, kind="ExternalInput").ap()
    yT = p.sbuf("bigbf_a", [128, 8 * TTOK], BF16)
    zT = p.sbuf("bigbf_b", [128, 8 * TTOK], BF16)
    B = mha_buffers(p, GEO, x_sb, yT, zT, ps, psn, ps[6], psn[6],
                    lnsc, biased, 0.125 if biased else 1.0,
                    ps01=ps01, ps23=ps23)
    emit_mha(p, com, GEO, B, wd, biased=biased, name="A")
    return _finish(nc, p, x_sb, out_d)


def build_l1ffn():
    nc, p, com, ps, psn, lnsc, x_sb, out_d, ps01, ps23 = _base_prog()
    wd = {
        "W1T": nc.dram_tensor("W1T", [128, 32 * 1024], F32,
                              kind="ExternalInput").ap(),
        "W2T": nc.dram_tensor("W2T", [128, 32 * 1024], F32,
                              kind="ExternalInput").ap(),
        "b1F": nc.dram_tensor("b1F_d", [128, 32], F32, kind="ExternalInput").ap(),
        "b2": nc.dram_tensor("b2_d", [1, 1024], F32, kind="ExternalInput").ap(),
        "scale": nc.dram_tensor("scale_d", [1, 1], F32, kind="ExternalInput").ap(),
    }
    yT = p.sbuf("bigbf_a", [128, 8 * TTOK], BF16)
    H = p.sbuf("bigbf_b", [128, 16384], BF16)
    B = l1ffn_buffers(p, GEO, x_sb, yT, H, ps, psn, lnsc)
    emit_l1_ffn(p, com, GEO, B, wd, name="F")
    return _finish(nc, p, x_sb, out_d)



def _hcm(w):
    X = w.shape[1]
    return np.ascontiguousarray(
        w.reshape(8, 128, X).transpose(1, 0, 2).reshape(128, 8 * X))


def _pair_major(w_cf):
    return np.ascontiguousarray(
        w_cf.reshape(8, 128, 8, 128).transpose(1, 2, 0, 3).reshape(128, 8192))


def _tok_major(x2d):
    return np.ascontiguousarray(
        x2d.reshape(NTK, 128, 1024).transpose(1, 0, 2).reshape(128, NTK * 1024))


def _prep_inputs(I):
    f32 = np.float32
    W = {}
    W["l0f"] = {
        "wq0": _hcm(np.ascontiguousarray(
            I["L0_ffn_q_proj"].transpose(1, 0, 2).reshape(1024, 1024))),
        "wk0": _hcm(np.ascontiguousarray(
            I["L0_ffn_k_proj"].transpose(1, 0, 2).reshape(1024, 1024))),
        "wv0": _hcm(np.ascontiguousarray(
            I["L0_ffn_v_proj"].transpose(1, 0, 2).reshape(1024, 1024))),
        "o": _hcm(np.ascontiguousarray(I["L0_ffn_o_proj"].reshape(1024, 1024))),
        "w1": np.stack([_hcm(I["L0_ffn_w1"][e]) for e in range(8)]),
        "w2": np.stack([_hcm(I["L0_ffn_w2"][e]) for e in range(8)]),
        "sim1": _hcm(np.ascontiguousarray(I["L0_ffn_smha_sim"])),
        "sim2": _hcm(np.ascontiguousarray(I["L0_ffn_moe_sim"])),
        "g1": I["L0_ffn_smha_gates"].reshape(1, 8).astype(f32),
        "g2": I["L0_ffn_moe_gates"].reshape(1, 8).astype(f32),
        "ln1wF_d": np.ascontiguousarray(I["L0_ffn_ln1_w"].reshape(8, 128).T),
        "ln1bF_d": np.ascontiguousarray(I["L0_ffn_ln1_b"].reshape(8, 128).T),
        "ln2wF_d": np.ascontiguousarray(I["L0_ffn_ln2_w"].reshape(8, 128).T),
        "ln2bF_d": np.ascontiguousarray(I["L0_ffn_ln2_b"].reshape(8, 128).T),
    }
    W["l0a"] = {
        "wq": _pair_major(np.ascontiguousarray(
            I["L0_attn_Wq"].transpose(2, 0, 1).reshape(1024, 1024))),
        "wk": _pair_major(np.ascontiguousarray(
            I["L0_attn_Wk"].transpose(2, 0, 1).reshape(1024, 1024))),
        "wv": _pair_major(np.ascontiguousarray(
            I["L0_attn_Wv"].transpose(2, 0, 1).reshape(1024, 1024))),
        "wo": _hcm(np.ascontiguousarray(I["L0_attn_proj"])),
        "bq": np.ascontiguousarray(I["L0_attn_bq"].reshape(-1).reshape(8, 128).T),
        "bk": np.ascontiguousarray(I["L0_attn_bk"].reshape(-1).reshape(8, 128).T),
        "bv": I["L0_attn_bv"].reshape(1, 1024).astype(f32),
    }
    W["l1a"] = {
        "wq": _pair_major(np.ascontiguousarray(I["L1_attn_wq"].T)),
        "wk": _pair_major(np.ascontiguousarray(I["L1_attn_wk"].T)),
        "wv": _pair_major(np.ascontiguousarray(I["L1_attn_wv"].T)),
        "wo": _hcm(np.ascontiguousarray(I["L1_attn_wo"].T)),
    }
    W["l1f"] = {
        "W1T": np.ascontiguousarray(
            I["L1_ffn_W1"].T.reshape(8, 128, 32, 128).transpose(1, 2, 0, 3)
            .reshape(128, 32768)),
        "W2T": np.ascontiguousarray(
            I["L1_ffn_W2"].T.reshape(32, 128, 1024).transpose(1, 0, 2)
            .reshape(128, 32768)),
        "b1F_d": np.ascontiguousarray(I["L1_ffn_b1"].reshape(32, 128).T),
        "b2_d": I["L1_ffn_b2"].reshape(1, 1024).astype(f32),
        "scale_d": np.asarray(I["L1_ffn_scale"], f32).reshape(1, 1),
    }
    for d in W.values():
        for k in d:
            d[k] = np.ascontiguousarray(np.asarray(d[k], f32))
    return W


_PROG_CACHE = {}


def _get_progs():
    if not _PROG_CACHE:
        _PROG_CACHE["l0f"] = build_l0ffn()
        _PROG_CACHE["l0a"] = build_mha(True)
        _PROG_CACHE["l1a"] = build_mha(False)
        _PROG_CACHE["l1f"] = build_l1ffn()
    return _PROG_CACHE



def build_fused():
    nc = bass.Bass("TRN2", target_bir_lowering=False)
    x_d = nc.dram_tensor("x", [128, NTK * 1024], F32, kind="ExternalInput").ap()
    out_d = nc.dram_tensor("out", [128, NTK * 1024], F32,
                           kind="ExternalOutput").ap()
    wd_l0f = {}
    for wn in ("wq0", "wk0", "wv0", "o"):
        wd_l0f[wn] = nc.dram_tensor("l0f_" + wn, [128, 8192], F32,
                                    kind="ExternalInput").ap()
    wd_l0f["w1"] = nc.dram_tensor("l0f_w1", [8, 128, 8192], F32,
                                  kind="ExternalInput").ap()
    wd_l0f["w2"] = nc.dram_tensor("l0f_w2", [8, 128, 8192], F32,
                                  kind="ExternalInput").ap()
    wd_l0f["sim1"] = nc.dram_tensor("l0f_sim1", [128, 64], F32,
                                    kind="ExternalInput").ap()
    wd_l0f["sim2"] = nc.dram_tensor("l0f_sim2", [128, 64], F32,
                                    kind="ExternalInput").ap()
    wd_l0f["g1"] = nc.dram_tensor("l0f_g1", [1, 8], F32,
                                  kind="ExternalInput").ap()
    wd_l0f["g2"] = nc.dram_tensor("l0f_g2", [1, 8], F32,
                                  kind="ExternalInput").ap()
    for wn in ("ln1wF", "ln1bF", "ln2wF", "ln2bF"):
        wd_l0f[wn] = nc.dram_tensor("l0f_" + wn, [128, 8], F32,
                                    kind="ExternalInput").ap()
    wd_l0a = {}
    for wn in ("wq", "wk", "wv", "wo"):
        wd_l0a[wn] = nc.dram_tensor("l0a_" + wn, [128, 8192], F32,
                                    kind="ExternalInput").ap()
    wd_l0a["bq"] = nc.dram_tensor("l0a_bq", [128, 8], F32,
                                  kind="ExternalInput").ap()
    wd_l0a["bk"] = nc.dram_tensor("l0a_bk", [128, 8], F32,
                                  kind="ExternalInput").ap()
    wd_l0a["bv"] = nc.dram_tensor("l0a_bv", [1, 1024], F32,
                                  kind="ExternalInput").ap()
    wd_l1a = {}
    for wn in ("wq", "wk", "wv", "wo"):
        wd_l1a[wn] = nc.dram_tensor("l1a_" + wn, [128, 8192], F32,
                                    kind="ExternalInput").ap()
    wd_l1f = {
        "W1T": nc.dram_tensor("l1f_W1T", [128, 32768], F32,
                              kind="ExternalInput").ap(),
        "W2T": nc.dram_tensor("l1f_W2T", [128, 32768], F32,
                              kind="ExternalInput").ap(),
        "b1F": nc.dram_tensor("l1f_b1F", [128, 32], F32,
                              kind="ExternalInput").ap(),
        "b2": nc.dram_tensor("l1f_b2", [1, 1024], F32,
                             kind="ExternalInput").ap(),
        "scale": nc.dram_tensor("l1f_scale", [1, 1], F32,
                                kind="ExternalInput").ap(),
    }

    p = Prog(nc)
    com = Common(p)
    ps, psn, ps01, ps23, hs = alloc_psum_pool(p)
    lnsc = ln_scratch(p, hs[7], hs[6])
    x_sb = p.sbuf("x_res", [128, NTK * 1024], F32)
    bigbf_a = p.sbuf("bigbf_a", [128, 8 * TTOK], BF16)
    bigbf_b = p.sbuf("bigbf_b", [128, max(8 * TTOK, 16384)], BF16)
    p.sbuf("barsc", [1, 1], F32)

    for ti in range(NTK):
        p.dma("sp", x_sb[:, ti * 1024:(ti + 1) * 1024],
              x_d[:, ti * 1024:(ti + 1) * 1024], writes=[rk("x_res", ti)])

    def phase_l0ffn(pre):
        p.push_scope()
        gat = gating_scratch(p, GEO, pre=pre)
        B = l0ffn_buffers(p, com, GEO, x_sb, bigbf_a, bigbf_b, ps, psn,
                          lnsc, gat, pre=pre)
        l0ffn_load_weights(p, B, wd_l0f)
        prep_sim(p, com, B, wd_l0f["sim1"], B["simn1"], "simn1", name=pre + "s1")
        prep_sim(p, com, B, wd_l0f["sim2"], B["simn2"], "simn2", name=pre + "s2")
        prep_sig(p, com, B, wd_l0f["g1"], B["sigb1"], "sigb1", name=pre + "G1")
        prep_sig(p, com, B, wd_l0f["g2"], B["sigb2"], "sigb2", name=pre + "G2")
        emit_l0_ffn(p, com, GEO, B, wd_l0f, name=pre)
        p.pop_scope()
        p.fence()

    def phase_mha(pre, wd, biased, hook=None):
        p.push_scope()
        B = mha_buffers(p, GEO, x_sb, bigbf_a, bigbf_b, ps, psn,
                        ps[6], psn[6], lnsc, biased,
                        0.125 if biased else 1.0, pre=pre,
                        ps01=ps01, ps23=ps23)
        emit_mha(p, com, GEO, B, wd, biased=biased, name=pre,
                 per_hp_hook=hook)
        p.pop_scope()
        p.fence()

    w1bf = nc.dram_tensor("l1f_w1bf", [128, 32768], BF16,
                          kind="Internal").ap()
    w2bf = nc.dram_tensor("l1f_w2bf", [128, 32768], BF16,
                          kind="Internal").ap()
    l1f_wbf = [(w1bf, w2bf)]

    def prepass_hook(hp):
        src, dst, qh, key = ((wd_l1f["W1T"], w1bf, hp, "w1bf_d") if hp < 4
                             else (wd_l1f["W2T"], w2bf, hp - 4, "w2bf_d"))
        p.dma("pool", dst[:, qh * 8192:(qh + 1) * 8192],
              src[:, qh * 8192:(qh + 1) * 8192],
              writes=[rk(key, qh)], tag=f"Bpc.{hp}")

    def phase_l1ffn(pre):
        p.push_scope()
        B = l1ffn_buffers(p, GEO, x_sb, bigbf_a, bigbf_b, ps, psn, lnsc,
                          pre=pre)
        emit_l1_ffn(p, com, GEO, B, wd_l1f, name=pre, wbf=l1f_wbf[0],
                    do_prepass=False)
        p.pop_scope()
        p.fence()

    phase_l0ffn("A_")
    phase_mha("B_", wd_l0a, True, hook=prepass_hook)
    phase_l0ffn("C_")
    phase_l1ffn("D_")
    phase_mha("E_", wd_l1a, False)
    phase_l1ffn("F_")

    for ti in range(NTK):
        p.dma("sp", out_d[:, ti * 1024:(ti + 1) * 1024],
              x_sb[:, ti * 1024:(ti + 1) * 1024], reads=[rk("x_res", ti)])
    p.build()
    return nc


def _fused_inputs(W, x):
    m = {"x": x}
    for k, v in W["l0f"].items():
        key = {"ln1wF_d": "ln1wF", "ln1bF_d": "ln1bF",
               "ln2wF_d": "ln2wF", "ln2bF_d": "ln2bF"}.get(k, k)
        m["l0f_" + key] = v
    for k, v in W["l0a"].items():
        m["l0a_" + k] = v
    for k, v in W["l1a"].items():
        m["l1a_" + k] = v
    for k, v in W["l1f"].items():
        key = {"b1F_d": "b1F", "b2_d": "b2", "scale_d": "scale"}.get(k, k)
        m["l1f_" + key] = v
    return m


FUSED = True


def kernel(**inputs):
    I = {k: np.asarray(v) for k, v in inputs.items()}
    W = _prep_inputs(I)
    xs = [_tok_major(np.asarray(I["L0_attn_x"][b], np.float32))
          for b in range(8)]
    if FUSED:
        if "fused" not in _PROG_CACHE:
            _PROG_CACHE["fused"] = build_fused()
        nc = _PROG_CACHE["fused"]
        in_maps = [_fused_inputs(W, xs[b]) for b in range(8)]
        res = run_bass_kernel_spmd(nc, in_maps, core_ids=list(range(8)))
        xs = [np.asarray(res.results[b]["out"], np.float32) for b in range(8)]
    else:
        progs = _get_progs()
        phases = [("l0f", "l0f"), ("l0a", "l0a"), ("l0f", "l0f"),
                  ("l1f", "l1f"), ("l1a", "l1a"), ("l1f", "l1f")]
        for prog_key, wkey in phases:
            in_maps = [dict(W[wkey], x=xs[b]) for b in range(8)]
            res = run_bass_kernel_spmd(progs[prog_key], in_maps,
                                       core_ids=list(range(8)))
            xs = [np.asarray(res.results[b]["out"], np.float32)
                  for b in range(8)]
    out = np.stack([xs[b].reshape(128, NTK, 1024).transpose(1, 0, 2)
                    .reshape(TTOK, 1024) for b in range(8)])
    return out.astype(np.float32)


# version 09
# speedup vs baseline: 1.2896x; 1.0370x over previous
import contextlib
from collections import defaultdict

import numpy as np
import concourse.bass as bass
import concourse.mybir as mybir

F32 = mybir.dt.float32
BF16 = mybir.dt.bfloat16
I32 = mybir.dt.int32
AF = mybir.ActivationFunctionType
ALU = mybir.AluOpType
AX = mybir.AxisListType

ALL = "__all__"

ENG_SECTION = {"pe": "tensor", "dve": "vector", "act": "scalar",
               "pool": "gpsimd", "sp": "sync"}


def rk(name, sub=ALL):
    return (name, sub)


class Prog:

    def __init__(self, nc, n_sp_dma_sems=6, n_pool_dma_sems=4):
        self.nc = nc
        self.stack = contextlib.ExitStack()
        self.ops = []
        self.sems = {}
        self._sem_order = []
        for e in ("pe", "dve", "act", "pool"):
            self._mk_sem(e)
        self.dma_sems = {"sp": [], "pool": []}
        for i in range(n_sp_dma_sems):
            self.dma_sems["sp"].append(self._mk_sem(f"dsp{i}"))
        for i in range(n_pool_dma_sems):
            self.dma_sems["pool"].append(self._mk_sem(f"dpl{i}"))
        self._dma_rr = {"sp": 0, "pool": 0}
        self.cum = defaultdict(int)
        self.last_writer = {}
        self.readers = defaultdict(list)
        self.name_keys = defaultdict(set)
        self.clock = defaultdict(dict)
        self.snap = {}
        self.eng_inc_ops = defaultdict(list)
        self.buffers = {}
        self.scopes = []
        self.pending_fence = {}
        nums = [self.sems[n].num for n in self._sem_order]
        assert nums == list(range(nums[0], nums[0] + len(nums))), nums
        self._sem_lo, self._sem_hi = nums[0], nums[-1]

    def _mk_sem(self, name):
        h = self.stack.enter_context(self.nc.semaphore(name))
        self.sems[name] = h
        self._sem_order.append(name)
        return name

    def sbuf(self, name, shape, dtype):
        st = self.scopes[-1] if self.scopes else self.stack
        h = st.enter_context(self.nc.sbuf_tensor(name, list(shape), dtype))
        self.buffers[name] = h
        return h

    def push_scope(self):
        self.scopes.append(contextlib.ExitStack())

    def pop_scope(self):
        self.scopes.pop().close()

    def fence(self):
        for s in self._sem_order:
            if self.cum[s] > 0:
                self.pending_fence[s] = self.cum[s]

    def psum(self, name, shape, dtype=F32):
        h = self.stack.enter_context(self.nc.psum_tensor(name, list(shape), dtype))
        self.buffers[name] = h
        return h

    def _conflict_keys(self, key):
        name, sub = key
        subs = self.name_keys.get(name)
        if not subs:
            return ()
        if sub == ALL:
            return [(name, s) for s in subs]
        out = []
        if sub in subs:
            out.append((name, sub))
        if ALL in subs:
            out.append((name, ALL))
        return out

    def op(self, engine, emit, reads=(), writes=(), sem=None, inc=1, tag=""):
        idx = len(self.ops)
        sem = sem or engine
        deps = set()
        for r in reads:
            for ck in self._conflict_keys(r):
                if ck in self.last_writer:
                    deps.add(self.last_writer[ck])
        for w in writes:
            for ck in self._conflict_keys(w):
                if ck in self.last_writer:
                    deps.add(self.last_writer[ck])
                for rd in self.readers.get(ck, ()):
                    deps.add(rd)
        deps.discard(idx)

        clk = self.clock[engine]
        need = {}
        for d in sorted(deps):
            dop = self.ops[d]
            if dop["engine"] == "pe" and engine == "pe":
                continue
            if dop["inc"] > 0:
                s, v = dop["sem"], dop["cum"]
            else:
                cover = None
                for (ci, cs, cv) in self.eng_inc_ops[dop["engine"]]:
                    if ci > d:
                        cover = (ci, cs, cv)
                        break
                assert cover is not None and cover[0] < idx, (
                    f"op {d} ({dop['tag']}) has inc=0 and no covering inc-op "
                    f"before consumer {idx} ({tag})")
                s, v = cover[1], cover[2]
            if clk.get(s, 0) >= v:
                continue
            if need.get(s, 0) < v:
                need[s] = v

        for s, v in self.pending_fence.items():
            if clk.get(s, 0) < v and need.get(s, 0) < v:
                need[s] = v

        if inc == 16:
            prev = self.cum[sem]
            if prev > 0 and clk.get(sem, 0) < prev and need.get(sem, 0) < prev:
                need[sem] = prev

        waits = sorted(need.items())
        for s, v in waits:
            if clk.get(s, 0) < v:
                clk[s] = v
        for d in deps:
            snap = self.snap.get(d)
            if snap:
                for s, v in snap.items():
                    if clk.get(s, 0) < v:
                        clk[s] = v

        self.cum[sem] += inc
        o = dict(idx=idx, engine=engine, emit=emit, sem=sem, inc=inc,
                 cum=self.cum[sem], waits=waits, tag=tag)
        self.ops.append(o)
        if inc > 0:
            self.eng_inc_ops[engine].append((idx, sem, self.cum[sem]))
        sn = dict(clk)
        sn[sem] = max(sn.get(sem, 0), self.cum[sem])
        self.snap[idx] = sn
        for r in reads:
            self.name_keys[r[0]].add(r[1])
            self.readers[r].append(idx)
        for w in writes:
            self.name_keys[w[0]].add(w[1])
            self.last_writer[w] = idx
            self.readers[w] = []
        return idx

    def dma(self, queue, out, in_, reads=(), writes=(), tag="", **kw):
        sems = self.dma_sems[queue]
        sem = sems[self._dma_rr[queue] % len(sems)]
        self._dma_rr[queue] += 1

        def emit(eng, out=out, in_=in_, kw=kw):
            return eng.dma_start(out=out, in_=in_, **kw)
        return self.op(queue, emit, reads=reads, writes=writes, sem=sem,
                       inc=16, tag=tag)

    def build(self):
        per_engine = defaultdict(list)
        for o in self.ops:
            per_engine[o["engine"]].append(o)
        sems = self.sems
        blk = self.nc.Block()
        block = blk.__enter__()
        for ename in ("sp", "pool", "pe", "dve", "act"):
            ops = per_engine.get(ename, [])
            section = getattr(block, ENG_SECTION[ename])

            def body(eng, _ops=ops):
                for o in _ops:
                    for s, v in o["waits"]:
                        eng.wait_ge(sems[s], v)
                    inst = o["emit"](eng)
                    if o["inc"] and inst is not None:
                        inst.then_inc(sems[o["sem"]], o["inc"])
            section(body)
        finals = [(s, self.cum[s]) for s in self._sem_order if self.cum[s] > 0]

        def tail(eng, finals=finals):
            for s, v in finals:
                if s != "pool":
                    eng.wait_ge(sems[s], v)
        block.gpsimd(tail)
        blk.__exit__(None, None, None)
        self.nc.gpsimd.sem_clear(range(self._sem_lo, self._sem_hi + 1))

    def close(self):
        self.stack.close()



class Common:

    def __init__(self, p: Prog):
        nc = p.nc
        self.p = p
        self.ident_bf = p.sbuf("c_identbf", [128, 128], BF16)
        self.mask_keepge = p.sbuf("c_maskge", [128, 128], BF16)
        self.maskadd = p.sbuf("c_maskadd", [128, 128], BF16)
        self.iota = p.sbuf("c_iota", [128, 128], I32)
        self.ones_col_bf = p.sbuf("c_ones_col", [128, 1], BF16)
        self.ones_row_f = p.sbuf("c_ones_row", [1, 128], F32)
        self.ones_row_bf = p.sbuf("c_ones_row_bf", [1, 128], BF16)
        self.eps_col = p.sbuf("c_eps", [128, 1], F32)
        self.zero_col = p.sbuf("c_zero", [128, 1], F32)
        self.tiny = p.sbuf("c_tiny", [128, 1], F32)

        p.op("pool", lambda e: e.iota(self.iota[:], pattern=[[-1, 128]], base=0,
                                      channel_multiplier=1),
             writes=[rk("c_iota")], tag="iota")
        p.op("dve", lambda e: e.tensor_scalar(self.ident_bf[:], self.iota[:],
                                              0.0, None, ALU.is_equal),
             reads=[rk("c_iota")], writes=[rk("c_identbf")], tag="ident")
        p.op("dve", lambda e: e.tensor_scalar(self.mask_keepge[:], self.iota[:],
                                              0.0, None, ALU.is_le),
             reads=[rk("c_iota")], writes=[rk("c_maskge")], tag="maskge")
        p.op("dve", lambda e: e.tensor_scalar(self.maskadd[:], self.iota[:],
                                              0.0, -1e30, ALU.is_gt, ALU.mult),
             reads=[rk("c_iota")], writes=[rk("c_maskadd")], tag="maskadd")
        p.op("dve", lambda e: e.memset(self.ones_col_bf[:], 1.0),
             writes=[rk("c_ones_col")])
        p.op("dve", lambda e: e.memset(self.ones_row_f[:], 1.0),
             writes=[rk("c_ones_row")])
        p.op("dve", lambda e: e.memset(self.ones_row_bf[:], 1.0),
             writes=[rk("c_ones_row_bf")])
        p.op("dve", lambda e: e.memset(self.eps_col[:], 1e-5),
             writes=[rk("c_eps")])
        p.op("dve", lambda e: e.memset(self.zero_col[:], 0.0),
             writes=[rk("c_zero")])
        p.op("dve", lambda e: e.memset(self.tiny[:], 1e-30),
             writes=[rk("c_tiny")])



def emit_ln(p: Prog, com: Common, geo, src_buf, src_name, dst_buf, dst_name,
            *, affine=None, fuse_double=False, out_tm=None, out_tm_name=None,
            scratch=None, tilesel=None, keep_tile0=None, out_stats=None):
    NT, T = geo["NT"], geo["T"]
    sc = scratch
    tiles = list(range(NT) if tilesel is None else tilesel)
    mv = sc["mv_all"]

    for ti in tiles:
        par = ti % 2
        bn = sc[par]["bn"]
        P_ = str(par)
        xsl = src_buf[:, ti * 1024: ti * 1024 + 1024]
        rx = rk(src_name, ti)
        p.op("dve", lambda e, a=xsl, bn=bn: e.bn_stats(bn[:, 0:6], a[:, 0:512]),
             reads=[rx], writes=[rk("s_bn" + P_)], tag=f"bns{ti}")
        p.op("dve", lambda e, a=xsl, bn=bn: e.bn_stats(bn[:, 6:12], a[:, 512:1024]),
             reads=[rx], writes=[rk("s_bn2" + P_)], tag=f"bns2{ti}")
        p.op("dve", lambda e, ti=ti, bn=bn: e.bn_aggr(
                mv[:, 2 * ti:2 * ti + 2], bn[:]),
             reads=[rk("s_bn" + P_), rk("s_bn2" + P_)], writes=[rk("s_mv", ti)],
             tag=f"bna{ti}")

    mvk = [rk("s_mv", ti) for ti in tiles]
    mvv = mv[:, 0:2 * NT].rearrange("p (t two) -> p two t", two=2)
    var_cols = mvv[:, 1:2, :].squeeze(1)
    m_cols = mvv[:, 0:1, :].squeeze(1)
    r0 = sc["r0_all"]
    rho = sc["rho_all"]
    nm = sc["nm_all"]

    p.op("act", lambda e: e.activation(r0[:, 0:NT], var_cols, AF.Ln,
                                       bias=com.eps_col[:, 0:1], scale=1.0),
         reads=mvk + [rk("c_eps")], writes=[rk("s_r0a")], tag="lnv")
    p.op("act", lambda e: e.activation(r0[:, 0:NT], r0[:, 0:NT], AF.Exp,
                                       bias=com.zero_col[:, 0:1], scale=-0.5),
         reads=[rk("s_r0a"), rk("c_zero")], writes=[rk("s_r0a")], tag="exv")
    if fuse_double:
        p.op("dve", lambda e: e.tensor_tensor(rho[:, 0:NT], r0[:, 0:NT],
                                              r0[:, 0:NT], ALU.mult),
             reads=[rk("s_r0a")], writes=[rk("s_rhoa")], tag="fd1")
        p.op("dve", lambda e: e.tensor_tensor(rho[:, 0:NT], rho[:, 0:NT],
                                              var_cols, ALU.mult),
             reads=[rk("s_rhoa")] + mvk, writes=[rk("s_rhoa")], tag="fd2")
        p.op("act", lambda e: e.activation(rho[:, 0:NT], rho[:, 0:NT], AF.Ln,
                                           bias=com.eps_col[:, 0:1], scale=1.0),
             reads=[rk("s_rhoa"), rk("c_eps")], writes=[rk("s_rhoa")], tag="fd3")
        p.op("act", lambda e: e.activation(rho[:, 0:NT], rho[:, 0:NT], AF.Exp,
                                           bias=com.zero_col[:, 0:1], scale=-0.5),
             reads=[rk("s_rhoa"), rk("c_zero")], writes=[rk("s_rhoa")], tag="fd4")
        p.op("dve", lambda e: e.tensor_tensor(rho[:, 0:NT], rho[:, 0:NT],
                                              r0[:, 0:NT], ALU.mult),
             reads=[rk("s_rhoa"), rk("s_r0a")], writes=[rk("s_rhoa")], tag="fd5")
        rhok = "s_rhoa"
    else:
        rho = r0
        rhok = "s_r0a"
    p.op("dve", lambda e, rho=rho: e.tensor_tensor(nm[:, 0:NT], m_cols,
                                                   rho[:, 0:NT], ALU.mult),
         reads=mvk + [rk(rhok)], writes=[rk("s_nma")], tag="nm1")
    p.op("dve", lambda e: e.tensor_scalar(nm[:, 0:NT], nm[:, 0:NT], -1.0, None,
                                          ALU.mult),
         reads=[rk("s_nma")], writes=[rk("s_nma")], tag="nm2")
    if out_stats is not None:
        stb, stk = out_stats
        st2 = stb[:].rearrange("p (t two) -> p two t", two=2)
        p.op("dve", lambda e, st2=st2: e.tensor_copy(
                st2[:, 0:1, 0:NT].squeeze(1), r0[:, 0:NT]),
             reads=[rk("s_r0a")], writes=[rk(stk, "r")], tag="str")
        p.op("dve", lambda e, st2=st2: e.tensor_tensor(
                st2[:, 1:2, 0:NT].squeeze(1), m_cols, r0[:, 0:NT], ALU.mult),
             reads=mvk + [rk("s_r0a")], writes=[rk(stk, "n")], tag="stn1")
        p.op("dve", lambda e, st2=st2: e.tensor_scalar(
                st2[:, 1:2, 0:NT].squeeze(1), st2[:, 1:2, 0:NT].squeeze(1),
                -1.0, None, ALU.mult),
             reads=[rk(stk, "n")], writes=[rk(stk, "n")], tag="stn2")

    if out_tm is not None:
        nm0 = sc["nm0_all"]
        p.op("dve", lambda e: e.tensor_tensor(nm0[:, 0:NT], m_cols,
                                              r0[:, 0:NT], ALU.mult),
             reads=mvk + [rk("s_r0a")], writes=[rk("s_nm0a")], tag="ynm1")
        p.op("dve", lambda e: e.tensor_scalar(nm0[:, 0:NT], nm0[:, 0:NT],
                                              -1.0, None, ALU.mult),
             reads=[rk("s_nm0a")], writes=[rk("s_nm0a")], tag="ynm2")

    ps6h = sc["ps6"]
    assert ps6h is not None
    pst6 = lambda a, b: ps6h[:, a // 2: b // 2].bitcast(BF16)

    def rec_evac(ti, tb, tbk):
        if affine is not None:
            wc, bc, wname, bname = affine
            for j in range(8):
                if j % 2 == 0:
                    p.op("dve", lambda e, j=j, ti=ti, tb=tb: e.tensor_scalar(
                            dst_buf[:, j * T + ti * 128: j * T + ti * 128 + 128],
                            tb(j * 128, (j + 1) * 128),
                            wc[:, j:j + 1], bc[:, j:j + 1], ALU.mult, ALU.add),
                         reads=[rk(tbk), rk(wname), rk(bname)],
                         writes=[rk(dst_name, ti)], tag=f"aff{ti}.{j}")
                else:
                    p.op("act", lambda e, j=j, ti=ti, tb=tb: e.activation(
                            dst_buf[:, j * T + ti * 128: j * T + ti * 128 + 128],
                            tb(j * 128, (j + 1) * 128), AF.Identity,
                            bias=bc[:, j:j + 1], scale=wc[:, j:j + 1]),
                         reads=[rk(tbk), rk(wname), rk(bname)],
                         writes=[rk(dst_name, ti)], tag=f"affa{ti}.{j}")
        else:
            dst3 = dst_buf[:].rearrange("p (c t) -> p c t", c=8)
            p.op("act", lambda e, ti=ti, d3=dst3, tb=tb: e.activation(
                    d3[:, :, ti * 128: ti * 128 + 128],
                    tb(0, 1024).rearrange("p (c t) -> p c t", c=8),
                    AF.Copy, bias=0.0, scale=1.0),
                 reads=[rk(tbk)],
                 writes=[rk(dst_name, ti)], tag=f"ev{ti}")

    pend = None
    for ti in tiles:
        par = ti % 2
        P_ = str(par)
        xsl = src_buf[:, ti * 1024: ti * 1024 + 1024]
        rx = rk(src_name, ti)
        if out_tm is not None:
            ysl = out_tm[:, ti * 1024: ti * 1024 + 1024]
            p.op("dve", lambda e, a=xsl, y=ysl, ti=ti, sc=sc: e.tensor_scalar(
                    y, a, r0[:, ti:ti + 1], sc["nm0_all"][:, ti:ti + 1],
                    ALU.mult, ALU.add),
                 reads=[rx, rk("s_r0a"), rk("s_nm0a")],
                 writes=[rk(out_tm_name, ti)], tag=f"y{ti}")
        if keep_tile0 is not None and ti == 0:
            nbf_buf, nbf_key = keep_tile0
        else:
            nbf_buf, nbf_key = sc[par]["nbf"], "s_nbf" + P_
        tb = sc["pst"] if par == 1 else pst6
        tbk = "ps7" if par == 1 else "ps6"
        for hh in range(2):
            if hh == 0:
                p.op("dve", lambda e, a=xsl, ti=ti, nb=nbf_buf, rho=rho:
                     e.tensor_scalar(
                        nb[:, 0:512], a[:, 0:512],
                        rho[:, ti:ti + 1], nm[:, ti:ti + 1], ALU.mult, ALU.add),
                     reads=[rx, rk(rhok), rk("s_nma")],
                     writes=[rk(nbf_key, 0)], tag=f"nbf{ti}.0")
            else:
                p.op("act", lambda e, a=xsl, ti=ti, nb=nbf_buf, rho=rho:
                     e.activation(
                        nb[:, 512:1024], a[:, 512:1024], AF.Identity,
                        bias=nm[:, ti:ti + 1], scale=rho[:, ti:ti + 1]),
                     reads=[rx, rk(rhok), rk("s_nma")],
                     writes=[rk(nbf_key, 1)], tag=f"nbf{ti}.1")
            for j in range(4 * hh, 4 * hh + 4):
                p.op("pe", lambda e, j=j, nb=nbf_buf, tb=tb: e.transpose(
                        tb(j * 128, (j + 1) * 128),
                        nb[:, j * 128:(j + 1) * 128], com.ident_bf[:]),
                     reads=[rk(nbf_key, hh), rk("c_identbf")],
                     writes=[rk(tbk)], inc=(1 if j == 4 * hh + 3 else 0),
                     tag=f"tr{ti}.{j}")
        if pend is not None:
            rec_evac(*pend)
        pend = (ti, tb, tbk)
    rec_evac(*pend)


def ln_scratch(p: Prog, ps7=None, ps6=None):
    sc = {}
    for par in (0, 1):
        sc[par] = {
            "bn": p.sbuf(f"s_bn{par}", [128, 12], F32),
            "nbf": p.sbuf(f"s_nbf{par}", [128, 1024], BF16),
        }
    sc["mv_all"] = p.sbuf("s_mv_all", [128, 32], F32)
    sc["r0_all"] = p.sbuf("s_r0_all", [128, 16], F32)
    sc["rho_all"] = p.sbuf("s_rho_all", [128, 16], F32)
    sc["nm_all"] = p.sbuf("s_nm_all", [128, 16], F32)
    sc["nm0_all"] = p.sbuf("s_nm0_all", [128, 16], F32)
    assert ps7 is not None
    sc["ps7"] = ps7
    sc["ps6"] = ps6
    sc["pst"] = lambda a, b: ps7[:, a // 2: b // 2].bitcast(BF16)
    return sc



def emit_mha(p: Prog, com: Common, geo, B, wd, *, biased, name="",
             per_hp_hook=None):
    NT, T = geo["NT"], geo["T"]
    NW = T // 512
    sc = B["lnsc"]
    n = name
    psb = B["psb"]
    psbn = B["psb_name"]
    psm, psmn = B["ps_misc"], B["ps_misc_name"]
    vx4 = B["v_ext"][:].rearrange("p (t h d) -> p t h d", h=2, d=65)

    emit_ln(p, com, geo, B["x"], "x_res", B["yT"], B["yT_name"], scratch=sc)

    if biased:
        p.dma("sp", B["bv_row"][:], wd["bv"], writes=[rk("bv_row")], tag=n + "bv")
        p.dma("sp", B["bqF"][:], wd["bq"], writes=[rk("bqF")])
        p.dma("sp", B["bkF"][:], wd["bk"], writes=[rk("bkF")])
        for h2 in range(2):
            p.op("pe", lambda e, h2=h2: e.matmul(
                    psm[:, 0:512], com.ones_row_f[0:1, :],
                    B["bv_row"][0:1, h2 * 512:(h2 + 1) * 512],
                    start=True, stop=True),
                 reads=[rk("bv_row"), rk("c_ones_row")], writes=[rk(psmn)],
                 tag=n + f"bvb{h2}")
            p.op("act", lambda e, h2=h2: e.activation(
                    B["bvb"][:, h2 * 512:(h2 + 1) * 512], psm[:, 0:512],
                    AF.Copy, bias=0.0, scale=1.0),
                 reads=[rk(psmn)], writes=[rk("bvb")], tag=n + f"bvbe{h2}")

    p.op("dve", lambda e: e.memset(vx4[:, :, :, 64:65], 1.0),
         writes=[rk("v_ext")], tag=n + "vones")

    for hp in range(8):
        if per_hp_hook is not None:
            per_hp_hook(hp)
        for i, wn in enumerate(("wq", "wk", "wv")):
            p.dma("pool", B["wpair"][:, i * 1024:(i + 1) * 1024],
                  wd[wn][:, hp * 1024:(hp + 1) * 1024],
                  writes=[rk("wpair", wn)], tag=n + f"ld{wn}{hp}")

        for i, wn in enumerate(("wq", "wk", "wv")):
            for w in range(NW):
                bank, bname = ((psb[6], psbn[6]) if (i * NW + w) % 2 == 0
                               else (psb[4], psbn[4]))
                ytk = [rk(B["yT_name"], t) for t in range(4 * w, 4 * w + 4)]
                for ch in range(8):
                    p.op("pe", lambda e, i=i, w=w, ch=ch, bank=bank: e.matmul(
                            bank[:, 0:512],
                            B["wpair"][:, i * 1024 + ch * 128:
                                       i * 1024 + (ch + 1) * 128],
                            B["yT"][:, ch * T + w * 512: ch * T + w * 512 + 512],
                            start=(ch == 0), stop=(ch == 7)),
                         reads=[rk("wpair", wn)] + ytk,
                         writes=[rk(bname)], inc=(1 if ch == 7 else 0),
                         tag=n + f"prj{wn}{hp}.{w}.{ch}")
                if wn == "wv":
                    p.op("act", lambda e, w=w, bank=bank: e.activation(
                            B["vT"][:, w * 512:(w + 1) * 512], bank[:, 0:512],
                            AF.Copy, bias=0.0, scale=1.0),
                         reads=[rk(bname)], writes=[rk("vT", w)],
                         tag=n + f"vTev{hp}.{w}")
                else:
                    dstb = B["qT"] if wn == "wq" else B["kT"]
                    dname = "qT" if wn == "wq" else "kT"
                    if biased:
                        bcol = B["bqF"] if wn == "wq" else B["bkF"]
                        bnm = "bqF" if wn == "wq" else "bkF"
                        p.op("dve", lambda e, w=w, bank=bank, bcol=bcol,
                             dstb=dstb: e.tensor_scalar(
                                dstb[:, w * 512:(w + 1) * 512], bank[:, 0:512],
                                bcol[:, hp:hp + 1], None, ALU.add),
                             reads=[rk(bname), rk(bnm)],
                             writes=[rk(dname, w)], tag=n + f"{dname}ev{hp}.{w}")
                    else:
                        p.op("act", lambda e, w=w, bank=bank, dstb=dstb:
                             e.activation(dstb[:, w * 512:(w + 1) * 512],
                                          bank[:, 0:512], AF.Copy,
                                          bias=0.0, scale=1.0),
                             reads=[rk(bname)], writes=[rk(dname, w)],
                             tag=n + f"{dname}ev{hp}.{w}")

        pst6v = lambda a, b: sc["ps6"][:, a // 2: b // 2].bitcast(BF16)
        for ti in range(NT):
            tbv = sc["pst"] if ti % 2 else pst6v
            tbvk = "ps7" if ti % 2 else "ps6"
            p.op("pe", lambda e, ti=ti, tbv=tbv: e.transpose(
                    tbv(0, 128), B["vT"][:, ti * 128:(ti + 1) * 128],
                    com.ident_bf[:]),
                 reads=[rk("vT", ti // (NT // NW)), rk("c_identbf")],
                 writes=[rk(tbvk)], tag=n + f"vtr{hp}.{ti}")
            dst = vx4[:, ti:ti + 1, :, 0:64]
            src = tbv(0, 128).rearrange("p (h d) -> p h d", h=2
                                        ).unsqueeze(1)
            if biased:
                bsl = B["bvb"][:, hp * 128:(hp + 1) * 128].rearrange(
                    "p (h d) -> p h d", h=2).unsqueeze(1)
                p.op("dve", lambda e, dst=dst, src=src, bsl=bsl: e.tensor_tensor(
                        dst, src, bsl, ALU.add),
                     reads=[rk(tbvk), rk("bvb")], writes=[rk("v_ext")],
                     tag=n + f"vev{hp}.{ti}")
            else:
                p.op("dve", lambda e, dst=dst, src=src: e.tensor_copy(dst, src),
                     reads=[rk(tbvk)], writes=[rk("v_ext")],
                     tag=n + f"vev{hp}.{ti}")

        p.op("dve", lambda e: e.memset(B["gmax"][:], 0.0), writes=[rk("gmax")],
             tag=n + f"gz{hp}")
        sqv = B["vT"][:, 0:T]
        for ci, (which, buf, wname) in enumerate(
                (("q", B["qT"], "qT"), ("k", B["kT"], "kT"))):
            for w2 in range(NW):
                p.op("dve", lambda e, buf=buf, sqv=sqv, w2=w2: e.tensor_tensor(
                        sqv[:, w2 * 512:(w2 + 1) * 512],
                        buf[:, w2 * 512:(w2 + 1) * 512],
                        buf[:, w2 * 512:(w2 + 1) * 512], ALU.mult),
                     reads=[rk(wname, w2)],
                     writes=[rk("vT", w2)], tag=n + f"sq{which}{hp}.{w2}")
            for w in range(NW):
                nb_, nbk_ = (psm, psmn) if w % 2 == 0 else (psb[5], psbn[5])
                p.op("pe", lambda e, w=w, sqv=sqv, nb_=nb_: e.matmul(
                        nb_[0:2, 0:512], B["blk2"][:],
                        sqv[:, w * 512:(w + 1) * 512], start=True, stop=True),
                     reads=[rk("vT", w), rk("blk2")], writes=[rk(nbk_)],
                     tag=n + f"nrm{which}{hp}.{w}")
                p.op("dve", lambda e, nb_=nb_: e.tensor_reduce(
                        B["n2"][:], nb_[0:2, 0:512], AX.X, ALU.max),
                     reads=[rk(nbk_)], writes=[rk("n2")],
                     tag=n + f"nrd{which}{hp}.{w}")
                p.op("dve", lambda e, ci=ci: e.tensor_tensor(
                        B["gmax"][:, ci:ci + 1], B["gmax"][:, ci:ci + 1],
                        B["n2"][:], ALU.max),
                     reads=[rk("n2"), rk("gmax")], writes=[rk("gmax")],
                     tag=n + f"nmx{which}{hp}.{w}")
        p.op("dve", lambda e: e.tensor_tensor(
                B["cb"][0:2, 0:1], B["gmax"][0:2, 0:1], B["gmax"][0:2, 1:2],
                ALU.mult),
             reads=[rk("gmax")], writes=[rk("cb")], tag=n + f"cbm{hp}")
        p.op("act", lambda e: e.activation(B["cb"][0:2, 0:1], B["cb"][0:2, 0:1],
                                           AF.Ln, bias=B["tiny"][0:2, 0:1],
                                           scale=1.0),
             reads=[rk("cb"), rk("tiny")], writes=[rk("cb")], tag=n + f"cbl{hp}")
        p.op("act", lambda e: e.activation(B["cb"][0:2, 0:1], B["cb"][0:2, 0:1],
                                           AF.Exp, bias=B["lnsm"][0:2, 0:1],
                                           scale=0.5),
             reads=[rk("cb"), rk("lnsm")], writes=[rk("cb")], tag=n + f"cbx{hp}")
        p.op("dve", lambda e: e.tensor_scalar(B["cb"][0:2, 0:1], B["cb"][0:2, 0:1],
                                              -1.0, None, ALU.mult),
             reads=[rk("cb")], writes=[rk("cb")], tag=n + f"cbn{hp}")
        p.dma("pool", B["cb2"][0:1, 0:1], B["cb"][1:2, 0:1], reads=[rk("cb")],
              writes=[rk("cb2")], tag=n + f"cbmv{hp}")
        p.op("dve", lambda e: e.tensor_tensor(
                B["cb"][0:1, 0:1], B["cb"][0:1, 0:1], B["cb2"][0:1, 0:1],
                ALU.min),
             reads=[rk("cb"), rk("cb2")], writes=[rk("cb")], tag=n + f"cbmn{hp}")
        p.op("pe", lambda e: e.matmul(
                psm[:, 0:1], com.ones_row_f[0:1, :], B["cb"][0:1, 0:1],
                start=True, stop=True),
             reads=[rk("cb"), rk("c_ones_row")], writes=[rk(psmn)],
             tag=n + f"cbb{hp}")
        p.op("dve", lambda e: e.tensor_copy(B["negc"][:, 0:1], psm[:, 0:1]),
             reads=[rk(psmn)], writes=[rk("negc")], tag=n + f"cbe{hp}")

        ps01, ps23 = B["ps01"], B["ps23"]

        def record_norm_lr(w):
            for sub in range(2):
                h = 2 * hp + sub
                pvs, pvsn = (psb[4], psbn[4]) if sub == 0 else (psb[5], psbn[5])
                p.op("act", lambda e, pvs=pvs, sub=sub: e.activation(
                        B["srow"][0:1, sub * 512:sub * 512 + 512],
                        pvs[64:65, 0:512], AF.Ln,
                        bias=B["tiny"][0:1, 0:1], scale=1.0),
                     reads=[rk(pvsn), rk("tiny")], writes=[rk("srow", sub)],
                     tag=n + f"L{h}.{w}")
                p.op("act", lambda e, sub=sub: e.activation(
                        B["rrow_bf"][0:1, sub * 512:sub * 512 + 512],
                        B["srow"][0:1, sub * 512:sub * 512 + 512], AF.Exp,
                        bias=B["lnpsc"][0:1, 0:1], scale=-1.0),
                     reads=[rk("srow", sub), rk("lnpsc")],
                     writes=[rk("rrow_bf", sub)], tag=n + f"R{h}.{w}")

        def record_norm_fin(w):
            for sub in range(2):
                h = 2 * hp + sub
                base = sub * 64
                pvs, pvsn = (psb[4], psbn[4]) if sub == 0 else (psb[5], psbn[5])
                p.op("pe", lambda e, sub=sub: e.matmul(
                        psm[0:64, 0:512], com.ones_row_bf[0:1, 0:64],
                        B["rrow_bf"][0:1, sub * 512:sub * 512 + 512],
                        start=True, stop=True),
                     reads=[rk("rrow_bf", sub), rk("c_ones_row_bf")],
                     writes=[rk(psmn)], tag=n + f"RB{h}.{w}")
                p.op("dve", lambda e, sub=sub: e.tensor_copy(
                        B["recb"][0:64, sub * 512:sub * 512 + 512],
                        psm[0:64, 0:512]),
                     reads=[rk(psmn)], writes=[rk("recb", sub)],
                     tag=n + f"RE{h}.{w}")
                p.op("dve", lambda e, pvs=pvs, base=base, hp=hp, w=w, sub=sub,
                     pvsn=pvsn: e.tensor_tensor(
                        B["zT"][base:base + 64,
                                hp * T + w * 512: hp * T + w * 512 + 512],
                        pvs[0:64, 0:512],
                        B["recb"][0:64, sub * 512:sub * 512 + 512], ALU.mult),
                     reads=[rk(pvsn), rk("recb", sub)],
                     writes=[rk(B["zT_name"], (hp, w))], tag=n + f"Z{h}.{w}")

        for w in range(NW):
            kmax = (w * 512 + 511) // 128
            for kc in range(kmax + 1):
                off = max(0, kc * 128 - w * 512)
                nq = 512 - off
                par = kc % 2
                spair = ps01 if par == 0 else ps23
                sk = (psbn[0], psbn[1]) if par == 0 else (psbn[2], psbn[3])
                diag = off > 0 or (kc + 1) * 128 > w * 512
                for sub in range(2):
                    p.op("pe", lambda e, w=w, kc=kc, off=off, nq=nq, sub=sub,
                         spair=spair, diag=diag: e.matmul(
                            spair[:, sub * 512: sub * 512 + nq],
                            B["kT"][sub * 64:sub * 64 + 64,
                                    kc * 128:(kc + 1) * 128],
                            B["qT"][sub * 64:sub * 64 + 64,
                                    w * 512 + off: w * 512 + 512],
                            start=True, stop=not diag),
                         reads=[rk("kT", ALL), rk("qT", ALL)],
                         writes=[rk(sk[sub])],
                         inc=(0 if diag else 1),
                         tag=n + f"S{2*hp+sub}.{w}.{kc}")
                    if diag:
                        p.op("pe", lambda e, sub=sub, spair=spair: e.matmul(
                                spair[:, sub * 512: sub * 512 + 128],
                                com.ident_bf[:], com.maskadd[:, 0:128],
                                start=False, stop=True),
                             reads=[rk("c_identbf"), rk("c_maskadd")],
                             writes=[rk(sk[sub])], tag=n + f"SM{2*hp+sub}.{w}.{kc}")
                if off == 0:
                    p.op("act", lambda e, kc=kc, spair=spair: e.activation(
                            B["PT"][:, kc * 1024: kc * 1024 + 1024],
                            spair[:, 0:1024],
                            AF.Exp, bias=B["negc"][:, 0:1], scale=0.125),
                         reads=[rk(sk[0]), rk(sk[1]), rk("negc")],
                         writes=[rk("PT", kc)],
                         tag=n + f"E{hp}.{w}.{kc}")
                else:
                    for sub in range(2):
                        p.op("act", lambda e, kc=kc, nq=nq, sub=sub,
                             spair=spair: e.activation(
                                B["PT"][:, kc * 1024 + sub * 512:
                                        kc * 1024 + sub * 512 + nq],
                                spair[:, sub * 512: sub * 512 + nq],
                                AF.Exp, bias=B["negc"][:, 0:1], scale=0.125),
                             reads=[rk(sk[sub]), rk("negc")],
                             writes=[rk("PT", kc)],
                             tag=n + f"E{2*hp+sub}.{w}.{kc}")
            if w > 0:
                record_norm_fin(w - 1)
            for kc in range(kmax + 1):
                off = max(0, kc * 128 - w * 512)
                nq = 512 - off
                for sub in range(2):
                    h = 2 * hp + sub
                    pvs, pvsn = psb[4], psbn[4]
                    if sub == 1:
                        pvs, pvsn = psb[5], psbn[5]
                    vsl = vx4[:, kc, sub, :]
                    p.op("pe", lambda e, kc=kc, off=off, nq=nq, vsl=vsl,
                         pvs=pvs, kmax=kmax, sub=sub: e.matmul(
                            pvs[0:65, off:off + nq], vsl,
                            B["PT"][:, kc * 1024 + sub * 512:
                                    kc * 1024 + sub * 512 + nq],
                            start=(kc == 0), stop=(kc == kmax)),
                         reads=[rk("v_ext"), rk("PT", kc)],
                         writes=[rk(pvsn)], inc=(1 if kc == kmax else 0),
                         tag=n + f"PV{h}.{w}.{kc}")
            record_norm_lr(w)
        record_norm_fin(NW - 1)

    wo_sb = B["PT"][:, 0:8192]
    p.dma("pool", wo_sb, wd["wo"], writes=[rk("PT", ALL)], tag=n + "ldwo")
    for ti in range(NT):
        for cf in range(2):
            bi = (2 * ti + cf) % 4
            bank, bname = psb[bi], psbn[bi]
            for ch in range(8):
                p.op("pe", lambda e, ti=ti, cf=cf, ch=ch, bank=bank: e.matmul(
                        bank[:, 0:512],
                        B["zT"][:, ch * T + ti * 128: ch * T + ti * 128 + 128],
                        wo_sb[:, ch * 1024 + cf * 512:
                              ch * 1024 + cf * 512 + 512],
                        start=(ch == 0), stop=(ch == 7)),
                     reads=[rk(B["zT_name"], ALL), rk("PT", ALL)],
                     writes=[rk(bname)], inc=(1 if ch == 7 else 0),
                     tag=n + f"O{ti}.{cf}")
            p.op("dve", lambda e, ti=ti, cf=cf, bank=bank: e.tensor_tensor(
                    B["x"][:, ti * 1024 + cf * 512: ti * 1024 + cf * 512 + 512],
                    B["x"][:, ti * 1024 + cf * 512: ti * 1024 + cf * 512 + 512],
                    bank[:, 0:512], ALU.add),
                 reads=[rk(bname), rk("x_res", ti)], writes=[rk("x_res", ti)],
                 tag=n + f"Oev{ti}.{cf}")


def mha_buffers(p: Prog, geo, x_buf, yT_buf, zT_buf, psb, psb_name,
                ps_misc, ps_misc_name, lnsc, biased, post_scale, pre="",
                ps01=None, ps23=None):
    import math
    _sbuf0 = p.sbuf
    p_sbuf = lambda nm, sh, dt: _sbuf0(pre + nm, sh, dt)
    NT, T = geo["NT"], geo["T"]
    B = {
        "x": x_buf, "yT": yT_buf, "yT_name": "bigbf_a",
        "zT": zT_buf, "zT_name": "bigbf_b",
        "psb": psb, "psb_name": psb_name,
        "ps_misc": ps_misc, "ps_misc_name": ps_misc_name,
        "lnsc": lnsc, "ps01": ps01, "ps23": ps23,
        "v_ext": p_sbuf("v_ext", [128, NT * 130], BF16),
        "qT": p_sbuf("qT", [128, T], BF16),
        "kT": p_sbuf("kT", [128, T], BF16),
        "vT": p_sbuf("vT", [128, T], BF16),
        "PT": p_sbuf("PT", [128, 16 * 1024], BF16),
        "wpair": p_sbuf("wpair", [128, 3 * 1024], BF16),
        "blk2": p_sbuf("blk2", [128, 2], BF16),
        "gmax": p_sbuf("gmax", [2, 2], F32),
        "n2": p_sbuf("n2", [2, 1], F32),
        "cb": p_sbuf("cb", [2, 1], F32),
        "cb2": p_sbuf("cb2", [1, 1], F32),
        "negc": p_sbuf("negc", [128, 2], F32),
        "bvb": p_sbuf("bvb", [128, 1024], BF16),
        "srow": p_sbuf("srow", [1, 1024], F32),
        "rrow_bf": p_sbuf("rrow_bf", [1, 1024], BF16),
        "recb": p_sbuf("recb", [64, 1024], BF16),
        "tiny": p_sbuf("tiny", [128, 1], F32),
        "lnsm": p_sbuf("lnsm", [128, 1], F32),
        "lnpsc": p_sbuf("lnpsc", [128, 1], F32),
        "bqF": p_sbuf("bqF", [128, 8], F32),
        "bkF": p_sbuf("bkF", [128, 8], F32),
        "bv_row": p_sbuf("bv_row", [1, 1024], F32),
    }
    p.op("dve", lambda e: e.memset(B["tiny"][:], 1e-30), writes=[rk("tiny")])
    p.op("dve", lambda e: e.memset(B["lnsm"][:], math.log(0.125)),
         writes=[rk("lnsm")])
    p.op("dve", lambda e: e.memset(B["lnpsc"][:], math.log(post_scale)),
         writes=[rk("lnpsc")])
    p.op("dve", lambda e: e.memset(B["blk2"][:], 0.0), writes=[rk("blk2")])
    p.op("dve", lambda e: e.memset(B["blk2"][0:64, 0:1], 1.0),
         reads=[rk("blk2")], writes=[rk("blk2")])
    p.op("dve", lambda e: e.memset(B["blk2"][64:128, 1:2], 1.0),
         reads=[rk("blk2")], writes=[rk("blk2")])
    return B



def emit_l1_ffn(p: Prog, com: Common, geo, B, wd, name="", wbf=None,
                do_prepass=True):
    NT, T = geo["NT"], geo["T"]
    NWIN = T // 512
    sc = B["lnsc"]
    n = name
    ps = B["ps8"]
    psn = B["ps8_name"]
    yT = B["yT"]
    if wbf is None:
        w1bf = p.nc.dram_tensor(n + "w1bf", [128, 32768], BF16,
                                kind="Internal").ap()
        w2bf = p.nc.dram_tensor(n + "w2bf", [128, 32768], BF16,
                                kind="Internal").ap()
    else:
        w1bf, w2bf = wbf
    if do_prepass:
        for qh in range(4):
            p.dma("pool", w1bf[:, qh * 8192:(qh + 1) * 8192],
                  wd["W1T"][:, qh * 8192:(qh + 1) * 8192],
                  writes=[rk("w1bf_d", qh)], tag=n + f"pc1.{qh}")
            p.dma("pool", w2bf[:, qh * 8192:(qh + 1) * 8192],
                  wd["W2T"][:, qh * 8192:(qh + 1) * 8192],
                  writes=[rk("w2bf_d", qh)], tag=n + f"pc2.{qh}")

    emit_ln(p, com, geo, B["x"], "x_res", yT, B["yT_name"], scratch=sc,
            keep_tile0=(B["lt_tm"], "lt_tm"))

    p.dma("sp", B["scale_sb"][:], wd["scale"], writes=[rk("scale_sb")],
          tag=n + "sc")
    for j in range(4):
        p.dma("pool", B["sc4"][j:j + 1, :], B["scale_sb"][0:1, :],
              reads=[rk("scale_sb")], writes=[rk("sc4")], tag=n + f"sc4.{j}")
    p.dma("sp", B["b1F"][:], wd["b1F"], writes=[rk("b1Fc")], tag=n + "b1")
    p.op("dve", lambda e: e.memset(B["b2pad"][:], 0.0), writes=[rk("b2pad")])
    p.dma("pool", B["b2pad"][0:1, :], wd["b2"], reads=[rk("b2pad")],
          writes=[rk("b2pad")], tag=n + "b2")
    p.op("dve", lambda e: e.memset(B["onespad"][:], 0.0), writes=[rk("onespad")])
    p.op("dve", lambda e: e.memset(B["onespad"][0:1, :], 1.0),
         reads=[rk("onespad")], writes=[rk("onespad")])
    for j in range(4):
        p.dma("pool", B["se_rows"][0:1, j * 1024:(j + 1) * 1024],
              B["lt_tm"][1 + 2 * j:2 + 2 * j, :], reads=[rk("lt_tm")],
              writes=[rk("se_rows")], tag=n + f"ser{j}")
        p.dma("pool", B["e4_rows"][0:1, j * 1024:(j + 1) * 1024],
              B["lt_tm"][9 + 2 * j:10 + 2 * j, :], reads=[rk("lt_tm")],
              writes=[rk("e4_rows")], tag=n + f"e4r{j}")
    p.op("dve", lambda e: e.memset(B["se_pad"][:], 0.0), writes=[rk("se_pad")])
    p.op("dve", lambda e: e.tensor_scalar(
            B["se_pad"][0:1, :], B["se_rows"][0:1, :],
            B["scale_sb"][0:1, 0:1], None, ALU.mult),
         reads=[rk("se_rows"), rk("scale_sb")], writes=[rk("se_pad")],
         tag=n + "sescl")
    p.op("dve", lambda e: e.memset(B["E_pad"][:], 0.0), writes=[rk("E_pad")])
    for j in range(4):
        for pm in range(4):
            src = B["e4_rows"][0:1, j * 1024 + pm: (j + 1) * 1024: 4]
            p.dma("pool", B["E_pad"][pm:pm + 1, j * 256:(j + 1) * 256], src,
                  reads=[rk("e4_rows")], writes=[rk("E_pad")],
                  tag=n + f"Ep{j}.{pm}", allow_slow_non_contiguous=True)

    p.op("dve", lambda e: e.memset(B["a1pad"][:], 0.0),
         writes=[rk("a1pad", "z")] + [rk("a1pad", j) for j in range(4)],
         tag=n + "a1pz")
    p.op("dve", lambda e: e.memset(B["g_pad"][:], 0.0),
         writes=[rk("g_pad", "z")] + [rk("g_pad", j) for j in range(4)],
         tag=n + "gpz")

    for w in range(NWIN):
        ytk = [rk(B["yT_name"], 0)] + [rk(B["yT_name"], t)
                                       for t in range(4 * w, 4 * w + 4)]
        for ch in range(8):
            p.op("pe", lambda e, w=w, ch=ch: e.matmul(
                    ps[5][0:4, 0:512],
                    yT[:, ch * T + 2: ch * T + 10: 2],
                    yT[:, ch * T + w * 512: ch * T + (w + 1) * 512],
                    start=(ch == 0), stop=(ch == 7)),
                 reads=ytk, writes=[rk(psn[5])],
                 inc=(1 if ch == 7 else 0), tag=n + f"a1.{w}.{ch}")
        p.op("dve", lambda e, w=w: e.tensor_copy(
                B["a1s"][0:4, w * 512:(w + 1) * 512], ps[5][0:4, 0:512]),
             reads=[rk(psn[5])], writes=[rk("a1s")], tag=n + f"a1e{w}")

    for w in range(NWIN):
        for j in range(4):
            p.dma("pool", B["a1pad"][0:1, j * 512:(j + 1) * 512],
                  B["a1s"][j:j + 1, w * 512:(w + 1) * 512],
                  reads=[rk("a1s")], writes=[rk("a1pad", j)],
                  tag=n + f"a1p{w}.{j}")
        def rec_gT(hc, w=w):
            j = hc // 8
            dch = hc % 8
            p.op("pe", lambda e, hc=hc, j=j, dch=dch: e.matmul(
                    ps[2 + j][0:4, 0:512],
                    yT[:, dch * T + 10: dch * T + 18: 2],
                    B["H"][:, hc * 512:(hc + 1) * 512],
                    start=(dch == 0), stop=(dch == 7)),
                 reads=[rk(B["yT_name"], 0), rk("H", hc)],
                 writes=[rk(psn[2 + j])], inc=(1 if dch == 7 else 0),
                 tag=n + f"gT.{w}.{hc}")

        for hc in range(32):
            j = hc // 8
            slot = hc % 2
            if hc % 4 == 0:
                grp = (hc // 4) % 2
                p.dma("sp", B["w1buf"][:, grp * 4096:(grp + 1) * 4096],
                      w1bf[:, hc * 1024:(hc + 4) * 1024],
                      reads=[rk("w1bf_d", hc // 8)],
                      writes=[rk("wbuf", grp)], tag=n + f"w1.{w}.{hc}")
            grp = (hc // 4) % 2
            wof = grp * 4096 + (hc % 4) * 1024
            hps, hpsn = ps[slot], psn[slot]
            for ch in range(8):
                p.op("pe", lambda e, hc=hc, ch=ch, wof=wof, w=w, hps=hps:
                     e.matmul(
                        hps[:, 0:512],
                        B["w1buf"][:, wof + ch * 128: wof + (ch + 1) * 128],
                        yT[:, ch * T + w * 512: ch * T + (w + 1) * 512],
                        start=(ch == 0), stop=False),
                     reads=[rk("wbuf", grp)] + [rk(B["yT_name"], t)
                                                for t in range(4 * w, 4 * w + 4)],
                     writes=[rk(hpsn)], inc=0, tag=n + f"W1.{w}.{hc}.{ch}")
            p.op("pe", lambda e, hc=hc, j=j, hps=hps: e.matmul(
                    hps[:, 0:512],
                    B["se_pad"][:, hc * 128:(hc + 1) * 128],
                    B["a1pad"][:, j * 512:(j + 1) * 512],
                    start=False, stop=True),
                 reads=[rk("se_pad"), rk("a1pad", j), rk("a1pad", "z")],
                 writes=[rk(hpsn)], tag=n + f"lora1.{w}.{hc}")
            if hc > 0:
                rec_gT(hc - 1)
            p.op("act", lambda e, hc=hc, hps=hps: e.activation(
                    B["H"][:, hc * 512:(hc + 1) * 512], hps[:, 0:512],
                    AF.Gelu, bias=B["b1F"][:, hc:hc + 1], scale=1.0),
                 reads=[rk(hpsn), rk("b1Fc")], writes=[rk("H", hc)],
                 tag=n + f"gel.{w}.{hc}")
        rec_gT(31)
        for pm in range(4):
            p.op("dve", lambda e, pm=pm: e.tensor_scalar(
                    B["gT_sb"][0:4, pm * 512:(pm + 1) * 512],
                    ps[2 + pm][0:4, 0:512],
                    B["sc4"][0:4, 0:1], None, ALU.mult),
                 reads=[rk(psn[2 + pm]), rk("sc4")], writes=[rk("gT_sb", pm)],
                 tag=n + f"gTe.{w}.{pm}")
        for j in range(4):
            for pm in range(4):
                p.dma("pool", B["g_pad"][pm:pm + 1, j * 512:(j + 1) * 512],
                      B["gT_sb"][j:j + 1, pm * 512:(pm + 1) * 512],
                      reads=[rk("gT_sb", pm)], writes=[rk("g_pad", j)],
                      tag=n + f"gp.{w}.{j}.{pm}")

        for hc in range(32):
            if hc % 4 == 0:
                grp = (hc // 4) % 2
                p.dma("sp", B["w1buf"][:, grp * 4096:(grp + 1) * 4096],
                      w2bf[:, hc * 1024:(hc + 4) * 1024],
                      reads=[rk("w2bf_d", hc // 8)],
                      writes=[rk("wbuf", grp)], tag=n + f"w2.{w}.{hc}")
            grp = (hc // 4) % 2
            wof = grp * 4096 + (hc % 4) * 1024
            for tt in range(4):
                for cf in range(2):
                    bi = tt * 2 + cf
                    p.op("pe", lambda e, hc=hc, tt=tt, cf=cf, bi=bi, wof=wof,
                         w=w: e.matmul(
                            ps[bi][:, 0:512],
                            B["H"][:, hc * 512 + tt * 128:
                                   hc * 512 + (tt + 1) * 128],
                            B["w1buf"][:, wof + cf * 512:
                                       wof + (cf + 1) * 512],
                            start=(hc == 0), stop=False),
                         reads=[rk("wbuf", grp), rk("H", hc)],
                         writes=[rk(psn[bi])],
                         inc=(1 if (tt == 3 and cf == 1) else 0),
                         tag=n + f"W2.{w}.{hc}.{tt}.{cf}")
        for tt in range(4):
            for j in range(4):
                cf = j // 2
                bi = tt * 2 + cf
                p.op("pe", lambda e, tt=tt, j=j, bi=bi, cf=cf: e.matmul(
                        ps[bi][:, (j % 2) * 256: (j % 2) * 256 + 256],
                        B["g_pad"][:, j * 512 + tt * 128:
                                   j * 512 + (tt + 1) * 128],
                        B["E_pad"][:, j * 256:(j + 1) * 256],
                        start=False, stop=False),
                     reads=[rk("g_pad", j), rk("g_pad", "z"), rk("E_pad")],
                     writes=[rk(psn[bi])], inc=0, tag=n + f"lora2.{w}.{tt}.{j}")
            for cf in range(2):
                bi = tt * 2 + cf
                p.op("pe", lambda e, tt=tt, cf=cf, bi=bi: e.matmul(
                        ps[bi][:, 0:512], B["onespad"][:, 0:128],
                        B["b2pad"][:, cf * 512:(cf + 1) * 512],
                        start=False, stop=True),
                     reads=[rk("onespad"), rk("b2pad")],
                     writes=[rk(psn[bi])], tag=n + f"b2f.{w}.{tt}.{cf}")
                ti = w * 4 + tt
                p.op("dve", lambda e, ti=ti, cf=cf, bi=bi: e.scalar_tensor_tensor(
                        B["x"][:, ti * 1024 + cf * 512:
                               ti * 1024 + cf * 512 + 512],
                        ps[bi][:, 0:512], 0.5,
                        B["x"][:, ti * 1024 + cf * 512:
                               ti * 1024 + cf * 512 + 512],
                        ALU.mult, ALU.add),
                     reads=[rk(psn[bi]), rk("x_res", ti)],
                     writes=[rk("x_res", ti)], tag=n + f"xup.{w}.{tt}.{cf}")


def l1ffn_buffers(p: Prog, geo, x_buf, yT_buf, H_buf, ps8, ps8_name, lnsc,
                  pre=""):
    T = geo["T"]
    _sbuf0 = p.sbuf
    p_sbuf = lambda nm, sh, dt: _sbuf0(pre + nm, sh, dt)
    B = {
        "x": x_buf, "yT": yT_buf, "yT_name": "bigbf_a",
        "H": H_buf, "ps8": ps8, "ps8_name": ps8_name, "lnsc": lnsc,
        "lt_tm": p_sbuf("lt_tm", [128, 1024], BF16),
        "scale_sb": p_sbuf("scale_sb", [1, 1], F32),
        "sc4": p_sbuf("sc4", [4, 1], F32),
        "b1F": p_sbuf("b1Fc", [128, 32], F32),
        "b2pad": p_sbuf("b2pad", [128, 1024], BF16),
        "onespad": p_sbuf("onespad", [128, 128], BF16),
        "se_rows": p_sbuf("se_rows", [1, 4096], BF16),
        "e4_rows": p_sbuf("e4_rows", [1, 4096], BF16),
        "se_pad": p_sbuf("se_pad", [128, 4096], BF16),
        "E_pad": p_sbuf("E_pad", [128, 1024], BF16),
        "a1s": p_sbuf("a1s", [4, T], BF16),
        "a1pad": p_sbuf("a1pad", [128, 2048], BF16),
        "gT_sb": p_sbuf("gT_sb", [4, 2048], BF16),
        "g_pad": p_sbuf("g_pad", [128, 2048], BF16),
        "w1buf": p_sbuf("wbuf", [128, 8192], BF16),
    }
    return B


def alloc_psum_pool(p: Prog):
    ps01 = p.psum("psum01", [128, 1024], F32)
    ps23 = p.psum("psum23", [128, 1024], F32)
    ps = [ps01[:, 0:512], ps01[:, 512:1024],
          ps23[:, 0:512], ps23[:, 512:1024]]
    psn = ["ps0", "ps1", "ps2", "ps3"]
    hs = {}
    for i in range(4, 8):
        h = p.psum(f"ps{i}", [128, 512], F32)
        ps.append(h)
        psn.append(f"ps{i}")
        hs[i] = h
    return ps, psn, ps01, ps23, hs


def phase_barrier(p: Prog, names):
    if "barsc" not in p.buffers:
        p.sbuf("barsc", [1, 1], F32)
    bs = p.buffers["barsc"]
    for nm in names:
        p.op("dve", lambda e: e.memset(bs[:], 0.0),
             reads=[rk(nm)], writes=[rk(nm), rk("barsc")],
             tag=f"barrier-{nm}")



def emit_gating(p: Prog, com: Common, geo, B, nT, nT_name, simn, simn_key,
                sigb, sigb_key, rw_buf, rw_key, fold, name=""):
    NT, T = geo["NT"], geo["T"]
    NWIN = T // 512
    n = name
    ps = B["ps8"]
    psn = B["ps8_name"]
    g = B["gat"]

    for w in range(NWIN):
        for ch in range(8):
            p.op("dve", lambda e, ch=ch, w=w: e.tensor_tensor(
                    B["sq2w"][:], nT[:, ch * T + w * 512: ch * T + (w + 1) * 512],
                    nT[:, ch * T + w * 512: ch * T + (w + 1) * 512], ALU.mult),
                 reads=[rk(nT_name, ALL)], writes=[rk("sq2w")],
                 tag=n + f"gsq{w}.{ch}")
            p.op("pe", lambda e, ch=ch: e.matmul(
                    ps[6][0:1, 0:512], com.ones_col_bf[:], B["sq2w"][:],
                    start=(ch == 0), stop=(ch == 7)),
                 reads=[rk("sq2w"), rk("c_ones_col")], writes=[rk(psn[6])],
                 tag=n + f"gn{w}.{ch}")
        p.op("dve", lambda e, w=w: e.tensor_copy(
                g["n2row"][0:1, w * 512:(w + 1) * 512], ps[6][0:1, 0:512]),
             reads=[rk(psn[6])], writes=[rk("g_n2row")], tag=n + f"gnev{w}")
    for ti in range(NT):
        p.op("pe", lambda e, ti=ti: e.transpose(
                ps[6][:, 0:1], g["n2row"][0:1, ti * 128:(ti + 1) * 128],
                com.ones_row_f[0:1, 0:1]),
             reads=[rk("g_n2row"), rk("c_ones_row")], writes=[rk(psn[6])],
             tag=n + f"gtr{ti}")
        p.op("dve", lambda e, ti=ti: e.tensor_copy(
                g["n2c"][:, ti:ti + 1], ps[6][:, 0:1]),
             reads=[rk(psn[6])], writes=[rk("g_n2c")], tag=n + f"gtc{ti}")
    p.op("act", lambda e: e.activation(g["n2c"][:], g["n2c"][:], AF.Ln,
                                       bias=com.tiny[:, 0:1], scale=1.0),
         reads=[rk("g_n2c"), rk("c_tiny")], writes=[rk("g_n2c")], tag=n + "gln")
    p.op("act", lambda e: e.activation(g["rnc"][:], g["n2c"][:], AF.Exp,
                                       bias=com.zero_col[:, 0:1], scale=-0.5),
         reads=[rk("g_n2c"), rk("c_zero")], writes=[rk("g_rnc")], tag=n + "gex")

    for ti in range(NT):
        for ch in range(8):
            p.op("pe", lambda e, ti=ti, ch=ch: e.matmul(
                    ps[5][:, 0:8], nT[:, ch * T + ti * 128: ch * T + ti * 128 + 128],
                    simn[:, ch * 8:(ch + 1) * 8], start=(ch == 0), stop=(ch == 7)),
                 reads=[rk(nT_name, ALL), rk(simn_key)], writes=[rk(psn[5])],
                 inc=(1 if ch == 7 else 0), tag=n + f"glg{ti}.{ch}")
        p.op("dve", lambda e, ti=ti: e.tensor_scalar(
                g["lg"][:, ti * 8:(ti + 1) * 8], ps[5][:, 0:8],
                g["rnc"][:, ti:ti + 1], None, ALU.mult),
             reads=[rk(psn[5]), rk("g_rnc")], writes=[rk("g_lg", ti)],
             tag=n + f"glge{ti}")
        p.op("dve", lambda e, ti=ti: e.tensor_tensor(
                g["lg"][:, ti * 8:(ti + 1) * 8],
                g["lg"][:, ti * 8:(ti + 1) * 8], sigb[:], ALU.subtract),
             reads=[rk("g_lg", ti), rk(sigb_key)], writes=[rk("g_lg", ti)],
             tag=n + f"glgs{ti}")

    lgk = [rk("g_lg", ti) for ti in range(NT)]
    p.op("dve", lambda e: e.tensor_scalar(g["gated"][:, 0:NT * 8],
                                          g["lg"][:, 0:NT * 8], 0.0, None,
                                          ALU.max),
         reads=lgk, writes=[rk("g_gated")], tag=n + "ggt")
    p.op("dve", lambda e: e.tensor_scalar(g["amask"][:, 0:NT * 8],
                                          g["lg"][:, 0:NT * 8], 0.0, None,
                                          ALU.is_gt),
         reads=lgk, writes=[rk("g_amask")], tag=n + "gam")
    lg3 = g["lg"][:, 0:NT * 8].rearrange("p (t e) -> p t e", e=8)
    p.op("dve", lambda e: e.tensor_reduce(
            g["rowhas"][:, 0:NT], g["amask"][:, 0:NT * 8].rearrange(
                "p (t e) -> p t e", e=8), AX.X, ALU.add),
         reads=[rk("g_amask")], writes=[rk("g_rowhas")], tag=n + "grh")
    p.op("dve", lambda e: e.tensor_scalar(g["inact"][:, 0:NT],
                                          g["rowhas"][:, 0:NT], 0.0, None,
                                          ALU.is_le),
         reads=[rk("g_rowhas")], writes=[rk("g_inact")], tag=n + "gin")
    p.op("dve", lambda e, lg3=lg3: e.tensor_reduce(
            g["m1"][:, 0:NT], lg3, AX.X, ALU.max),
         reads=lgk, writes=[rk("g_m1")], tag=n + "gm1")
    for ti in range(NT):
        p.op("dve", lambda e, ti=ti: e.tensor_scalar(
                g["eq1"][:, 0:8], g["lg"][:, ti * 8:(ti + 1) * 8],
                g["m1"][:, ti:ti + 1], -1e30, ALU.is_equal, ALU.mult),
             reads=[rk("g_lg", ti), rk("g_m1")], writes=[rk("g_eq1")],
             tag=n + f"geq{ti}")
        p.op("dve", lambda e, ti=ti: e.tensor_tensor(
                g["lmask"][:, ti * 8:(ti + 1) * 8], g["eq1"][:, 0:8],
                g["lg"][:, ti * 8:(ti + 1) * 8], ALU.add),
             reads=[rk("g_eq1"), rk("g_lg", ti)], writes=[rk("g_lmask", ti)],
             tag=n + f"glm{ti}")
    p.op("dve", lambda e: e.tensor_reduce(
            g["m2"][:, 0:NT], g["lmask"][:, 0:NT * 8].rearrange(
                "p (t e) -> p t e", e=8), AX.X, ALU.max),
         reads=[rk("g_lmask", ti) for ti in range(NT)], writes=[rk("g_m2")],
         tag=n + "gm2")
    for ti in range(NT):
        p.op("dve", lambda e, ti=ti: e.tensor_scalar(
                g["eq1"][:, 0:8], g["lg"][:, ti * 8:(ti + 1) * 8],
                g["m2"][:, ti:ti + 1], None, ALU.is_ge),
             reads=[rk("g_lg", ti), rk("g_m2")], writes=[rk("g_eq1")],
             tag=n + f"gt2{ti}")
        p.op("dve", lambda e, ti=ti: e.tensor_tensor(
                g["eq1"][:, 0:8], g["eq1"][:, 0:8],
                g["amask"][:, ti * 8:(ti + 1) * 8], ALU.subtract),
             reads=[rk("g_eq1"), rk("g_amask")], writes=[rk("g_eq1")],
             tag=n + f"gd{ti}")
        p.op("dve", lambda e, ti=ti: e.scalar_tensor_tensor(
                g["mask"][:, ti * 8:(ti + 1) * 8], g["eq1"][:, 0:8],
                g["inact"][:, ti:ti + 1],
                g["amask"][:, ti * 8:(ti + 1) * 8], ALU.mult, ALU.add),
             reads=[rk("g_eq1"), rk("g_inact"), rk("g_amask")],
             writes=[rk("g_mask", ti)], tag=n + f"gmk{ti}")
    p.op("act", lambda e: e.activation(g["expg"][:, 0:NT * 8],
                                       g["gated"][:, 0:NT * 8], AF.Exp,
                                       bias=com.zero_col[:, 0:1], scale=1.0),
         reads=[rk("g_gated"), rk("c_zero")], writes=[rk("g_expg")],
         tag=n + "gxp")
    p.op("dve", lambda e: e.tensor_tensor(
            g["rwun"][:, 0:NT * 8], g["mask"][:, 0:NT * 8],
            g["expg"][:, 0:NT * 8], ALU.mult),
         reads=[rk("g_mask", ti) for ti in range(NT)] + [rk("g_expg")],
         writes=[rk("g_rwun")], tag=n + "gru")
    p.op("dve", lambda e: e.tensor_reduce(
            g["zsum"][:, 0:NT], g["rwun"][:, 0:NT * 8].rearrange(
                "p (t e) -> p t e", e=8), AX.X, ALU.add),
         reads=[rk("g_rwun")], writes=[rk("g_zsum")], tag=n + "gz")
    p.op("dve", lambda e: e.reciprocal(g["rz"][:, 0:NT], g["zsum"][:, 0:NT]),
         reads=[rk("g_zsum")], writes=[rk("g_rz")], tag=n + "grz")
    for ti in range(NT):
        p.op("dve", lambda e, ti=ti: e.tensor_scalar(
                rw_buf[:, ti * 8:(ti + 1) * 8], g["rwun"][:, ti * 8:(ti + 1) * 8],
                g["rz"][:, ti:ti + 1], fold, ALU.mult, ALU.mult),
             reads=[rk("g_rwun"), rk("g_rz")], writes=[rk(rw_key, ti)],
             tag=n + f"grw{ti}")


def gating_scratch(p: Prog, geo, pre=""):
    NT, T = geo["NT"], geo["T"]
    names = {"n2row": [1, T], "n2c": [128, NT], "rnc": [128, NT],
             "lg": [128, NT * 8], "gated": [128, NT * 8],
             "amask": [128, NT * 8], "rowhas": [128, NT], "inact": [128, NT],
             "m1": [128, NT], "m2": [128, NT], "eq1": [128, 8],
             "lmask": [128, NT * 8], "mask": [128, NT * 8],
             "expg": [128, NT * 8], "rwun": [128, NT * 8],
             "zsum": [128, NT], "rz": [128, NT]}
    d = {k: p.sbuf(pre + "g_" + k, sh, F32) for k, sh in names.items()}
    return d



def prep_sim(p: Prog, com: Common, B, sim_d, simn_buf, simn_key, name=""):
    n = name
    ps = B["ps8"]
    psn = B["ps8_name"]
    p.dma("sp", B["sim_f"][:], sim_d, writes=[rk("sim_f")], tag=n + "ld")
    p.op("dve", lambda e: e.tensor_tensor(B["sim_sq"][:], B["sim_f"][:],
                                          B["sim_f"][:], ALU.mult),
         reads=[rk("sim_f")], writes=[rk("sim_sq")], tag=n + "sq")
    for ch in range(8):
        p.op("pe", lambda e, ch=ch: e.matmul(
                ps[6][0:1, 0:8], B["ones_col_f"][:],
                B["sim_sq"][:, ch * 8:(ch + 1) * 8],
                start=(ch == 0), stop=(ch == 7)),
             reads=[rk("sim_sq"), rk("ones_col_f")], writes=[rk(psn[6])],
             inc=(1 if ch == 7 else 0), tag=n + f"n{ch}")
    p.op("act", lambda e: e.activation(B["srn"][0:1, 0:8], ps[6][0:1, 0:8],
                                       AF.Ln, bias=com.tiny[0:1, 0:1], scale=1.0),
         reads=[rk(psn[6]), rk("c_tiny")], writes=[rk("srn")], tag=n + "ln")
    p.op("act", lambda e: e.activation(B["srn"][0:1, 0:8], B["srn"][0:1, 0:8],
                                       AF.Exp, bias=com.zero_col[0:1, 0:1],
                                       scale=-0.5),
         reads=[rk("srn"), rk("c_zero")], writes=[rk("srn")], tag=n + "ex")
    p.op("pe", lambda e: e.matmul(ps[6][:, 0:8], com.ones_row_f[0:1, :],
                                  B["srn"][0:1, 0:8], start=True, stop=True),
         reads=[rk("srn"), rk("c_ones_row")], writes=[rk(psn[6])], tag=n + "bc")
    p.op("dve", lambda e: e.tensor_copy(B["srnb"][:], ps[6][:, 0:8]),
         reads=[rk(psn[6])], writes=[rk("srnb")], tag=n + "be")
    for ch in range(8):
        p.op("dve", lambda e, ch=ch: e.tensor_tensor(
                simn_buf[:, ch * 8:(ch + 1) * 8],
                B["sim_f"][:, ch * 8:(ch + 1) * 8], B["srnb"][:], ALU.mult),
             reads=[rk("sim_f"), rk("srnb")], writes=[rk(simn_key)],
             tag=n + f"m{ch}")


def prep_sig(p: Prog, com: Common, B, gates_d, sigb_buf, sigb_key, name=""):
    n = name
    ps = B["ps8"]
    psn = B["ps8_name"]
    p.dma("sp", B["grow"][:], gates_d, writes=[rk("grow")], tag=n + "ld")
    p.op("act", lambda e: e.activation(B["grow"][0:1, :], B["grow"][0:1, :],
                                       AF.Exp, bias=com.zero_col[0:1, 0:1],
                                       scale=-1.0),
         reads=[rk("grow"), rk("c_zero")], writes=[rk("grow")], tag=n + "ex")
    p.op("dve", lambda e: e.tensor_scalar(B["grow"][0:1, :], B["grow"][0:1, :],
                                          1.0, None, ALU.add),
         reads=[rk("grow")], writes=[rk("grow")], tag=n + "a1")
    p.op("dve", lambda e: e.reciprocal(B["grow"][0:1, :], B["grow"][0:1, :]),
         reads=[rk("grow")], writes=[rk("grow")], tag=n + "rc")
    p.op("pe", lambda e: e.matmul(ps[6][:, 0:8], com.ones_row_f[0:1, :],
                                  B["grow"][0:1, 0:8], start=True, stop=True),
         reads=[rk("grow"), rk("c_ones_row")], writes=[rk(psn[6])], tag=n + "bc")
    p.op("dve", lambda e: e.tensor_copy(sigb_buf[:], ps[6][:, 0:8]),
         reads=[rk(psn[6])], writes=[rk(sigb_key)], tag=n + "be")


def emit_l0_ffn(p: Prog, com: Common, geo, B, wd, name=""):
    NT, T = geo["NT"], geo["T"]
    n = name
    ps = B["ps8"]
    psn = B["ps8_name"]
    sc = B["lnsc"]
    n1T = B["bigbf_a"]
    n2T = B["bigbf_b"]
    vx = B["v_ext"][:, 0:NT * 129].rearrange("p (t d) -> p t d", d=129)

    emit_ln(p, com, geo, B["x"], "x_res", n1T, "bigbf_a",
            affine=(B["ln1w"], B["ln1b"], "ln1wF", "ln1bF"), fuse_double=True,
            out_stats=(B["ystats"], "ystats"), scratch=sc)

    emit_gating(p, com, geo, B, n1T, "bigbf_a", B["simn1"], "simn1",
                B["sigb1"], "sigb1", B["rw"], "rw", 1.0, name=n + "g1")

    pst6q = lambda a, b: ps[6][:, a // 2: b // 2].bitcast(BF16)
    p.op("dve", lambda e: e.memset(vx[:, :, 128:129], 1.0),
         writes=[rk("v_ext")], tag=n + "vone")
    qkv_src = ("wq0", "wk0", "wv0")
    qkv_buf = ((B["wbig"], "wbig"), (B["wbig2"], "wbig2"), (B["wbig"], "wbig"))

    def ld_qkv(iw):
        wb_, wkk_ = qkv_buf[iw]
        for c4 in range(4):
            p.dma("pool", wb_[:, c4 * 2048:(c4 + 1) * 2048],
                  wd[qkv_src[iw]][:, c4 * 2048:(c4 + 1) * 2048],
                  writes=[rk(wkk_, 2 * c4), rk(wkk_, 2 * c4 + 1)],
                  tag=n + f"ld{qkv_src[iw]}.{c4}")

    ld_qkv(0)
    ld_qkv(1)
    for iw, wn in enumerate(("q", "k", "v")):
        wb, wkk = qkv_buf[iw]
        if iw == 1:
            ld_qkv(2)

        def rec_t(ti, wn=wn):
            tp = ti % 2
            dst = B["qT"] if wn == "q" else B["kT"]
            dk = "qT" if wn == "q" else "kT"
            qb = B["qbf"][:, tp * 128:(tp + 1) * 128]
            tb = (pst6q if tp == 0 else sc["pst"])
            tbk = psn[6] if tp == 0 else "ps7"
            p.op("pe", lambda e, qb=qb, tb=tb: e.transpose(
                    tb(0, 128), qb, com.ident_bf[:]),
                 reads=[rk("qbf", tp), rk("c_identbf")], writes=[rk(tbk)],
                 tag=n + f"t{wn}{ti}")
            p.op("act", lambda e, ti=ti, dst=dst, tb=tb: e.activation(
                    dst[:, ti * 128:(ti + 1) * 128], tb(0, 128),
                    AF.Copy, bias=0.0, scale=1.0),
                 reads=[rk(tbk)], writes=[rk(dk, ti)],
                 tag=n + f"e{wn}{ti}")

        for ti in range(NT):
            for half in range(2):
                for ch in range(8):
                    p.op("pe", lambda e, ti=ti, half=half, ch=ch, wb=wb:
                         e.matmul(
                            ps[half][:, 0:512],
                            n1T[:, ch * T + ti * 128: ch * T + ti * 128 + 128],
                            wb[:, ch * 1024 + half * 512:
                               ch * 1024 + half * 512 + 512],
                            start=(ch == 0), stop=(ch == 7)),
                         reads=[rk("bigbf_a", ti), rk(wkk, ch)],
                         writes=[rk(psn[half])], inc=(1 if ch == 7 else 0),
                         tag=n + f"P{wn}{ti}.{half}.{ch}")
            if wn != "v" and ti > 0:
                rec_t(ti - 1)
            tp = ti % 2
            qa = B["qacc"][:, tp * 128:(tp + 1) * 128]
            for e_ in range(8):
                bank = ps[e_ // 4]
                bnm = psn[e_ // 4]
                sl = (e_ % 4) * 128
                if e_ == 0:
                    p.op("dve", lambda e, bank=bank, sl=sl, ti=ti, qa=qa:
                         e.tensor_scalar(
                            qa, bank[:, sl:sl + 128],
                            B["rw"][:, ti * 8:ti * 8 + 1], None, ALU.mult),
                         reads=[rk(bnm), rk("rw", ti)], writes=[rk("qacc", tp)],
                         tag=n + f"W{wn}{ti}.0")
                else:
                    p.op("dve", lambda e, bank=bank, sl=sl, ti=ti, e_=e_, qa=qa:
                         e.scalar_tensor_tensor(
                            qa, bank[:, sl:sl + 128],
                            B["rw"][:, ti * 8 + e_:ti * 8 + e_ + 1],
                            qa, ALU.mult, ALU.add),
                         reads=[rk(bnm), rk("rw", ti), rk("qacc", tp)],
                         writes=[rk("qacc", tp)], tag=n + f"W{wn}{ti}.{e_}")
            if wn == "v":
                p.op("dve", lambda e, ti=ti, qa=qa: e.tensor_copy(
                        vx[:, ti, 0:128], qa),
                     reads=[rk("qacc", tp)], writes=[rk("v_ext")],
                     tag=n + f"vev{ti}")
            else:
                qb = B["qbf"][:, tp * 128:(tp + 1) * 128]
                p.op("dve", lambda e, qa=qa, qb=qb: e.tensor_copy(qb, qa),
                     reads=[rk("qacc", tp)], writes=[rk("qbf", tp)],
                     tag=n + f"c{wn}{ti}")
        if wn != "v":
            rec_t(NT - 1)

    p.op("dve", lambda e: e.memset(B["gm1"][:], 0.0), writes=[rk("gm1")],
         tag=n + "gm0")
    sq2v = B["PT"][:, 0:T]
    for ci, (buf, bk) in enumerate(((B["qT"], "qT"), (B["kT"], "kT"))):
        p.op("dve", lambda e, buf=buf, sq2v=sq2v: e.tensor_tensor(
                sq2v, buf[:, 0:T], buf[:, 0:T], ALU.mult),
             reads=[rk(bk, ti) for ti in range(NT)], writes=[rk("PT")],
             tag=n + f"s2{ci}")
        for w in range(T // 512):
            sb_, sbk_ = (ps[6], psn[6]) if w % 2 == 0 else (ps[5], psn[5])
            p.op("pe", lambda e, w=w, sq2v=sq2v, sb_=sb_: e.matmul(
                    sb_[0:1, 0:512], com.ones_col_bf[:],
                    sq2v[:, w * 512:(w + 1) * 512], start=True, stop=True),
                 reads=[rk("PT"), rk("c_ones_col")], writes=[rk(sbk_)],
                 tag=n + f"s2n{ci}.{w}")
            p.op("dve", lambda e, sb_=sb_: e.tensor_reduce(
                    B["n21"][0:1, 0:1], sb_[0:1, 0:512], AX.X, ALU.max),
                 reads=[rk(sbk_)], writes=[rk("n21")], tag=n + f"s2r{ci}.{w}")
            p.op("dve", lambda e, ci=ci: e.tensor_tensor(
                    B["gm1"][0:1, ci:ci + 1], B["gm1"][0:1, ci:ci + 1],
                    B["n21"][0:1, 0:1], ALU.max),
                 reads=[rk("n21"), rk("gm1")], writes=[rk("gm1")],
                 tag=n + f"s2m{ci}.{w}")
    p.op("dve", lambda e: e.tensor_tensor(B["gm1"][0:1, 0:1], B["gm1"][0:1, 0:1],
                                          B["gm1"][0:1, 1:2], ALU.mult),
         reads=[rk("gm1")], writes=[rk("gm1")], tag=n + "cbp")
    p.op("act", lambda e: e.activation(B["gm1"][0:1, 0:1], B["gm1"][0:1, 0:1],
                                       AF.Ln, bias=com.tiny[0:1, 0:1], scale=1.0),
         reads=[rk("gm1"), rk("c_tiny")], writes=[rk("gm1")], tag=n + "cbl")
    p.op("act", lambda e: e.activation(B["gm1"][0:1, 0:1], B["gm1"][0:1, 0:1],
                                       AF.Exp, bias=B["lnsm128"][0:1, 0:1],
                                       scale=0.5),
         reads=[rk("gm1"), rk("lnsm128")], writes=[rk("gm1")], tag=n + "cbx")
    p.op("dve", lambda e: e.tensor_scalar(B["gm1"][0:1, 0:1], B["gm1"][0:1, 0:1],
                                          -1.0, None, ALU.mult),
         reads=[rk("gm1")], writes=[rk("gm1")], tag=n + "cbn")
    p.op("pe", lambda e: e.matmul(ps[6][:, 0:1], com.ones_row_f[0:1, :],
                                  B["gm1"][0:1, 0:1], start=True, stop=True),
         reads=[rk("gm1"), rk("c_ones_row")], writes=[rk(psn[6])], tag=n + "cbb")
    p.op("dve", lambda e: e.tensor_copy(B["negc1"][:], ps[6][:, 0:1]),
         reads=[rk(psn[6])], writes=[rk("negc1")], tag=n + "cbe")

    p.dma("pool", B["wbig2"][:], wd["o"], writes=[rk("wbig2")], tag=n + "ldo")
    sm128 = 1.0 / float(np.sqrt(128.0))
    pst6 = lambda a, b: ps[6][:, a // 2: b // 2].bitcast(BF16)
    for qt in range(NT):
        qp = qt % 2
        for kc in range(qt + 1):
            sps, spn = (ps[kc % 2], psn[kc % 2])
            diag = (kc == qt)
            p.op("pe", lambda e, qt=qt, kc=kc, sps=sps, diag=diag: e.matmul(
                    sps[:, 0:128], B["kT"][:, kc * 128:(kc + 1) * 128],
                    B["qT"][:, qt * 128:(qt + 1) * 128], start=True,
                    stop=not diag),
                 reads=[rk("kT", ALL), rk("qT", ALL)], writes=[rk(spn)],
                 inc=(0 if diag else 1), tag=n + f"S{qt}.{kc}")
            if diag:
                p.op("pe", lambda e, sps=sps: e.matmul(
                        sps[:, 0:128], com.ident_bf[:], com.maskadd[:],
                        start=False, stop=True),
                     reads=[rk("c_identbf"), rk("c_maskadd")], writes=[rk(spn)],
                     tag=n + f"SM{qt}.{kc}")
            p.op("act", lambda e, kc=kc, sps=sps, sm128=sm128, qp=qp:
                 e.activation(
                    B["PT"][:, qp * T + kc * 128: qp * T + (kc + 1) * 128],
                    sps[:, 0:128],
                    AF.Exp, bias=B["negc1"][:, 0:1], scale=sm128),
                 reads=[rk(spn), rk("negc1")], writes=[rk("PT", (qp, kc))],
                 tag=n + f"E{qt}.{kc}")
        pvb, pvbn = ps[2 + qp], psn[2 + qp]
        for kc in range(qt + 1):
            p.op("pe", lambda e, qt=qt, kc=kc, qp=qp, pvb=pvb: e.matmul(
                    pvb[:, 0:129],
                    B["PT"][:, qp * T + kc * 128: qp * T + (kc + 1) * 128],
                    B["v_ext"][:, kc * 129:(kc + 1) * 129],
                    start=(kc == 0), stop=(kc == qt)),
                 reads=[rk("PT", (qp, kc)), rk("v_ext")], writes=[rk(pvbn)],
                 inc=(1 if kc == qt else 0), tag=n + f"PV{qt}.{kc}")
        p.op("dve", lambda e, qp=qp, pvb=pvb: e.reciprocal(
                B["recipc"][:, qp:qp + 1], pvb[:, 128:129]),
             reads=[rk(pvbn)], writes=[rk("recipc", qp)], tag=n + f"rc{qt}")
        p.op("dve", lambda e, qt=qt, qp=qp: e.tensor_scalar(
                B["rwm"][:, qp * 8:(qp + 1) * 8], B["rw"][:, qt * 8:(qt + 1) * 8],
                B["recipc"][:, qp:qp + 1], None, ALU.mult),
             reads=[rk("rw", qt), rk("recipc", qp)], writes=[rk("rwm", qp)],
             tag=n + f"rwm{qt}")
        for e_ in range(8):
            ep = e_ % 2
            tb = (pst6 if ep == 0 else sc["pst"])
            tbk = psn[6] if ep == 0 else "ps7"
            p.op("dve", lambda e, e_=e_, qp=qp, ep=ep, pvb=pvb: e.tensor_scalar(
                    B["qbf"][:, ep * 128:(ep + 1) * 128], pvb[:, 0:128],
                    B["rwm"][:, qp * 8 + e_:qp * 8 + e_ + 1], None, ALU.mult),
                 reads=[rk(pvbn), rk("rwm", qp)], writes=[rk("qbf", ep)],
                 tag=n + f"aw{qt}.{e_}")
            p.op("pe", lambda e, ep=ep, tb=tb: e.transpose(
                    tb(0, 128), B["qbf"][:, ep * 128:(ep + 1) * 128],
                    com.ident_bf[:]),
                 reads=[rk("qbf", ep), rk("c_identbf")], writes=[rk(tbk)],
                 tag=n + f"awt{qt}.{e_}")
            p.op("act", lambda e, qt=qt, e_=e_, tb=tb: e.activation(
                    n2T[:, e_ * T + qt * 128: e_ * T + qt * 128 + 128],
                    tb(0, 128), AF.Copy, bias=0.0, scale=1.0),
                 reads=[rk(tbk)], writes=[rk("bigbf_b", (e_, qt))],
                 tag=n + f"awe{qt}.{e_}")

    phase_barrier(p, ["bigbf_a"])
    hsb = n1T
    for ti in range(NT):
        for half in range(2):
            bank, bnm = ps[3 + half], psn[3 + half]
            for ch in range(8):
                p.op("pe", lambda e, ti=ti, half=half, ch=ch, bank=bank: e.matmul(
                        bank[:, 0:512],
                        n2T[:, ch * T + ti * 128: ch * T + ti * 128 + 128],
                        B["wbig2"][:, ch * 1024 + half * 512:
                                   ch * 1024 + half * 512 + 512],
                        start=(ch == 0), stop=(ch == 7)),
                     reads=[rk("bigbf_b", ALL), rk("wbig2")],
                     writes=[rk(bnm)], inc=(1 if ch == 7 else 0),
                     tag=n + f"O{ti}.{half}")
            xsl = B["x"][:, ti * 1024 + half * 512: ti * 1024 + half * 512 + 512]
            p.op("dve", lambda e, ti=ti, xsl=xsl: e.tensor_scalar(
                    B["yscr"][:], xsl, B["ystats"][:, 2 * ti:2 * ti + 1],
                    B["ystats"][:, 2 * ti + 1:2 * ti + 2], ALU.mult, ALU.add),
                 reads=[rk("x_res", ti), rk("ystats", ti)], writes=[rk("yscr")],
                 tag=n + f"yr{ti}.{half}")
            p.op("dve", lambda e, ti=ti, half=half, bank=bank: e.tensor_tensor(
                    hsb[:, ti * 1024 + half * 512: ti * 1024 + half * 512 + 512],
                    B["yscr"][:], bank[:, 0:512], ALU.add),
                 reads=[rk(bnm), rk("yscr")], writes=[rk("bigbf_a", ti)],
                 tag=n + f"hs{ti}.{half}")
            p.op("dve", lambda e, bank=bank, xsl=xsl: e.scalar_tensor_tensor(
                    xsl, bank[:, 0:512], 0.5, xsl, ALU.mult, ALU.add),
                 reads=[rk(bnm), rk("x_res", ti), rk("yscr")],
                 writes=[rk("x_res", ti)], tag=n + f"xa{ti}.{half}")
            p.op("dve", lambda e, xsl=xsl: e.scalar_tensor_tensor(
                    xsl, B["yscr"][:], 0.5, xsl, ALU.mult, ALU.add),
                 reads=[rk("yscr"), rk("x_res", ti)], writes=[rk("x_res", ti)],
                 tag=n + f"xy{ti}.{half}")

    phase_barrier(p, ["bigbf_b"])
    emit_ln(p, com, geo, hsb, "bigbf_a", n2T, "bigbf_b",
            affine=(B["ln2w"], B["ln2b"], "ln2wF", "ln2bF"), scratch=sc)
    emit_gating(p, com, geo, B, n2T, "bigbf_b", B["simn2"], "simn2",
                B["sigb2"], "sigb2", B["rw2"], "rw2", 0.5, name=n + "g2")

    phase_barrier(p, ["bigbf_a"])
    He = n1T
    for ex in range(8):
        wslot = ex % 2
        wb1 = B["wbig"] if wslot == 0 else B["wbig2"]
        wk1 = "wbig" if wslot == 0 else "wbig2"
        wb2 = B["wbig2"] if wslot == 0 else B["wbig"]
        wk2 = "wbig2" if wslot == 0 else "wbig"
        for ic in range(8):
            p.dma("pool", wb1[:, ic * 1024:(ic + 1) * 1024],
                  wd["w1"][ex][:, ic * 1024:(ic + 1) * 1024],
                  writes=[rk(wk1, ic)], tag=n + f"lw1.{ex}.{ic}")
        for ic in range(8):
            p.dma("pool", wb2[:, ic * 1024:(ic + 1) * 1024],
                  wd["w2"][ex][:, ic * 1024:(ic + 1) * 1024],
                  writes=[rk(wk2, ic)], tag=n + f"lw2.{ex}.{ic}")
        w1banks = (0, 1, 6, 7)
        for ic in range(8):
            for w in range(T // 512):
                bi = w1banks[(ic * (T // 512) + w) % 4]
                bank, bnm = ps[bi], psn[bi]
                for ch in range(8):
                    p.op("pe", lambda e, ic=ic, w=w, ch=ch, bank=bank, wb1=wb1:
                         e.matmul(
                            bank[:, 0:512],
                            wb1[:, ic * 1024 + ch * 128: ic * 1024 + ch * 128 + 128],
                            n2T[:, ch * T + w * 512: ch * T + w * 512 + 512],
                            start=(ch == 0), stop=(ch == 7)),
                         reads=[rk(wk1, ic), rk("bigbf_b", ALL)],
                         writes=[rk(bnm)], inc=(1 if ch == 7 else 0),
                         tag=n + f"H{ex}.{ic}.{w}")
                p.op("act", lambda e, ic=ic, w=w, bank=bank: e.activation(
                        He[:, ic * T + w * 512: ic * T + w * 512 + 512],
                        bank[:, 0:512], AF.Gelu, bias=com.zero_col[:, 0:1],
                        scale=1.0),
                     reads=[rk(bnm), rk("c_zero")],
                     writes=[rk("bigbf_a", (ic, w))], tag=n + f"G{ex}.{ic}.{w}")
        for ti in range(NT):
            for half in range(2):
                bi = 2 + (2 * ti + half) % 4
                bank, bnm = ps[bi], psn[bi]
                for ic in range(8):
                    p.op("pe", lambda e, ti=ti, half=half, ic=ic, bank=bank,
                         wb2=wb2: e.matmul(
                            bank[:, 0:512],
                            He[:, ic * T + ti * 128: ic * T + ti * 128 + 128],
                            wb2[:, ic * 1024 + half * 512:
                                ic * 1024 + half * 512 + 512],
                            start=(ic == 0), stop=(ic == 7)),
                         reads=[rk("bigbf_a", ALL), rk(wk2, ic)],
                         writes=[rk(bnm)], inc=(1 if ic == 7 else 0),
                         tag=n + f"D{ex}.{ti}.{half}")
                xsl = B["x"][:, ti * 1024 + half * 512:
                             ti * 1024 + half * 512 + 512]
                p.op("dve", lambda e, ti=ti, ex=ex, bank=bank, xsl=xsl:
                     e.scalar_tensor_tensor(
                        xsl, bank[:, 0:512],
                        B["rw2"][:, ti * 8 + ex: ti * 8 + ex + 1], xsl,
                        ALU.mult, ALU.add),
                     reads=[rk(bnm), rk("rw2", ti), rk("x_res", ti)],
                     writes=[rk("x_res", ti)], tag=n + f"xm{ex}.{ti}.{half}")


def l0ffn_buffers(p: Prog, com: Common, geo, x_buf, bigbf_a, bigbf_b,
                  ps8, ps8_name, lnsc, gat, pre=""):
    import math
    _sbuf0 = p.sbuf
    p_sbuf = lambda nm, sh, dt: _sbuf0(pre + nm, sh, dt)
    NT, T = geo["NT"], geo["T"]
    B = {
        "x": x_buf, "bigbf_a": bigbf_a, "bigbf_b": bigbf_b,
        "ystats": p_sbuf("ystats", [128, 2 * NT], F32),
        "yscr": p_sbuf("yscr", [128, 512], F32),
        "ps8": ps8, "ps8_name": ps8_name, "lnsc": lnsc, "gat": gat,
        "v_ext": p_sbuf("v_ext", [128, NT * 130], BF16),
        "qT": p_sbuf("qT", [128, T], BF16),
        "kT": p_sbuf("kT", [128, T], BF16),
        "PT": p_sbuf("PT", [128, 2 * T], BF16),
        "sq2w": p_sbuf("sq2w", [128, 512], BF16),
        "qacc": p_sbuf("qacc", [128, 256], F32),
        "qbf": p_sbuf("qbf", [128, 256], BF16),
        "rw": p_sbuf("rw", [128, NT * 8], F32),
        "rw2": p_sbuf("rw2", [128, NT * 8], F32),
        "rwm": p_sbuf("rwm", [128, 16], F32),
        "recipc": p_sbuf("recipc", [128, 2], F32),
        "wbig": p_sbuf("wbig", [128, 8192], BF16),
        "wbig2": p_sbuf("wbig2", [128, 8192], BF16),
        "simn1": p_sbuf("simn1", [128, 64], BF16),
        "simn2": p_sbuf("simn2", [128, 64], BF16),
        "sigb1": p_sbuf("sigb1", [128, 8], F32),
        "sigb2": p_sbuf("sigb2", [128, 8], F32),
        "sim_f": p_sbuf("sim_f", [128, 64], F32),
        "sim_sq": p_sbuf("sim_sq", [128, 64], F32),
        "srn": p_sbuf("srn", [1, 8], F32),
        "srnb": p_sbuf("srnb", [128, 8], F32),
        "grow": p_sbuf("grow", [1, 8], F32),
        "ones_col_f": p_sbuf("ones_col_f", [128, 1], F32),
        "gm1": p_sbuf("gm1", [1, 2], F32),
        "n21": p_sbuf("n21", [1, 1], F32),
        "negc1": p_sbuf("negc1", [128, 1], F32),
        "lnsm128": p_sbuf("lnsm128", [128, 1], F32),
        "ln1w": p_sbuf("ln1wF", [128, 8], F32),
        "ln1b": p_sbuf("ln1bF", [128, 8], F32),
        "ln2w": p_sbuf("ln2wF", [128, 8], F32),
        "ln2b": p_sbuf("ln2bF", [128, 8], F32),
    }
    p.op("dve", lambda e: e.memset(B["ones_col_f"][:], 1.0),
         writes=[rk("ones_col_f")])
    p.op("dve", lambda e: e.memset(B["lnsm128"][:], math.log(1.0 / math.sqrt(128.0))),
         writes=[rk("lnsm128")])
    return B


def l0ffn_load_weights(p: Prog, B, wd):
    p.dma("sp", B["ln1w"][:], wd["ln1wF"], writes=[rk("ln1wF")])
    p.dma("sp", B["ln1b"][:], wd["ln1bF"], writes=[rk("ln1bF")])
    p.dma("sp", B["ln2w"][:], wd["ln2wF"], writes=[rk("ln2wF")])
    p.dma("sp", B["ln2b"][:], wd["ln2bF"], writes=[rk("ln2bF")])



from concourse.bass_utils import run_bass_kernel_spmd

GEO = {"NT": 16, "T": 2048}
NTK, TTOK = GEO["NT"], GEO["T"]


def _base_prog():
    nc = bass.Bass("TRN2", target_bir_lowering=False)
    x_d = nc.dram_tensor("x", [128, NTK * 1024], F32, kind="ExternalInput").ap()
    out_d = nc.dram_tensor("out", [128, NTK * 1024], F32,
                           kind="ExternalOutput").ap()
    p = Prog(nc)
    com = Common(p)
    ps, psn, ps01, ps23, hs = alloc_psum_pool(p)
    lnsc = ln_scratch(p, hs[7], hs[6])
    x_sb = p.sbuf("x_res", [128, NTK * 1024], F32)
    for ti in range(NTK):
        p.dma("sp", x_sb[:, ti * 1024:(ti + 1) * 1024],
              x_d[:, ti * 1024:(ti + 1) * 1024], writes=[rk("x_res", ti)])
    return nc, p, com, ps, psn, lnsc, x_sb, out_d, ps01, ps23


def _finish(nc, p, x_sb, out_d):
    for ti in range(NTK):
        p.dma("sp", out_d[:, ti * 1024:(ti + 1) * 1024],
              x_sb[:, ti * 1024:(ti + 1) * 1024], reads=[rk("x_res", ti)])
    p.build()
    return nc


def build_l0ffn():
    nc, p, com, ps, psn, lnsc, x_sb, out_d, ps01, ps23 = _base_prog()
    wd = {}
    for wn in ("wq0", "wk0", "wv0", "o"):
        wd[wn] = nc.dram_tensor(wn, [128, 8192], F32, kind="ExternalInput").ap()
    wd["w1"] = nc.dram_tensor("w1", [8, 128, 8192], F32, kind="ExternalInput").ap()
    wd["w2"] = nc.dram_tensor("w2", [8, 128, 8192], F32, kind="ExternalInput").ap()
    wd["sim1"] = nc.dram_tensor("sim1", [128, 64], F32, kind="ExternalInput").ap()
    wd["sim2"] = nc.dram_tensor("sim2", [128, 64], F32, kind="ExternalInput").ap()
    wd["g1"] = nc.dram_tensor("g1", [1, 8], F32, kind="ExternalInput").ap()
    wd["g2"] = nc.dram_tensor("g2", [1, 8], F32, kind="ExternalInput").ap()
    for wn in ("ln1wF", "ln1bF", "ln2wF", "ln2bF"):
        wd[wn] = nc.dram_tensor(wn + "_d", [128, 8], F32,
                                kind="ExternalInput").ap()
    gat = gating_scratch(p, GEO)
    bigbf_a = p.sbuf("bigbf_a", [128, 8 * TTOK], BF16)
    bigbf_b = p.sbuf("bigbf_b", [128, 8 * TTOK], BF16)
    B = l0ffn_buffers(p, com, GEO, x_sb, bigbf_a, bigbf_b, ps, psn,
                      lnsc, gat)
    l0ffn_load_weights(p, B, wd)
    prep_sim(p, com, B, wd["sim1"], B["simn1"], "simn1", name="s1")
    prep_sim(p, com, B, wd["sim2"], B["simn2"], "simn2", name="s2")
    prep_sig(p, com, B, wd["g1"], B["sigb1"], "sigb1", name="G1")
    prep_sig(p, com, B, wd["g2"], B["sigb2"], "sigb2", name="G2")
    emit_l0_ffn(p, com, GEO, B, wd, name="F")
    return _finish(nc, p, x_sb, out_d)


def build_mha(biased):
    nc, p, com, ps, psn, lnsc, x_sb, out_d, ps01, ps23 = _base_prog()
    wd = {}
    for wn in ("wq", "wk", "wv", "wo"):
        wd[wn] = nc.dram_tensor(wn, [128, 8192], F32, kind="ExternalInput").ap()
    if biased:
        wd["bq"] = nc.dram_tensor("bq", [128, 8], F32, kind="ExternalInput").ap()
        wd["bk"] = nc.dram_tensor("bk", [128, 8], F32, kind="ExternalInput").ap()
        wd["bv"] = nc.dram_tensor("bv", [1, 1024], F32, kind="ExternalInput").ap()
    yT = p.sbuf("bigbf_a", [128, 8 * TTOK], BF16)
    zT = p.sbuf("bigbf_b", [128, 8 * TTOK], BF16)
    B = mha_buffers(p, GEO, x_sb, yT, zT, ps, psn, ps[6], psn[6],
                    lnsc, biased, 0.125 if biased else 1.0,
                    ps01=ps01, ps23=ps23)
    emit_mha(p, com, GEO, B, wd, biased=biased, name="A")
    return _finish(nc, p, x_sb, out_d)


def build_l1ffn():
    nc, p, com, ps, psn, lnsc, x_sb, out_d, ps01, ps23 = _base_prog()
    wd = {
        "W1T": nc.dram_tensor("W1T", [128, 32 * 1024], F32,
                              kind="ExternalInput").ap(),
        "W2T": nc.dram_tensor("W2T", [128, 32 * 1024], F32,
                              kind="ExternalInput").ap(),
        "b1F": nc.dram_tensor("b1F_d", [128, 32], F32, kind="ExternalInput").ap(),
        "b2": nc.dram_tensor("b2_d", [1, 1024], F32, kind="ExternalInput").ap(),
        "scale": nc.dram_tensor("scale_d", [1, 1], F32, kind="ExternalInput").ap(),
    }
    yT = p.sbuf("bigbf_a", [128, 8 * TTOK], BF16)
    H = p.sbuf("bigbf_b", [128, 16384], BF16)
    B = l1ffn_buffers(p, GEO, x_sb, yT, H, ps, psn, lnsc)
    emit_l1_ffn(p, com, GEO, B, wd, name="F")
    return _finish(nc, p, x_sb, out_d)



def _hcm(w):
    X = w.shape[1]
    return np.ascontiguousarray(
        w.reshape(8, 128, X).transpose(1, 0, 2).reshape(128, 8 * X))


def _pair_major(w_cf):
    return np.ascontiguousarray(
        w_cf.reshape(8, 128, 8, 128).transpose(1, 2, 0, 3).reshape(128, 8192))


def _tok_major(x2d):
    return np.ascontiguousarray(
        x2d.reshape(NTK, 128, 1024).transpose(1, 0, 2).reshape(128, NTK * 1024))


def _prep_inputs(I):
    f32 = np.float32
    W = {}
    W["l0f"] = {
        "wq0": _hcm(np.ascontiguousarray(
            I["L0_ffn_q_proj"].transpose(1, 0, 2).reshape(1024, 1024))),
        "wk0": _hcm(np.ascontiguousarray(
            I["L0_ffn_k_proj"].transpose(1, 0, 2).reshape(1024, 1024))),
        "wv0": _hcm(np.ascontiguousarray(
            I["L0_ffn_v_proj"].transpose(1, 0, 2).reshape(1024, 1024))),
        "o": _hcm(np.ascontiguousarray(I["L0_ffn_o_proj"].reshape(1024, 1024))),
        "w1": np.stack([_hcm(I["L0_ffn_w1"][e]).reshape(128, 8, 8, 128)
                        .transpose(0, 2, 1, 3).reshape(128, 8192)
                        for e in range(8)]),
        "w2": np.stack([_hcm(I["L0_ffn_w2"][e]) for e in range(8)]),
        "sim1": _hcm(np.ascontiguousarray(I["L0_ffn_smha_sim"])),
        "sim2": _hcm(np.ascontiguousarray(I["L0_ffn_moe_sim"])),
        "g1": I["L0_ffn_smha_gates"].reshape(1, 8).astype(f32),
        "g2": I["L0_ffn_moe_gates"].reshape(1, 8).astype(f32),
        "ln1wF_d": np.ascontiguousarray(I["L0_ffn_ln1_w"].reshape(8, 128).T),
        "ln1bF_d": np.ascontiguousarray(I["L0_ffn_ln1_b"].reshape(8, 128).T),
        "ln2wF_d": np.ascontiguousarray(I["L0_ffn_ln2_w"].reshape(8, 128).T),
        "ln2bF_d": np.ascontiguousarray(I["L0_ffn_ln2_b"].reshape(8, 128).T),
    }
    W["l0a"] = {
        "wq": _pair_major(np.ascontiguousarray(
            I["L0_attn_Wq"].transpose(2, 0, 1).reshape(1024, 1024))),
        "wk": _pair_major(np.ascontiguousarray(
            I["L0_attn_Wk"].transpose(2, 0, 1).reshape(1024, 1024))),
        "wv": _pair_major(np.ascontiguousarray(
            I["L0_attn_Wv"].transpose(2, 0, 1).reshape(1024, 1024))),
        "wo": _hcm(np.ascontiguousarray(I["L0_attn_proj"])),
        "bq": np.ascontiguousarray(I["L0_attn_bq"].reshape(-1).reshape(8, 128).T),
        "bk": np.ascontiguousarray(I["L0_attn_bk"].reshape(-1).reshape(8, 128).T),
        "bv": I["L0_attn_bv"].reshape(1, 1024).astype(f32),
    }
    W["l1a"] = {
        "wq": _pair_major(np.ascontiguousarray(I["L1_attn_wq"].T)),
        "wk": _pair_major(np.ascontiguousarray(I["L1_attn_wk"].T)),
        "wv": _pair_major(np.ascontiguousarray(I["L1_attn_wv"].T)),
        "wo": _hcm(np.ascontiguousarray(I["L1_attn_wo"].T)),
    }
    W["l1f"] = {
        "W1T": np.ascontiguousarray(
            I["L1_ffn_W1"].T.reshape(8, 128, 32, 128).transpose(1, 2, 0, 3)
            .reshape(128, 32768)),
        "W2T": np.ascontiguousarray(
            I["L1_ffn_W2"].T.reshape(32, 128, 1024).transpose(1, 0, 2)
            .reshape(128, 32768)),
        "b1F_d": np.ascontiguousarray(I["L1_ffn_b1"].reshape(32, 128).T),
        "b2_d": I["L1_ffn_b2"].reshape(1, 1024).astype(f32),
        "scale_d": np.asarray(I["L1_ffn_scale"], f32).reshape(1, 1),
    }
    for d in W.values():
        for k in d:
            d[k] = np.ascontiguousarray(np.asarray(d[k], f32))
    return W


_PROG_CACHE = {}


def _get_progs():
    if not _PROG_CACHE:
        _PROG_CACHE["l0f"] = build_l0ffn()
        _PROG_CACHE["l0a"] = build_mha(True)
        _PROG_CACHE["l1a"] = build_mha(False)
        _PROG_CACHE["l1f"] = build_l1ffn()
    return _PROG_CACHE



def build_fused():
    nc = bass.Bass("TRN2", target_bir_lowering=False)
    x_d = nc.dram_tensor("x", [128, NTK * 1024], F32, kind="ExternalInput").ap()
    out_d = nc.dram_tensor("out", [128, NTK * 1024], F32,
                           kind="ExternalOutput").ap()
    wd_l0f = {}
    for wn in ("wq0", "wk0", "wv0", "o"):
        wd_l0f[wn] = nc.dram_tensor("l0f_" + wn, [128, 8192], F32,
                                    kind="ExternalInput").ap()
    wd_l0f["w1"] = nc.dram_tensor("l0f_w1", [8, 128, 8192], F32,
                                  kind="ExternalInput").ap()
    wd_l0f["w2"] = nc.dram_tensor("l0f_w2", [8, 128, 8192], F32,
                                  kind="ExternalInput").ap()
    wd_l0f["sim1"] = nc.dram_tensor("l0f_sim1", [128, 64], F32,
                                    kind="ExternalInput").ap()
    wd_l0f["sim2"] = nc.dram_tensor("l0f_sim2", [128, 64], F32,
                                    kind="ExternalInput").ap()
    wd_l0f["g1"] = nc.dram_tensor("l0f_g1", [1, 8], F32,
                                  kind="ExternalInput").ap()
    wd_l0f["g2"] = nc.dram_tensor("l0f_g2", [1, 8], F32,
                                  kind="ExternalInput").ap()
    for wn in ("ln1wF", "ln1bF", "ln2wF", "ln2bF"):
        wd_l0f[wn] = nc.dram_tensor("l0f_" + wn, [128, 8], F32,
                                    kind="ExternalInput").ap()
    wd_l0a = {}
    for wn in ("wq", "wk", "wv", "wo"):
        wd_l0a[wn] = nc.dram_tensor("l0a_" + wn, [128, 8192], F32,
                                    kind="ExternalInput").ap()
    wd_l0a["bq"] = nc.dram_tensor("l0a_bq", [128, 8], F32,
                                  kind="ExternalInput").ap()
    wd_l0a["bk"] = nc.dram_tensor("l0a_bk", [128, 8], F32,
                                  kind="ExternalInput").ap()
    wd_l0a["bv"] = nc.dram_tensor("l0a_bv", [1, 1024], F32,
                                  kind="ExternalInput").ap()
    wd_l1a = {}
    for wn in ("wq", "wk", "wv", "wo"):
        wd_l1a[wn] = nc.dram_tensor("l1a_" + wn, [128, 8192], F32,
                                    kind="ExternalInput").ap()
    wd_l1f = {
        "W1T": nc.dram_tensor("l1f_W1T", [128, 32768], F32,
                              kind="ExternalInput").ap(),
        "W2T": nc.dram_tensor("l1f_W2T", [128, 32768], F32,
                              kind="ExternalInput").ap(),
        "b1F": nc.dram_tensor("l1f_b1F", [128, 32], F32,
                              kind="ExternalInput").ap(),
        "b2": nc.dram_tensor("l1f_b2", [1, 1024], F32,
                             kind="ExternalInput").ap(),
        "scale": nc.dram_tensor("l1f_scale", [1, 1], F32,
                                kind="ExternalInput").ap(),
    }

    p = Prog(nc)
    com = Common(p)
    ps, psn, ps01, ps23, hs = alloc_psum_pool(p)
    lnsc = ln_scratch(p, hs[7], hs[6])
    x_sb = p.sbuf("x_res", [128, NTK * 1024], F32)
    bigbf_a = p.sbuf("bigbf_a", [128, 8 * TTOK], BF16)
    bigbf_b = p.sbuf("bigbf_b", [128, max(8 * TTOK, 16384)], BF16)
    p.sbuf("barsc", [1, 1], F32)

    for ti in range(NTK):
        p.dma("sp", x_sb[:, ti * 1024:(ti + 1) * 1024],
              x_d[:, ti * 1024:(ti + 1) * 1024], writes=[rk("x_res", ti)])

    def phase_l0ffn(pre):
        p.push_scope()
        gat = gating_scratch(p, GEO, pre=pre)
        B = l0ffn_buffers(p, com, GEO, x_sb, bigbf_a, bigbf_b, ps, psn,
                          lnsc, gat, pre=pre)
        l0ffn_load_weights(p, B, wd_l0f)
        prep_sim(p, com, B, wd_l0f["sim1"], B["simn1"], "simn1", name=pre + "s1")
        prep_sim(p, com, B, wd_l0f["sim2"], B["simn2"], "simn2", name=pre + "s2")
        prep_sig(p, com, B, wd_l0f["g1"], B["sigb1"], "sigb1", name=pre + "G1")
        prep_sig(p, com, B, wd_l0f["g2"], B["sigb2"], "sigb2", name=pre + "G2")
        emit_l0_ffn(p, com, GEO, B, wd_l0f, name=pre)
        p.pop_scope()
        p.fence()

    def phase_mha(pre, wd, biased, hook=None):
        p.push_scope()
        B = mha_buffers(p, GEO, x_sb, bigbf_a, bigbf_b, ps, psn,
                        ps[6], psn[6], lnsc, biased,
                        0.125 if biased else 1.0, pre=pre,
                        ps01=ps01, ps23=ps23)
        emit_mha(p, com, GEO, B, wd, biased=biased, name=pre,
                 per_hp_hook=hook)
        p.pop_scope()
        p.fence()

    w1bf = nc.dram_tensor("l1f_w1bf", [128, 32768], BF16,
                          kind="Internal").ap()
    w2bf = nc.dram_tensor("l1f_w2bf", [128, 32768], BF16,
                          kind="Internal").ap()
    l1f_wbf = [(w1bf, w2bf)]

    def prepass_hook(hp):
        src, dst, qh, key = ((wd_l1f["W1T"], w1bf, hp, "w1bf_d") if hp < 4
                             else (wd_l1f["W2T"], w2bf, hp - 4, "w2bf_d"))
        p.dma("pool", dst[:, qh * 8192:(qh + 1) * 8192],
              src[:, qh * 8192:(qh + 1) * 8192],
              writes=[rk(key, qh)], tag=f"Bpc.{hp}")

    def phase_l1ffn(pre):
        p.push_scope()
        B = l1ffn_buffers(p, GEO, x_sb, bigbf_a, bigbf_b, ps, psn, lnsc,
                          pre=pre)
        emit_l1_ffn(p, com, GEO, B, wd_l1f, name=pre, wbf=l1f_wbf[0],
                    do_prepass=False)
        p.pop_scope()
        p.fence()

    phase_l0ffn("A_")
    phase_mha("B_", wd_l0a, True, hook=prepass_hook)
    phase_l0ffn("C_")
    phase_l1ffn("D_")
    phase_mha("E_", wd_l1a, False)
    phase_l1ffn("F_")

    for ti in range(NTK):
        p.dma("sp", out_d[:, ti * 1024:(ti + 1) * 1024],
              x_sb[:, ti * 1024:(ti + 1) * 1024], reads=[rk("x_res", ti)])
    p.build()
    return nc


def _fused_inputs(W, x):
    m = {"x": x}
    for k, v in W["l0f"].items():
        key = {"ln1wF_d": "ln1wF", "ln1bF_d": "ln1bF",
               "ln2wF_d": "ln2wF", "ln2bF_d": "ln2bF"}.get(k, k)
        m["l0f_" + key] = v
    for k, v in W["l0a"].items():
        m["l0a_" + k] = v
    for k, v in W["l1a"].items():
        m["l1a_" + k] = v
    for k, v in W["l1f"].items():
        key = {"b1F_d": "b1F", "b2_d": "b2", "scale_d": "scale"}.get(k, k)
        m["l1f_" + key] = v
    return m


FUSED = True


def kernel(**inputs):
    I = {k: np.asarray(v) for k, v in inputs.items()}
    W = _prep_inputs(I)
    xs = [_tok_major(np.asarray(I["L0_attn_x"][b], np.float32))
          for b in range(8)]
    if FUSED:
        if "fused" not in _PROG_CACHE:
            _PROG_CACHE["fused"] = build_fused()
        nc = _PROG_CACHE["fused"]
        in_maps = [_fused_inputs(W, xs[b]) for b in range(8)]
        res = run_bass_kernel_spmd(nc, in_maps, core_ids=list(range(8)))
        xs = [np.asarray(res.results[b]["out"], np.float32) for b in range(8)]
    else:
        progs = _get_progs()
        phases = [("l0f", "l0f"), ("l0a", "l0a"), ("l0f", "l0f"),
                  ("l1f", "l1f"), ("l1a", "l1a"), ("l1f", "l1f")]
        for prog_key, wkey in phases:
            in_maps = [dict(W[wkey], x=xs[b]) for b in range(8)]
            res = run_bass_kernel_spmd(progs[prog_key], in_maps,
                                       core_ids=list(range(8)))
            xs = [np.asarray(res.results[b]["out"], np.float32)
                  for b in range(8)]
    out = np.stack([xs[b].reshape(128, NTK, 1024).transpose(1, 0, 2)
                    .reshape(TTOK, 1024) for b in range(8)])
    return out.astype(np.float32)


# version 10 (final)
# speedup vs baseline: 1.2903x; 1.0005x over previous
import contextlib
from collections import defaultdict

import numpy as np
import concourse.bass as bass
import concourse.mybir as mybir

F32 = mybir.dt.float32
BF16 = mybir.dt.bfloat16
I32 = mybir.dt.int32
AF = mybir.ActivationFunctionType
ALU = mybir.AluOpType
AX = mybir.AxisListType

ALL = "__all__"

ENG_SECTION = {"pe": "tensor", "dve": "vector", "act": "scalar",
               "pool": "gpsimd", "sp": "sync"}


def rk(name, sub=ALL):
    return (name, sub)


class Prog:

    def __init__(self, nc, n_sp_dma_sems=6, n_pool_dma_sems=4):
        self.nc = nc
        self.stack = contextlib.ExitStack()
        self.ops = []
        self.sems = {}
        self._sem_order = []
        for e in ("pe", "dve", "act", "pool"):
            self._mk_sem(e)
        self.dma_sems = {"sp": [], "pool": []}
        for i in range(n_sp_dma_sems):
            self.dma_sems["sp"].append(self._mk_sem(f"dsp{i}"))
        for i in range(n_pool_dma_sems):
            self.dma_sems["pool"].append(self._mk_sem(f"dpl{i}"))
        self._dma_rr = {"sp": 0, "pool": 0}
        self.cum = defaultdict(int)
        self.last_writer = {}
        self.readers = defaultdict(list)
        self.name_keys = defaultdict(set)
        self.clock = defaultdict(dict)
        self.snap = {}
        self.eng_inc_ops = defaultdict(list)
        self.buffers = {}
        self.scopes = []
        self.pending_fence = {}
        nums = [self.sems[n].num for n in self._sem_order]
        assert nums == list(range(nums[0], nums[0] + len(nums))), nums
        self._sem_lo, self._sem_hi = nums[0], nums[-1]

    def _mk_sem(self, name):
        h = self.stack.enter_context(self.nc.semaphore(name))
        self.sems[name] = h
        self._sem_order.append(name)
        return name

    def sbuf(self, name, shape, dtype):
        st = self.scopes[-1] if self.scopes else self.stack
        h = st.enter_context(self.nc.sbuf_tensor(name, list(shape), dtype))
        self.buffers[name] = h
        return h

    def push_scope(self):
        self.scopes.append(contextlib.ExitStack())

    def pop_scope(self):
        self.scopes.pop().close()

    def fence(self):
        for s in self._sem_order:
            if self.cum[s] > 0:
                self.pending_fence[s] = self.cum[s]

    def psum(self, name, shape, dtype=F32):
        h = self.stack.enter_context(self.nc.psum_tensor(name, list(shape), dtype))
        self.buffers[name] = h
        return h

    def _conflict_keys(self, key):
        name, sub = key
        subs = self.name_keys.get(name)
        if not subs:
            return ()
        if sub == ALL:
            return [(name, s) for s in subs]
        out = []
        if sub in subs:
            out.append((name, sub))
        if ALL in subs:
            out.append((name, ALL))
        return out

    def op(self, engine, emit, reads=(), writes=(), sem=None, inc=1, tag=""):
        idx = len(self.ops)
        sem = sem or engine
        deps = set()
        for r in reads:
            for ck in self._conflict_keys(r):
                if ck in self.last_writer:
                    deps.add(self.last_writer[ck])
        for w in writes:
            for ck in self._conflict_keys(w):
                if ck in self.last_writer:
                    deps.add(self.last_writer[ck])
                for rd in self.readers.get(ck, ()):
                    deps.add(rd)
        deps.discard(idx)

        clk = self.clock[engine]
        need = {}
        for d in sorted(deps):
            dop = self.ops[d]
            if dop["engine"] == "pe" and engine == "pe":
                continue
            if dop["inc"] > 0:
                s, v = dop["sem"], dop["cum"]
            else:
                cover = None
                for (ci, cs, cv) in self.eng_inc_ops[dop["engine"]]:
                    if ci > d:
                        cover = (ci, cs, cv)
                        break
                assert cover is not None and cover[0] < idx, (
                    f"op {d} ({dop['tag']}) has inc=0 and no covering inc-op "
                    f"before consumer {idx} ({tag})")
                s, v = cover[1], cover[2]
            if clk.get(s, 0) >= v:
                continue
            if need.get(s, 0) < v:
                need[s] = v

        for s, v in self.pending_fence.items():
            if clk.get(s, 0) < v and need.get(s, 0) < v:
                need[s] = v

        if inc == 16:
            prev = self.cum[sem]
            if prev > 0 and clk.get(sem, 0) < prev and need.get(sem, 0) < prev:
                need[sem] = prev

        waits = sorted(need.items())
        for s, v in waits:
            if clk.get(s, 0) < v:
                clk[s] = v
        for d in deps:
            snap = self.snap.get(d)
            if snap:
                for s, v in snap.items():
                    if clk.get(s, 0) < v:
                        clk[s] = v

        self.cum[sem] += inc
        o = dict(idx=idx, engine=engine, emit=emit, sem=sem, inc=inc,
                 cum=self.cum[sem], waits=waits, tag=tag)
        self.ops.append(o)
        if inc > 0:
            self.eng_inc_ops[engine].append((idx, sem, self.cum[sem]))
        sn = dict(clk)
        sn[sem] = max(sn.get(sem, 0), self.cum[sem])
        self.snap[idx] = sn
        for r in reads:
            self.name_keys[r[0]].add(r[1])
            self.readers[r].append(idx)
        for w in writes:
            self.name_keys[w[0]].add(w[1])
            self.last_writer[w] = idx
            self.readers[w] = []
        return idx

    def dma(self, queue, out, in_, reads=(), writes=(), tag="", **kw):
        sems = self.dma_sems[queue]
        sem = sems[self._dma_rr[queue] % len(sems)]
        self._dma_rr[queue] += 1

        def emit(eng, out=out, in_=in_, kw=kw):
            return eng.dma_start(out=out, in_=in_, **kw)
        return self.op(queue, emit, reads=reads, writes=writes, sem=sem,
                       inc=16, tag=tag)

    def build(self):
        per_engine = defaultdict(list)
        for o in self.ops:
            per_engine[o["engine"]].append(o)
        sems = self.sems
        blk = self.nc.Block()
        block = blk.__enter__()
        for ename in ("sp", "pool", "pe", "dve", "act"):
            ops = per_engine.get(ename, [])
            section = getattr(block, ENG_SECTION[ename])

            def body(eng, _ops=ops):
                for o in _ops:
                    for s, v in o["waits"]:
                        eng.wait_ge(sems[s], v)
                    inst = o["emit"](eng)
                    if o["inc"] and inst is not None:
                        inst.then_inc(sems[o["sem"]], o["inc"])
            section(body)
        finals = [(s, self.cum[s]) for s in self._sem_order if self.cum[s] > 0]

        def tail(eng, finals=finals):
            for s, v in finals:
                if s != "pool":
                    eng.wait_ge(sems[s], v)
        block.gpsimd(tail)
        blk.__exit__(None, None, None)
        self.nc.gpsimd.sem_clear(range(self._sem_lo, self._sem_hi + 1))

    def close(self):
        self.stack.close()



class Common:

    def __init__(self, p: Prog):
        nc = p.nc
        self.p = p
        self.ident_bf = p.sbuf("c_identbf", [128, 128], BF16)
        self.mask_keepge = p.sbuf("c_maskge", [128, 128], BF16)
        self.maskadd = p.sbuf("c_maskadd", [128, 128], BF16)
        self.iota = p.sbuf("c_iota", [128, 128], I32)
        self.ones_col_bf = p.sbuf("c_ones_col", [128, 1], BF16)
        self.ones_row_f = p.sbuf("c_ones_row", [1, 128], F32)
        self.ones_row_bf = p.sbuf("c_ones_row_bf", [1, 128], BF16)
        self.eps_col = p.sbuf("c_eps", [128, 1], F32)
        self.zero_col = p.sbuf("c_zero", [128, 1], F32)
        self.tiny = p.sbuf("c_tiny", [128, 1], F32)

        p.op("pool", lambda e: e.iota(self.iota[:], pattern=[[-1, 128]], base=0,
                                      channel_multiplier=1),
             writes=[rk("c_iota")], tag="iota")
        p.op("dve", lambda e: e.tensor_scalar(self.ident_bf[:], self.iota[:],
                                              0.0, None, ALU.is_equal),
             reads=[rk("c_iota")], writes=[rk("c_identbf")], tag="ident")
        p.op("dve", lambda e: e.tensor_scalar(self.mask_keepge[:], self.iota[:],
                                              0.0, None, ALU.is_le),
             reads=[rk("c_iota")], writes=[rk("c_maskge")], tag="maskge")
        p.op("dve", lambda e: e.tensor_scalar(self.maskadd[:], self.iota[:],
                                              0.0, -1e30, ALU.is_gt, ALU.mult),
             reads=[rk("c_iota")], writes=[rk("c_maskadd")], tag="maskadd")
        p.op("dve", lambda e: e.memset(self.ones_col_bf[:], 1.0),
             writes=[rk("c_ones_col")])
        p.op("dve", lambda e: e.memset(self.ones_row_f[:], 1.0),
             writes=[rk("c_ones_row")])
        p.op("dve", lambda e: e.memset(self.ones_row_bf[:], 1.0),
             writes=[rk("c_ones_row_bf")])
        p.op("dve", lambda e: e.memset(self.eps_col[:], 1e-5),
             writes=[rk("c_eps")])
        p.op("dve", lambda e: e.memset(self.zero_col[:], 0.0),
             writes=[rk("c_zero")])
        p.op("dve", lambda e: e.memset(self.tiny[:], 1e-30),
             writes=[rk("c_tiny")])



def emit_ln(p: Prog, com: Common, geo, src_buf, src_name, dst_buf, dst_name,
            *, affine=None, fuse_double=False, out_tm=None, out_tm_name=None,
            scratch=None, tilesel=None, keep_tile0=None, out_stats=None):
    NT, T = geo["NT"], geo["T"]
    sc = scratch
    tiles = list(range(NT) if tilesel is None else tilesel)
    mv = sc["mv_all"]

    for ti in tiles:
        par = ti % 2
        bn = sc[par]["bn"]
        P_ = str(par)
        xsl = src_buf[:, ti * 1024: ti * 1024 + 1024]
        rx = rk(src_name, ti)
        p.op("dve", lambda e, a=xsl, bn=bn: e.bn_stats(bn[:, 0:6], a[:, 0:512]),
             reads=[rx], writes=[rk("s_bn" + P_)], tag=f"bns{ti}")
        p.op("dve", lambda e, a=xsl, bn=bn: e.bn_stats(bn[:, 6:12], a[:, 512:1024]),
             reads=[rx], writes=[rk("s_bn2" + P_)], tag=f"bns2{ti}")
        p.op("dve", lambda e, ti=ti, bn=bn: e.bn_aggr(
                mv[:, 2 * ti:2 * ti + 2], bn[:]),
             reads=[rk("s_bn" + P_), rk("s_bn2" + P_)], writes=[rk("s_mv", ti)],
             tag=f"bna{ti}")

    mvk = [rk("s_mv", ti) for ti in tiles]
    mvv = mv[:, 0:2 * NT].rearrange("p (t two) -> p two t", two=2)
    var_cols = mvv[:, 1:2, :].squeeze(1)
    m_cols = mvv[:, 0:1, :].squeeze(1)
    r0 = sc["r0_all"]
    rho = sc["rho_all"]
    nm = sc["nm_all"]

    p.op("act", lambda e: e.activation(r0[:, 0:NT], var_cols, AF.Ln,
                                       bias=com.eps_col[:, 0:1], scale=1.0),
         reads=mvk + [rk("c_eps")], writes=[rk("s_r0a")], tag="lnv")
    p.op("act", lambda e: e.activation(r0[:, 0:NT], r0[:, 0:NT], AF.Exp,
                                       bias=com.zero_col[:, 0:1], scale=-0.5),
         reads=[rk("s_r0a"), rk("c_zero")], writes=[rk("s_r0a")], tag="exv")
    if fuse_double:
        p.op("dve", lambda e: e.tensor_tensor(rho[:, 0:NT], r0[:, 0:NT],
                                              r0[:, 0:NT], ALU.mult),
             reads=[rk("s_r0a")], writes=[rk("s_rhoa")], tag="fd1")
        p.op("dve", lambda e: e.tensor_tensor(rho[:, 0:NT], rho[:, 0:NT],
                                              var_cols, ALU.mult),
             reads=[rk("s_rhoa")] + mvk, writes=[rk("s_rhoa")], tag="fd2")
        p.op("act", lambda e: e.activation(rho[:, 0:NT], rho[:, 0:NT], AF.Ln,
                                           bias=com.eps_col[:, 0:1], scale=1.0),
             reads=[rk("s_rhoa"), rk("c_eps")], writes=[rk("s_rhoa")], tag="fd3")
        p.op("act", lambda e: e.activation(rho[:, 0:NT], rho[:, 0:NT], AF.Exp,
                                           bias=com.zero_col[:, 0:1], scale=-0.5),
             reads=[rk("s_rhoa"), rk("c_zero")], writes=[rk("s_rhoa")], tag="fd4")
        p.op("dve", lambda e: e.tensor_tensor(rho[:, 0:NT], rho[:, 0:NT],
                                              r0[:, 0:NT], ALU.mult),
             reads=[rk("s_rhoa"), rk("s_r0a")], writes=[rk("s_rhoa")], tag="fd5")
        rhok = "s_rhoa"
    else:
        rho = r0
        rhok = "s_r0a"
    p.op("dve", lambda e, rho=rho: e.tensor_tensor(nm[:, 0:NT], m_cols,
                                                   rho[:, 0:NT], ALU.mult),
         reads=mvk + [rk(rhok)], writes=[rk("s_nma")], tag="nm1")
    p.op("dve", lambda e: e.tensor_scalar(nm[:, 0:NT], nm[:, 0:NT], -1.0, None,
                                          ALU.mult),
         reads=[rk("s_nma")], writes=[rk("s_nma")], tag="nm2")
    if out_stats is not None:
        stb, stk = out_stats
        st2 = stb[:].rearrange("p (t two) -> p two t", two=2)
        p.op("dve", lambda e, st2=st2: e.tensor_copy(
                st2[:, 0:1, 0:NT].squeeze(1), r0[:, 0:NT]),
             reads=[rk("s_r0a")], writes=[rk(stk, "r")], tag="str")
        p.op("dve", lambda e, st2=st2: e.tensor_tensor(
                st2[:, 1:2, 0:NT].squeeze(1), m_cols, r0[:, 0:NT], ALU.mult),
             reads=mvk + [rk("s_r0a")], writes=[rk(stk, "n")], tag="stn1")
        p.op("dve", lambda e, st2=st2: e.tensor_scalar(
                st2[:, 1:2, 0:NT].squeeze(1), st2[:, 1:2, 0:NT].squeeze(1),
                -1.0, None, ALU.mult),
             reads=[rk(stk, "n")], writes=[rk(stk, "n")], tag="stn2")

    if out_tm is not None:
        nm0 = sc["nm0_all"]
        p.op("dve", lambda e: e.tensor_tensor(nm0[:, 0:NT], m_cols,
                                              r0[:, 0:NT], ALU.mult),
             reads=mvk + [rk("s_r0a")], writes=[rk("s_nm0a")], tag="ynm1")
        p.op("dve", lambda e: e.tensor_scalar(nm0[:, 0:NT], nm0[:, 0:NT],
                                              -1.0, None, ALU.mult),
             reads=[rk("s_nm0a")], writes=[rk("s_nm0a")], tag="ynm2")

    ps6h = sc["ps6"]
    assert ps6h is not None
    pst6 = lambda a, b: ps6h[:, a // 2: b // 2].bitcast(BF16)

    def rec_evac(ti, tb, tbk):
        if affine is not None:
            wc, bc, wname, bname = affine
            for j in range(8):
                if j in (0, 4):
                    p.op("dve", lambda e, j=j, ti=ti, tb=tb: e.tensor_scalar(
                            dst_buf[:, j * T + ti * 128: j * T + ti * 128 + 128],
                            tb(j * 128, (j + 1) * 128),
                            wc[:, j:j + 1], bc[:, j:j + 1], ALU.mult, ALU.add),
                         reads=[rk(tbk), rk(wname), rk(bname)],
                         writes=[rk(dst_name, ti)], tag=f"aff{ti}.{j}")
                else:
                    p.op("act", lambda e, j=j, ti=ti, tb=tb: e.activation(
                            dst_buf[:, j * T + ti * 128: j * T + ti * 128 + 128],
                            tb(j * 128, (j + 1) * 128), AF.Identity,
                            bias=bc[:, j:j + 1], scale=wc[:, j:j + 1]),
                         reads=[rk(tbk), rk(wname), rk(bname)],
                         writes=[rk(dst_name, ti)], tag=f"affa{ti}.{j}")
        else:
            dst3 = dst_buf[:].rearrange("p (c t) -> p c t", c=8)
            p.op("act", lambda e, ti=ti, d3=dst3, tb=tb: e.activation(
                    d3[:, :, ti * 128: ti * 128 + 128],
                    tb(0, 1024).rearrange("p (c t) -> p c t", c=8),
                    AF.Copy, bias=0.0, scale=1.0),
                 reads=[rk(tbk)],
                 writes=[rk(dst_name, ti)], tag=f"ev{ti}")

    pend = None
    for ti in tiles:
        par = ti % 2
        P_ = str(par)
        xsl = src_buf[:, ti * 1024: ti * 1024 + 1024]
        rx = rk(src_name, ti)
        if out_tm is not None:
            ysl = out_tm[:, ti * 1024: ti * 1024 + 1024]
            p.op("dve", lambda e, a=xsl, y=ysl, ti=ti, sc=sc: e.tensor_scalar(
                    y, a, r0[:, ti:ti + 1], sc["nm0_all"][:, ti:ti + 1],
                    ALU.mult, ALU.add),
                 reads=[rx, rk("s_r0a"), rk("s_nm0a")],
                 writes=[rk(out_tm_name, ti)], tag=f"y{ti}")
        if keep_tile0 is not None and ti == 0:
            nbf_buf, nbf_key = keep_tile0
        else:
            nbf_buf, nbf_key = sc[par]["nbf"], "s_nbf" + P_
        tb = sc["pst"] if par == 1 else pst6
        tbk = "ps7" if par == 1 else "ps6"
        for hh in range(2):
            if hh == 0:
                p.op("dve", lambda e, a=xsl, ti=ti, nb=nbf_buf, rho=rho:
                     e.tensor_scalar(
                        nb[:, 0:512], a[:, 0:512],
                        rho[:, ti:ti + 1], nm[:, ti:ti + 1], ALU.mult, ALU.add),
                     reads=[rx, rk(rhok), rk("s_nma")],
                     writes=[rk(nbf_key, 0)], tag=f"nbf{ti}.0")
            else:
                p.op("act", lambda e, a=xsl, ti=ti, nb=nbf_buf, rho=rho:
                     e.activation(
                        nb[:, 512:1024], a[:, 512:1024], AF.Identity,
                        bias=nm[:, ti:ti + 1], scale=rho[:, ti:ti + 1]),
                     reads=[rx, rk(rhok), rk("s_nma")],
                     writes=[rk(nbf_key, 1)], tag=f"nbf{ti}.1")
            for j in range(4 * hh, 4 * hh + 4):
                p.op("pe", lambda e, j=j, nb=nbf_buf, tb=tb: e.transpose(
                        tb(j * 128, (j + 1) * 128),
                        nb[:, j * 128:(j + 1) * 128], com.ident_bf[:]),
                     reads=[rk(nbf_key, hh), rk("c_identbf")],
                     writes=[rk(tbk)], inc=(1 if j == 4 * hh + 3 else 0),
                     tag=f"tr{ti}.{j}")
        if pend is not None:
            rec_evac(*pend)
        pend = (ti, tb, tbk)
    rec_evac(*pend)


def ln_scratch(p: Prog, ps7=None, ps6=None):
    sc = {}
    for par in (0, 1):
        sc[par] = {
            "bn": p.sbuf(f"s_bn{par}", [128, 12], F32),
            "nbf": p.sbuf(f"s_nbf{par}", [128, 1024], BF16),
        }
    sc["mv_all"] = p.sbuf("s_mv_all", [128, 32], F32)
    sc["r0_all"] = p.sbuf("s_r0_all", [128, 16], F32)
    sc["rho_all"] = p.sbuf("s_rho_all", [128, 16], F32)
    sc["nm_all"] = p.sbuf("s_nm_all", [128, 16], F32)
    sc["nm0_all"] = p.sbuf("s_nm0_all", [128, 16], F32)
    assert ps7 is not None
    sc["ps7"] = ps7
    sc["ps6"] = ps6
    sc["pst"] = lambda a, b: ps7[:, a // 2: b // 2].bitcast(BF16)
    return sc



def emit_mha(p: Prog, com: Common, geo, B, wd, *, biased, name="",
             per_hp_hook=None):
    NT, T = geo["NT"], geo["T"]
    NW = T // 512
    sc = B["lnsc"]
    n = name
    psb = B["psb"]
    psbn = B["psb_name"]
    psm, psmn = B["ps_misc"], B["ps_misc_name"]
    vx4 = B["v_ext"][:].rearrange("p (t h d) -> p t h d", h=2, d=65)

    emit_ln(p, com, geo, B["x"], "x_res", B["yT"], B["yT_name"], scratch=sc)

    if biased:
        p.dma("sp", B["bv_row"][:], wd["bv"], writes=[rk("bv_row")], tag=n + "bv")
        p.dma("sp", B["bqF"][:], wd["bq"], writes=[rk("bqF")])
        p.dma("sp", B["bkF"][:], wd["bk"], writes=[rk("bkF")])
        for h2 in range(2):
            p.op("pe", lambda e, h2=h2: e.matmul(
                    psm[:, 0:512], com.ones_row_f[0:1, :],
                    B["bv_row"][0:1, h2 * 512:(h2 + 1) * 512],
                    start=True, stop=True),
                 reads=[rk("bv_row"), rk("c_ones_row")], writes=[rk(psmn)],
                 tag=n + f"bvb{h2}")
            p.op("act", lambda e, h2=h2: e.activation(
                    B["bvb"][:, h2 * 512:(h2 + 1) * 512], psm[:, 0:512],
                    AF.Copy, bias=0.0, scale=1.0),
                 reads=[rk(psmn)], writes=[rk("bvb")], tag=n + f"bvbe{h2}")

    p.op("dve", lambda e: e.memset(vx4[:, :, :, 64:65], 1.0),
         writes=[rk("v_ext")], tag=n + "vones")

    for hp in range(8):
        if per_hp_hook is not None:
            per_hp_hook(hp)
        for i, wn in enumerate(("wq", "wk", "wv")):
            p.dma("pool", B["wpair"][:, i * 1024:(i + 1) * 1024],
                  wd[wn][:, hp * 1024:(hp + 1) * 1024],
                  writes=[rk("wpair", wn)], tag=n + f"ld{wn}{hp}")

        for i, wn in enumerate(("wq", "wk", "wv")):
            for w in range(NW):
                bank, bname = ((psb[6], psbn[6]) if (i * NW + w) % 2 == 0
                               else (psb[4], psbn[4]))
                ytk = [rk(B["yT_name"], t) for t in range(4 * w, 4 * w + 4)]
                for ch in range(8):
                    p.op("pe", lambda e, i=i, w=w, ch=ch, bank=bank: e.matmul(
                            bank[:, 0:512],
                            B["wpair"][:, i * 1024 + ch * 128:
                                       i * 1024 + (ch + 1) * 128],
                            B["yT"][:, ch * T + w * 512: ch * T + w * 512 + 512],
                            start=(ch == 0), stop=(ch == 7)),
                         reads=[rk("wpair", wn)] + ytk,
                         writes=[rk(bname)], inc=(1 if ch == 7 else 0),
                         tag=n + f"prj{wn}{hp}.{w}.{ch}")
                if wn == "wv":
                    p.op("act", lambda e, w=w, bank=bank: e.activation(
                            B["vT"][:, w * 512:(w + 1) * 512], bank[:, 0:512],
                            AF.Copy, bias=0.0, scale=1.0),
                         reads=[rk(bname)], writes=[rk("vT", w)],
                         tag=n + f"vTev{hp}.{w}")
                else:
                    dstb = B["qT"] if wn == "wq" else B["kT"]
                    dname = "qT" if wn == "wq" else "kT"
                    if biased:
                        bcol = B["bqF"] if wn == "wq" else B["bkF"]
                        bnm = "bqF" if wn == "wq" else "bkF"
                        p.op("dve", lambda e, w=w, bank=bank, bcol=bcol,
                             dstb=dstb: e.tensor_scalar(
                                dstb[:, w * 512:(w + 1) * 512], bank[:, 0:512],
                                bcol[:, hp:hp + 1], None, ALU.add),
                             reads=[rk(bname), rk(bnm)],
                             writes=[rk(dname, w)], tag=n + f"{dname}ev{hp}.{w}")
                    else:
                        p.op("act", lambda e, w=w, bank=bank, dstb=dstb:
                             e.activation(dstb[:, w * 512:(w + 1) * 512],
                                          bank[:, 0:512], AF.Copy,
                                          bias=0.0, scale=1.0),
                             reads=[rk(bname)], writes=[rk(dname, w)],
                             tag=n + f"{dname}ev{hp}.{w}")

        pst6v = lambda a, b: sc["ps6"][:, a // 2: b // 2].bitcast(BF16)
        for ti in range(NT):
            tbv = sc["pst"] if ti % 2 else pst6v
            tbvk = "ps7" if ti % 2 else "ps6"
            p.op("pe", lambda e, ti=ti, tbv=tbv: e.transpose(
                    tbv(0, 128), B["vT"][:, ti * 128:(ti + 1) * 128],
                    com.ident_bf[:]),
                 reads=[rk("vT", ti // (NT // NW)), rk("c_identbf")],
                 writes=[rk(tbvk)], tag=n + f"vtr{hp}.{ti}")
            dst = vx4[:, ti:ti + 1, :, 0:64]
            src = tbv(0, 128).rearrange("p (h d) -> p h d", h=2
                                        ).unsqueeze(1)
            if biased:
                bsl = B["bvb"][:, hp * 128:(hp + 1) * 128].rearrange(
                    "p (h d) -> p h d", h=2).unsqueeze(1)
                p.op("dve", lambda e, dst=dst, src=src, bsl=bsl: e.tensor_tensor(
                        dst, src, bsl, ALU.add),
                     reads=[rk(tbvk), rk("bvb")], writes=[rk("v_ext")],
                     tag=n + f"vev{hp}.{ti}")
            else:
                p.op("dve", lambda e, dst=dst, src=src: e.tensor_copy(dst, src),
                     reads=[rk(tbvk)], writes=[rk("v_ext")],
                     tag=n + f"vev{hp}.{ti}")

        p.op("dve", lambda e: e.memset(B["gmax"][:], 0.0), writes=[rk("gmax")],
             tag=n + f"gz{hp}")
        sqv = B["vT"][:, 0:T]
        for ci, (which, buf, wname) in enumerate(
                (("q", B["qT"], "qT"), ("k", B["kT"], "kT"))):
            for w2 in range(NW):
                p.op("dve", lambda e, buf=buf, sqv=sqv, w2=w2: e.tensor_tensor(
                        sqv[:, w2 * 512:(w2 + 1) * 512],
                        buf[:, w2 * 512:(w2 + 1) * 512],
                        buf[:, w2 * 512:(w2 + 1) * 512], ALU.mult),
                     reads=[rk(wname, w2)],
                     writes=[rk("vT", w2)], tag=n + f"sq{which}{hp}.{w2}")
            for w in range(NW):
                nb_, nbk_ = (psm, psmn) if w % 2 == 0 else (psb[5], psbn[5])
                p.op("pe", lambda e, w=w, sqv=sqv, nb_=nb_: e.matmul(
                        nb_[0:2, 0:512], B["blk2"][:],
                        sqv[:, w * 512:(w + 1) * 512], start=True, stop=True),
                     reads=[rk("vT", w), rk("blk2")], writes=[rk(nbk_)],
                     tag=n + f"nrm{which}{hp}.{w}")
                p.op("dve", lambda e, nb_=nb_: e.tensor_reduce(
                        B["n2"][:], nb_[0:2, 0:512], AX.X, ALU.max),
                     reads=[rk(nbk_)], writes=[rk("n2")],
                     tag=n + f"nrd{which}{hp}.{w}")
                p.op("dve", lambda e, ci=ci: e.tensor_tensor(
                        B["gmax"][:, ci:ci + 1], B["gmax"][:, ci:ci + 1],
                        B["n2"][:], ALU.max),
                     reads=[rk("n2"), rk("gmax")], writes=[rk("gmax")],
                     tag=n + f"nmx{which}{hp}.{w}")
        p.op("dve", lambda e: e.tensor_tensor(
                B["cb"][0:2, 0:1], B["gmax"][0:2, 0:1], B["gmax"][0:2, 1:2],
                ALU.mult),
             reads=[rk("gmax")], writes=[rk("cb")], tag=n + f"cbm{hp}")
        p.op("act", lambda e: e.activation(B["cb"][0:2, 0:1], B["cb"][0:2, 0:1],
                                           AF.Ln, bias=B["tiny"][0:2, 0:1],
                                           scale=1.0),
             reads=[rk("cb"), rk("tiny")], writes=[rk("cb")], tag=n + f"cbl{hp}")
        p.op("act", lambda e: e.activation(B["cb"][0:2, 0:1], B["cb"][0:2, 0:1],
                                           AF.Exp, bias=B["lnsm"][0:2, 0:1],
                                           scale=0.5),
             reads=[rk("cb"), rk("lnsm")], writes=[rk("cb")], tag=n + f"cbx{hp}")
        p.op("dve", lambda e: e.tensor_scalar(B["cb"][0:2, 0:1], B["cb"][0:2, 0:1],
                                              -1.0, None, ALU.mult),
             reads=[rk("cb")], writes=[rk("cb")], tag=n + f"cbn{hp}")
        p.dma("pool", B["cb2"][0:1, 0:1], B["cb"][1:2, 0:1], reads=[rk("cb")],
              writes=[rk("cb2")], tag=n + f"cbmv{hp}")
        p.op("dve", lambda e: e.tensor_tensor(
                B["cb"][0:1, 0:1], B["cb"][0:1, 0:1], B["cb2"][0:1, 0:1],
                ALU.min),
             reads=[rk("cb"), rk("cb2")], writes=[rk("cb")], tag=n + f"cbmn{hp}")
        p.op("pe", lambda e: e.matmul(
                psm[:, 0:1], com.ones_row_f[0:1, :], B["cb"][0:1, 0:1],
                start=True, stop=True),
             reads=[rk("cb"), rk("c_ones_row")], writes=[rk(psmn)],
             tag=n + f"cbb{hp}")
        p.op("dve", lambda e: e.tensor_copy(B["negc"][:, 0:1], psm[:, 0:1]),
             reads=[rk(psmn)], writes=[rk("negc")], tag=n + f"cbe{hp}")

        ps01, ps23 = B["ps01"], B["ps23"]

        def record_norm_lr(w):
            for sub in range(2):
                h = 2 * hp + sub
                pvs, pvsn = (psb[4], psbn[4]) if sub == 0 else (psb[5], psbn[5])
                p.op("act", lambda e, pvs=pvs, sub=sub: e.activation(
                        B["srow"][0:1, sub * 512:sub * 512 + 512],
                        pvs[64:65, 0:512], AF.Ln,
                        bias=B["tiny"][0:1, 0:1], scale=1.0),
                     reads=[rk(pvsn), rk("tiny")], writes=[rk("srow", sub)],
                     tag=n + f"L{h}.{w}")
                p.op("act", lambda e, sub=sub: e.activation(
                        B["rrow_bf"][0:1, sub * 512:sub * 512 + 512],
                        B["srow"][0:1, sub * 512:sub * 512 + 512], AF.Exp,
                        bias=B["lnpsc"][0:1, 0:1], scale=-1.0),
                     reads=[rk("srow", sub), rk("lnpsc")],
                     writes=[rk("rrow_bf", sub)], tag=n + f"R{h}.{w}")

        def record_norm_fin(w):
            for sub in range(2):
                h = 2 * hp + sub
                base = sub * 64
                pvs, pvsn = (psb[4], psbn[4]) if sub == 0 else (psb[5], psbn[5])
                p.op("pe", lambda e, sub=sub: e.matmul(
                        psm[0:64, 0:512], com.ones_row_bf[0:1, 0:64],
                        B["rrow_bf"][0:1, sub * 512:sub * 512 + 512],
                        start=True, stop=True),
                     reads=[rk("rrow_bf", sub), rk("c_ones_row_bf")],
                     writes=[rk(psmn)], tag=n + f"RB{h}.{w}")
                p.op("dve", lambda e, sub=sub: e.tensor_copy(
                        B["recb"][0:64, sub * 512:sub * 512 + 512],
                        psm[0:64, 0:512]),
                     reads=[rk(psmn)], writes=[rk("recb", sub)],
                     tag=n + f"RE{h}.{w}")
                p.op("dve", lambda e, pvs=pvs, base=base, hp=hp, w=w, sub=sub,
                     pvsn=pvsn: e.tensor_tensor(
                        B["zT"][base:base + 64,
                                hp * T + w * 512: hp * T + w * 512 + 512],
                        pvs[0:64, 0:512],
                        B["recb"][0:64, sub * 512:sub * 512 + 512], ALU.mult),
                     reads=[rk(pvsn), rk("recb", sub)],
                     writes=[rk(B["zT_name"], (hp, w))], tag=n + f"Z{h}.{w}")

        for w in range(NW):
            kmax = (w * 512 + 511) // 128
            for kc in range(kmax + 1):
                off = max(0, kc * 128 - w * 512)
                nq = 512 - off
                par = kc % 2
                spair = ps01 if par == 0 else ps23
                sk = (psbn[0], psbn[1]) if par == 0 else (psbn[2], psbn[3])
                diag = off > 0 or (kc + 1) * 128 > w * 512
                for sub in range(2):
                    p.op("pe", lambda e, w=w, kc=kc, off=off, nq=nq, sub=sub,
                         spair=spair, diag=diag: e.matmul(
                            spair[:, sub * 512: sub * 512 + nq],
                            B["kT"][sub * 64:sub * 64 + 64,
                                    kc * 128:(kc + 1) * 128],
                            B["qT"][sub * 64:sub * 64 + 64,
                                    w * 512 + off: w * 512 + 512],
                            start=True, stop=not diag),
                         reads=[rk("kT", ALL), rk("qT", ALL)],
                         writes=[rk(sk[sub])],
                         inc=(0 if diag else 1),
                         tag=n + f"S{2*hp+sub}.{w}.{kc}")
                    if diag:
                        p.op("pe", lambda e, sub=sub, spair=spair: e.matmul(
                                spair[:, sub * 512: sub * 512 + 128],
                                com.ident_bf[:], com.maskadd[:, 0:128],
                                start=False, stop=True),
                             reads=[rk("c_identbf"), rk("c_maskadd")],
                             writes=[rk(sk[sub])], tag=n + f"SM{2*hp+sub}.{w}.{kc}")
                if off == 0:
                    p.op("act", lambda e, kc=kc, spair=spair: e.activation(
                            B["PT"][:, kc * 1024: kc * 1024 + 1024],
                            spair[:, 0:1024],
                            AF.Exp, bias=B["negc"][:, 0:1], scale=0.125),
                         reads=[rk(sk[0]), rk(sk[1]), rk("negc")],
                         writes=[rk("PT", kc)],
                         tag=n + f"E{hp}.{w}.{kc}")
                else:
                    for sub in range(2):
                        p.op("act", lambda e, kc=kc, nq=nq, sub=sub,
                             spair=spair: e.activation(
                                B["PT"][:, kc * 1024 + sub * 512:
                                        kc * 1024 + sub * 512 + nq],
                                spair[:, sub * 512: sub * 512 + nq],
                                AF.Exp, bias=B["negc"][:, 0:1], scale=0.125),
                             reads=[rk(sk[sub]), rk("negc")],
                             writes=[rk("PT", kc)],
                             tag=n + f"E{2*hp+sub}.{w}.{kc}")
            if w > 0:
                record_norm_fin(w - 1)
            for kc in range(kmax + 1):
                off = max(0, kc * 128 - w * 512)
                nq = 512 - off
                for sub in range(2):
                    h = 2 * hp + sub
                    pvs, pvsn = psb[4], psbn[4]
                    if sub == 1:
                        pvs, pvsn = psb[5], psbn[5]
                    vsl = vx4[:, kc, sub, :]
                    p.op("pe", lambda e, kc=kc, off=off, nq=nq, vsl=vsl,
                         pvs=pvs, kmax=kmax, sub=sub: e.matmul(
                            pvs[0:65, off:off + nq], vsl,
                            B["PT"][:, kc * 1024 + sub * 512:
                                    kc * 1024 + sub * 512 + nq],
                            start=(kc == 0), stop=(kc == kmax)),
                         reads=[rk("v_ext"), rk("PT", kc)],
                         writes=[rk(pvsn)], inc=(1 if kc == kmax else 0),
                         tag=n + f"PV{h}.{w}.{kc}")
            record_norm_lr(w)
        record_norm_fin(NW - 1)

    wo_sb = B["PT"][:, 0:8192]
    p.dma("pool", wo_sb, wd["wo"], writes=[rk("PT", ALL)], tag=n + "ldwo")
    for ti in range(NT):
        for cf in range(2):
            bi = (2 * ti + cf) % 4
            bank, bname = psb[bi], psbn[bi]
            for ch in range(8):
                p.op("pe", lambda e, ti=ti, cf=cf, ch=ch, bank=bank: e.matmul(
                        bank[:, 0:512],
                        B["zT"][:, ch * T + ti * 128: ch * T + ti * 128 + 128],
                        wo_sb[:, ch * 1024 + cf * 512:
                              ch * 1024 + cf * 512 + 512],
                        start=(ch == 0), stop=(ch == 7)),
                     reads=[rk(B["zT_name"], ALL), rk("PT", ALL)],
                     writes=[rk(bname)], inc=(1 if ch == 7 else 0),
                     tag=n + f"O{ti}.{cf}")
            p.op("dve", lambda e, ti=ti, cf=cf, bank=bank: e.tensor_tensor(
                    B["x"][:, ti * 1024 + cf * 512: ti * 1024 + cf * 512 + 512],
                    B["x"][:, ti * 1024 + cf * 512: ti * 1024 + cf * 512 + 512],
                    bank[:, 0:512], ALU.add),
                 reads=[rk(bname), rk("x_res", ti)], writes=[rk("x_res", ti)],
                 tag=n + f"Oev{ti}.{cf}")


def mha_buffers(p: Prog, geo, x_buf, yT_buf, zT_buf, psb, psb_name,
                ps_misc, ps_misc_name, lnsc, biased, post_scale, pre="",
                ps01=None, ps23=None):
    import math
    _sbuf0 = p.sbuf
    p_sbuf = lambda nm, sh, dt: _sbuf0(pre + nm, sh, dt)
    NT, T = geo["NT"], geo["T"]
    B = {
        "x": x_buf, "yT": yT_buf, "yT_name": "bigbf_a",
        "zT": zT_buf, "zT_name": "bigbf_b",
        "psb": psb, "psb_name": psb_name,
        "ps_misc": ps_misc, "ps_misc_name": ps_misc_name,
        "lnsc": lnsc, "ps01": ps01, "ps23": ps23,
        "v_ext": p_sbuf("v_ext", [128, NT * 130], BF16),
        "qT": p_sbuf("qT", [128, T], BF16),
        "kT": p_sbuf("kT", [128, T], BF16),
        "vT": p_sbuf("vT", [128, T], BF16),
        "PT": p_sbuf("PT", [128, 16 * 1024], BF16),
        "wpair": p_sbuf("wpair", [128, 3 * 1024], BF16),
        "blk2": p_sbuf("blk2", [128, 2], BF16),
        "gmax": p_sbuf("gmax", [2, 2], F32),
        "n2": p_sbuf("n2", [2, 1], F32),
        "cb": p_sbuf("cb", [2, 1], F32),
        "cb2": p_sbuf("cb2", [1, 1], F32),
        "negc": p_sbuf("negc", [128, 2], F32),
        "bvb": p_sbuf("bvb", [128, 1024], BF16),
        "srow": p_sbuf("srow", [1, 1024], F32),
        "rrow_bf": p_sbuf("rrow_bf", [1, 1024], BF16),
        "recb": p_sbuf("recb", [64, 1024], BF16),
        "tiny": p_sbuf("tiny", [128, 1], F32),
        "lnsm": p_sbuf("lnsm", [128, 1], F32),
        "lnpsc": p_sbuf("lnpsc", [128, 1], F32),
        "bqF": p_sbuf("bqF", [128, 8], F32),
        "bkF": p_sbuf("bkF", [128, 8], F32),
        "bv_row": p_sbuf("bv_row", [1, 1024], F32),
    }
    p.op("dve", lambda e: e.memset(B["tiny"][:], 1e-30), writes=[rk("tiny")])
    p.op("dve", lambda e: e.memset(B["lnsm"][:], math.log(0.125)),
         writes=[rk("lnsm")])
    p.op("dve", lambda e: e.memset(B["lnpsc"][:], math.log(post_scale)),
         writes=[rk("lnpsc")])
    p.op("dve", lambda e: e.memset(B["blk2"][:], 0.0), writes=[rk("blk2")])
    p.op("dve", lambda e: e.memset(B["blk2"][0:64, 0:1], 1.0),
         reads=[rk("blk2")], writes=[rk("blk2")])
    p.op("dve", lambda e: e.memset(B["blk2"][64:128, 1:2], 1.0),
         reads=[rk("blk2")], writes=[rk("blk2")])
    return B



def emit_l1_ffn(p: Prog, com: Common, geo, B, wd, name="", wbf=None,
                do_prepass=True):
    NT, T = geo["NT"], geo["T"]
    NWIN = T // 512
    sc = B["lnsc"]
    n = name
    ps = B["ps8"]
    psn = B["ps8_name"]
    yT = B["yT"]
    if wbf is None:
        w1bf = p.nc.dram_tensor(n + "w1bf", [128, 32768], BF16,
                                kind="Internal").ap()
        w2bf = p.nc.dram_tensor(n + "w2bf", [128, 32768], BF16,
                                kind="Internal").ap()
    else:
        w1bf, w2bf = wbf
    if do_prepass:
        for qh in range(4):
            p.dma("pool", w1bf[:, qh * 8192:(qh + 1) * 8192],
                  wd["W1T"][:, qh * 8192:(qh + 1) * 8192],
                  writes=[rk("w1bf_d", qh)], tag=n + f"pc1.{qh}")
            p.dma("pool", w2bf[:, qh * 8192:(qh + 1) * 8192],
                  wd["W2T"][:, qh * 8192:(qh + 1) * 8192],
                  writes=[rk("w2bf_d", qh)], tag=n + f"pc2.{qh}")

    emit_ln(p, com, geo, B["x"], "x_res", yT, B["yT_name"], scratch=sc,
            keep_tile0=(B["lt_tm"], "lt_tm"))

    p.dma("sp", B["scale_sb"][:], wd["scale"], writes=[rk("scale_sb")],
          tag=n + "sc")
    for j in range(4):
        p.dma("pool", B["sc4"][j:j + 1, :], B["scale_sb"][0:1, :],
              reads=[rk("scale_sb")], writes=[rk("sc4")], tag=n + f"sc4.{j}")
    p.dma("sp", B["b1F"][:], wd["b1F"], writes=[rk("b1Fc")], tag=n + "b1")
    p.op("dve", lambda e: e.memset(B["b2pad"][:], 0.0), writes=[rk("b2pad")])
    p.dma("pool", B["b2pad"][0:1, :], wd["b2"], reads=[rk("b2pad")],
          writes=[rk("b2pad")], tag=n + "b2")
    p.op("dve", lambda e: e.memset(B["onespad"][:], 0.0), writes=[rk("onespad")])
    p.op("dve", lambda e: e.memset(B["onespad"][0:1, :], 1.0),
         reads=[rk("onespad")], writes=[rk("onespad")])
    for j in range(4):
        p.dma("pool", B["se_rows"][0:1, j * 1024:(j + 1) * 1024],
              B["lt_tm"][1 + 2 * j:2 + 2 * j, :], reads=[rk("lt_tm")],
              writes=[rk("se_rows")], tag=n + f"ser{j}")
        p.dma("pool", B["e4_rows"][0:1, j * 1024:(j + 1) * 1024],
              B["lt_tm"][9 + 2 * j:10 + 2 * j, :], reads=[rk("lt_tm")],
              writes=[rk("e4_rows")], tag=n + f"e4r{j}")
    p.op("dve", lambda e: e.memset(B["se_pad"][:], 0.0), writes=[rk("se_pad")])
    p.op("dve", lambda e: e.tensor_scalar(
            B["se_pad"][0:1, :], B["se_rows"][0:1, :],
            B["scale_sb"][0:1, 0:1], None, ALU.mult),
         reads=[rk("se_rows"), rk("scale_sb")], writes=[rk("se_pad")],
         tag=n + "sescl")
    p.op("dve", lambda e: e.memset(B["E_pad"][:], 0.0), writes=[rk("E_pad")])

    def emit_epad_rows():
        for j in range(4):
            for pm in range(4):
                src = B["e4_rows"][0:1, j * 1024 + pm: (j + 1) * 1024: 4]
                p.dma("pool", B["E_pad"][pm:pm + 1, j * 256:(j + 1) * 256], src,
                      reads=[rk("e4_rows")], writes=[rk("E_pad")],
                      tag=n + f"Ep{j}.{pm}", allow_slow_non_contiguous=True)

    p.op("dve", lambda e: e.memset(B["a1pad"][:], 0.0),
         writes=[rk("a1pad", "z")] + [rk("a1pad", j) for j in range(4)],
         tag=n + "a1pz")
    p.op("dve", lambda e: e.memset(B["g_pad"][:], 0.0),
         writes=[rk("g_pad", "z")] + [rk("g_pad", j) for j in range(4)],
         tag=n + "gpz")

    for w in range(NWIN):
        ytk = [rk(B["yT_name"], 0)] + [rk(B["yT_name"], t)
                                       for t in range(4 * w, 4 * w + 4)]
        for ch in range(8):
            p.op("pe", lambda e, w=w, ch=ch: e.matmul(
                    ps[5][0:4, 0:512],
                    yT[:, ch * T + 2: ch * T + 10: 2],
                    yT[:, ch * T + w * 512: ch * T + (w + 1) * 512],
                    start=(ch == 0), stop=(ch == 7)),
                 reads=ytk, writes=[rk(psn[5])],
                 inc=(1 if ch == 7 else 0), tag=n + f"a1.{w}.{ch}")
        p.op("dve", lambda e, w=w: e.tensor_copy(
                B["a1s"][0:4, w * 512:(w + 1) * 512], ps[5][0:4, 0:512]),
             reads=[rk(psn[5])], writes=[rk("a1s")], tag=n + f"a1e{w}")

    for w in range(NWIN):
        for j in range(4):
            p.dma("pool", B["a1pad"][0:1, j * 512:(j + 1) * 512],
                  B["a1s"][j:j + 1, w * 512:(w + 1) * 512],
                  reads=[rk("a1s")], writes=[rk("a1pad", j)],
                  tag=n + f"a1p{w}.{j}")
        if w == 0:
            emit_epad_rows()
        def rec_gT(hc, w=w):
            j = hc // 8
            dch = hc % 8
            p.op("pe", lambda e, hc=hc, j=j, dch=dch: e.matmul(
                    ps[2 + j][0:4, 0:512],
                    yT[:, dch * T + 10: dch * T + 18: 2],
                    B["H"][:, hc * 512:(hc + 1) * 512],
                    start=(dch == 0), stop=(dch == 7)),
                 reads=[rk(B["yT_name"], 0), rk("H", hc)],
                 writes=[rk(psn[2 + j])], inc=(1 if dch == 7 else 0),
                 tag=n + f"gT.{w}.{hc}")

        for hc in range(32):
            j = hc // 8
            slot = hc % 2
            if hc % 4 == 0:
                grp = (hc // 4) % 2
                p.dma("sp", B["w1buf"][:, grp * 4096:(grp + 1) * 4096],
                      w1bf[:, hc * 1024:(hc + 4) * 1024],
                      reads=[rk("w1bf_d", hc // 8)],
                      writes=[rk("wbuf", grp)], tag=n + f"w1.{w}.{hc}")
            grp = (hc // 4) % 2
            wof = grp * 4096 + (hc % 4) * 1024
            hps, hpsn = ps[slot], psn[slot]
            for ch in range(8):
                p.op("pe", lambda e, hc=hc, ch=ch, wof=wof, w=w, hps=hps:
                     e.matmul(
                        hps[:, 0:512],
                        B["w1buf"][:, wof + ch * 128: wof + (ch + 1) * 128],
                        yT[:, ch * T + w * 512: ch * T + (w + 1) * 512],
                        start=(ch == 0), stop=False),
                     reads=[rk("wbuf", grp)] + [rk(B["yT_name"], t)
                                                for t in range(4 * w, 4 * w + 4)],
                     writes=[rk(hpsn)], inc=0, tag=n + f"W1.{w}.{hc}.{ch}")
            p.op("pe", lambda e, hc=hc, j=j, hps=hps: e.matmul(
                    hps[:, 0:512],
                    B["se_pad"][:, hc * 128:(hc + 1) * 128],
                    B["a1pad"][:, j * 512:(j + 1) * 512],
                    start=False, stop=True),
                 reads=[rk("se_pad"), rk("a1pad", j), rk("a1pad", "z")],
                 writes=[rk(hpsn)], tag=n + f"lora1.{w}.{hc}")
            if hc > 0:
                rec_gT(hc - 1)
            p.op("act", lambda e, hc=hc, hps=hps: e.activation(
                    B["H"][:, hc * 512:(hc + 1) * 512], hps[:, 0:512],
                    AF.Gelu, bias=B["b1F"][:, hc:hc + 1], scale=1.0),
                 reads=[rk(hpsn), rk("b1Fc")], writes=[rk("H", hc)],
                 tag=n + f"gel.{w}.{hc}")
        rec_gT(31)
        for pm in range(4):
            p.op("dve", lambda e, pm=pm: e.tensor_scalar(
                    B["gT_sb"][0:4, pm * 512:(pm + 1) * 512],
                    ps[2 + pm][0:4, 0:512],
                    B["sc4"][0:4, 0:1], None, ALU.mult),
                 reads=[rk(psn[2 + pm]), rk("sc4")], writes=[rk("gT_sb", pm)],
                 tag=n + f"gTe.{w}.{pm}")
        for j in range(4):
            for pm in range(4):
                p.dma("pool", B["g_pad"][pm:pm + 1, j * 512:(j + 1) * 512],
                      B["gT_sb"][j:j + 1, pm * 512:(pm + 1) * 512],
                      reads=[rk("gT_sb", pm)], writes=[rk("g_pad", j)],
                      tag=n + f"gp.{w}.{j}.{pm}")

        for hc in range(32):
            if hc % 4 == 0:
                grp = (hc // 4) % 2
                p.dma("sp", B["w1buf"][:, grp * 4096:(grp + 1) * 4096],
                      w2bf[:, hc * 1024:(hc + 4) * 1024],
                      reads=[rk("w2bf_d", hc // 8)],
                      writes=[rk("wbuf", grp)], tag=n + f"w2.{w}.{hc}")
            grp = (hc // 4) % 2
            wof = grp * 4096 + (hc % 4) * 1024
            for tt in range(4):
                for cf in range(2):
                    bi = tt * 2 + cf
                    p.op("pe", lambda e, hc=hc, tt=tt, cf=cf, bi=bi, wof=wof,
                         w=w: e.matmul(
                            ps[bi][:, 0:512],
                            B["H"][:, hc * 512 + tt * 128:
                                   hc * 512 + (tt + 1) * 128],
                            B["w1buf"][:, wof + cf * 512:
                                       wof + (cf + 1) * 512],
                            start=(hc == 0), stop=False),
                         reads=[rk("wbuf", grp), rk("H", hc)],
                         writes=[rk(psn[bi])],
                         inc=(1 if (tt == 3 and cf == 1) else 0),
                         tag=n + f"W2.{w}.{hc}.{tt}.{cf}")
        for tt in range(4):
            for j in range(4):
                cf = j // 2
                bi = tt * 2 + cf
                p.op("pe", lambda e, tt=tt, j=j, bi=bi, cf=cf: e.matmul(
                        ps[bi][:, (j % 2) * 256: (j % 2) * 256 + 256],
                        B["g_pad"][:, j * 512 + tt * 128:
                                   j * 512 + (tt + 1) * 128],
                        B["E_pad"][:, j * 256:(j + 1) * 256],
                        start=False, stop=False),
                     reads=[rk("g_pad", j), rk("g_pad", "z"), rk("E_pad")],
                     writes=[rk(psn[bi])], inc=0, tag=n + f"lora2.{w}.{tt}.{j}")
            for cf in range(2):
                bi = tt * 2 + cf
                p.op("pe", lambda e, tt=tt, cf=cf, bi=bi: e.matmul(
                        ps[bi][:, 0:512], B["onespad"][:, 0:128],
                        B["b2pad"][:, cf * 512:(cf + 1) * 512],
                        start=False, stop=True),
                     reads=[rk("onespad"), rk("b2pad")],
                     writes=[rk(psn[bi])], tag=n + f"b2f.{w}.{tt}.{cf}")
                ti = w * 4 + tt
                p.op("dve", lambda e, ti=ti, cf=cf, bi=bi: e.scalar_tensor_tensor(
                        B["x"][:, ti * 1024 + cf * 512:
                               ti * 1024 + cf * 512 + 512],
                        ps[bi][:, 0:512], 0.5,
                        B["x"][:, ti * 1024 + cf * 512:
                               ti * 1024 + cf * 512 + 512],
                        ALU.mult, ALU.add),
                     reads=[rk(psn[bi]), rk("x_res", ti)],
                     writes=[rk("x_res", ti)], tag=n + f"xup.{w}.{tt}.{cf}")


def l1ffn_buffers(p: Prog, geo, x_buf, yT_buf, H_buf, ps8, ps8_name, lnsc,
                  pre=""):
    T = geo["T"]
    _sbuf0 = p.sbuf
    p_sbuf = lambda nm, sh, dt: _sbuf0(pre + nm, sh, dt)
    B = {
        "x": x_buf, "yT": yT_buf, "yT_name": "bigbf_a",
        "H": H_buf, "ps8": ps8, "ps8_name": ps8_name, "lnsc": lnsc,
        "lt_tm": p_sbuf("lt_tm", [128, 1024], BF16),
        "scale_sb": p_sbuf("scale_sb", [1, 1], F32),
        "sc4": p_sbuf("sc4", [4, 1], F32),
        "b1F": p_sbuf("b1Fc", [128, 32], F32),
        "b2pad": p_sbuf("b2pad", [128, 1024], BF16),
        "onespad": p_sbuf("onespad", [128, 128], BF16),
        "se_rows": p_sbuf("se_rows", [1, 4096], BF16),
        "e4_rows": p_sbuf("e4_rows", [1, 4096], BF16),
        "se_pad": p_sbuf("se_pad", [128, 4096], BF16),
        "E_pad": p_sbuf("E_pad", [128, 1024], BF16),
        "a1s": p_sbuf("a1s", [4, T], BF16),
        "a1pad": p_sbuf("a1pad", [128, 2048], BF16),
        "gT_sb": p_sbuf("gT_sb", [4, 2048], BF16),
        "g_pad": p_sbuf("g_pad", [128, 2048], BF16),
        "w1buf": p_sbuf("wbuf", [128, 8192], BF16),
    }
    return B


def alloc_psum_pool(p: Prog):
    ps01 = p.psum("psum01", [128, 1024], F32)
    ps23 = p.psum("psum23", [128, 1024], F32)
    ps = [ps01[:, 0:512], ps01[:, 512:1024],
          ps23[:, 0:512], ps23[:, 512:1024]]
    psn = ["ps0", "ps1", "ps2", "ps3"]
    hs = {}
    for i in range(4, 8):
        h = p.psum(f"ps{i}", [128, 512], F32)
        ps.append(h)
        psn.append(f"ps{i}")
        hs[i] = h
    return ps, psn, ps01, ps23, hs


def phase_barrier(p: Prog, names):
    if "barsc" not in p.buffers:
        p.sbuf("barsc", [1, 1], F32)
    bs = p.buffers["barsc"]
    for nm in names:
        p.op("dve", lambda e: e.memset(bs[:], 0.0),
             reads=[rk(nm)], writes=[rk(nm), rk("barsc")],
             tag=f"barrier-{nm}")



def emit_gating(p: Prog, com: Common, geo, B, nT, nT_name, simn, simn_key,
                sigb, sigb_key, rw_buf, rw_key, fold, name=""):
    NT, T = geo["NT"], geo["T"]
    NWIN = T // 512
    n = name
    ps = B["ps8"]
    psn = B["ps8_name"]
    g = B["gat"]

    for w in range(NWIN):
        for ch in range(8):
            p.op("dve", lambda e, ch=ch, w=w: e.tensor_tensor(
                    B["sq2w"][:], nT[:, ch * T + w * 512: ch * T + (w + 1) * 512],
                    nT[:, ch * T + w * 512: ch * T + (w + 1) * 512], ALU.mult),
                 reads=[rk(nT_name, ALL)], writes=[rk("sq2w")],
                 tag=n + f"gsq{w}.{ch}")
            p.op("pe", lambda e, ch=ch: e.matmul(
                    ps[6][0:1, 0:512], com.ones_col_bf[:], B["sq2w"][:],
                    start=(ch == 0), stop=(ch == 7)),
                 reads=[rk("sq2w"), rk("c_ones_col")], writes=[rk(psn[6])],
                 tag=n + f"gn{w}.{ch}")
        p.op("dve", lambda e, w=w: e.tensor_copy(
                g["n2row"][0:1, w * 512:(w + 1) * 512], ps[6][0:1, 0:512]),
             reads=[rk(psn[6])], writes=[rk("g_n2row")], tag=n + f"gnev{w}")
    for ti in range(NT):
        p.op("pe", lambda e, ti=ti: e.transpose(
                ps[6][:, 0:1], g["n2row"][0:1, ti * 128:(ti + 1) * 128],
                com.ones_row_f[0:1, 0:1]),
             reads=[rk("g_n2row"), rk("c_ones_row")], writes=[rk(psn[6])],
             tag=n + f"gtr{ti}")
        p.op("dve", lambda e, ti=ti: e.tensor_copy(
                g["n2c"][:, ti:ti + 1], ps[6][:, 0:1]),
             reads=[rk(psn[6])], writes=[rk("g_n2c")], tag=n + f"gtc{ti}")
    p.op("act", lambda e: e.activation(g["n2c"][:], g["n2c"][:], AF.Ln,
                                       bias=com.tiny[:, 0:1], scale=1.0),
         reads=[rk("g_n2c"), rk("c_tiny")], writes=[rk("g_n2c")], tag=n + "gln")
    p.op("act", lambda e: e.activation(g["rnc"][:], g["n2c"][:], AF.Exp,
                                       bias=com.zero_col[:, 0:1], scale=-0.5),
         reads=[rk("g_n2c"), rk("c_zero")], writes=[rk("g_rnc")], tag=n + "gex")

    for ti in range(NT):
        for ch in range(8):
            p.op("pe", lambda e, ti=ti, ch=ch: e.matmul(
                    ps[5][:, 0:8], nT[:, ch * T + ti * 128: ch * T + ti * 128 + 128],
                    simn[:, ch * 8:(ch + 1) * 8], start=(ch == 0), stop=(ch == 7)),
                 reads=[rk(nT_name, ALL), rk(simn_key)], writes=[rk(psn[5])],
                 inc=(1 if ch == 7 else 0), tag=n + f"glg{ti}.{ch}")
        p.op("dve", lambda e, ti=ti: e.tensor_scalar(
                g["lg"][:, ti * 8:(ti + 1) * 8], ps[5][:, 0:8],
                g["rnc"][:, ti:ti + 1], None, ALU.mult),
             reads=[rk(psn[5]), rk("g_rnc")], writes=[rk("g_lg", ti)],
             tag=n + f"glge{ti}")
        p.op("dve", lambda e, ti=ti: e.tensor_tensor(
                g["lg"][:, ti * 8:(ti + 1) * 8],
                g["lg"][:, ti * 8:(ti + 1) * 8], sigb[:], ALU.subtract),
             reads=[rk("g_lg", ti), rk(sigb_key)], writes=[rk("g_lg", ti)],
             tag=n + f"glgs{ti}")

    lgk = [rk("g_lg", ti) for ti in range(NT)]
    p.op("dve", lambda e: e.tensor_scalar(g["gated"][:, 0:NT * 8],
                                          g["lg"][:, 0:NT * 8], 0.0, None,
                                          ALU.max),
         reads=lgk, writes=[rk("g_gated")], tag=n + "ggt")
    p.op("dve", lambda e: e.tensor_scalar(g["amask"][:, 0:NT * 8],
                                          g["lg"][:, 0:NT * 8], 0.0, None,
                                          ALU.is_gt),
         reads=lgk, writes=[rk("g_amask")], tag=n + "gam")
    lg3 = g["lg"][:, 0:NT * 8].rearrange("p (t e) -> p t e", e=8)
    p.op("dve", lambda e: e.tensor_reduce(
            g["rowhas"][:, 0:NT], g["amask"][:, 0:NT * 8].rearrange(
                "p (t e) -> p t e", e=8), AX.X, ALU.add),
         reads=[rk("g_amask")], writes=[rk("g_rowhas")], tag=n + "grh")
    p.op("dve", lambda e: e.tensor_scalar(g["inact"][:, 0:NT],
                                          g["rowhas"][:, 0:NT], 0.0, None,
                                          ALU.is_le),
         reads=[rk("g_rowhas")], writes=[rk("g_inact")], tag=n + "gin")
    p.op("dve", lambda e, lg3=lg3: e.tensor_reduce(
            g["m1"][:, 0:NT], lg3, AX.X, ALU.max),
         reads=lgk, writes=[rk("g_m1")], tag=n + "gm1")
    for ti in range(NT):
        p.op("dve", lambda e, ti=ti: e.tensor_scalar(
                g["eq1"][:, 0:8], g["lg"][:, ti * 8:(ti + 1) * 8],
                g["m1"][:, ti:ti + 1], -1e30, ALU.is_equal, ALU.mult),
             reads=[rk("g_lg", ti), rk("g_m1")], writes=[rk("g_eq1")],
             tag=n + f"geq{ti}")
        p.op("dve", lambda e, ti=ti: e.tensor_tensor(
                g["lmask"][:, ti * 8:(ti + 1) * 8], g["eq1"][:, 0:8],
                g["lg"][:, ti * 8:(ti + 1) * 8], ALU.add),
             reads=[rk("g_eq1"), rk("g_lg", ti)], writes=[rk("g_lmask", ti)],
             tag=n + f"glm{ti}")
    p.op("dve", lambda e: e.tensor_reduce(
            g["m2"][:, 0:NT], g["lmask"][:, 0:NT * 8].rearrange(
                "p (t e) -> p t e", e=8), AX.X, ALU.max),
         reads=[rk("g_lmask", ti) for ti in range(NT)], writes=[rk("g_m2")],
         tag=n + "gm2")
    for ti in range(NT):
        p.op("dve", lambda e, ti=ti: e.tensor_scalar(
                g["eq1"][:, 0:8], g["lg"][:, ti * 8:(ti + 1) * 8],
                g["m2"][:, ti:ti + 1], None, ALU.is_ge),
             reads=[rk("g_lg", ti), rk("g_m2")], writes=[rk("g_eq1")],
             tag=n + f"gt2{ti}")
        p.op("dve", lambda e, ti=ti: e.tensor_tensor(
                g["eq1"][:, 0:8], g["eq1"][:, 0:8],
                g["amask"][:, ti * 8:(ti + 1) * 8], ALU.subtract),
             reads=[rk("g_eq1"), rk("g_amask")], writes=[rk("g_eq1")],
             tag=n + f"gd{ti}")
        p.op("dve", lambda e, ti=ti: e.scalar_tensor_tensor(
                g["mask"][:, ti * 8:(ti + 1) * 8], g["eq1"][:, 0:8],
                g["inact"][:, ti:ti + 1],
                g["amask"][:, ti * 8:(ti + 1) * 8], ALU.mult, ALU.add),
             reads=[rk("g_eq1"), rk("g_inact"), rk("g_amask")],
             writes=[rk("g_mask", ti)], tag=n + f"gmk{ti}")
    p.op("act", lambda e: e.activation(g["expg"][:, 0:NT * 8],
                                       g["gated"][:, 0:NT * 8], AF.Exp,
                                       bias=com.zero_col[:, 0:1], scale=1.0),
         reads=[rk("g_gated"), rk("c_zero")], writes=[rk("g_expg")],
         tag=n + "gxp")
    p.op("dve", lambda e: e.tensor_tensor(
            g["rwun"][:, 0:NT * 8], g["mask"][:, 0:NT * 8],
            g["expg"][:, 0:NT * 8], ALU.mult),
         reads=[rk("g_mask", ti) for ti in range(NT)] + [rk("g_expg")],
         writes=[rk("g_rwun")], tag=n + "gru")
    p.op("dve", lambda e: e.tensor_reduce(
            g["zsum"][:, 0:NT], g["rwun"][:, 0:NT * 8].rearrange(
                "p (t e) -> p t e", e=8), AX.X, ALU.add),
         reads=[rk("g_rwun")], writes=[rk("g_zsum")], tag=n + "gz")
    p.op("dve", lambda e: e.reciprocal(g["rz"][:, 0:NT], g["zsum"][:, 0:NT]),
         reads=[rk("g_zsum")], writes=[rk("g_rz")], tag=n + "grz")
    for ti in range(NT):
        p.op("dve", lambda e, ti=ti: e.tensor_scalar(
                rw_buf[:, ti * 8:(ti + 1) * 8], g["rwun"][:, ti * 8:(ti + 1) * 8],
                g["rz"][:, ti:ti + 1], fold, ALU.mult, ALU.mult),
             reads=[rk("g_rwun"), rk("g_rz")], writes=[rk(rw_key, ti)],
             tag=n + f"grw{ti}")


def gating_scratch(p: Prog, geo, pre=""):
    NT, T = geo["NT"], geo["T"]
    names = {"n2row": [1, T], "n2c": [128, NT], "rnc": [128, NT],
             "lg": [128, NT * 8], "gated": [128, NT * 8],
             "amask": [128, NT * 8], "rowhas": [128, NT], "inact": [128, NT],
             "m1": [128, NT], "m2": [128, NT], "eq1": [128, 8],
             "lmask": [128, NT * 8], "mask": [128, NT * 8],
             "expg": [128, NT * 8], "rwun": [128, NT * 8],
             "zsum": [128, NT], "rz": [128, NT]}
    d = {k: p.sbuf(pre + "g_" + k, sh, F32) for k, sh in names.items()}
    return d



def prep_sim(p: Prog, com: Common, B, sim_d, simn_buf, simn_key, name=""):
    n = name
    ps = B["ps8"]
    psn = B["ps8_name"]
    p.dma("sp", B["sim_f"][:], sim_d, writes=[rk("sim_f")], tag=n + "ld")
    p.op("dve", lambda e: e.tensor_tensor(B["sim_sq"][:], B["sim_f"][:],
                                          B["sim_f"][:], ALU.mult),
         reads=[rk("sim_f")], writes=[rk("sim_sq")], tag=n + "sq")
    for ch in range(8):
        p.op("pe", lambda e, ch=ch: e.matmul(
                ps[6][0:1, 0:8], B["ones_col_f"][:],
                B["sim_sq"][:, ch * 8:(ch + 1) * 8],
                start=(ch == 0), stop=(ch == 7)),
             reads=[rk("sim_sq"), rk("ones_col_f")], writes=[rk(psn[6])],
             inc=(1 if ch == 7 else 0), tag=n + f"n{ch}")
    p.op("act", lambda e: e.activation(B["srn"][0:1, 0:8], ps[6][0:1, 0:8],
                                       AF.Ln, bias=com.tiny[0:1, 0:1], scale=1.0),
         reads=[rk(psn[6]), rk("c_tiny")], writes=[rk("srn")], tag=n + "ln")
    p.op("act", lambda e: e.activation(B["srn"][0:1, 0:8], B["srn"][0:1, 0:8],
                                       AF.Exp, bias=com.zero_col[0:1, 0:1],
                                       scale=-0.5),
         reads=[rk("srn"), rk("c_zero")], writes=[rk("srn")], tag=n + "ex")
    p.op("pe", lambda e: e.matmul(ps[6][:, 0:8], com.ones_row_f[0:1, :],
                                  B["srn"][0:1, 0:8], start=True, stop=True),
         reads=[rk("srn"), rk("c_ones_row")], writes=[rk(psn[6])], tag=n + "bc")
    p.op("dve", lambda e: e.tensor_copy(B["srnb"][:], ps[6][:, 0:8]),
         reads=[rk(psn[6])], writes=[rk("srnb")], tag=n + "be")
    for ch in range(8):
        p.op("dve", lambda e, ch=ch: e.tensor_tensor(
                simn_buf[:, ch * 8:(ch + 1) * 8],
                B["sim_f"][:, ch * 8:(ch + 1) * 8], B["srnb"][:], ALU.mult),
             reads=[rk("sim_f"), rk("srnb")], writes=[rk(simn_key)],
             tag=n + f"m{ch}")


def prep_sig(p: Prog, com: Common, B, gates_d, sigb_buf, sigb_key, name=""):
    n = name
    ps = B["ps8"]
    psn = B["ps8_name"]
    p.dma("sp", B["grow"][:], gates_d, writes=[rk("grow")], tag=n + "ld")
    p.op("act", lambda e: e.activation(B["grow"][0:1, :], B["grow"][0:1, :],
                                       AF.Exp, bias=com.zero_col[0:1, 0:1],
                                       scale=-1.0),
         reads=[rk("grow"), rk("c_zero")], writes=[rk("grow")], tag=n + "ex")
    p.op("dve", lambda e: e.tensor_scalar(B["grow"][0:1, :], B["grow"][0:1, :],
                                          1.0, None, ALU.add),
         reads=[rk("grow")], writes=[rk("grow")], tag=n + "a1")
    p.op("dve", lambda e: e.reciprocal(B["grow"][0:1, :], B["grow"][0:1, :]),
         reads=[rk("grow")], writes=[rk("grow")], tag=n + "rc")
    p.op("pe", lambda e: e.matmul(ps[6][:, 0:8], com.ones_row_f[0:1, :],
                                  B["grow"][0:1, 0:8], start=True, stop=True),
         reads=[rk("grow"), rk("c_ones_row")], writes=[rk(psn[6])], tag=n + "bc")
    p.op("dve", lambda e: e.tensor_copy(sigb_buf[:], ps[6][:, 0:8]),
         reads=[rk(psn[6])], writes=[rk(sigb_key)], tag=n + "be")


def emit_l0_ffn(p: Prog, com: Common, geo, B, wd, name=""):
    NT, T = geo["NT"], geo["T"]
    n = name
    ps = B["ps8"]
    psn = B["ps8_name"]
    sc = B["lnsc"]
    n1T = B["bigbf_a"]
    n2T = B["bigbf_b"]
    vx = B["v_ext"][:, 0:NT * 129].rearrange("p (t d) -> p t d", d=129)

    emit_ln(p, com, geo, B["x"], "x_res", n1T, "bigbf_a",
            affine=(B["ln1w"], B["ln1b"], "ln1wF", "ln1bF"), fuse_double=True,
            out_stats=(B["ystats"], "ystats"), scratch=sc)

    emit_gating(p, com, geo, B, n1T, "bigbf_a", B["simn1"], "simn1",
                B["sigb1"], "sigb1", B["rw"], "rw", 1.0, name=n + "g1")

    pst6q = lambda a, b: ps[6][:, a // 2: b // 2].bitcast(BF16)
    p.op("dve", lambda e: e.memset(vx[:, :, 128:129], 1.0),
         writes=[rk("v_ext")], tag=n + "vone")
    qkv_src = ("wq0", "wk0", "wv0")
    qkv_buf = ((B["wbig"], "wbig"), (B["wbig2"], "wbig2"), (B["wbig"], "wbig"))

    def ld_qkv(iw):
        wb_, wkk_ = qkv_buf[iw]
        for c4 in range(4):
            p.dma("pool", wb_[:, c4 * 2048:(c4 + 1) * 2048],
                  wd[qkv_src[iw]][:, c4 * 2048:(c4 + 1) * 2048],
                  writes=[rk(wkk_, 2 * c4), rk(wkk_, 2 * c4 + 1)],
                  tag=n + f"ld{qkv_src[iw]}.{c4}")

    ld_qkv(0)
    ld_qkv(1)
    for iw, wn in enumerate(("q", "k", "v")):
        wb, wkk = qkv_buf[iw]
        if iw == 1:
            ld_qkv(2)

        def rec_t(ti, wn=wn):
            tp = ti % 2
            dst = B["qT"] if wn == "q" else B["kT"]
            dk = "qT" if wn == "q" else "kT"
            qb = B["qbf"][:, tp * 128:(tp + 1) * 128]
            tb = (pst6q if tp == 0 else sc["pst"])
            tbk = psn[6] if tp == 0 else "ps7"
            p.op("pe", lambda e, qb=qb, tb=tb: e.transpose(
                    tb(0, 128), qb, com.ident_bf[:]),
                 reads=[rk("qbf", tp), rk("c_identbf")], writes=[rk(tbk)],
                 tag=n + f"t{wn}{ti}")
            p.op("act", lambda e, ti=ti, dst=dst, tb=tb: e.activation(
                    dst[:, ti * 128:(ti + 1) * 128], tb(0, 128),
                    AF.Copy, bias=0.0, scale=1.0),
                 reads=[rk(tbk)], writes=[rk(dk, ti)],
                 tag=n + f"e{wn}{ti}")

        for ti in range(NT):
            for half in range(2):
                for ch in range(8):
                    p.op("pe", lambda e, ti=ti, half=half, ch=ch, wb=wb:
                         e.matmul(
                            ps[half][:, 0:512],
                            n1T[:, ch * T + ti * 128: ch * T + ti * 128 + 128],
                            wb[:, ch * 1024 + half * 512:
                               ch * 1024 + half * 512 + 512],
                            start=(ch == 0), stop=(ch == 7)),
                         reads=[rk("bigbf_a", ti), rk(wkk, ch)],
                         writes=[rk(psn[half])], inc=(1 if ch == 7 else 0),
                         tag=n + f"P{wn}{ti}.{half}.{ch}")
            if wn != "v" and ti > 0:
                rec_t(ti - 1)
            tp = ti % 2
            qa = B["qacc"][:, tp * 128:(tp + 1) * 128]
            for e_ in range(8):
                bank = ps[e_ // 4]
                bnm = psn[e_ // 4]
                sl = (e_ % 4) * 128
                if e_ == 0:
                    p.op("dve", lambda e, bank=bank, sl=sl, ti=ti, qa=qa:
                         e.tensor_scalar(
                            qa, bank[:, sl:sl + 128],
                            B["rw"][:, ti * 8:ti * 8 + 1], None, ALU.mult),
                         reads=[rk(bnm), rk("rw", ti)], writes=[rk("qacc", tp)],
                         tag=n + f"W{wn}{ti}.0")
                else:
                    p.op("dve", lambda e, bank=bank, sl=sl, ti=ti, e_=e_, qa=qa:
                         e.scalar_tensor_tensor(
                            qa, bank[:, sl:sl + 128],
                            B["rw"][:, ti * 8 + e_:ti * 8 + e_ + 1],
                            qa, ALU.mult, ALU.add),
                         reads=[rk(bnm), rk("rw", ti), rk("qacc", tp)],
                         writes=[rk("qacc", tp)], tag=n + f"W{wn}{ti}.{e_}")
            if wn == "v":
                p.op("dve", lambda e, ti=ti, qa=qa: e.tensor_copy(
                        vx[:, ti, 0:128], qa),
                     reads=[rk("qacc", tp)], writes=[rk("v_ext")],
                     tag=n + f"vev{ti}")
            else:
                qb = B["qbf"][:, tp * 128:(tp + 1) * 128]
                p.op("dve", lambda e, qa=qa, qb=qb: e.tensor_copy(qb, qa),
                     reads=[rk("qacc", tp)], writes=[rk("qbf", tp)],
                     tag=n + f"c{wn}{ti}")
        if wn != "v":
            rec_t(NT - 1)

    p.op("dve", lambda e: e.memset(B["gm1"][:], 0.0), writes=[rk("gm1")],
         tag=n + "gm0")
    sq2v = B["PT"][:, 0:T]
    for ci, (buf, bk) in enumerate(((B["qT"], "qT"), (B["kT"], "kT"))):
        p.op("dve", lambda e, buf=buf, sq2v=sq2v: e.tensor_tensor(
                sq2v, buf[:, 0:T], buf[:, 0:T], ALU.mult),
             reads=[rk(bk, ti) for ti in range(NT)], writes=[rk("PT")],
             tag=n + f"s2{ci}")
        for w in range(T // 512):
            sb_, sbk_ = (ps[6], psn[6]) if w % 2 == 0 else (ps[5], psn[5])
            p.op("pe", lambda e, w=w, sq2v=sq2v, sb_=sb_: e.matmul(
                    sb_[0:1, 0:512], com.ones_col_bf[:],
                    sq2v[:, w * 512:(w + 1) * 512], start=True, stop=True),
                 reads=[rk("PT"), rk("c_ones_col")], writes=[rk(sbk_)],
                 tag=n + f"s2n{ci}.{w}")
            p.op("dve", lambda e, sb_=sb_: e.tensor_reduce(
                    B["n21"][0:1, 0:1], sb_[0:1, 0:512], AX.X, ALU.max),
                 reads=[rk(sbk_)], writes=[rk("n21")], tag=n + f"s2r{ci}.{w}")
            p.op("dve", lambda e, ci=ci: e.tensor_tensor(
                    B["gm1"][0:1, ci:ci + 1], B["gm1"][0:1, ci:ci + 1],
                    B["n21"][0:1, 0:1], ALU.max),
                 reads=[rk("n21"), rk("gm1")], writes=[rk("gm1")],
                 tag=n + f"s2m{ci}.{w}")
    p.op("dve", lambda e: e.tensor_tensor(B["gm1"][0:1, 0:1], B["gm1"][0:1, 0:1],
                                          B["gm1"][0:1, 1:2], ALU.mult),
         reads=[rk("gm1")], writes=[rk("gm1")], tag=n + "cbp")
    p.op("act", lambda e: e.activation(B["gm1"][0:1, 0:1], B["gm1"][0:1, 0:1],
                                       AF.Ln, bias=com.tiny[0:1, 0:1], scale=1.0),
         reads=[rk("gm1"), rk("c_tiny")], writes=[rk("gm1")], tag=n + "cbl")
    p.op("act", lambda e: e.activation(B["gm1"][0:1, 0:1], B["gm1"][0:1, 0:1],
                                       AF.Exp, bias=B["lnsm128"][0:1, 0:1],
                                       scale=0.5),
         reads=[rk("gm1"), rk("lnsm128")], writes=[rk("gm1")], tag=n + "cbx")
    p.op("dve", lambda e: e.tensor_scalar(B["gm1"][0:1, 0:1], B["gm1"][0:1, 0:1],
                                          -1.0, None, ALU.mult),
         reads=[rk("gm1")], writes=[rk("gm1")], tag=n + "cbn")
    p.op("pe", lambda e: e.matmul(ps[6][:, 0:1], com.ones_row_f[0:1, :],
                                  B["gm1"][0:1, 0:1], start=True, stop=True),
         reads=[rk("gm1"), rk("c_ones_row")], writes=[rk(psn[6])], tag=n + "cbb")
    p.op("dve", lambda e: e.tensor_copy(B["negc1"][:], ps[6][:, 0:1]),
         reads=[rk(psn[6])], writes=[rk("negc1")], tag=n + "cbe")

    p.dma("pool", B["wbig2"][:], wd["o"], writes=[rk("wbig2")], tag=n + "ldo")
    sm128 = 1.0 / float(np.sqrt(128.0))
    pst6 = lambda a, b: ps[6][:, a // 2: b // 2].bitcast(BF16)
    for qt in range(NT):
        qp = qt % 2
        for kc in range(qt + 1):
            sps, spn = (ps[kc % 2], psn[kc % 2])
            diag = (kc == qt)
            p.op("pe", lambda e, qt=qt, kc=kc, sps=sps, diag=diag: e.matmul(
                    sps[:, 0:128], B["kT"][:, kc * 128:(kc + 1) * 128],
                    B["qT"][:, qt * 128:(qt + 1) * 128], start=True,
                    stop=not diag),
                 reads=[rk("kT", ALL), rk("qT", ALL)], writes=[rk(spn)],
                 inc=(0 if diag else 1), tag=n + f"S{qt}.{kc}")
            if diag:
                p.op("pe", lambda e, sps=sps: e.matmul(
                        sps[:, 0:128], com.ident_bf[:], com.maskadd[:],
                        start=False, stop=True),
                     reads=[rk("c_identbf"), rk("c_maskadd")], writes=[rk(spn)],
                     tag=n + f"SM{qt}.{kc}")
            p.op("act", lambda e, kc=kc, sps=sps, sm128=sm128, qp=qp:
                 e.activation(
                    B["PT"][:, qp * T + kc * 128: qp * T + (kc + 1) * 128],
                    sps[:, 0:128],
                    AF.Exp, bias=B["negc1"][:, 0:1], scale=sm128),
                 reads=[rk(spn), rk("negc1")], writes=[rk("PT", (qp, kc))],
                 tag=n + f"E{qt}.{kc}")
        pvb, pvbn = ps[2 + qp], psn[2 + qp]
        for kc in range(qt + 1):
            p.op("pe", lambda e, qt=qt, kc=kc, qp=qp, pvb=pvb: e.matmul(
                    pvb[:, 0:129],
                    B["PT"][:, qp * T + kc * 128: qp * T + (kc + 1) * 128],
                    B["v_ext"][:, kc * 129:(kc + 1) * 129],
                    start=(kc == 0), stop=(kc == qt)),
                 reads=[rk("PT", (qp, kc)), rk("v_ext")], writes=[rk(pvbn)],
                 inc=(1 if kc == qt else 0), tag=n + f"PV{qt}.{kc}")
        p.op("dve", lambda e, qp=qp, pvb=pvb: e.reciprocal(
                B["recipc"][:, qp:qp + 1], pvb[:, 128:129]),
             reads=[rk(pvbn)], writes=[rk("recipc", qp)], tag=n + f"rc{qt}")
        p.op("dve", lambda e, qt=qt, qp=qp: e.tensor_scalar(
                B["rwm"][:, qp * 8:(qp + 1) * 8], B["rw"][:, qt * 8:(qt + 1) * 8],
                B["recipc"][:, qp:qp + 1], None, ALU.mult),
             reads=[rk("rw", qt), rk("recipc", qp)], writes=[rk("rwm", qp)],
             tag=n + f"rwm{qt}")
        for e_ in range(8):
            ep = e_ % 2
            tb = (pst6 if ep == 0 else sc["pst"])
            tbk = psn[6] if ep == 0 else "ps7"
            p.op("dve", lambda e, e_=e_, qp=qp, ep=ep, pvb=pvb: e.tensor_scalar(
                    B["qbf"][:, ep * 128:(ep + 1) * 128], pvb[:, 0:128],
                    B["rwm"][:, qp * 8 + e_:qp * 8 + e_ + 1], None, ALU.mult),
                 reads=[rk(pvbn), rk("rwm", qp)], writes=[rk("qbf", ep)],
                 tag=n + f"aw{qt}.{e_}")
            p.op("pe", lambda e, ep=ep, tb=tb: e.transpose(
                    tb(0, 128), B["qbf"][:, ep * 128:(ep + 1) * 128],
                    com.ident_bf[:]),
                 reads=[rk("qbf", ep), rk("c_identbf")], writes=[rk(tbk)],
                 tag=n + f"awt{qt}.{e_}")
            p.op("act", lambda e, qt=qt, e_=e_, tb=tb: e.activation(
                    n2T[:, e_ * T + qt * 128: e_ * T + qt * 128 + 128],
                    tb(0, 128), AF.Copy, bias=0.0, scale=1.0),
                 reads=[rk(tbk)], writes=[rk("bigbf_b", (e_, qt))],
                 tag=n + f"awe{qt}.{e_}")

    phase_barrier(p, ["bigbf_a"])
    hsb = n1T
    for ti in range(NT):
        for half in range(2):
            bank, bnm = ps[3 + half], psn[3 + half]
            for ch in range(8):
                p.op("pe", lambda e, ti=ti, half=half, ch=ch, bank=bank: e.matmul(
                        bank[:, 0:512],
                        n2T[:, ch * T + ti * 128: ch * T + ti * 128 + 128],
                        B["wbig2"][:, ch * 1024 + half * 512:
                                   ch * 1024 + half * 512 + 512],
                        start=(ch == 0), stop=(ch == 7)),
                     reads=[rk("bigbf_b", ALL), rk("wbig2")],
                     writes=[rk(bnm)], inc=(1 if ch == 7 else 0),
                     tag=n + f"O{ti}.{half}")
            xsl = B["x"][:, ti * 1024 + half * 512: ti * 1024 + half * 512 + 512]
            p.op("dve", lambda e, ti=ti, xsl=xsl: e.tensor_scalar(
                    B["yscr"][:], xsl, B["ystats"][:, 2 * ti:2 * ti + 1],
                    B["ystats"][:, 2 * ti + 1:2 * ti + 2], ALU.mult, ALU.add),
                 reads=[rk("x_res", ti), rk("ystats", ti)], writes=[rk("yscr")],
                 tag=n + f"yr{ti}.{half}")
            p.op("dve", lambda e, ti=ti, half=half, bank=bank: e.tensor_tensor(
                    hsb[:, ti * 1024 + half * 512: ti * 1024 + half * 512 + 512],
                    B["yscr"][:], bank[:, 0:512], ALU.add),
                 reads=[rk(bnm), rk("yscr")], writes=[rk("bigbf_a", ti)],
                 tag=n + f"hs{ti}.{half}")
            p.op("dve", lambda e, bank=bank, xsl=xsl: e.scalar_tensor_tensor(
                    xsl, bank[:, 0:512], 0.5, xsl, ALU.mult, ALU.add),
                 reads=[rk(bnm), rk("x_res", ti), rk("yscr")],
                 writes=[rk("x_res", ti)], tag=n + f"xa{ti}.{half}")
            p.op("dve", lambda e, xsl=xsl: e.scalar_tensor_tensor(
                    xsl, B["yscr"][:], 0.5, xsl, ALU.mult, ALU.add),
                 reads=[rk("yscr"), rk("x_res", ti)], writes=[rk("x_res", ti)],
                 tag=n + f"xy{ti}.{half}")

    phase_barrier(p, ["bigbf_b"])
    emit_ln(p, com, geo, hsb, "bigbf_a", n2T, "bigbf_b",
            affine=(B["ln2w"], B["ln2b"], "ln2wF", "ln2bF"), scratch=sc)
    emit_gating(p, com, geo, B, n2T, "bigbf_b", B["simn2"], "simn2",
                B["sigb2"], "sigb2", B["rw2"], "rw2", 0.5, name=n + "g2")

    phase_barrier(p, ["bigbf_a"])
    He = n1T
    for ex in range(8):
        wslot = ex % 2
        wb1 = B["wbig"] if wslot == 0 else B["wbig2"]
        wk1 = "wbig" if wslot == 0 else "wbig2"
        wb2 = B["wbig2"] if wslot == 0 else B["wbig"]
        wk2 = "wbig2" if wslot == 0 else "wbig"
        for ic in range(8):
            p.dma("pool", wb1[:, ic * 1024:(ic + 1) * 1024],
                  wd["w1"][ex][:, ic * 1024:(ic + 1) * 1024],
                  writes=[rk(wk1, ic)], tag=n + f"lw1.{ex}.{ic}")
        for ic in range(8):
            p.dma("pool", wb2[:, ic * 1024:(ic + 1) * 1024],
                  wd["w2"][ex][:, ic * 1024:(ic + 1) * 1024],
                  writes=[rk(wk2, ic)], tag=n + f"lw2.{ex}.{ic}")
        w1banks = (0, 1, 6, 7)
        for ic in range(8):
            for w in range(T // 512):
                bi = w1banks[(ic * (T // 512) + w) % 4]
                bank, bnm = ps[bi], psn[bi]
                for ch in range(8):
                    p.op("pe", lambda e, ic=ic, w=w, ch=ch, bank=bank, wb1=wb1:
                         e.matmul(
                            bank[:, 0:512],
                            wb1[:, ic * 1024 + ch * 128: ic * 1024 + ch * 128 + 128],
                            n2T[:, ch * T + w * 512: ch * T + w * 512 + 512],
                            start=(ch == 0), stop=(ch == 7)),
                         reads=[rk(wk1, ic), rk("bigbf_b", ALL)],
                         writes=[rk(bnm)], inc=(1 if ch == 7 else 0),
                         tag=n + f"H{ex}.{ic}.{w}")
                p.op("act", lambda e, ic=ic, w=w, bank=bank: e.activation(
                        He[:, ic * T + w * 512: ic * T + w * 512 + 512],
                        bank[:, 0:512], AF.Gelu, bias=com.zero_col[:, 0:1],
                        scale=1.0),
                     reads=[rk(bnm), rk("c_zero")],
                     writes=[rk("bigbf_a", (ic, w))], tag=n + f"G{ex}.{ic}.{w}")
        for ti in range(NT):
            for half in range(2):
                bi = 2 + (2 * ti + half) % 4
                bank, bnm = ps[bi], psn[bi]
                for ic in range(8):
                    p.op("pe", lambda e, ti=ti, half=half, ic=ic, bank=bank,
                         wb2=wb2: e.matmul(
                            bank[:, 0:512],
                            He[:, ic * T + ti * 128: ic * T + ti * 128 + 128],
                            wb2[:, ic * 1024 + half * 512:
                                ic * 1024 + half * 512 + 512],
                            start=(ic == 0), stop=(ic == 7)),
                         reads=[rk("bigbf_a", ALL), rk(wk2, ic)],
                         writes=[rk(bnm)], inc=(1 if ic == 7 else 0),
                         tag=n + f"D{ex}.{ti}.{half}")
                xsl = B["x"][:, ti * 1024 + half * 512:
                             ti * 1024 + half * 512 + 512]
                p.op("dve", lambda e, ti=ti, ex=ex, bank=bank, xsl=xsl:
                     e.scalar_tensor_tensor(
                        xsl, bank[:, 0:512],
                        B["rw2"][:, ti * 8 + ex: ti * 8 + ex + 1], xsl,
                        ALU.mult, ALU.add),
                     reads=[rk(bnm), rk("rw2", ti), rk("x_res", ti)],
                     writes=[rk("x_res", ti)], tag=n + f"xm{ex}.{ti}.{half}")


def l0ffn_buffers(p: Prog, com: Common, geo, x_buf, bigbf_a, bigbf_b,
                  ps8, ps8_name, lnsc, gat, pre=""):
    import math
    _sbuf0 = p.sbuf
    p_sbuf = lambda nm, sh, dt: _sbuf0(pre + nm, sh, dt)
    NT, T = geo["NT"], geo["T"]
    B = {
        "x": x_buf, "bigbf_a": bigbf_a, "bigbf_b": bigbf_b,
        "ystats": p_sbuf("ystats", [128, 2 * NT], F32),
        "yscr": p_sbuf("yscr", [128, 512], F32),
        "ps8": ps8, "ps8_name": ps8_name, "lnsc": lnsc, "gat": gat,
        "v_ext": p_sbuf("v_ext", [128, NT * 130], BF16),
        "qT": p_sbuf("qT", [128, T], BF16),
        "kT": p_sbuf("kT", [128, T], BF16),
        "PT": p_sbuf("PT", [128, 2 * T], BF16),
        "sq2w": p_sbuf("sq2w", [128, 512], BF16),
        "qacc": p_sbuf("qacc", [128, 256], F32),
        "qbf": p_sbuf("qbf", [128, 256], BF16),
        "rw": p_sbuf("rw", [128, NT * 8], F32),
        "rw2": p_sbuf("rw2", [128, NT * 8], F32),
        "rwm": p_sbuf("rwm", [128, 16], F32),
        "recipc": p_sbuf("recipc", [128, 2], F32),
        "wbig": p_sbuf("wbig", [128, 8192], BF16),
        "wbig2": p_sbuf("wbig2", [128, 8192], BF16),
        "simn1": p_sbuf("simn1", [128, 64], BF16),
        "simn2": p_sbuf("simn2", [128, 64], BF16),
        "sigb1": p_sbuf("sigb1", [128, 8], F32),
        "sigb2": p_sbuf("sigb2", [128, 8], F32),
        "sim_f": p_sbuf("sim_f", [128, 64], F32),
        "sim_sq": p_sbuf("sim_sq", [128, 64], F32),
        "srn": p_sbuf("srn", [1, 8], F32),
        "srnb": p_sbuf("srnb", [128, 8], F32),
        "grow": p_sbuf("grow", [1, 8], F32),
        "ones_col_f": p_sbuf("ones_col_f", [128, 1], F32),
        "gm1": p_sbuf("gm1", [1, 2], F32),
        "n21": p_sbuf("n21", [1, 1], F32),
        "negc1": p_sbuf("negc1", [128, 1], F32),
        "lnsm128": p_sbuf("lnsm128", [128, 1], F32),
        "ln1w": p_sbuf("ln1wF", [128, 8], F32),
        "ln1b": p_sbuf("ln1bF", [128, 8], F32),
        "ln2w": p_sbuf("ln2wF", [128, 8], F32),
        "ln2b": p_sbuf("ln2bF", [128, 8], F32),
    }
    p.op("dve", lambda e: e.memset(B["ones_col_f"][:], 1.0),
         writes=[rk("ones_col_f")])
    p.op("dve", lambda e: e.memset(B["lnsm128"][:], math.log(1.0 / math.sqrt(128.0))),
         writes=[rk("lnsm128")])
    return B


def l0ffn_load_weights(p: Prog, B, wd):
    p.dma("sp", B["ln1w"][:], wd["ln1wF"], writes=[rk("ln1wF")])
    p.dma("sp", B["ln1b"][:], wd["ln1bF"], writes=[rk("ln1bF")])
    p.dma("sp", B["ln2w"][:], wd["ln2wF"], writes=[rk("ln2wF")])
    p.dma("sp", B["ln2b"][:], wd["ln2bF"], writes=[rk("ln2bF")])



from concourse.bass_utils import run_bass_kernel_spmd

GEO = {"NT": 16, "T": 2048}
NTK, TTOK = GEO["NT"], GEO["T"]


def _base_prog():
    nc = bass.Bass("TRN2", target_bir_lowering=False)
    x_d = nc.dram_tensor("x", [128, NTK * 1024], F32, kind="ExternalInput").ap()
    out_d = nc.dram_tensor("out", [128, NTK * 1024], F32,
                           kind="ExternalOutput").ap()
    p = Prog(nc)
    com = Common(p)
    ps, psn, ps01, ps23, hs = alloc_psum_pool(p)
    lnsc = ln_scratch(p, hs[7], hs[6])
    x_sb = p.sbuf("x_res", [128, NTK * 1024], F32)
    for ti in range(NTK):
        p.dma("sp", x_sb[:, ti * 1024:(ti + 1) * 1024],
              x_d[:, ti * 1024:(ti + 1) * 1024], writes=[rk("x_res", ti)])
    return nc, p, com, ps, psn, lnsc, x_sb, out_d, ps01, ps23


def _finish(nc, p, x_sb, out_d):
    for ti in range(NTK):
        p.dma("sp", out_d[:, ti * 1024:(ti + 1) * 1024],
              x_sb[:, ti * 1024:(ti + 1) * 1024], reads=[rk("x_res", ti)])
    p.build()
    return nc


def build_l0ffn():
    nc, p, com, ps, psn, lnsc, x_sb, out_d, ps01, ps23 = _base_prog()
    wd = {}
    for wn in ("wq0", "wk0", "wv0", "o"):
        wd[wn] = nc.dram_tensor(wn, [128, 8192], F32, kind="ExternalInput").ap()
    wd["w1"] = nc.dram_tensor("w1", [8, 128, 8192], F32, kind="ExternalInput").ap()
    wd["w2"] = nc.dram_tensor("w2", [8, 128, 8192], F32, kind="ExternalInput").ap()
    wd["sim1"] = nc.dram_tensor("sim1", [128, 64], F32, kind="ExternalInput").ap()
    wd["sim2"] = nc.dram_tensor("sim2", [128, 64], F32, kind="ExternalInput").ap()
    wd["g1"] = nc.dram_tensor("g1", [1, 8], F32, kind="ExternalInput").ap()
    wd["g2"] = nc.dram_tensor("g2", [1, 8], F32, kind="ExternalInput").ap()
    for wn in ("ln1wF", "ln1bF", "ln2wF", "ln2bF"):
        wd[wn] = nc.dram_tensor(wn + "_d", [128, 8], F32,
                                kind="ExternalInput").ap()
    gat = gating_scratch(p, GEO)
    bigbf_a = p.sbuf("bigbf_a", [128, 8 * TTOK], BF16)
    bigbf_b = p.sbuf("bigbf_b", [128, 8 * TTOK], BF16)
    B = l0ffn_buffers(p, com, GEO, x_sb, bigbf_a, bigbf_b, ps, psn,
                      lnsc, gat)
    l0ffn_load_weights(p, B, wd)
    prep_sim(p, com, B, wd["sim1"], B["simn1"], "simn1", name="s1")
    prep_sim(p, com, B, wd["sim2"], B["simn2"], "simn2", name="s2")
    prep_sig(p, com, B, wd["g1"], B["sigb1"], "sigb1", name="G1")
    prep_sig(p, com, B, wd["g2"], B["sigb2"], "sigb2", name="G2")
    emit_l0_ffn(p, com, GEO, B, wd, name="F")
    return _finish(nc, p, x_sb, out_d)


def build_mha(biased):
    nc, p, com, ps, psn, lnsc, x_sb, out_d, ps01, ps23 = _base_prog()
    wd = {}
    for wn in ("wq", "wk", "wv", "wo"):
        wd[wn] = nc.dram_tensor(wn, [128, 8192], F32, kind="ExternalInput").ap()
    if biased:
        wd["bq"] = nc.dram_tensor("bq", [128, 8], F32, kind="ExternalInput").ap()
        wd["bk"] = nc.dram_tensor("bk", [128, 8], F32, kind="ExternalInput").ap()
        wd["bv"] = nc.dram_tensor("bv", [1, 1024], F32, kind="ExternalInput").ap()
    yT = p.sbuf("bigbf_a", [128, 8 * TTOK], BF16)
    zT = p.sbuf("bigbf_b", [128, 8 * TTOK], BF16)
    B = mha_buffers(p, GEO, x_sb, yT, zT, ps, psn, ps[6], psn[6],
                    lnsc, biased, 0.125 if biased else 1.0,
                    ps01=ps01, ps23=ps23)
    emit_mha(p, com, GEO, B, wd, biased=biased, name="A")
    return _finish(nc, p, x_sb, out_d)


def build_l1ffn():
    nc, p, com, ps, psn, lnsc, x_sb, out_d, ps01, ps23 = _base_prog()
    wd = {
        "W1T": nc.dram_tensor("W1T", [128, 32 * 1024], F32,
                              kind="ExternalInput").ap(),
        "W2T": nc.dram_tensor("W2T", [128, 32 * 1024], F32,
                              kind="ExternalInput").ap(),
        "b1F": nc.dram_tensor("b1F_d", [128, 32], F32, kind="ExternalInput").ap(),
        "b2": nc.dram_tensor("b2_d", [1, 1024], F32, kind="ExternalInput").ap(),
        "scale": nc.dram_tensor("scale_d", [1, 1], F32, kind="ExternalInput").ap(),
    }
    yT = p.sbuf("bigbf_a", [128, 8 * TTOK], BF16)
    H = p.sbuf("bigbf_b", [128, 16384], BF16)
    B = l1ffn_buffers(p, GEO, x_sb, yT, H, ps, psn, lnsc)
    emit_l1_ffn(p, com, GEO, B, wd, name="F")
    return _finish(nc, p, x_sb, out_d)



def _hcm(w):
    X = w.shape[1]
    return np.ascontiguousarray(
        w.reshape(8, 128, X).transpose(1, 0, 2).reshape(128, 8 * X))


def _pair_major(w_cf):
    return np.ascontiguousarray(
        w_cf.reshape(8, 128, 8, 128).transpose(1, 2, 0, 3).reshape(128, 8192))


def _tok_major(x2d):
    return np.ascontiguousarray(
        x2d.reshape(NTK, 128, 1024).transpose(1, 0, 2).reshape(128, NTK * 1024))


def _prep_inputs(I):
    f32 = np.float32
    W = {}
    W["l0f"] = {
        "wq0": _hcm(np.ascontiguousarray(
            I["L0_ffn_q_proj"].transpose(1, 0, 2).reshape(1024, 1024))),
        "wk0": _hcm(np.ascontiguousarray(
            I["L0_ffn_k_proj"].transpose(1, 0, 2).reshape(1024, 1024))),
        "wv0": _hcm(np.ascontiguousarray(
            I["L0_ffn_v_proj"].transpose(1, 0, 2).reshape(1024, 1024))),
        "o": _hcm(np.ascontiguousarray(I["L0_ffn_o_proj"].reshape(1024, 1024))),
        "w1": np.stack([_hcm(I["L0_ffn_w1"][e]).reshape(128, 8, 8, 128)
                        .transpose(0, 2, 1, 3).reshape(128, 8192)
                        for e in range(8)]),
        "w2": np.stack([_hcm(I["L0_ffn_w2"][e]) for e in range(8)]),
        "sim1": _hcm(np.ascontiguousarray(I["L0_ffn_smha_sim"])),
        "sim2": _hcm(np.ascontiguousarray(I["L0_ffn_moe_sim"])),
        "g1": I["L0_ffn_smha_gates"].reshape(1, 8).astype(f32),
        "g2": I["L0_ffn_moe_gates"].reshape(1, 8).astype(f32),
        "ln1wF_d": np.ascontiguousarray(I["L0_ffn_ln1_w"].reshape(8, 128).T),
        "ln1bF_d": np.ascontiguousarray(I["L0_ffn_ln1_b"].reshape(8, 128).T),
        "ln2wF_d": np.ascontiguousarray(I["L0_ffn_ln2_w"].reshape(8, 128).T),
        "ln2bF_d": np.ascontiguousarray(I["L0_ffn_ln2_b"].reshape(8, 128).T),
    }
    W["l0a"] = {
        "wq": _pair_major(np.ascontiguousarray(
            I["L0_attn_Wq"].transpose(2, 0, 1).reshape(1024, 1024))),
        "wk": _pair_major(np.ascontiguousarray(
            I["L0_attn_Wk"].transpose(2, 0, 1).reshape(1024, 1024))),
        "wv": _pair_major(np.ascontiguousarray(
            I["L0_attn_Wv"].transpose(2, 0, 1).reshape(1024, 1024))),
        "wo": _hcm(np.ascontiguousarray(I["L0_attn_proj"])),
        "bq": np.ascontiguousarray(I["L0_attn_bq"].reshape(-1).reshape(8, 128).T),
        "bk": np.ascontiguousarray(I["L0_attn_bk"].reshape(-1).reshape(8, 128).T),
        "bv": I["L0_attn_bv"].reshape(1, 1024).astype(f32),
    }
    W["l1a"] = {
        "wq": _pair_major(np.ascontiguousarray(I["L1_attn_wq"].T)),
        "wk": _pair_major(np.ascontiguousarray(I["L1_attn_wk"].T)),
        "wv": _pair_major(np.ascontiguousarray(I["L1_attn_wv"].T)),
        "wo": _hcm(np.ascontiguousarray(I["L1_attn_wo"].T)),
    }
    W["l1f"] = {
        "W1T": np.ascontiguousarray(
            I["L1_ffn_W1"].T.reshape(8, 128, 32, 128).transpose(1, 2, 0, 3)
            .reshape(128, 32768)),
        "W2T": np.ascontiguousarray(
            I["L1_ffn_W2"].T.reshape(32, 128, 1024).transpose(1, 0, 2)
            .reshape(128, 32768)),
        "b1F_d": np.ascontiguousarray(I["L1_ffn_b1"].reshape(32, 128).T),
        "b2_d": I["L1_ffn_b2"].reshape(1, 1024).astype(f32),
        "scale_d": np.asarray(I["L1_ffn_scale"], f32).reshape(1, 1),
    }
    for d in W.values():
        for k in d:
            d[k] = np.ascontiguousarray(np.asarray(d[k], f32))
    return W


_PROG_CACHE = {}


def _get_progs():
    if not _PROG_CACHE:
        _PROG_CACHE["l0f"] = build_l0ffn()
        _PROG_CACHE["l0a"] = build_mha(True)
        _PROG_CACHE["l1a"] = build_mha(False)
        _PROG_CACHE["l1f"] = build_l1ffn()
    return _PROG_CACHE



def build_fused():
    nc = bass.Bass("TRN2", target_bir_lowering=False)
    x_d = nc.dram_tensor("x", [128, NTK * 1024], F32, kind="ExternalInput").ap()
    out_d = nc.dram_tensor("out", [128, NTK * 1024], F32,
                           kind="ExternalOutput").ap()
    wd_l0f = {}
    for wn in ("wq0", "wk0", "wv0", "o"):
        wd_l0f[wn] = nc.dram_tensor("l0f_" + wn, [128, 8192], F32,
                                    kind="ExternalInput").ap()
    wd_l0f["w1"] = nc.dram_tensor("l0f_w1", [8, 128, 8192], F32,
                                  kind="ExternalInput").ap()
    wd_l0f["w2"] = nc.dram_tensor("l0f_w2", [8, 128, 8192], F32,
                                  kind="ExternalInput").ap()
    wd_l0f["sim1"] = nc.dram_tensor("l0f_sim1", [128, 64], F32,
                                    kind="ExternalInput").ap()
    wd_l0f["sim2"] = nc.dram_tensor("l0f_sim2", [128, 64], F32,
                                    kind="ExternalInput").ap()
    wd_l0f["g1"] = nc.dram_tensor("l0f_g1", [1, 8], F32,
                                  kind="ExternalInput").ap()
    wd_l0f["g2"] = nc.dram_tensor("l0f_g2", [1, 8], F32,
                                  kind="ExternalInput").ap()
    for wn in ("ln1wF", "ln1bF", "ln2wF", "ln2bF"):
        wd_l0f[wn] = nc.dram_tensor("l0f_" + wn, [128, 8], F32,
                                    kind="ExternalInput").ap()
    wd_l0a = {}
    for wn in ("wq", "wk", "wv", "wo"):
        wd_l0a[wn] = nc.dram_tensor("l0a_" + wn, [128, 8192], F32,
                                    kind="ExternalInput").ap()
    wd_l0a["bq"] = nc.dram_tensor("l0a_bq", [128, 8], F32,
                                  kind="ExternalInput").ap()
    wd_l0a["bk"] = nc.dram_tensor("l0a_bk", [128, 8], F32,
                                  kind="ExternalInput").ap()
    wd_l0a["bv"] = nc.dram_tensor("l0a_bv", [1, 1024], F32,
                                  kind="ExternalInput").ap()
    wd_l1a = {}
    for wn in ("wq", "wk", "wv", "wo"):
        wd_l1a[wn] = nc.dram_tensor("l1a_" + wn, [128, 8192], F32,
                                    kind="ExternalInput").ap()
    wd_l1f = {
        "W1T": nc.dram_tensor("l1f_W1T", [128, 32768], F32,
                              kind="ExternalInput").ap(),
        "W2T": nc.dram_tensor("l1f_W2T", [128, 32768], F32,
                              kind="ExternalInput").ap(),
        "b1F": nc.dram_tensor("l1f_b1F", [128, 32], F32,
                              kind="ExternalInput").ap(),
        "b2": nc.dram_tensor("l1f_b2", [1, 1024], F32,
                             kind="ExternalInput").ap(),
        "scale": nc.dram_tensor("l1f_scale", [1, 1], F32,
                                kind="ExternalInput").ap(),
    }

    p = Prog(nc)
    com = Common(p)
    ps, psn, ps01, ps23, hs = alloc_psum_pool(p)
    lnsc = ln_scratch(p, hs[7], hs[6])
    x_sb = p.sbuf("x_res", [128, NTK * 1024], F32)
    bigbf_a = p.sbuf("bigbf_a", [128, 8 * TTOK], BF16)
    bigbf_b = p.sbuf("bigbf_b", [128, max(8 * TTOK, 16384)], BF16)
    p.sbuf("barsc", [1, 1], F32)

    for ti in range(NTK):
        p.dma("sp", x_sb[:, ti * 1024:(ti + 1) * 1024],
              x_d[:, ti * 1024:(ti + 1) * 1024], writes=[rk("x_res", ti)])

    def phase_l0ffn(pre):
        p.push_scope()
        gat = gating_scratch(p, GEO, pre=pre)
        B = l0ffn_buffers(p, com, GEO, x_sb, bigbf_a, bigbf_b, ps, psn,
                          lnsc, gat, pre=pre)
        l0ffn_load_weights(p, B, wd_l0f)
        prep_sim(p, com, B, wd_l0f["sim1"], B["simn1"], "simn1", name=pre + "s1")
        prep_sim(p, com, B, wd_l0f["sim2"], B["simn2"], "simn2", name=pre + "s2")
        prep_sig(p, com, B, wd_l0f["g1"], B["sigb1"], "sigb1", name=pre + "G1")
        prep_sig(p, com, B, wd_l0f["g2"], B["sigb2"], "sigb2", name=pre + "G2")
        emit_l0_ffn(p, com, GEO, B, wd_l0f, name=pre)
        p.pop_scope()
        p.fence()

    def phase_mha(pre, wd, biased, hook=None):
        p.push_scope()
        B = mha_buffers(p, GEO, x_sb, bigbf_a, bigbf_b, ps, psn,
                        ps[6], psn[6], lnsc, biased,
                        0.125 if biased else 1.0, pre=pre,
                        ps01=ps01, ps23=ps23)
        emit_mha(p, com, GEO, B, wd, biased=biased, name=pre,
                 per_hp_hook=hook)
        p.pop_scope()
        p.fence()

    w1bf = nc.dram_tensor("l1f_w1bf", [128, 32768], BF16,
                          kind="Internal").ap()
    w2bf = nc.dram_tensor("l1f_w2bf", [128, 32768], BF16,
                          kind="Internal").ap()
    l1f_wbf = [(w1bf, w2bf)]

    def prepass_hook(hp):
        src, dst, qh, key = ((wd_l1f["W1T"], w1bf, hp, "w1bf_d") if hp < 4
                             else (wd_l1f["W2T"], w2bf, hp - 4, "w2bf_d"))
        p.dma("pool", dst[:, qh * 8192:(qh + 1) * 8192],
              src[:, qh * 8192:(qh + 1) * 8192],
              writes=[rk(key, qh)], tag=f"Bpc.{hp}")

    def phase_l1ffn(pre):
        p.push_scope()
        B = l1ffn_buffers(p, GEO, x_sb, bigbf_a, bigbf_b, ps, psn, lnsc,
                          pre=pre)
        emit_l1_ffn(p, com, GEO, B, wd_l1f, name=pre, wbf=l1f_wbf[0],
                    do_prepass=False)
        p.pop_scope()
        p.fence()

    phase_l0ffn("A_")
    phase_mha("B_", wd_l0a, True, hook=prepass_hook)
    phase_l0ffn("C_")
    phase_l1ffn("D_")
    phase_mha("E_", wd_l1a, False)
    phase_l1ffn("F_")

    for ti in range(NTK):
        p.dma("sp", out_d[:, ti * 1024:(ti + 1) * 1024],
              x_sb[:, ti * 1024:(ti + 1) * 1024], reads=[rk("x_res", ti)])
    p.build()
    return nc


def _fused_inputs(W, x):
    m = {"x": x}
    for k, v in W["l0f"].items():
        key = {"ln1wF_d": "ln1wF", "ln1bF_d": "ln1bF",
               "ln2wF_d": "ln2wF", "ln2bF_d": "ln2bF"}.get(k, k)
        m["l0f_" + key] = v
    for k, v in W["l0a"].items():
        m["l0a_" + k] = v
    for k, v in W["l1a"].items():
        m["l1a_" + k] = v
    for k, v in W["l1f"].items():
        key = {"b1F_d": "b1F", "b2_d": "b2", "scale_d": "scale"}.get(k, k)
        m["l1f_" + key] = v
    return m


FUSED = True


def kernel(**inputs):
    I = {k: np.asarray(v) for k, v in inputs.items()}
    W = _prep_inputs(I)
    xs = [_tok_major(np.asarray(I["L0_attn_x"][b], np.float32))
          for b in range(8)]
    if FUSED:
        if "fused" not in _PROG_CACHE:
            _PROG_CACHE["fused"] = build_fused()
        nc = _PROG_CACHE["fused"]
        in_maps = [_fused_inputs(W, xs[b]) for b in range(8)]
        res = run_bass_kernel_spmd(nc, in_maps, core_ids=list(range(8)))
        xs = [np.asarray(res.results[b]["out"], np.float32) for b in range(8)]
    else:
        progs = _get_progs()
        phases = [("l0f", "l0f"), ("l0a", "l0a"), ("l0f", "l0f"),
                  ("l1f", "l1f"), ("l1a", "l1a"), ("l1f", "l1f")]
        for prog_key, wkey in phases:
            in_maps = [dict(W[wkey], x=xs[b]) for b in range(8)]
            res = run_bass_kernel_spmd(progs[prog_key], in_maps,
                                       core_ids=list(range(8)))
            xs = [np.asarray(res.results[b]["out"], np.float32)
                  for b in range(8)]
    out = np.stack([xs[b].reshape(128, NTK, 1024).transpose(1, 0, 2)
                    .reshape(TTOK, 1024) for b in range(8)])
    return out.astype(np.float32)
